# version 01
# baseline (speedup 1.0000x reference)
import numpy as np
import concourse.bass as bass
import concourse.mybir as mybir
from concourse.bass_utils import run_bass_kernel_spmd
from contextlib import ExitStack

F32 = mybir.dt.float32
BF16 = mybir.dt.bfloat16
I32 = mybir.dt.int32
AF = mybir.ActivationFunctionType
ALU = mybir.AluOpType
AX = mybir.AxisListType

NCORES = 8
TWO_PI = 6.283185307179586

ENGN = {"te": "tensor", "ve": "vector", "sc": "scalar", "gp": "gpsimd", "sy": "sync"}
ENG_SEM = {"te": "MM", "ve": "EV", "sc": "SC", "gp": "GS", "sy": "LS"}


def default_params():
    return dict(T=2048, D=1024, HD=64, F=4096, E=8, NROT=32, dbg=(), cut=99)


class Bld:
    def __init__(self, nc, es):
        self.nc = nc
        self.es = es
        self.sem = {}
        for s in ["LS", "GS", "AS", "MM", "EV", "SC", "CC", "X1", "X2", "X3", "X4", "X5"]:
            self.sem[s] = es.enter_context(nc.semaphore(s))
        self.c = {s: 0 for s in self.sem}

    _uid = [0]

    def _nm(self, name):
        self._uid[0] += 1
        return f"{name}_u{self._uid[0]}"

    def sb(self, es_, name, shape, dt=BF16):
        return es_.enter_context(self.nc.sbuf_tensor(self._nm(name), shape, dt))

    def ps(self, es_, name, shape, dt=F32):
        return es_.enter_context(self.nc.psum_tensor(self._nm(name), shape, dt))

    def snap(self):
        return dict(self.c)

    def seq(self, name, ops):
        nc = self.nc
        base = self.snap()
        prior, counts, incs = [], {k: 0 for k in ENG_SEM}, []
        for (ek, fn, is_dma) in ops:
            prior.append(dict(counts))
            inc = 16 if is_dma is True else 1
            counts[ek] += inc
            incs.append(inc)
        with nc.Block() as blk:
            for ek in ["te", "ve", "sc", "gp", "sy"]:
                if not any(e == ek for (e, _, _) in ops):
                    continue

                def mk(ek):
                    def closure(eng):
                        eng.wait_ge(self.sem["LS"], base["LS"])
                        eng.wait_ge(self.sem["GS"], base["GS"])
                        for i, (e, fn, is_dma) in enumerate(ops):
                            if e != ek:
                                continue
                            for oe, osem in ENG_SEM.items():
                                if oe != ek and prior[i][oe] > 0:
                                    eng.wait_ge(self.sem[osem], base[osem] + prior[i][oe])
                            if prior[i][ek] > 0:
                                eng.wait_ge(self.sem[ENG_SEM[ek]], base[ENG_SEM[ek]] + prior[i][ek])
                            inst = fn(eng)
                            assert inst is not None, f"seq op {i} in {name} returned None"
                            inst.then_inc(self.sem[ENG_SEM[ek]], incs[i])
                            if ek in ("ve", "sc", "gp") and not is_dma:
                                eng.drain()
                    return closure

                getattr(blk, ENGN[ek])(mk(ek))
        for ek, n in counts.items():
            self.c[ENG_SEM[ek]] += n

    def proj(self, name, n_oi, n_tc, n_kc, psum_shape, stat_fn, mov_fn, steps,
             wload=None, nps=2):
        nc = self.nc
        base = self.snap()
        ntiles = n_oi * n_tc
        nsteps = len(steps)
        step_inc = [16 if ek in ("sy", "gp") else 1 for (ek, _) in steps]
        bsem = ["MM"] + [f"X{j}" for j in range(1, nsteps)]
        binc = [1] + step_inc[:-1]
        wfn, wn = wload if wload else (None, 0)
        with ExitStack() as pes:
            psl = [pes.enter_context(nc.psum_tensor(f"{name}_ps{j}", list(psum_shape), F32))
                   for j in range(nps)]
            def emit_step(eng, j, i, oi, tc):
                sek, sfn = steps[j]
                eng.wait_ge(self.sem[bsem[j]], base[bsem[j]] + (i + 1) * binc[j])
                inst = sfn(eng, psl[i % nps], i, oi, tc)
                osem = "EV" if j == nsteps - 1 else bsem[j + 1]
                inst.then_inc(self.sem[osem], step_inc[j])
                if sek in ("ve", "sc"):
                    eng.drain()

            with nc.Block() as blk:
                @blk.tensor
                def _(te):
                    te.wait_ge(self.sem["LS"], base["LS"])
                    te.wait_ge(self.sem["GS"], base["GS"])
                    for i in range(ntiles):
                        oi, tc = divmod(i, n_tc)
                        if wfn and tc == 0:
                            te.wait_ge(self.sem["GS"], base["GS"] + 16 * wn * (oi + 1))
                        if i >= nps:
                            te.wait_ge(self.sem["EV"],
                                       base["EV"] + (i - nps + 1) * step_inc[-1])
                        inst = None
                        for kc in range(n_kc):
                            inst = te.matmul(psl[i % nps][:, :], stat_fn(kc, oi, tc),
                                             mov_fn(kc, oi, tc),
                                             start=(kc == 0), stop=(kc == n_kc - 1))
                        inst.then_inc(self.sem["MM"], 1)
                        for j, (sek, _) in enumerate(steps):
                            if sek == "te":
                                emit_step(te, j, i, oi, tc)

                engs = []
                for (ek, _) in steps:
                    if ek not in engs and ek not in ("te", "gp"):
                        engs.append(ek)
                for ek in engs:
                    def mk(ek):
                        def closure(eng):
                            eng.wait_ge(self.sem["LS"], base["LS"])
                            eng.wait_ge(self.sem["GS"], base["GS"])
                            for i in range(ntiles):
                                oi, tc = divmod(i, n_tc)
                                for j, (sek, _) in enumerate(steps):
                                    if sek != ek:
                                        continue
                                    emit_step(eng, j, i, oi, tc)
                        return closure
                    getattr(blk, ENGN[ek])(mk(ek))

                has_gp_steps = any(ek == "gp" for (ek, _) in steps)
                if wfn or has_gp_steps:
                    @blk.gpsimd
                    def _(gp):
                        gp.wait_ge(self.sem["LS"], base["LS"])
                        if wfn and not has_gp_steps:
                            for oi in range(n_oi):
                                if oi >= 2:
                                    gp.wait_ge(self.sem["MM"], base["MM"] + (oi - 1) * n_tc)
                                wfn(gp, oi)
                        else:
                            for i in range(ntiles):
                                oi, tc = divmod(i, n_tc)
                                if wfn and tc == 0:
                                    if oi >= 2:
                                        gp.wait_ge(self.sem["MM"], base["MM"] + (oi - 1) * n_tc)
                                    wfn(gp, oi)
                                for j, (sek, _) in enumerate(steps):
                                    if sek == "gp":
                                        emit_step(gp, j, i, oi, tc)

        self.c["MM"] += ntiles
        for j in range(1, nsteps):
            self.c[f"X{j}"] += ntiles * step_inc[j - 1]
        self.c["EV"] += ntiles * step_inc[-1]
        self.c["GS"] += 16 * wn * n_oi


def build(P):
    T, D, HD, F, E, NROT = P["T"], P["D"], P["HD"], P["F"], P["E"], P["NROT"]
    H = D // HD
    FC = D // 128
    HC = F // 128
    TC = T // 512
    KT = T // 128
    HE = H * E
    OC = HE * HD // 128
    GRP = OC // 8
    dbg = set(P["dbg"])

    nc = bass.Bass("TRN2", target_bir_lowering=False)

    def din(name, shape):
        return nc.dram_tensor(name, list(shape), F32, kind="ExternalInput")

    x_in = din("x", (T, D))
    wqkv = din("wqkv", (D, 3 * D))
    bqkvf = din("bqkvf", (128, 3 * FC))
    wout0 = din("wout0", (D, D))
    boutf = din("boutf", (128, FC))
    w1t = din("w1t", (D, F))
    b1f0 = din("b1f0", (128, HC))
    w2t = din("w2t", (F, D))
    b2f0 = din("b2f0", (128, FC))
    wit = din("wit", (D, 3 * D))
    inbf = din("inbf", (128, 3 * FC))
    wot = din("wot", (D, D))
    outbf = din("outbf", (128, FC))
    wq2 = din("wq2", (D, D))
    wk2 = din("wk2", (D, D))
    wv4 = din("wv4", (H, D, E * HD))
    wo2 = din("wo2", (HE * HD, D))
    wselv = din("wselv", (D, HE))
    wselo = din("wselo", (D, HE))
    ln1g = din("ln1g", (1, D))
    ln1b = din("ln1b", (1, D))
    ln2g = din("ln2g", (1, D))
    ln2b = din("ln2b", (1, D))
    w1L = din("w1L", (D, F))
    b1Lf = din("b1Lf", (128, HC))
    w2L = din("w2L", (F, D))
    b2Lf = din("b2Lf", (128, FC))
    ropeP = din("ropeP", (128, 128))

    out = nc.dram_tensor("out", [T, D], F32, kind="ExternalOutput")

    def scr(name, shape, dt=BF16):
        kind = "ExternalOutput" if name in dbg else "Internal"
        return nc.dram_tensor(name, list(shape), dt, kind=kind)

    ag_in = [nc.dram_tensor(f"ag_in{i}", [1, 16], F32) for i in range(3)]
    ag_out = [nc.dram_tensor(f"ag_out{i}", [NCORES, 16], F32) for i in range(3)]
    d_kv_in = scr("d_kv_in", (T, 2 * D))
    d_kv_all = nc.dram_tensor("d_kv_all", [NCORES * T, 2 * D], BF16, addr_space="Shared")

    d_xn0F = scr("d_xn0F", (D, T))
    d_xq0F = scr("d_xq0F", (D, T))
    d_QF = scr("d_QF", (D, T))
    d_KF = scr("d_KF", (D, T))
    d_VF = scr("d_VF", (D, T))
    d_s0 = scr("d_s0", (D, T))
    d_g0 = scr("d_g0", (D, T))
    d_w1q = scr("d_w1q", (D, F))
    d_w2q = scr("d_w2q", (F, D))
    d_x1 = scr("d_x1", (T, D), F32)
    d_xn1F = scr("d_xn1F", (D, T))
    d_wiq = scr("d_wiq", (D, 3 * D))
    d_woq = scr("d_woq", (D, D))
    d_qT = scr("d_qT", (T, D))
    d_heads = scr("d_heads", (T, D), F32)
    d_yqT = scr("d_yqT", (T, D))
    d_g1 = scr("d_g1", (T, D))
    d_srcB = scr("d_srcB", (D, T), F32)

    oes = ExitStack()
    with oes:
        b = Bld(nc, oes)
        SEM = b.sem

        identB = b.sb(oes, "identB", [128, 128], BF16)
        identF = b.sb(oes, "identF", [128, 128], F32)
        identB2 = b.sb(oes, "identB2", [128, 128], BF16)
        onesT = b.sb(oes, "onesT", [128, 128], F32)
        sel64b = b.sb(oes, "sel64b", [128, 2], BF16)
        ropePb = b.sb(oes, "ropePb", [128, 128], BF16)
        cbias = b.sb(oes, "cbias", [128, 4], F32)
        scol = b.sb(oes, "scol", [128, 16], F32)
        sc_row = b.sb(oes, "sc_row", [128, 16], F32)
        negB0 = b.sb(oes, "negB0", [128, 16], F32)
        negB1 = b.sb(oes, "negB1", [128, 16], F32)
        nrm = b.sb(oes, "nrm", [128, 128], F32)
        bqkv_s = b.sb(oes, "bqkv_s", [128, 3 * FC], F32)
        bout_s = b.sb(oes, "bout_s", [128, FC], F32)
        b1q_s = b.sb(oes, "b1q_s", [128, HC], F32)
        b2q_s = b.sb(oes, "b2q_s", [128, FC], F32)
        b1q16 = b.sb(oes, "b1q16", [128, HC], F32)
        inb_s = b.sb(oes, "inb_s", [128, 3 * FC], F32)
        outb_s = b.sb(oes, "outb_s", [128, FC], F32)
        outb2_s = b.sb(oes, "outb2_s", [128, FC], F32)
        b1L_s = b.sb(oes, "b1L_s", [128, HC], F32)
        b2L_s = b.sb(oes, "b2L_s", [128, FC], F32)
        A1 = b.sb(oes, "A1", [128, FC, T], BF16)

        with nc.Block() as blk:
            @blk.sync
            def _(s):
                for dst, src in [(bqkv_s, bqkvf), (bout_s, boutf), (b1q_s, b1f0),
                                 (b2q_s, b2f0), (inb_s, inbf), (outb_s, outbf),
                                 (b1L_s, b1Lf), (b2L_s, b2Lf)]:
                    s.dma_start(out=dst[:, :], in_=src[:, :]).then_inc(SEM["LS"], 16)
                    b.c["LS"] += 16

            @blk.gpsimd
            def _(gp):
                gp.dma_start(out=ropePb[:, :], in_=ropeP[:, :]).then_inc(SEM["GS"], 16)
                b.c["GS"] += 16
                for t, v in [(identB, 1.0), (identF, 1.0), (onesT, 1.0), (sel64b, 1.0)]:
                    gp.memset(t[:, :], v)
                    gp.drain()
                gp.memset(cbias[:, :], 0.0)
                gp.drain()
                gp.memset(cbias[:, 0:1], 1e-5)
                gp.drain()
                gp.memset(scol[:, :], 1.0)
                gp.drain()
                gp.memset(sc_row[:, :], 1.0)
                gp.drain()
                for ident in (identB, identF):
                    gp.affine_select(out=ident[:, :], in_=ident[:, :], pattern=[[-1, 128]],
                                     compare_op=ALU.is_equal, fill=0.0, base=0,
                                     channel_multiplier=1)
                    gp.drain()
                gp.affine_select(out=sel64b[:, :], in_=sel64b[:, :], pattern=[[64, 2]],
                                 compare_op=ALU.is_ge, fill=0.0, base=0,
                                 channel_multiplier=-1)
                gp.drain()
                gp.affine_select(out=sel64b[:, :], in_=sel64b[:, :], pattern=[[-64, 2]],
                                 compare_op=ALU.is_ge, fill=0.0, base=63,
                                 channel_multiplier=1)
                gp.drain()

        with ExitStack() as ses:
            scr1 = b.sb(ses, "s0_scr1", [128, 128], BF16)
            i32b = b.sb(ses, "s0_i32", [128, HC], I32)
            ops = []

            def mk_diag(dst, sh):
                def f(gp):
                    gp.memset(dst[:, :], 1.0)
                    gp.drain()
                    return gp.affine_select(out=dst[:, :], in_=dst[:, :],
                                            pattern=[[-1, 128]], compare_op=ALU.is_equal,
                                            fill=0.0, base=sh, channel_multiplier=1)
                return f

            ops.append(("gp", mk_diag(identB2, 0), False))
            ops.append(("gp", mk_diag(scr1, 64), False))
            ops.append(("ve", lambda ve: ve.tensor_tensor(identB2[:, :], identB2[:, :], scr1[:, :], ALU.add), False))
            ops.append(("gp", mk_diag(scr1, -64), False))
            ops.append(("ve", lambda ve: ve.tensor_tensor(identB2[:, :], identB2[:, :], scr1[:, :], ALU.add), False))
            for t, w in [(b1q_s, HC), (b2q_s, FC)]:
                ops.append(("ve", (lambda t=t, w=w: lambda ve: ve.tensor_scalar(
                    t[:, 0:w], t[:, 0:w], 64.0, 127.0, ALU.mult, ALU.min))(), False))
                ops.append(("ve", (lambda t=t, w=w: lambda ve: ve.tensor_scalar(
                    i32b[:, 0:w], t[:, 0:w], -128.0, None, ALU.max))(), False))
                ops.append(("ve", (lambda t=t, w=w: lambda ve: ve.tensor_scalar(
                    t[:, 0:w], i32b[:, 0:w], 0.015625, None, ALU.mult))(), False))
            ops.append(("ve", lambda ve: ve.tensor_scalar(b1q16[:, :], b1q_s[:, :], 16.0, None, ALU.mult), False))
            b.seq("s0b", ops)

        def ln_stage(name, src_dram, dst_F, quant_dst):
            with ExitStack() as ses:
                xt = b.sb(ses, f"{name}_x", [128, D], F32)
                xn = b.sb(ses, f"{name}_xn", [128, D], BF16)
                xq = b.sb(ses, f"{name}_xq", [128, D], BF16)
                xqf = b.sb(ses, f"{name}_xqf", [128, D], F32)
                i32t = b.sb(ses, f"{name}_i32", [128, D], I32)
                bn6 = b.sb(ses, f"{name}_bn", [128, 6 * (D // 512)], F32)
                mv = b.sb(ses, f"{name}_mv", [128, 4], F32)
                fst = b.sb(ses, f"{name}_fs", [128, FC, 128], BF16)
                fst2 = b.sb(ses, f"{name}_fs2", [128, FC, 128], BF16)
                tps = b.ps(ses, f"{name}_tp", [128, 128], BF16)
                nst = D // 512
                for kt in range(KT):
                    ops = []
                    ops.append(("sy", (lambda kt=kt: lambda s: s.dma_start(
                        out=xt[:, :], in_=src_dram[kt * 128:(kt + 1) * 128, :]))(), True))

                    def bnf(ve):
                        for j in range(nst):
                            ve.bn_stats(bn6[:, j * 6:(j + 1) * 6], xt[:, j * 512:(j + 1) * 512])
                            ve.drain()
                        return ve.bn_aggr(mv[:, 0:2], bn6[:, 0:6 * nst])
                    ops.append(("ve", bnf, False))
                    ops.append(("sc", lambda sc: sc.activation(mv[:, 2:3], mv[:, 1:2],
                                                               AF.Sqrt, bias=cbias[:, 0:1]), False))

                    def appf(ve):
                        ve.reciprocal(mv[:, 3:4], mv[:, 2:3])
                        ve.drain()
                        return ve.tensor_scalar(xn[:, :], xt[:, :], mv[:, 0:1], mv[:, 3:4],
                                                ALU.subtract, ALU.mult)
                    ops.append(("ve", appf, False))
                    if quant_dst is not None:
                        def qf1(ve):
                            ve.tensor_scalar(xqf[:, :], xt[:, :], mv[:, 0:1], mv[:, 3:4],
                                             ALU.subtract, ALU.mult)
                            ve.drain()
                            ve.tensor_scalar(xqf[:, :], xqf[:, :], 16.0, 127.0, ALU.mult, ALU.min)
                            ve.drain()
                            return ve.tensor_scalar(i32t[:, :], xqf[:, :], -128.0, None, ALU.max)
                        ops.append(("ve", qf1, False))
                        ops.append(("sc", lambda sc: sc.activation(xq[:, :], i32t[:, :],
                                                                   AF.Identity, scale=0.0625), False))
                    pairs = [(xn, fst, dst_F)] + ([(xq, fst2, quant_dst)] if quant_dst is not None else [])
                    for (tmj, fs_, dd) in pairs:
                        for fc in range(FC):
                            ops.append(("te", (lambda tmj=tmj, fc=fc: lambda te: te.transpose(
                                tps[:, :], tmj[:, fc * 128:(fc + 1) * 128], identB[:, :]))(), False))
                            ops.append(("ve", (lambda fs_=fs_, fc=fc: lambda ve: ve.tensor_copy(
                                fs_[:, fc, :], tps[:, :]))(), False))
                        ops.append(("sy", (lambda fs_=fs_, dd=dd, kt=kt: lambda s: s.dma_start(
                            out=dd.ap().rearrange("(c p) t -> p c t", p=128)[:, :, kt * 128:(kt + 1) * 128],
                            in_=fs_[:, :, :]))(), True))
                    b.seq(f"{name}_{kt}", ops)

        def norm_stage(name, QF, KF, negB_dst, scale):
            with ExitStack() as ses:
                q2t = b.sb(ses, f"{name}_q2", [128, T], BF16)
                nb2 = b.sb(ses, f"{name}_nb", [128, 16], F32)
                nps_ = b.ps(ses, f"{name}_ps", [128, 512])
                tpp = b.ps(ses, f"{name}_tp", [128, 16])
                ops = []
                for t_i, SRC in enumerate((QF, KF)):
                    for fc in range(FC):
                        ops.append(("sc", (lambda SRC=SRC, fc=fc: lambda sc: sc.activation(
                            q2t[:, :], SRC[:, fc, :], AF.Square))(), False))
                        for tc in range(TC):
                            ops.append(("te", (lambda tc=tc: lambda te: te.matmul(
                                nps_[0:2, :], sel64b[:, :], q2t[:, tc * 512:(tc + 1) * 512],
                                start=True, stop=True))(), False))
                            col = t_i * (FC * TC) + fc * TC + tc
                            ops.append(("ve", (lambda col=col: lambda ve: ve.reduce_max(
                                nrm[0:2, col:col + 1], nps_[0:2, :], AX.X))(), False))

                def comb(ve):
                    v = nrm[0:2, 0:2 * FC * TC].rearrange("p (g c) -> p g c", c=TC)
                    o = nrm[0:2, 64:64 + 2 * FC].rearrange("p g -> p g 1" if False else "p (g u) -> p g u", u=1)
                    ve.tensor_reduce(o, v, AX.X, ALU.max)
                    ve.drain()
                    return ve.tensor_tensor(nrm[0:2, 96:96 + FC], nrm[0:2, 64:64 + FC],
                                            nrm[0:2, 64 + FC:64 + 2 * FC], ALU.mult)
                ops.append(("ve", comb, False))
                ops.append(("sc", lambda sc: sc.activation(nrm[0:2, 96:96 + FC],
                                                           nrm[0:2, 96:96 + FC], AF.Sqrt), False))
                ops.append(("ve", lambda ve: ve.tensor_scalar(
                    nrm[0:2, 96:96 + FC], nrm[0:2, 96:96 + FC], -1.01 * scale, None, ALU.mult), False))
                ops.append(("gp", lambda gp: gp.dma_start(
                    out=nb2[0:1, 0:2 * FC],
                    in_=nrm[0:2, 96:96 + FC], allow_slow_non_contiguous=True), True))
                ops.append(("te", lambda te: te.matmul(tpp[:, 0:H], onesT[0:1, :],
                                                       nb2[0:1, 0:H], start=True, stop=True), False))
                ops.append(("ve", lambda ve: ve.tensor_copy(negB_dst[:, 0:H], tpp[:, 0:H]), False))
                b.seq(name, ops)

        def attn_stage(name, QF, KF, VTs, negB, masks, ctx_dst, causal):
            with ExitStack() as ses:
                PT = b.sb(ses, f"{name}_PT", [128, KT, 512], BF16)
                rr = b.sb(ses, f"{name}_rr", [128, 512], F32)
                ctmp = b.sb(ses, f"{name}_ct", [128, 512], BF16)
                ps_s = [b.ps(ses, f"{name}_s{j}", [128, 512]) for j in range(2)]
                ps_c = b.ps(ses, f"{name}_c", [65, 512])
                ps_b = b.ps(ses, f"{name}_b", [64, 512])
                ps_p = b.ps(ses, f"{name}_p", [128, 512])
                for qc in range(TC):
                    nkt = min(KT, 4 * qc + 4) if causal else KT
                    ops = []
                    for h in range(H):
                        off = (h % 2) * 64
                        fc = h // 2
                        for kt in range(nkt):
                            ops.append(("te", (lambda off=off, fc=fc, kt=kt, qc=qc: lambda te: te.matmul(
                                ps_s[kt % 2][:, :],
                                KF[off:off + 64, fc, kt * 128:(kt + 1) * 128],
                                QF[off:off + 64, fc, qc * 512:(qc + 1) * 512],
                                start=True, stop=True))(), False))
                            hcol = (h % 2) * FC + h // 2
                            ops.append(("sc", (lambda h=h, kt=kt, hcol=hcol: lambda sc: sc.activation(
                                PT[:, kt, :], ps_s[kt % 2][:, :], AF.Exp,
                                bias=negB[:, hcol:hcol + 1], scale=0.125))(), False))
                            if causal and kt >= 4 * qc:
                                ops.append(("ve", (lambda kt=kt, qc=qc: lambda ve: ve.tensor_tensor(
                                    PT[:, kt, :], PT[:, kt, :], masks[:, kt - 4 * qc, :],
                                    ALU.mult))(), False))
                        for kt in range(nkt):
                            ops.append(("te", (lambda h=h, kt=kt, nkt=nkt: lambda te: te.matmul(
                                ps_c[:, :], VTs[:, kt, h, 0:65], PT[:, kt, :],
                                start=(kt == 0), stop=(kt == nkt - 1)))(), False))
                        ops.append(("ve", lambda ve: ve.reciprocal(rr[64:65, :], ps_c[64:65, :]), False))
                        ops.append(("te", lambda te: te.matmul(ps_b[0:64, :], onesT[64:65, 0:64],
                                                               rr[64:65, :], start=True, stop=True), False))
                        ops.append(("ve", lambda ve: ve.tensor_copy(ctmp[0:64, :], ps_c[0:64, :]), False))
                        if h % 2 == 0:
                            ops.append(("ve", (lambda fc=fc, qc=qc: lambda ve: ve.tensor_tensor(
                                ctx_dst[0:64, fc, qc * 512:(qc + 1) * 512], ctmp[0:64, :],
                                ps_b[0:64, :], ALU.mult))(), False))
                        else:
                            ops.append(("ve", lambda ve: ve.tensor_tensor(
                                ctmp[0:64, :], ctmp[0:64, :], ps_b[0:64, :], ALU.mult), False))
                            ops.append(("te", lambda te: te.matmul(
                                ps_p[64:128, :], identB[0:64, 0:64], ctmp[0:64, :],
                                start=True, stop=True, tile_position=(0, 64)), False))
                            ops.append(("ve", (lambda fc=fc, qc=qc: lambda ve: ve.tensor_copy(
                                ctx_dst[64:128, fc, qc * 512:(qc + 1) * 512],
                                ps_p[64:128, :]))(), False))
                    b.seq(f"{name}_qc{qc}", ops)

        def quantw_stage(name, src, dst, rows, cols, scale_num, recip_scale):
            with ExitStack() as ses:
                wtmp = b.sb(ses, f"{name}_w", [128, min(cols, 2048)], F32)
                wi = b.sb(ses, f"{name}_i", [128, min(cols, 2048)], I32)
                wb = b.sb(ses, f"{name}_b", [128, min(cols, 2048)], BF16)
                CW = min(cols, 2048)
                ops = []
                for r in range(rows // 128):
                    for cb in range(cols // CW):
                        sl = (slice(r * 128, (r + 1) * 128), slice(cb * CW, (cb + 1) * CW))
                        ops.append(("sy", (lambda sl=sl: lambda s: s.dma_start(
                            out=wtmp[:, :], in_=src[sl[0], sl[1]]))(), True))
                        def q1(ve):
                            ve.tensor_scalar(wtmp[:, :], wtmp[:, :], scale_num, 127.0,
                                             ALU.mult, ALU.min)
                            ve.drain()
                            return ve.tensor_scalar(wi[:, :], wtmp[:, :], -128.0, None, ALU.max)
                        ops.append(("ve", q1, False))
                        ops.append(("sc", lambda sc: sc.activation(wb[:, :], wi[:, :],
                                                                   AF.Identity, scale=recip_scale), False))
                        ops.append(("sy", (lambda sl=sl: lambda s: s.dma_start(
                            out=dst[sl[0], sl[1]], in_=wb[:, :]))(), True))
                b.seq(name, ops)

        CUT = P["cut"]

        if CUT >= 2:
            ln_stage("s1", x_in, d_xn0F, d_xq0F)

        if CUT >= 3:
          with nc.Block() as blk:
            @blk.sync
            def _(s):
                s.dma_start(out=A1[:, :, :],
                            in_=d_xn0F.ap().rearrange("(c p) t -> p c t", p=128)).then_inc(SEM["LS"], 16)
                b.c["LS"] += 16

        if CUT >= 3:
          with ExitStack() as rl0:
            ctx0 = b.sb(rl0, "ctx0", [128, FC, T], BF16)
            QFr = b.sb(rl0, "QFr", [128, FC, T], BF16)
            KFr = b.sb(rl0, "KFr", [128, FC, T], BF16)
            VT = b.sb(rl0, "VT", [128, KT, H, 66], BF16)

            with ExitStack() as ses:
                wbuf = b.sb(ses, "s2_w", [128, 2, FC, 128], BF16)
                st = b.sb(ses, "s2_st", [128, 2, 512], BF16)

                def wload(gp, oi):
                    src = wqkv.ap().rearrange("(kc p) o -> p kc o", p=128)[:, :, oi * 128:(oi + 1) * 128]
                    gp.dma_start(out=wbuf[:, oi % 2, :, :], in_=src).then_inc(SEM["GS"], 16)

                def ev(sc, ps, i, oi, tc):
                    return sc.activation(st[:, i % 2, :], ps[:, :], AF.Identity,
                                         bias=bqkv_s[:, oi:oi + 1])

                def stf(s, ps, i, oi, tc):
                    dd = (d_QF, d_KF, d_VF)[oi // FC]
                    o = oi % FC
                    return s.dma_start(out=dd[o * 128:(o + 1) * 128, tc * 512:(tc + 1) * 512],
                                       in_=st[:, i % 2, :])

                b.proj("s2", 3 * FC, TC, FC, [128, 512],
                       lambda kc, oi, tc: wbuf[:, oi % 2, kc, :],
                       lambda kc, oi, tc: A1[:, kc, tc * 512:(tc + 1) * 512],
                       steps=[("sc", ev), ("sy", stf)], wload=(wload, 1), nps=2)

            with nc.Block() as blk:
                @blk.sync
                def _(s):
                    s.dma_start(out=QFr[:, :, :],
                                in_=d_QF.ap().rearrange("(c p) t -> p c t", p=128)).then_inc(SEM["LS"], 16)
                    s.dma_start(out=KFr[:, :, :],
                                in_=d_KF.ap().rearrange("(c p) t -> p c t", p=128)).then_inc(SEM["LS"], 16)
                    b.c["LS"] += 32

            if CUT >= 4:
             with ExitStack() as ses:
                vfch = b.sb(ses, "s2b_v", [128, T], BF16)
                vtps = b.ps(ses, "s2b_ps", [128, 128], BF16)
                ops = [("gp", lambda gp: gp.memset(VT[:, :, :, 64:65], 1.0), False)]
                for fc in range(FC):
                    ops.append(("sy", (lambda fc=fc: lambda s: s.dma_start(
                        out=vfch[:, :], in_=d_VF[fc * 128:(fc + 1) * 128, :]))(), True))
                    for kt in range(KT):
                        ops.append(("te", (lambda kt=kt: lambda te: te.transpose(
                            vtps[:, :], vfch[:, kt * 128:(kt + 1) * 128], identB[:, :]))(), False))
                        ops.append(("ve", (lambda fc=fc, kt=kt: lambda ve: ve.tensor_copy(
                            VT[:, kt, 2 * fc, 0:64], vtps[:, 0:64]))(), False))
                        ops.append(("ve", (lambda fc=fc, kt=kt: lambda ve: ve.tensor_copy(
                            VT[:, kt, 2 * fc + 1, 0:64], vtps[:, 64:128]))(), False))
                b.seq("s2b", ops)

            if CUT >= 4:
                norm_stage("s2n", QFr, KFr, negB0, 0.125)

            if CUT >= 5:
             with ExitStack() as ses:
                masks0 = b.sb(ses, "masks0", [128, 4, 512], BF16)
                with nc.Block() as blk:
                    @blk.gpsimd
                    def _(gp):
                        gp.memset(masks0[:, :, :], 1.0)
                        gp.drain()
                        for m in range(4):
                            gp.affine_select(out=masks0[:, m, :], in_=masks0[:, m, :],
                                             pattern=[[1, 512]], compare_op=ALU.is_ge,
                                             fill=0.0, base=-128 * m, channel_multiplier=-1)
                            gp.drain()
                attn_stage("s3", QFr, KFr, VT, negB0, masks0, ctx0, causal=True)

            if CUT >= 5:
             with ExitStack() as ses:
                wout_s = b.sb(ses, "s3o_w", [128, FC, D], BF16)
                st3 = b.sb(ses, "s3o_st", [128, 2, 512], BF16)
                with nc.Block() as blk:
                    @blk.gpsimd
                    def _(gp):
                        gp.dma_start(out=wout_s[:, :, :],
                                     in_=wout0.ap().rearrange("(kc p) o -> p kc o", p=128)).then_inc(SEM["GS"], 16)
                        b.c["GS"] += 16

                def ev3(sc, ps, i, oi, tc):
                    return sc.activation(st3[:, i % 2, :], ps[:, :], AF.Identity,
                                         bias=bout_s[:, oi:oi + 1])

                def st3f(s, ps, i, oi, tc):
                    return s.dma_start(out=d_s0[oi * 128:(oi + 1) * 128, tc * 512:(tc + 1) * 512],
                                       in_=st3[:, i % 2, :])

                b.proj("s3o", FC, TC, FC, [128, 512],
                       lambda kc, oi, tc: wout_s[:, kc, oi * 128:(oi + 1) * 128],
                       lambda kc, oi, tc: ctx0[:, kc, tc * 512:(tc + 1) * 512],
                       steps=[("sc", ev3), ("sy", st3f)], nps=2)

        if CUT >= 6:
            quantw_stage("s4w1", w1t, d_w1q, D, F, 64.0, 0.015625)
            quantw_stage("s4w2", w2t, d_w2q, F, D, 64.0, 0.015625)
        if CUT >= 6:
         with ExitStack() as rs4:
            xqF = b.sb(rs4, "xqF", [128, FC, T], BF16)
            hstage = b.sb(rs4, "hstage", [128, HC, 512], BF16)
            with nc.Block() as blk:
                @blk.sync
                def _(s):
                    s.dma_start(out=xqF[:, :, :],
                                in_=d_xq0F.ap().rearrange("(c p) t -> p c t", p=128)).then_inc(SEM["LS"], 16)
                    b.c["LS"] += 16
            for tc in range(TC):
                with ExitStack() as ses:
                    wb1 = b.sb(ses, "s4_w1", [128, 2, FC, 128], BF16)
                    i32a = b.sb(ses, "s4_i32", [128, 2, 512], I32)

                    def wl1(gp, oi):
                        src = d_w1q.ap().rearrange("(kc p) o -> p kc o", p=128)[:, :, oi * 128:(oi + 1) * 128]
                        gp.dma_start(out=wb1[:, oi % 2, :, :], in_=src).then_inc(SEM["GS"], 16)

                    def evA1(sc, ps, i, oi, tc_):
                        return sc.activation(i32a[:, i % 2, :], ps[:, :], AF.Identity,
                                             scale=16.0, bias=b1q16[:, oi:oi + 1])

                    def evA2(ve, ps, i, oi, tc_):
                        return ve.tensor_scalar(i32a[:, i % 2, :], i32a[:, i % 2, :],
                                                0.0, 127.0, ALU.max, ALU.min)

                    def evA3(sc, ps, i, oi, tc_):
                        return sc.activation(hstage[:, oi, :], i32a[:, i % 2, :],
                                             AF.Identity, scale=0.0625)

                    b.proj(f"s4a_{tc}", HC, 1, FC, [128, 512],
                           lambda kc, oi, tc_: wb1[:, oi % 2, kc, :],
                           (lambda tc: lambda kc, oi, tc_: xqF[:, kc, tc * 512:(tc + 1) * 512])(tc),
                           steps=[("sc", evA1), ("ve", evA2), ("sc", evA3)],
                           wload=(wl1, 1), nps=2)
                with ExitStack() as ses:
                    wb2 = b.sb(ses, "s4_w2", [128, 2, HC, 128], BF16)
                    gst = b.sb(ses, "s4_gst", [128, 2, 512], BF16)

                    def wl2(gp, oi):
                        src = d_w2q.ap().rearrange("(kc p) o -> p kc o", p=128)[:, :, oi * 128:(oi + 1) * 128]
                        gp.dma_start(out=wb2[:, oi % 2, :, :], in_=src).then_inc(SEM["GS"], 16)

                    def evB1(sc, ps, i, oi, tc_):
                        return sc.activation(gst[:, i % 2, :], ps[:, :], AF.Silu,
                                             bias=b2q_s[:, oi:oi + 1])

                    def stB(s, ps, i, oi, tc_):
                        return (lambda tc: s.dma_start(
                            out=d_g0[oi * 128:(oi + 1) * 128, tc * 512:(tc + 1) * 512],
                            in_=gst[:, i % 2, :]))(tc)

                    b.proj(f"s4b_{tc}", FC, 1, HC, [128, 512],
                           lambda kc, oi, tc_: wb2[:, oi % 2, kc, :],
                           lambda kc, oi, tc_: hstage[:, kc, :],
                           steps=[("sc", evB1), ("sy", stB)], wload=(wl2, 1), nps=2)

        if CUT >= 7:
         with ExitStack() as ses:
            xt5 = b.sb(ses, "s5_x", [128, D], F32)
            sT5 = b.sb(ses, "s5_s", [128, D], BF16)
            gT5 = b.sb(ses, "s5_g", [128, D], BF16)
            sg5 = b.sb(ses, "s5_sg", [128, D], BF16)
            x15 = b.sb(ses, "s5_x1", [128, D], F32)
            for kt in range(KT):
                ops = []
                ops.append(("sy", (lambda kt=kt: lambda s: s.dma_start(
                    out=xt5[:, :], in_=x_in[kt * 128:(kt + 1) * 128, :]))(), True))
                ops.append(("sy", (lambda kt=kt: lambda s: s.dma_start_transpose(
                    sT5[:, :], d_s0[:, kt * 128:(kt + 1) * 128]))(), True))
                ops.append(("sy", (lambda kt=kt: lambda s: s.dma_start_transpose(
                    gT5[:, :], d_g0[:, kt * 128:(kt + 1) * 128]))(), True))
                ops.append(("ve", lambda ve: ve.tensor_tensor(sg5[:, :], sT5[:, :], gT5[:, :], ALU.mult), False))
                ops.append(("ve", lambda ve: ve.tensor_tensor(x15[:, :], xt5[:, :], sg5[:, :], ALU.add), False))
                ops.append(("sy", (lambda kt=kt: lambda s: s.dma_start(
                    out=d_x1[kt * 128:(kt + 1) * 128, :], in_=x15[:, :]))(), True))
                b.seq(f"s5_{kt}", ops)

        if CUT >= 7:
            ln_stage("s6", d_x1, d_xn1F, None)
        if CUT >= 7:
         with nc.Block() as blk:
            @blk.sync
            def _(s):
                s.dma_start(out=A1[:, :, :],
                            in_=d_xn1F.ap().rearrange("(c p) t -> p c t", p=128)).then_inc(SEM["LS"], 16)
                b.c["LS"] += 16

        if CUT >= 8 or CUT == 1:
            build_s7(b, nc, locals())

        if CUT >= 9:
            build_s8(b, nc, locals())

        if CUT >= 10:
         with ExitStack() as ses:
            x19 = b.sb(ses, "s9_x1", [128, D], F32)
            yq9 = b.sb(ses, "s9_yq", [128, D], BF16)
            g9 = b.sb(ses, "s9_g", [128, D], BF16)
            sg9 = b.sb(ses, "s9_sg", [128, D], BF16)
            o9 = b.sb(ses, "s9_o", [128, D], F32)
            for kt in range(KT):
                ops = []
                ops.append(("sy", (lambda kt=kt: lambda s: s.dma_start(
                    out=x19[:, :], in_=d_x1[kt * 128:(kt + 1) * 128, :]))(), True))
                ops.append(("sy", (lambda kt=kt: lambda s: s.dma_start(
                    out=yq9[:, :], in_=d_yqT[kt * 128:(kt + 1) * 128, :]))(), True))
                ops.append(("sy", (lambda kt=kt: lambda s: s.dma_start(
                    out=g9[:, :], in_=d_g1[kt * 128:(kt + 1) * 128, :]))(), True))
                ops.append(("ve", lambda ve: ve.scalar_tensor_tensor(
                    sg9[:, :], yq9[:, :], scol[:, 11:12], g9[:, :], ALU.mult, ALU.mult), False))
                ops.append(("ve", lambda ve: ve.tensor_tensor(o9[:, :], x19[:, :], sg9[:, :], ALU.add), False))
                ops.append(("sy", (lambda kt=kt: lambda s: s.dma_start(
                    out=out[kt * 128:(kt + 1) * 128, :], in_=o9[:, :]))(), True))
                b.seq(f"s9_{kt}", ops)

        with nc.Block() as blk:
            @blk.sync
            def _(s):
                s.wait_ge(SEM["LS"], b.c["LS"])

    return nc


def build_s7(b, nc, L):
    (T, D, F, H, E, HD, FC, HC, TC, KT) = (L["T"], L["D"], L["F"], L["H"], L["E"],
                                           L["HD"], L["FC"], L["HC"], L["TC"], L["KT"])
    C7 = L["P"].get("cut7", 99) if isinstance(L.get("P"), dict) else 99
    SEM = b.sem
    A1, scol, sc_row, nrm = L["A1"], L["scol"], L["sc_row"], L["nrm"]
    inb_s, outb_s, outb2_s = L["inb_s"], L["outb_s"], L["outb2_s"]
    identB, identF, onesT = L["identB"], L["identF"], L["onesT"]
    wit, wot = L["wit"], L["wot"]
    d_wiq, d_woq, d_qT, d_kv_in, d_kv_all = (L["d_wiq"], L["d_woq"], L["d_qT"],
                                             L["d_kv_in"], L["d_kv_all"])
    d_heads, d_yqT = L["d_heads"], L["d_yqT"]
    ag_in, ag_out = L["ag_in"], L["ag_out"]
    RG = [list(range(NCORES))]

    C_SWI, C_RWI, C_SWO, C_RWO = 0, 1, 2, 3
    C_SX, C_RSX, C_FX, C_SQ, C_FQ, C_ATT = 4, 5, 6, 7, 8, 9
    C_SH, C_SY, C_FH, C_FY, C_SHWO, C_RSY = 10, 11, 12, 13, 14, 15

    def bcast_scol(ops, tpp):
        ops.append(("te", lambda te: te.matmul(tpp[:, 0:16], onesT[0:1, :],
                                               sc_row[0:1, 0:16], start=True, stop=True), False))
        ops.append(("ve", lambda ve: ve.tensor_copy(scol[:, :], tpp[:, 0:16]), False))

    def fakequantw(name, src, dst, rows, cols, c_scale, c_recip):
        nch = rows // 128
        CW = min(cols, 1024)
        ncb = cols // CW
        with ExitStack() as ses:
            wtmp = b.sb(ses, f"{name}_w", [128, CW], F32)
            wi = b.sb(ses, f"{name}_i", [128, CW], I32)
            wb = b.sb(ses, f"{name}_b", [128, CW], BF16)
            tpp = b.ps(ses, f"{name}_tp", [128, 128])
            ops = []
            for r in range(nch):
                for cb in range(ncb):
                    sl = (slice(r * 128, (r + 1) * 128), slice(cb * CW, (cb + 1) * CW))
                    ops.append(("sy", (lambda sl=sl: lambda s: s.dma_start(
                        out=wtmp[:, :], in_=src[sl[0], sl[1]]))(), True))
                    col = r * ncb + cb
                    ops.append(("ve", (lambda col=col: lambda ve: ve.reduce_max(
                        nrm[:, col:col + 1], wtmp[:, :], AX.X,
                        apply_absolute_value=True))(), False))
            nc_ = nch * ncb
            ops.append(("ve", lambda ve: ve.reduce_max(
                nrm[:, 120:121], nrm[:, 0:nc_], AX.X), False))
            ops.append(("te", lambda te: te.transpose(tpp[0:1, 0:128], nrm[:, 120:121],
                                                      identF[:, :]), False))
            ops.append(("ve", lambda ve: ve.reduce_max(nrm[0:1, 121:122], tpp[0:1, 0:128], AX.X), False))
            ops.append(("ve", (lambda: lambda ve: ve.tensor_scalar(
                sc_row[0:1, c_scale:c_scale + 1], nrm[0:1, 121:122],
                1.0 / 127.0, 1e-8, ALU.mult, ALU.max))(), False))
            ops.append(("ve", lambda ve: ve.reciprocal(
                sc_row[0:1, c_recip:c_recip + 1], sc_row[0:1, c_scale:c_scale + 1]), False))
            bcast_scol(ops, tpp)
            b.seq(f"{name}_p1", ops)
            ops = []
            for r in range(nch):
                for cb in range(ncb):
                    sl = (slice(r * 128, (r + 1) * 128), slice(cb * CW, (cb + 1) * CW))
                    ops.append(("sy", (lambda sl=sl: lambda s: s.dma_start(
                        out=wtmp[:, :], in_=src[sl[0], sl[1]]))(), True))
                    ops.append(("ve", lambda ve: ve.tensor_scalar(
                        wi[:, :], wtmp[:, :], scol[:, c_recip:c_recip + 1], None, ALU.mult), False))
                    ops.append(("sc", lambda sc: sc.activation(wb[:, :], wi[:, :], AF.Identity), False))
                    ops.append(("sy", (lambda sl=sl: lambda s: s.dma_start(
                        out=dst[sl[0], sl[1]], in_=wb[:, :]))(), True))
            b.seq(f"{name}_p2", ops)

    fakequantw("s7wi", wit, d_wiq, D, 3 * D, C_SWI, C_RWI)
    fakequantw("s7wo", wot, d_woq, D, D, C_SWO, C_RWO)
    if C7 < 2:
        return

    with ExitStack() as r7:
        hqF = b.sb(r7, "hqF", [128, FC, T], BF16)
        g7 = b.sb(r7, "g7", [128, 16], F32)
        agt = b.sb(r7, "agt", [NCORES, 16], F32)
        i7one = b.sb(r7, "i7one", [128, 2], I32)

        with ExitStack() as ses:
            wb7 = b.sb(ses, "s7a_w", [128, 2, FC, 128], BF16)
            xst = b.sb(ses, "s7a_st", [128, 2, 512], BF16)

            def wl(gp, oi):
                src = d_wiq.ap().rearrange("(kc p) o -> p kc o", p=128)[:, :, oi * 128:(oi + 1) * 128]
                gp.dma_start(out=wb7[:, oi % 2, :, :], in_=src).then_inc(SEM["GS"], 16)

            def ev1(sc, ps, i, oi, tc):
                return sc.activation(xst[:, i % 2, :], ps[:, :], AF.Identity,
                                     scale=scol[:, C_SWI:C_SWI + 1], bias=inb_s[:, oi:oi + 1])

            def ev2(ve, ps, i, oi, tc):
                return ve.reduce_max(nrm[:, i:i + 1], xst[:, i % 2, :], AX.X,
                                     apply_absolute_value=True)

            b.proj("s7a", 3 * FC, TC, FC, [128, 512],
                   lambda kc, oi, tc: wb7[:, oi % 2, kc, :],
                   lambda kc, oi, tc: A1[:, kc, tc * 512:(tc + 1) * 512],
                   steps=[("sc", ev1), ("ve", ev2)], wload=(wl, 1), nps=2)

        if C7 < 3:
            return
        with ExitStack() as ses:
            tpp = b.ps(ses, "s7g1_tp", [128, 128])
            ops = []
            ops.append(("ve", lambda ve: ve.reduce_max(nrm[:, 120:121], nrm[:, 0:FC * TC], AX.X), False))
            ops.append(("ve", lambda ve: ve.reduce_max(nrm[:, 121:122], nrm[:, 0:3 * FC * TC], AX.X), False))
            ops.append(("te", lambda te: te.transpose(tpp[0:2, 0:128], nrm[:, 120:122], identF[:, :]), False))
            ops.append(("ve", lambda ve: ve.reduce_max(g7[0:2, 0:1], tpp[0:2, 0:128], AX.X), False))
            ops.append(("gp", lambda gp: gp.dma_start(
                out=ag_in[0].ap()[0:1, 0:2], in_=g7[0:2, 0:1],
                allow_slow_non_contiguous=True), True))
            import os as _os
            if _os.environ.get("NOCC"):
                ops.append(("gp", lambda gp: gp.dma_start(
                    out=ag_out[0].ap()[0:1, :], in_=ag_in[0].ap()), True))
            else:
                ops.append(("gp", lambda gp: gp.collective_compute(
                    "AllGather", ALU.bypass, replica_groups=RG,
                    ins=[ag_in[0].ap().opt()], outs=[ag_out[0].ap().opt()]), "cc"))
            ops.append(("gp", lambda gp: gp.dma_start(out=agt[:, :], in_=ag_out[0].ap()), True))
            ops.append(("te", lambda te: te.transpose(tpp[0:16, 0:NCORES], agt[:, :], identF[0:NCORES, 0:NCORES]), False))
            ops.append(("ve", lambda ve: ve.reduce_max(g7[0:16, 1:2], tpp[0:16, 0:NCORES], AX.X), False))
            ops.append(("te", lambda te: te.transpose(tpp[0:1, 0:128], g7[:, 1:2], identF[:, :]), False))
            ops.append(("ve", lambda ve: ve.tensor_copy(g7[0:1, 0:16], tpp[0:1, 0:16]), False))
            ops.append(("ve", lambda ve: ve.tensor_scalar(
                sc_row[0:1, C_SX:C_SX + 1], g7[0:1, 1:2], 1.0 / 127.0, 1e-8,
                ALU.mult, ALU.max), False))
            ops.append(("ve", lambda ve: ve.reciprocal(
                sc_row[0:1, C_RSX:C_RSX + 1], sc_row[0:1, C_SX:C_SX + 1]), False))
            ops.append(("ve", lambda ve: ve.tensor_tensor(
                sc_row[0:1, C_FX:C_FX + 1], sc_row[0:1, C_SWI:C_SWI + 1],
                sc_row[0:1, C_RSX:C_RSX + 1], ALU.mult), False))
            ops.append(("ve", lambda ve: ve.tensor_tensor(
                g7[0:1, 2:3], g7[0:1, 0:1], sc_row[0:1, C_RSX:C_RSX + 1], ALU.mult), False))
            ops.append(("ve", lambda ve: ve.tensor_scalar(
                i7one[0:1, 0:1], g7[0:1, 2:3], 127.0, None, ALU.min), False))
            ops.append(("ve", lambda ve: ve.tensor_tensor(
                g7[0:1, 3:4], i7one[0:1, 0:1], sc_row[0:1, C_SX:C_SX + 1], ALU.mult), False))
            ops.append(("ve", lambda ve: ve.tensor_scalar(
                g7[0:1, 3:4], g7[0:1, 3:4], 0.125, None, ALU.mult), False))
            ops.append(("ve", lambda ve: ve.tensor_scalar(
                sc_row[0:1, C_SQ:C_SQ + 1], g7[0:1, 3:4], 1.0 / 127.0, 1e-8,
                ALU.mult, ALU.max), False))
            ops.append(("ve", lambda ve: ve.reciprocal(g7[0:1, 4:5], sc_row[0:1, C_SQ:C_SQ + 1]), False))
            ops.append(("ve", lambda ve: ve.tensor_tensor(
                g7[0:1, 4:5], g7[0:1, 4:5], sc_row[0:1, C_SX:C_SX + 1], ALU.mult), False))
            ops.append(("ve", lambda ve: ve.tensor_scalar(
                sc_row[0:1, C_FQ:C_FQ + 1], g7[0:1, 4:5], 0.125, None, ALU.mult), False))
            ops.append(("ve", lambda ve: ve.tensor_tensor(
                sc_row[0:1, C_ATT:C_ATT + 1], sc_row[0:1, C_SQ:C_SQ + 1],
                sc_row[0:1, C_SX:C_SX + 1], ALU.mult), False))
            bcast_scol(ops, tpp)
            ops.append(("ve", lambda ve: ve.tensor_scalar(
                inb_s[:, :], inb_s[:, :], scol[:, C_RSX:C_RSX + 1], None, ALU.mult), False))
            b.seq("s7ag1", ops)

        if C7 < 4:
            return
        with ExitStack() as ses:
            wb7 = b.sb(ses, "s7b_w", [128, 2, FC, 128], BF16)
            i7a = b.sb(ses, "s7b_i1", [128, 2, 512], I32)
            i7b = b.sb(ses, "s7b_i2", [128, 2, 512], I32)
            st7 = b.sb(ses, "s7b_st", [128, 2, 512], BF16)
            tT7 = b.sb(ses, "s7b_tt", [128, 2, 512], BF16)
            pst7 = [b.ps(ses, f"s7b_tp{j}", [128, 512], BF16) for j in range(2)]

            def wl(gp, oi):
                src = d_wiq.ap().rearrange("(kc p) o -> p kc o", p=128)[:, :, oi * 128:(oi + 1) * 128]
                gp.dma_start(out=wb7[:, oi % 2, :, :], in_=src).then_inc(SEM["GS"], 16)

            def e1(sc, ps, i, oi, tc):
                return sc.activation(i7a[:, i % 2, :], ps[:, :], AF.Identity,
                                     scale=scol[:, C_FX:C_FX + 1], bias=inb_s[:, oi:oi + 1])

            def e2(sc, ps, i, oi, tc):
                if oi < FC:
                    return sc.activation(i7b[:, i % 2, :], i7a[:, i % 2, :],
                                         AF.Identity, scale=scol[:, C_FQ:C_FQ + 1])
                return sc.nop()

            def e3(ve, ps, i, oi, tc):
                src = i7b if oi < FC else i7a
                return ve.tensor_copy(st7[:, i % 2, :], src[:, i % 2, :])

            def e4(te, ps, i, oi, tc):
                inst = None
                for qs in range(4):
                    inst = te.transpose(pst7[i % 2][:, qs * 128:(qs + 1) * 128],
                                        st7[:, i % 2, qs * 128:(qs + 1) * 128], identB[:, :])
                return inst

            def e5(ve, ps, i, oi, tc):
                return ve.tensor_copy(tT7[:, i % 2, :], pst7[i % 2][:, :])

            def e6(s, ps, i, oi, tc):
                if oi < FC:
                    dst = d_qT.ap().rearrange("(tc qs p) c -> p tc qs c", p=128, qs=4)[
                        :, tc, :, oi * 128:(oi + 1) * 128]
                elif oi < 2 * FC:
                    dst = d_kv_in.ap().rearrange("(tc qs p) c -> p tc qs c", p=128, qs=4)[
                        :, tc, :, (oi - FC) * 128:(oi - FC + 1) * 128]
                else:
                    dst = d_kv_in.ap().rearrange("(tc qs p) c -> p tc qs c", p=128, qs=4)[
                        :, tc, :, D + (oi - 2 * FC) * 128:D + (oi - 2 * FC + 1) * 128]
                return s.dma_start(out=dst, in_=tT7[:, i % 2, :].rearrange("p (qs c) -> p qs c", qs=4))

            b.proj("s7b", 3 * FC, TC, FC, [128, 512],
                   lambda kc, oi, tc: wb7[:, oi % 2, kc, :],
                   lambda kc, oi, tc: A1[:, kc, tc * 512:(tc + 1) * 512],
                   steps=[("sc", e1), ("sc", e2), ("ve", e3), ("te", e4),
                          ("ve", e5), ("sy", e6)],
                   wload=(wl, 1), nps=2)

        if C7 < 5:
            return
        ops = [("gp", lambda gp: gp.collective_compute(
            "AllGather", ALU.bypass, replica_groups=RG,
            ins=[d_kv_in.ap().opt()], outs=[d_kv_all.ap().opt()]), "cc")]
        b.seq("s7ag2", ops)

        if C7 < 6:
            return
        with ExitStack() as ses:
            qTt = b.sb(ses, "s7d_q", [128, D], BF16)
            kvl = b.sb(ses, "s7d_kv", [128, NCORES, 2 * D], BF16)
            tmpb = b.sb(ses, "s7d_t", [128, D], BF16)
            tmpf = b.sb(ses, "s7d_tf", [128, D], F32)
            acc = b.sb(ses, "s7d_a", [128, D], F32)
            sct = b.sb(ses, "s7d_s", [128, 128], F32)
            pexp = b.sb(ses, "s7d_p", [128, 128], F32)
            m16 = b.sb(ses, "s7d_m", [128, 16], F32)
            r16 = b.sb(ses, "s7d_r", [128, 16], F32)
            hn = b.sb(ses, "s7d_h", [128, D], F32)
            for sck in range(KT):
                ops = []
                ops.append(("sy", (lambda sck=sck: lambda s: s.dma_start(
                    out=qTt[:, :], in_=d_qT[sck * 128:(sck + 1) * 128, :]))(), True))
                for l in range(NCORES):
                    ops.append(("sy", (lambda sck=sck, l=l: lambda s: s.dma_start(
                        out=kvl[:, l, :],
                        in_=d_kv_all[l * T + sck * 128: l * T + (sck + 1) * 128, :]))(), True))
                for l in range(NCORES):
                    ops.append(("ve", (lambda l=l: lambda ve: ve.tensor_tensor(
                        tmpb[:, :], qTt[:, :], kvl[:, l, 0:D], ALU.mult))(), False))
                    ops.append(("ve", (lambda l=l: lambda ve: ve.tensor_reduce(
                        sct[:, :].rearrange("p (h l) -> p h l", l=NCORES)[:, :, l:l + 1],
                        tmpb[:, :].rearrange("p (h d) -> p h d", d=HD),
                        AX.X, ALU.add))(), False))
                ops.append(("ve", lambda ve: ve.tensor_reduce(
                    m16[:, :].rearrange("p (h u) -> p h u", u=1),
                    sct[:, :].rearrange("p (h l) -> p h l", l=NCORES), AX.X, ALU.max), False))
                ops.append(("ve", lambda ve: ve.tensor_tensor(
                    sct[:, :].rearrange("p (h l) -> p h l", l=NCORES),
                    sct[:, :].rearrange("p (h l) -> p h l", l=NCORES),
                    m16[:, :].rearrange("p (h u) -> p h u", u=1).broadcast_to((128, 16, NCORES)),
                    ALU.subtract), False))
                ops.append(("sc", lambda sc: sc.activation(
                    pexp[:, :], sct[:, :], AF.Exp, scale=scol[:, C_ATT:C_ATT + 1]), False))
                ops.append(("ve", lambda ve: ve.tensor_reduce(
                    m16[:, :].rearrange("p (h u) -> p h u", u=1),
                    pexp[:, :].rearrange("p (h l) -> p h l", l=NCORES), AX.X, ALU.add), False))
                ops.append(("ve", lambda ve: ve.reciprocal(r16[:, :], m16[:, :]), False))
                ops.append(("ve", lambda ve: ve.tensor_scalar(
                    r16[:, :], r16[:, :], scol[:, C_SX:C_SX + 1], None, ALU.mult), False))
                ops.append(("ve", lambda ve: ve.memset(acc[:, :], 0.0), False))
                for l in range(NCORES):
                    ops.append(("ve", (lambda l=l: lambda ve: ve.tensor_tensor(
                        tmpf[:, :].rearrange("p (h d) -> p h d", d=HD),
                        kvl[:, l, D:2 * D].rearrange("p (h d) -> p h d", d=HD),
                        pexp[:, :].rearrange("p (h l) -> p h l", l=NCORES)[:, :, l:l + 1]
                        .broadcast_to((128, 16, HD)),
                        ALU.mult))(), False))
                    ops.append(("ve", lambda ve: ve.tensor_tensor(
                        acc[:, :], acc[:, :], tmpf[:, :], ALU.add), False))
                ops.append(("ve", lambda ve: ve.tensor_tensor(
                    hn[:, :].rearrange("p (h d) -> p h d", d=HD),
                    acc[:, :].rearrange("p (h d) -> p h d", d=HD),
                    r16[:, :].rearrange("p (h u) -> p h u", u=1).broadcast_to((128, 16, HD)),
                    ALU.mult), False))
                ops.append(("ve", (lambda sck=sck: lambda ve: ve.reduce_max(
                    nrm[:, sck:sck + 1], hn[:, :], AX.X, apply_absolute_value=True))(), False))
                ops.append(("sy", (lambda sck=sck: lambda s: s.dma_start(
                    out=d_heads[sck * 128:(sck + 1) * 128, :], in_=hn[:, :]))(), True))
                b.seq(f"s7d_{sck}", ops)

        if C7 < 7:
            return
        with ExitStack() as ses:
            tpp = b.ps(ses, "s7g3_tp", [128, 128])
            ops = []
            ops.append(("ve", lambda ve: ve.reduce_max(nrm[:, 120:121], nrm[:, 0:KT], AX.X), False))
            ops.append(("te", lambda te: te.transpose(tpp[0:1, 0:128], nrm[:, 120:121], identF[:, :]), False))
            ops.append(("ve", lambda ve: ve.reduce_max(g7[0:1, 0:1], tpp[0:1, 0:128], AX.X), False))
            ops.append(("gp", lambda gp: gp.dma_start(
                out=ag_in[1].ap()[0:1, 0:1], in_=g7[0:1, 0:1],
                allow_slow_non_contiguous=True), True))
            ops.append(("gp", lambda gp: gp.collective_compute(
                "AllGather", ALU.bypass, replica_groups=RG,
                ins=[ag_in[1].ap().opt()], outs=[ag_out[1].ap().opt()]), "cc"))
            ops.append(("gp", lambda gp: gp.dma_start(out=agt[:, :], in_=ag_out[1].ap()), True))
            ops.append(("te", lambda te: te.transpose(tpp[0:16, 0:NCORES], agt[:, :], identF[0:NCORES, 0:NCORES]), False))
            ops.append(("ve", lambda ve: ve.reduce_max(g7[0:1, 1:2], tpp[0:1, 0:NCORES], AX.X), False))
            ops.append(("ve", lambda ve: ve.tensor_scalar(
                sc_row[0:1, C_SH:C_SH + 1], g7[0:1, 1:2], 1.0 / 127.0, 1e-8,
                ALU.mult, ALU.max), False))
            ops.append(("ve", lambda ve: ve.reciprocal(
                sc_row[0:1, C_FH:C_FH + 1], sc_row[0:1, C_SH:C_SH + 1]), False))
            ops.append(("ve", lambda ve: ve.tensor_tensor(
                sc_row[0:1, C_SHWO:C_SHWO + 1], sc_row[0:1, C_SH:C_SH + 1],
                sc_row[0:1, C_SWO:C_SWO + 1], ALU.mult), False))
            bcast_scol(ops, tpp)
            b.seq("s7ag3", ops)

        if C7 < 8:
            return
        with ExitStack() as ses:
            ht = b.sb(ses, "s7e_h", [128, D], F32)
            hi = b.sb(ses, "s7e_i", [128, D], I32)
            hb = b.sb(ses, "s7e_b", [128, D], BF16)
            tpe = b.ps(ses, "s7e_tp", [128, 128], BF16)
            for kt in range(KT):
                ops = []
                ops.append(("sy", (lambda kt=kt: lambda s: s.dma_start(
                    out=ht[:, :], in_=d_heads[kt * 128:(kt + 1) * 128, :]))(), True))
                ops.append(("sc", lambda sc: sc.activation(
                    hi[:, :], ht[:, :], AF.Identity, scale=scol[:, C_FH:C_FH + 1]), False))
                ops.append(("ve", lambda ve: ve.tensor_copy(hb[:, :], hi[:, :]), False))
                for fc in range(FC):
                    ops.append(("te", (lambda fc=fc: lambda te: te.transpose(
                        tpe[:, :], hb[:, fc * 128:(fc + 1) * 128], identB[:, :]))(), False))
                    ops.append(("ve", (lambda fc=fc, kt=kt: lambda ve: ve.tensor_copy(
                        hqF[:, fc, kt * 128:(kt + 1) * 128], tpe[:, :]))(), False))
                b.seq(f"s7e_{kt}", ops)

        with ExitStack() as ses:
            wb7 = b.sb(ses, "s7f_w", [128, 2, FC, 128], BF16)
            yst = b.sb(ses, "s7f_st", [128, 2, 512], BF16)

            def wl(gp, oi):
                src = d_woq.ap().rearrange("(kc p) o -> p kc o", p=128)[:, :, oi * 128:(oi + 1) * 128]
                gp.dma_start(out=wb7[:, oi % 2, :, :], in_=src).then_inc(SEM["GS"], 16)

            def f1(sc, ps, i, oi, tc):
                return sc.activation(yst[:, i % 2, :], ps[:, :], AF.Identity,
                                     scale=scol[:, C_SHWO:C_SHWO + 1], bias=outb_s[:, oi:oi + 1])

            def f2(ve, ps, i, oi, tc):
                return ve.reduce_max(nrm[:, i:i + 1], yst[:, i % 2, :], AX.X,
                                     apply_absolute_value=True)

            b.proj("s7f", FC, TC, FC, [128, 512],
                   lambda kc, oi, tc: wb7[:, oi % 2, kc, :],
                   lambda kc, oi, tc: hqF[:, kc, tc * 512:(tc + 1) * 512],
                   steps=[("sc", f1), ("ve", f2)], wload=(wl, 1), nps=2)

        with ExitStack() as ses:
            tpp = b.ps(ses, "s7g4_tp", [128, 128])
            ops = []
            ops.append(("ve", lambda ve: ve.reduce_max(nrm[:, 120:121], nrm[:, 0:FC * TC], AX.X), False))
            ops.append(("te", lambda te: te.transpose(tpp[0:1, 0:128], nrm[:, 120:121], identF[:, :]), False))
            ops.append(("ve", lambda ve: ve.reduce_max(g7[0:1, 0:1], tpp[0:1, 0:128], AX.X), False))
            ops.append(("gp", lambda gp: gp.dma_start(
                out=ag_in[2].ap()[0:1, 0:1], in_=g7[0:1, 0:1],
                allow_slow_non_contiguous=True), True))
            ops.append(("gp", lambda gp: gp.collective_compute(
                "AllGather", ALU.bypass, replica_groups=RG,
                ins=[ag_in[2].ap().opt()], outs=[ag_out[2].ap().opt()]), "cc"))
            ops.append(("gp", lambda gp: gp.dma_start(out=agt[:, :], in_=ag_out[2].ap()), True))
            ops.append(("te", lambda te: te.transpose(tpp[0:16, 0:NCORES], agt[:, :], identF[0:NCORES, 0:NCORES]), False))
            ops.append(("ve", lambda ve: ve.reduce_max(g7[0:1, 1:2], tpp[0:1, 0:NCORES], AX.X), False))
            ops.append(("ve", lambda ve: ve.tensor_scalar(
                sc_row[0:1, C_SY:C_SY + 1], g7[0:1, 1:2], 1.0 / 127.0, 1e-8,
                ALU.mult, ALU.max), False))
            ops.append(("ve", lambda ve: ve.reciprocal(
                sc_row[0:1, C_RSY:C_RSY + 1], sc_row[0:1, C_SY:C_SY + 1]), False))
            ops.append(("ve", lambda ve: ve.tensor_tensor(
                sc_row[0:1, C_FY:C_FY + 1], sc_row[0:1, C_SHWO:C_SHWO + 1],
                sc_row[0:1, C_RSY:C_RSY + 1], ALU.mult), False))
            bcast_scol(ops, tpp)
            ops2 = [("ve", lambda ve: ve.tensor_scalar(
                outb2_s[:, :], outb_s[:, :], scol[:, C_RSY:C_RSY + 1], None, ALU.mult), False)]
            b.seq("s7ag4", ops + ops2)

        with ExitStack() as ses:
            wb7 = b.sb(ses, "s7g_w", [128, 2, FC, 128], BF16)
            i7a = b.sb(ses, "s7g_i", [128, 2, 512], I32)
            st7 = b.sb(ses, "s7g_st", [128, 2, 512], BF16)
            tT7 = b.sb(ses, "s7g_tt", [128, 2, 512], BF16)
            pst7 = [b.ps(ses, f"s7g_tp{j}", [128, 512], BF16) for j in range(2)]

            def wl(gp, oi):
                src = d_woq.ap().rearrange("(kc p) o -> p kc o", p=128)[:, :, oi * 128:(oi + 1) * 128]
                gp.dma_start(out=wb7[:, oi % 2, :, :], in_=src).then_inc(SEM["GS"], 16)

            def g1(sc, ps, i, oi, tc):
                return sc.activation(i7a[:, i % 2, :], ps[:, :], AF.Identity,
                                     scale=scol[:, C_FY:C_FY + 1], bias=outb2_s[:, oi:oi + 1])

            def g2(ve, ps, i, oi, tc):
                return ve.tensor_copy(st7[:, i % 2, :], i7a[:, i % 2, :])

            def g3(te, ps, i, oi, tc):
                inst = None
                for qs in range(4):
                    inst = te.transpose(pst7[i % 2][:, qs * 128:(qs + 1) * 128],
                                        st7[:, i % 2, qs * 128:(qs + 1) * 128], identB[:, :])
                return inst

            def g4(ve, ps, i, oi, tc):
                return ve.tensor_copy(tT7[:, i % 2, :], pst7[i % 2][:, :])

            def g5(s, ps, i, oi, tc):
                dst = d_yqT.ap().rearrange("(tc qs p) c -> p tc qs c", p=128, qs=4)[
                    :, tc, :, oi * 128:(oi + 1) * 128]
                return s.dma_start(out=dst, in_=tT7[:, i % 2, :].rearrange("p (qs c) -> p qs c", qs=4))

            b.proj("s7g", FC, TC, FC, [128, 512],
                   lambda kc, oi, tc: wb7[:, oi % 2, kc, :],
                   lambda kc, oi, tc: hqF[:, kc, tc * 512:(tc + 1) * 512],
                   steps=[("sc", g1), ("ve", g2), ("te", g3), ("ve", g4), ("sy", g5)],
                   wload=(wl, 1), nps=2)


def build_s8(b, nc, L):
    (T, D, F, H, E, HD, FC, HC, TC, KT, NROT) = (
        L["T"], L["D"], L["F"], L["H"], L["E"], L["HD"], L["FC"], L["HC"],
        L["TC"], L["KT"], L["NROT"])
    HE, OC = H * E, H * E * HD // 128
    SEM = b.sem
    A1, scol, nrm, cbias = L["A1"], L["scol"], L["nrm"], L["cbias"]
    identB, identB2, identF, onesT = L["identB"], L["identB2"], L["identF"], L["onesT"]
    ropePb, negB1 = L["ropePb"], L["negB1"]
    b1L_s, b2L_s = L["b1L_s"], L["b2L_s"]
    wq2, wk2, wv4, wo2, wselv, wselo = (L["wq2"], L["wk2"], L["wv4"], L["wo2"],
                                        L["wselv"], L["wselo"])
    ln1g, ln1b, ln2g, ln2b = L["ln1g"], L["ln1b"], L["ln2g"], L["ln2b"]
    w1L, w2L = L["w1L"], L["w2L"]
    d_g1, d_srcB = L["d_g1"], L["d_srcB"]
    norm_stage, attn_stage = L["norm_stage"], L["attn_stage"]
    half = NROT // 2

    with ExitStack() as r8:
        ctx1 = b.sb(r8, "ctx1", [128, FC, T], BF16)
        gvT = b.sb(r8, "gvT", [128, KT, 128], BF16)
        goF = b.sb(r8, "goF", [128, T], BF16)

        with ExitStack() as ra:
            CC_t = b.sb(ra, "CC_t", [128, T], BF16)
            SS_t = b.sb(ra, "SS_t", [128, T], BF16)

            with ExitStack() as ses:
                ang = b.sb(ses, "s8a_a", [128, T], F32)
                angr = b.sb(ses, "s8a_ar", [128, T], F32)
                sinv = b.sb(ses, "s8a_sv", [128, T], F32)
                kint = b.sb(ses, "s8a_k", [128, T], I32)
                jv = b.sb(ses, "s8a_j", [128, 6], F32)
                tid = ang
                ops = []
                ops.append(("gp", lambda gp: gp.iota(
                    tid[:, :], [[1, T]], base=0, channel_multiplier=0,
                    allow_small_or_imprecise_dtypes=True), False))
                ops.append(("gp", lambda gp: gp.iota(
                    jv[:, 0:1], [[0, 1]], base=0, channel_multiplier=1,
                    allow_small_or_imprecise_dtypes=True), False))
                def jb(ve):
                    ve.tensor_scalar(jv[64:96, 0:1], jv[64:96, 0:1], 64.0, None, ALU.subtract)
                    ve.drain()
                    ve.tensor_scalar(jv[:, 1:2], jv[:, 0:1], 16.0, -16.0, ALU.is_ge, ALU.mult)
                    ve.drain()
                    return ve.tensor_tensor(jv[:, 0:1], jv[:, 0:1], jv[:, 1:2], ALU.add)
                ops.append(("ve", jb, False))
                ops.append(("gp", lambda gp: gp.iota(
                    jv[:, 4:5], [[0, 1]], base=0, channel_multiplier=1,
                    allow_small_or_imprecise_dtypes=True), False))
                def sgn_build(ve):
                    ve.tensor_scalar(jv[:, 5:6], jv[:, 4:5], 64.0, -64.0, ALU.is_ge, ALU.mult)
                    ve.drain()
                    ve.tensor_tensor(jv[:, 4:5], jv[:, 4:5], jv[:, 5:6], ALU.add)
                    ve.drain()
                    ve.tensor_scalar(jv[:, 2:3], jv[:, 4:5], 16.0, None, ALU.is_lt)
                    ve.drain()
                    ve.tensor_scalar(jv[:, 3:4], jv[:, 4:5], 32.0, None, ALU.is_lt)
                    ve.drain()
                    return ve.scalar_tensor_tensor(jv[:, 4:5], jv[:, 2:3], -2.0,
                                                   jv[:, 3:4], ALU.mult, ALU.add)
                ops.append(("ve", sgn_build, False))
                ops.append(("sc", lambda sc: sc.activation(
                    jv[:, 1:2], jv[:, 0:1], AF.Exp, scale=-LN1E4 / half), False))
                ops.append(("ve", lambda ve: ve.tensor_tensor(
                    ang[:, :], tid[:, :], jv[:, 1:2].broadcast_to((128, T)), ALU.mult), False))
                ops.append(("ve", lambda ve: ve.tensor_scalar(
                    kint[:, :], ang[:, :], 1.0 / TWO_PI, None, ALU.mult), False))
                ops.append(("ve", lambda ve: ve.scalar_tensor_tensor(
                    angr[:, :], kint[:, :], -TWO_PI, ang[:, :], ALU.mult, ALU.add), False))
                ops.append(("sc", lambda sc: sc.activation(sinv[:, :], angr[:, :], AF.Sin), False))
                ops.append(("ve", lambda ve: ve.tensor_tensor(
                    SS_t[:, :], sinv[:, :], jv[:, 4:5].broadcast_to((128, T)), ALU.mult), False))
                ops.append(("ve", lambda ve: ve.tensor_scalar(
                    ang[:, :], ang[:, :], TWO_PI / 4.0, None, ALU.add), False))
                ops.append(("ve", lambda ve: ve.tensor_scalar(
                    kint[:, :], ang[:, :], 1.0 / TWO_PI, None, ALU.mult), False))
                ops.append(("ve", lambda ve: ve.scalar_tensor_tensor(
                    angr[:, :], kint[:, :], -TWO_PI, ang[:, :], ALU.mult, ALU.add), False))
                ops.append(("sc", lambda sc: sc.activation(sinv[:, :], angr[:, :], AF.Sin), False))
                ops.append(("ve", lambda ve: ve.tensor_tensor(
                    sinv[:, :], sinv[:, :], jv[:, 3:4].broadcast_to((128, T)), ALU.mult), False))
                ops.append(("ve", lambda ve: ve.tensor_scalar(
                    jv[:, 5:6], jv[:, 3:4], -1.0, 1.0, ALU.mult, ALU.add), False))
                ops.append(("ve", lambda ve: ve.tensor_tensor(
                    CC_t[:, :], sinv[:, :], jv[:, 5:6].broadcast_to((128, T)), ALU.add), False))
                b.seq("s8a", ops)

            qmF = b.sb(ra, "qmF", [128, FC, T], BF16)
            kmF = b.sb(ra, "kmF", [128, FC, T], BF16)
            VT1 = b.sb(ra, "VT1", [128, KT, H, 66], BF16)

            for (wsrc, dstF, nm) in [(wq2, qmF, "s8bq"), (wk2, kmF, "s8bk")]:
                with ExitStack() as ses:
                    wb8 = b.sb(ses, f"{nm}_w", [128, 2, FC, 128], BF16)

                    def wl(gp, oi, wsrc=wsrc, wb8=wb8):
                        src = wsrc.ap().rearrange("(kc p) o -> p kc o", p=128)[:, :, oi * 128:(oi + 1) * 128]
                        gp.dma_start(out=wb8[:, oi % 2, :, :], in_=src).then_inc(SEM["GS"], 16)

                    def e1(sc, ps, i, oi, tc, dstF=dstF):
                        return sc.activation(dstF[:, oi, tc * 512:(tc + 1) * 512],
                                             ps[:, :], AF.Identity)

                    b.proj(nm, FC, TC, FC, [128, 512],
                           (lambda wb8=wb8: lambda kc, oi, tc: wb8[:, oi % 2, kc, :])(),
                           lambda kc, oi, tc: A1[:, kc, tc * 512:(tc + 1) * 512],
                           steps=[("sc", e1)], wload=(wl, 1), nps=2)
                with ExitStack() as ses:
                    u8 = b.sb(ses, f"{nm}_u", [128, 512], BF16)
                    t8_ = b.sb(ses, f"{nm}_t2", [128, 512], BF16)
                    psr = b.ps(ses, f"{nm}_ps", [128, 512])
                    for fc in range(FC):
                        ops = []
                        for tc in range(TC):
                            ops.append(("te", (lambda fc=fc, tc=tc, dstF=dstF: lambda te: te.matmul(
                                psr[:, :], ropePb[:, :], dstF[:, fc, tc * 512:(tc + 1) * 512],
                                start=True, stop=True))(), False))
                            ops.append(("ve", (lambda tc=tc: lambda ve: ve.tensor_tensor(
                                u8[:, :], psr[:, :], SS_t[:, tc * 512:(tc + 1) * 512],
                                ALU.mult))(), False))
                            ops.append(("ve", (lambda fc=fc, tc=tc, dstF=dstF: lambda ve: ve.tensor_tensor(
                                t8_[:, :], dstF[:, fc, tc * 512:(tc + 1) * 512],
                                CC_t[:, tc * 512:(tc + 1) * 512], ALU.mult))(), False))
                            ops.append(("ve", (lambda fc=fc, tc=tc, dstF=dstF: lambda ve: ve.tensor_tensor(
                                dstF[:, fc, tc * 512:(tc + 1) * 512], t8_[:, :], u8[:, :],
                                ALU.add))(), False))
                        b.seq(f"{nm}_rope{fc}", ops)

            norm_stage("s8n", qmF, kmF, negB1, 0.125)

            with ExitStack() as ses:
                wsel_s = b.sb(ses, "s8c_w", [128, 2, FC, 128], BF16)
                gt = b.sb(ses, "s8c_g", [128, 128], F32)
                eqt = b.sb(ses, "s8c_e", [128, 128], F32)
                m1g = b.sb(ses, "s8c_m", [128, 16], F32)
                psg = b.ps(ses, "s8c_ps", [128, 128])
                pst = b.ps(ses, "s8c_pt", [128, 128])
                with nc.Block() as blk:
                    @blk.gpsimd
                    def _(gp):
                        gp.dma_start(out=wsel_s[:, 0, :, :],
                                     in_=wselv.ap().rearrange("(kc p) o -> p kc o", p=128)).then_inc(SEM["GS"], 16)
                        gp.dma_start(out=wsel_s[:, 1, :, :],
                                     in_=wselo.ap().rearrange("(kc p) o -> p kc o", p=128)).then_inc(SEM["GS"], 16)
                        b.c["GS"] += 32
                for gidx in range(2):
                    for kt in range(KT):
                        ops = []
                        def mm(te, gidx=gidx, kt=kt):
                            inst = None
                            for dc in range(FC):
                                inst = te.matmul(psg[:, :], A1[:, dc, kt * 128:(kt + 1) * 128],
                                                 wsel_s[:, gidx, dc, :],
                                                 start=(dc == 0), stop=(dc == FC - 1))
                            return inst
                        ops.append(("te", mm, False))
                        ops.append(("sc", lambda sc: sc.activation(gt[:, :], psg[:, :], AF.Sigmoid), False))
                        gv_ = gt[:, :].rearrange("p (h e) -> p h e", e=E)
                        m1v = m1g[:, :].rearrange("p (h u) -> p h u", u=1)
                        ops.append(("ve", lambda ve: ve.tensor_reduce(m1v, gv_, AX.X, ALU.max), False))
                        ops.append(("ve", lambda ve: ve.tensor_tensor(
                            eqt[:, :].rearrange("p (h e) -> p h e", e=E), gv_,
                            m1v.broadcast_to((128, H, E)), ALU.is_equal), False))
                        ops.append(("ve", lambda ve: ve.scalar_tensor_tensor(
                            eqt[:, :], eqt[:, :], -1e30, gt[:, :], ALU.mult, ALU.add), False))
                        ops.append(("ve", lambda ve: ve.tensor_reduce(
                            m1v, eqt[:, :].rearrange("p (h e) -> p h e", e=E), AX.X, ALU.max), False))
                        ops.append(("ve", lambda ve: ve.tensor_tensor(
                            eqt[:, :].rearrange("p (h e) -> p h e", e=E), gv_,
                            m1v.broadcast_to((128, H, E)), ALU.is_ge), False))
                        if gidx == 0:
                            ops.append(("ve", (lambda kt=kt: lambda ve: ve.tensor_tensor(
                                gvT[:, kt, :], gt[:, :], eqt[:, :], ALU.mult))(), False))
                        else:
                            ops.append(("ve", lambda ve: ve.tensor_tensor(
                                gt[:, :], gt[:, :], eqt[:, :], ALU.mult), False))
                            ops.append(("te", lambda te: te.transpose(
                                pst[:, :], gt[:, :], identF[:, :]), False))
                            ops.append(("ve", (lambda kt=kt: lambda ve: ve.tensor_copy(
                                goF[:, kt * 128:(kt + 1) * 128], pst[:, :]))(), False))
                        b.seq(f"s8c_{gidx}_{kt}", ops)

            with ExitStack() as ses:
                NG = 2
                wvs = b.sb(ses, "s8d_w", [128, NG, FC, E * HD], BF16)
                vtmp = b.sb(ses, "s8d_t", [128, E * HD], F32)
                psv = [b.ps(ses, f"s8d_p{j}", [128, E * HD]) for j in range(NG)]
                ops0 = [("gp", lambda gp: gp.memset(VT1[:, :, :, 64:65], 1.0), False)]
                b.seq("s8d_ones", ops0)
                for hg in range(H // NG):
                    with nc.Block() as blk:
                        @blk.gpsimd
                        def _(gp, hg=hg):
                            for j in range(NG):
                                src = wv4.ap()[hg * NG + j, :, :].rearrange(
                                    "(kc p) o -> p kc o", p=128)
                                gp.dma_start(out=wvs[:, j, :, :], in_=src).then_inc(SEM["GS"], 16)
                                b.c["GS"] += 16
                    for kt in range(KT):
                        ops = []
                        def mm(te, hg=hg, kt=kt):
                            inst = None
                            for dc in range(FC):
                                for j in range(NG):
                                    inst = te.matmul(psv[j][:, :],
                                                     A1[:, dc, kt * 128:(kt + 1) * 128],
                                                     wvs[:, j, dc, :],
                                                     start=(dc == 0), stop=(dc == FC - 1))
                            return inst
                        ops.append(("te", mm, False))
                        for j in range(NG):
                            h = hg * NG + j
                            def gvm(ve, j=j, h=h, kt=kt):
                                return ve.tensor_tensor(
                                    vtmp[:, :].rearrange("p (pp e) -> p e pp", e=E),
                                    psv[j][:, :].rearrange("p (e pp) -> p e pp", e=E),
                                    gvT[:, kt, h * E:(h + 1) * E].rearrange(
                                        "p (e u) -> p e u", u=1).broadcast_to((128, E, HD)),
                                    ALU.mult)
                            ops.append(("ve", gvm, False))
                            def rd(ve, j=j, h=h, kt=kt):
                                with nc.allow_low_precision(reason="bf16 store of f32-accumulated expert sum"):
                                    return ve.tensor_reduce(
                                        VT1[:, kt, h, 0:64].rearrange("p (d u) -> p d u", u=1),
                                        vtmp[:, :].rearrange("p (d e) -> p d e", e=E),
                                        AX.X, ALU.add)
                            ops.append(("ve", rd, False))
                        b.seq(f"s8d_{hg}_{kt}", ops)

            attn_stage("s8e", qmF, kmF, VT1, negB1, None, ctx1, causal=False)

        with ExitStack() as rf:
            SELh = b.sb(rf, "SELh", [128, OC * 128], BF16)
            with nc.Block() as blk:
                @blk.gpsimd
                def _(gp):
                    gp.memset(SELh[:, :], 1.0)
                    gp.drain()
                    gp.affine_select(
                        out=SELh[:, :].rearrange("p (c h j) -> p c h j", h=2, j=64),
                        in_=SELh[:, :].rearrange("p (c h j) -> p c h j", h=2, j=64),
                        pattern=[[-2, OC], [-1, 2], [0, 64]],
                        compare_op=ALU.is_equal, fill=0.0, base=0, channel_multiplier=1)
                    gp.drain()
            with ExitStack() as ses:
                wo_s = b.sb(ses, "s8f_w", [128, 8, D], BF16)
                cR = b.sb(ses, "s8f_cr", [128, 2, 512], BF16)
                ohb = b.sb(ses, "s8f_oh", [128, 8, 512], BF16)
                ost = b.sb(ses, "s8f_ost", [128, 2, 512], F32)
                pcr = b.ps(ses, "s8f_pc", [128, 512])
                pgr = [b.ps(ses, f"s8f_pg{j}", [128, 512]) for j in range(2)]
                pso = b.ps(ses, "s8f_po", [128, 512])
                for grp in range(OC // 8):
                    with nc.Block() as blk:
                        @blk.gpsimd
                        def _(gp, grp=grp):
                            src = wo2.ap()[grp * 1024:(grp + 1) * 1024, :].rearrange(
                                "(kc p) o -> p kc o", p=128)
                            gp.dma_start(out=wo_s[:, :, :], in_=src).then_inc(SEM["GS"], 16)
                            b.c["GS"] += 16
                    for tc in range(TC):
                        ops = []
                        for jj in range(2):
                            hh = 2 * grp + jj
                            def crm(te, hh=hh, tc=tc):
                                off = (hh % 2) * 64
                                return te.matmul(pcr[:, :], identB2[off:off + 64, :],
                                                 ctx1[off:off + 64, hh // 2, tc * 512:(tc + 1) * 512],
                                                 start=True, stop=True)
                            ops.append(("te", crm, False))
                            def cre(ve, jj=jj):
                                return ve.tensor_copy(cR[:, jj, :], pcr[:, :])
                            ops.append(("ve", cre, False))
                        for c8 in range(8):
                            c = grp * 8 + c8
                            jj = (c // 4) % 2
                            def grm(te, c=c, tc=tc, c8=c8):
                                return te.matmul(pgr[c8 % 2][:, :], SELh[:, c * 128:(c + 1) * 128],
                                                 goF[:, tc * 512:(tc + 1) * 512],
                                                 start=True, stop=True)
                            ops.append(("te", grm, False))
                            def ohm(ve, c8=c8, jj=jj):
                                return ve.tensor_tensor(ohb[:, c8, :], cR[:, jj, :],
                                                        pgr[c8 % 2][:, :], ALU.mult)
                            ops.append(("ve", ohm, False))
                        for ofc in range(FC):
                            def opm(te, ofc=ofc, tc=tc):
                                inst = None
                                for c8 in range(8):
                                    inst = te.matmul(pso[:, :], wo_s[:, c8, ofc * 128:(ofc + 1) * 128],
                                                     ohb[:, c8, :], start=(c8 == 0), stop=(c8 == 7))
                                return inst
                            ops.append(("te", opm, False))
                            if grp == 0:
                                def acc0(ve, ofc=ofc, tc=tc):
                                    return ve.tensor_tensor(
                                        ost[:, ofc % 2, :], pso[:, :],
                                        A1[:, ofc, tc * 512:(tc + 1) * 512], ALU.add)
                                ops.append(("ve", acc0, False))
                            else:
                                def accn(ve, ofc=ofc):
                                    return ve.tensor_copy(ost[:, ofc % 2, :], pso[:, :])
                                ops.append(("ve", accn, False))
                            def accd(gp, ofc=ofc, tc=tc, grp=grp):
                                return gp.dma_start(
                                    out=d_srcB[ofc * 128:(ofc + 1) * 128, tc * 512:(tc + 1) * 512],
                                    in_=ost[:, ofc % 2, :],
                                    accum_op=(ALU.bypass if grp == 0 else ALU.add))
                            ops.append(("gp", accd, True))
                        b.seq(f"s8f_{grp}_{tc}", ops)

        if True:
            with ExitStack() as rg:
                xlnF = b.sb(rg, "xlnF", [128, FC, T], BF16)
                gbx = b.sb(rg, "s8_gb1", [128, D], F32)
                bbx = b.sb(rg, "s8_bb1", [128, D], F32)
                ln_tmaj(b, nc, L, "s8g", d_srcB, xlnF, ln1g, ln1b, gbx, bbx,
                        silu_out=None, g1_dst=None)

                with ExitStack() as ses:
                    hst = b.sb(ses, "s8h_h", [128, HC, 512], BF16)
                    for tc in range(TC):
                        with ExitStack() as s2:
                            wb8 = b.sb(s2, "s8h_w1", [128, 2, FC, 128], BF16)

                            def wl1(gp, oi, wb8=wb8):
                                src = w1L.ap().rearrange("(kc p) o -> p kc o", p=128)[
                                    :, :, oi * 128:(oi + 1) * 128]
                                gp.dma_start(out=wb8[:, oi % 2, :, :], in_=src).then_inc(SEM["GS"], 16)

                            def h1(sc, ps, i, oi, tc_, hst=hst):
                                return sc.activation(hst[:, oi, :], ps[:, :], AF.Relu,
                                                     bias=b1L_s[:, oi:oi + 1])

                            b.proj(f"s8h1_{tc}", HC, 1, FC, [128, 512],
                                   (lambda wb8=wb8: lambda kc, oi, tc_: wb8[:, oi % 2, kc, :])(),
                                   (lambda tc=tc: lambda kc, oi, tc_: xlnF[:, kc, tc * 512:(tc + 1) * 512])(),
                                   steps=[("sc", h1)], wload=(wl1, 1), nps=2)
                        with ExitStack() as s2:
                            wb9 = b.sb(s2, "s8h_w2", [128, 2, HC, 128], BF16)

                            def wl2(gp, oi, wb9=wb9):
                                src = w2L.ap().rearrange("(kc p) o -> p kc o", p=128)[
                                    :, :, oi * 128:(oi + 1) * 128]
                                gp.dma_start(out=wb9[:, oi % 2, :, :], in_=src).then_inc(SEM["GS"], 16)

                            mst = b.sb(s2, "s8h_mst", [128, 2, 512], F32)

                            def h2(ve, ps, i, oi, tc_, mst=mst):
                                return ve.tensor_scalar(mst[:, i % 2, :], ps[:, :],
                                                        b2L_s[:, oi:oi + 1], None, ALU.add)

                            def h3(gp, ps, i, oi, tc_, mst=mst, tc=tc):
                                return gp.dma_start(
                                    out=d_srcB[oi * 128:(oi + 1) * 128, tc * 512:(tc + 1) * 512],
                                    in_=mst[:, i % 2, :], accum_op=ALU.add)

                            b.proj(f"s8h2_{tc}", FC, 1, HC, [128, 512],
                                   (lambda wb9=wb9: lambda kc, oi, tc_: wb9[:, oi % 2, kc, :])(),
                                   (lambda hst=hst: lambda kc, oi, tc_: hst[:, kc, :])(),
                                   steps=[("ve", h2), ("gp", h3)], wload=(wl2, 1), nps=2)

                ln_tmaj(b, nc, L, "s8i", d_srcB, None, ln2g, ln2b, gbx, bbx,
                        silu_out=True, g1_dst=d_g1)


LN1E4 = float(np.log(10000.0))


def ln_tmaj(b, nc, L, name, src_dram_F, dst_F, grow, brow, gbx, bbx, silu_out, g1_dst):
    (T, D, FC, KT) = (L["T"], L["D"], L["FC"], L["KT"])
    SEM = b.sem
    identF, identB, onesT, cbias = L["identF"], L["identB"], L["onesT"], L["cbias"]
    with ExitStack() as ses:
        xF = b.sb(ses, f"{name}_xF", [128, FC, 128], F32)
        xT = b.sb(ses, f"{name}_xT", [128, D], F32)
        xl = b.sb(ses, f"{name}_xl", [128, D], BF16)
        bn6 = b.sb(ses, f"{name}_bn", [128, 6 * (D // 512)], F32)
        mv = b.sb(ses, f"{name}_mv", [128, 4], F32)
        tps = b.ps(ses, f"{name}_tp", [128, 128])
        tpsb = b.ps(ses, f"{name}_tpb", [128, 128], BF16)
        tpb = b.ps(ses, f"{name}_tb", [128, 512])
        nst = D // 512
        base = b.snap()
        with nc.Block() as blk:
            @blk.sync
            def _(s):
                s.dma_start(out=gbx[0:1, :], in_=grow[:, :]).then_inc(SEM["LS"], 16)
                s.dma_start(out=bbx[0:1, :], in_=brow[:, :]).then_inc(SEM["LS"], 16)
                b.c["LS"] += 32
        ops = []
        for row in (gbx, bbx):
            for c in range(D // 512):
                ops.append(("te", (lambda row=row, c=c: lambda te: te.matmul(
                    tpb[:, :], onesT[0:1, :], row[0:1, c * 512:(c + 1) * 512],
                    start=True, stop=True))(), False))
                ops.append(("ve", (lambda row=row, c=c: lambda ve: ve.tensor_copy(
                    row[:, c * 512:(c + 1) * 512], tpb[:, :]))(), False))
        b.seq(f"{name}_gb", ops)
        for kt in range(KT):
            ops = []
            ops.append(("sy", (lambda kt=kt: lambda s: s.dma_start(
                out=xF[:, :, :],
                in_=src_dram_F.ap().rearrange("(c p) t -> p c t", p=128)[
                    :, :, kt * 128:(kt + 1) * 128]))(), True))
            for fc in range(FC):
                ops.append(("te", (lambda fc=fc: lambda te: te.transpose(
                    tps[:, :], xF[:, fc, :], identF[:, :]))(), False))
                ops.append(("ve", (lambda fc=fc: lambda ve: ve.tensor_copy(
                    xT[:, fc * 128:(fc + 1) * 128], tps[:, :]))(), False))

            def bnf(ve):
                for j in range(nst):
                    ve.bn_stats(bn6[:, j * 6:(j + 1) * 6], xT[:, j * 512:(j + 1) * 512])
                    ve.drain()
                return ve.bn_aggr(mv[:, 0:2], bn6[:, 0:6 * nst])
            ops.append(("ve", bnf, False))
            ops.append(("sc", lambda sc: sc.activation(mv[:, 2:3], mv[:, 1:2],
                                                       AF.Sqrt, bias=cbias[:, 0:1]), False))

            def appf(ve):
                ve.reciprocal(mv[:, 3:4], mv[:, 2:3])
                ve.drain()
                ve.tensor_scalar(xT[:, :], xT[:, :], mv[:, 0:1], mv[:, 3:4],
                                 ALU.subtract, ALU.mult)
                ve.drain()
                ve.tensor_tensor(xT[:, :], xT[:, :], gbx[:, :], ALU.mult)
                ve.drain()
                return ve.tensor_tensor(xT[:, :], xT[:, :], bbx[:, :], ALU.add)
            ops.append(("ve", appf, False))
            if silu_out:
                ops.append(("sc", lambda sc: sc.activation(xl[:, :], xT[:, :], AF.Silu), False))
                ops.append(("sy", (lambda kt=kt: lambda s: s.dma_start(
                    out=g1_dst[kt * 128:(kt + 1) * 128, :], in_=xl[:, :]))(), True))
            else:
                ops.append(("ve", lambda ve: ve.tensor_copy(xl[:, :], xT[:, :]), False))
                for fc in range(FC):
                    ops.append(("te", (lambda fc=fc: lambda te: te.transpose(
                        tpsb[:, :], xl[:, fc * 128:(fc + 1) * 128], identB[:, :]))(), False))
                    ops.append(("ve", (lambda fc=fc, kt=kt: lambda ve: ve.tensor_copy(
                        dst_F[:, fc, kt * 128:(kt + 1) * 128], tpsb[:, :]))(), False))
                    ops.append(("te", (lambda fc=fc: lambda te: te.transpose(
                        tps[:, :], xT[:, fc * 128:(fc + 1) * 128], identF[:, :]))(), False))
                    ops.append(("ve", (lambda fc=fc: lambda ve: ve.tensor_copy(
                        xF[:, fc, :], tps[:, :]))(), False))
                ops.append(("sy", (lambda kt=kt: lambda s: s.dma_start(
                    out=src_dram_F.ap().rearrange("(c p) t -> p c t", p=128)[
                        :, :, kt * 128:(kt + 1) * 128], in_=xF[:, :, :]))(), True))
            b.seq(f"{name}_{kt}", ops)


def host_layout(inputs, core):
    g = lambda k: np.ascontiguousarray(np.asarray(inputs[k], dtype=np.float32))
    T, D, F, H, E, HD = 2048, 1024, 4096, 16, 8, 64
    m = {}
    m["x"] = np.ascontiguousarray(np.asarray(inputs["L0_attn_x"][core], dtype=np.float32))
    m["wqkv"] = g("L0_attn_Wqkv")
    m["bqkvf"] = np.ascontiguousarray(g("L0_attn_bqkv").reshape(24, 128).T)
    m["wout0"] = g("L0_attn_Wout")
    m["boutf"] = np.ascontiguousarray(g("L0_attn_bout").reshape(8, 128).T)
    m["w1t"] = np.ascontiguousarray(g("L0_ffn_w1").T)
    m["b1f0"] = np.ascontiguousarray(g("L0_ffn_b1").reshape(32, 128).T)
    m["w2t"] = np.ascontiguousarray(g("L0_ffn_w2").T)
    m["b2f0"] = np.ascontiguousarray(g("L0_ffn_b2").reshape(8, 128).T)
    m["wit"] = np.ascontiguousarray(g("L1_attn_in_w").T)
    m["inbf"] = np.ascontiguousarray(g("L1_attn_in_b").reshape(24, 128).T)
    m["wot"] = np.ascontiguousarray(g("L1_attn_out_w").T)
    m["outbf"] = np.ascontiguousarray(g("L1_attn_out_b").reshape(8, 128).T)
    m["wq2"] = np.ascontiguousarray(g("L1_ffn_Wq").transpose(1, 0, 2).reshape(D, D))
    m["wk2"] = np.ascontiguousarray(g("L1_ffn_Wk").transpose(1, 0, 2).reshape(D, D))
    m["wv4"] = np.ascontiguousarray(g("L1_ffn_Wv").transpose(0, 2, 1, 3).reshape(H, D, E * HD))
    m["wo2"] = np.ascontiguousarray(g("L1_ffn_Wo").reshape(H * E * HD, D))
    m["wselv"] = np.ascontiguousarray(g("L1_ffn_Wsel_v").transpose(2, 0, 1).reshape(D, H * E))
    m["wselo"] = np.ascontiguousarray(g("L1_ffn_Wsel_o").transpose(2, 0, 1).reshape(D, H * E))
    m["ln1g"] = g("L1_ffn_ln1_g").reshape(1, D)
    m["ln1b"] = g("L1_ffn_ln1_b").reshape(1, D)
    m["ln2g"] = g("L1_ffn_ln2_g").reshape(1, D)
    m["ln2b"] = g("L1_ffn_ln2_b").reshape(1, D)
    m["w1L"] = g("L1_ffn_W1")
    m["b1Lf"] = np.ascontiguousarray(g("L1_ffn_b1").reshape(32, 128).T)
    m["w2L"] = g("L1_ffn_W2")
    m["b2Lf"] = np.ascontiguousarray(g("L1_ffn_b2").reshape(8, 128).T)
    perm = np.arange(128)
    for bb in (0, 64):
        perm[bb:bb + 16] = np.arange(bb + 16, bb + 32)
        perm[bb + 16:bb + 32] = np.arange(bb, bb + 16)
    Psel = np.zeros((128, 128), np.float32)
    Psel[perm, np.arange(128)] = 1.0
    m["ropeP"] = Psel
    return m


def kernel(**inputs):
    P = default_params()
    nc = build(P)
    in_maps = [host_layout(inputs, c) for c in range(NCORES)]
    res = run_bass_kernel_spmd(nc, in_maps, core_ids=list(range(NCORES)))
    return np.stack([np.asarray(res.results[c]["out"], dtype=np.float32)
                     for c in range(NCORES)], axis=0)


# version 02
# speedup vs baseline: 1.1522x; 1.1522x over previous
import numpy as np
import concourse.bass as bass
import concourse.mybir as mybir
from concourse.bass_utils import run_bass_kernel_spmd
from contextlib import ExitStack

F32 = mybir.dt.float32
BF16 = mybir.dt.bfloat16
I32 = mybir.dt.int32
AF = mybir.ActivationFunctionType
ALU = mybir.AluOpType
AX = mybir.AxisListType

NCORES = 8
TWO_PI = 6.283185307179586

ENGN = {"te": "tensor", "ve": "vector", "sc": "scalar", "gp": "gpsimd", "sy": "sync"}
ENG_SEM = {"te": "MM", "ve": "EV", "sc": "SC", "gp": "GS", "sy": "LS"}


def default_params():
    return dict(T=2048, D=1024, HD=64, F=4096, E=8, NROT=32, dbg=(), cut=99)


class Bld:
    def __init__(self, nc, es):
        self.nc = nc
        self.es = es
        self.sem = {}
        for s in ["LS", "GS", "AS", "MM", "EV", "SC", "CC", "X1", "X2", "X3", "X4", "X5"]:
            self.sem[s] = es.enter_context(nc.semaphore(s))
        self.c = {s: 0 for s in self.sem}

    _uid = [0]

    def _nm(self, name):
        self._uid[0] += 1
        return f"{name}_u{self._uid[0]}"

    def sb(self, es_, name, shape, dt=BF16):
        return es_.enter_context(self.nc.sbuf_tensor(self._nm(name), shape, dt))

    def ps(self, es_, name, shape, dt=F32):
        return es_.enter_context(self.nc.psum_tensor(self._nm(name), shape, dt))

    def snap(self):
        return dict(self.c)

    def seq(self, name, ops):
        nc = self.nc
        base = self.snap()
        prior, counts, incs = [], {k: 0 for k in ENG_SEM}, []
        for (ek, fn, is_dma) in ops:
            prior.append(dict(counts))
            inc = 16 if is_dma is True else 1
            counts[ek] += inc
            incs.append(inc)
        with self.nc.named_scope(f"seq_{name}"), nc.Block() as blk:
            for ek in ["te", "ve", "sc", "gp", "sy"]:
                if not any(e == ek for (e, _, _) in ops):
                    continue

                def mk(ek):
                    def closure(eng):
                        eng.wait_ge(self.sem["LS"], base["LS"])
                        eng.wait_ge(self.sem["GS"], base["GS"])
                        for i, (e, fn, is_dma) in enumerate(ops):
                            if e != ek:
                                continue
                            for oe, osem in ENG_SEM.items():
                                if oe != ek and prior[i][oe] > 0:
                                    eng.wait_ge(self.sem[osem], base[osem] + prior[i][oe])
                            if prior[i][ek] > 0:
                                eng.wait_ge(self.sem[ENG_SEM[ek]], base[ENG_SEM[ek]] + prior[i][ek])
                            inst = fn(eng)
                            assert inst is not None, f"seq op {i} in {name} returned None"
                            inst.then_inc(self.sem[ENG_SEM[ek]], incs[i])
                            if ek in ("ve", "sc", "gp") and not is_dma:
                                eng.drain()
                    return closure

                getattr(blk, ENGN[ek])(mk(ek))
        for ek, n in counts.items():
            self.c[ENG_SEM[ek]] += n

    def proj(self, name, n_oi, n_tc, n_kc, psum_shape, stat_fn, mov_fn, steps,
             wload=None, nps=2):
        nc = self.nc
        base = self.snap()
        ntiles = n_oi * n_tc
        nsteps = len(steps)
        step_inc = [16 if ek in ("sy", "gp") else 1 for (ek, _) in steps]
        bsem = ["MM"] + [f"X{j}" for j in range(1, nsteps)]
        binc = [1] + step_inc[:-1]
        wfn, wn = wload if wload else (None, 0)
        with ExitStack() as pes:
            psl = [pes.enter_context(nc.psum_tensor(f"{name}_ps{j}", list(psum_shape), F32))
                   for j in range(nps)]
            def emit_step(eng, j, i, oi, tc):
                sek, sfn = steps[j]
                eng.wait_ge(self.sem[bsem[j]], base[bsem[j]] + (i + 1) * binc[j])
                inst = sfn(eng, psl[i % nps], i, oi, tc)
                osem = "EV" if j == nsteps - 1 else bsem[j + 1]
                inst.then_inc(self.sem[osem], step_inc[j])
                if sek in ("ve", "sc"):
                    eng.drain()

            with self.nc.named_scope(f"proj_{name}"), nc.Block() as blk:
                @blk.tensor
                def _(te):
                    te.wait_ge(self.sem["LS"], base["LS"])
                    te.wait_ge(self.sem["GS"], base["GS"])
                    for i in range(ntiles):
                        oi, tc = divmod(i, n_tc)
                        if wfn and tc == 0:
                            te.wait_ge(self.sem["GS"], base["GS"] + 16 * wn * (oi + 1))
                        if i >= nps:
                            te.wait_ge(self.sem["EV"],
                                       base["EV"] + (i - nps + 1) * step_inc[-1])
                        inst = None
                        for kc in range(n_kc):
                            inst = te.matmul(psl[i % nps][:, :], stat_fn(kc, oi, tc),
                                             mov_fn(kc, oi, tc),
                                             start=(kc == 0), stop=(kc == n_kc - 1))
                        inst.then_inc(self.sem["MM"], 1)
                        for j, (sek, _) in enumerate(steps):
                            if sek == "te":
                                emit_step(te, j, i, oi, tc)

                engs = []
                for (ek, _) in steps:
                    if ek not in engs and ek not in ("te", "gp"):
                        engs.append(ek)
                for ek in engs:
                    def mk(ek):
                        def closure(eng):
                            eng.wait_ge(self.sem["LS"], base["LS"])
                            eng.wait_ge(self.sem["GS"], base["GS"])
                            for i in range(ntiles):
                                oi, tc = divmod(i, n_tc)
                                for j, (sek, _) in enumerate(steps):
                                    if sek != ek:
                                        continue
                                    emit_step(eng, j, i, oi, tc)
                        return closure
                    getattr(blk, ENGN[ek])(mk(ek))

                has_gp_steps = any(ek == "gp" for (ek, _) in steps)
                if wfn or has_gp_steps:
                    @blk.gpsimd
                    def _(gp):
                        gp.wait_ge(self.sem["LS"], base["LS"])
                        if wfn and not has_gp_steps:
                            for oi in range(n_oi):
                                if oi >= 2:
                                    gp.wait_ge(self.sem["MM"], base["MM"] + (oi - 1) * n_tc)
                                wfn(gp, oi)
                        else:
                            for i in range(ntiles):
                                oi, tc = divmod(i, n_tc)
                                if wfn and tc == 0:
                                    if oi >= 2:
                                        gp.wait_ge(self.sem["MM"], base["MM"] + (oi - 1) * n_tc)
                                    wfn(gp, oi)
                                for j, (sek, _) in enumerate(steps):
                                    if sek == "gp":
                                        emit_step(gp, j, i, oi, tc)

        self.c["MM"] += ntiles
        for j in range(1, nsteps):
            self.c[f"X{j}"] += ntiles * step_inc[j - 1]
        self.c["EV"] += ntiles * step_inc[-1]
        self.c["GS"] += 16 * wn * n_oi


def build(P):
    T, D, HD, F, E, NROT = P["T"], P["D"], P["HD"], P["F"], P["E"], P["NROT"]
    H = D // HD
    FC = D // 128
    HC = F // 128
    TC = T // 512
    KT = T // 128
    HE = H * E
    OC = HE * HD // 128
    GRP = OC // 8
    dbg = set(P["dbg"])

    nc = bass.Bass("TRN2", target_bir_lowering=False)

    def din(name, shape):
        return nc.dram_tensor(name, list(shape), F32, kind="ExternalInput")

    x_in = din("x", (T, D))
    wqkv = din("wqkv", (D, 3 * D))
    bqkvf = din("bqkvf", (128, 3 * FC))
    wout0 = din("wout0", (D, D))
    boutf = din("boutf", (128, FC))
    w1t = din("w1t", (D, F))
    b1f0 = din("b1f0", (128, HC))
    w2t = din("w2t", (F, D))
    b2f0 = din("b2f0", (128, FC))
    wit = din("wit", (D, 3 * D))
    inbf = din("inbf", (128, 3 * FC))
    wot = din("wot", (D, D))
    outbf = din("outbf", (128, FC))
    wq2 = din("wq2", (D, D))
    wk2 = din("wk2", (D, D))
    wv4 = din("wv4", (H, D, E * HD))
    wo2 = din("wo2", (HE * HD, D))
    wselv = din("wselv", (D, HE))
    wselo = din("wselo", (D, HE))
    ln1g = din("ln1g", (1, D))
    ln1b = din("ln1b", (1, D))
    ln2g = din("ln2g", (1, D))
    ln2b = din("ln2b", (1, D))
    w1L = din("w1L", (D, F))
    b1Lf = din("b1Lf", (128, HC))
    w2L = din("w2L", (F, D))
    b2Lf = din("b2Lf", (128, FC))
    ropeP = din("ropeP", (128, 128))

    out = nc.dram_tensor("out", [T, D], F32, kind="ExternalOutput")

    def scr(name, shape, dt=BF16):
        kind = "ExternalOutput" if name in dbg else "Internal"
        return nc.dram_tensor(name, list(shape), dt, kind=kind)

    ag_in = [nc.dram_tensor(f"ag_in{i}", [1, 16], F32) for i in range(3)]
    ag_out = [nc.dram_tensor(f"ag_out{i}", [NCORES, 16], F32) for i in range(3)]
    d_kv_in = scr("d_kv_in", (T, 2 * D))
    d_kv_all = nc.dram_tensor("d_kv_all", [NCORES * T, 2 * D], BF16, addr_space="Shared")

    d_xn0F = scr("d_xn0F", (D, T))
    d_xq0F = scr("d_xq0F", (D, T))
    d_QF = scr("d_QF", (D, T))
    d_KF = scr("d_KF", (D, T))
    d_VF = scr("d_VF", (D, T))
    d_s0 = scr("d_s0", (D, T))
    d_g0 = scr("d_g0", (D, T))
    d_w1q = scr("d_w1q", (D, F))
    d_w2q = scr("d_w2q", (F, D))
    d_x1 = scr("d_x1", (T, D), F32)
    d_xn1F = scr("d_xn1F", (D, T))
    d_wiq = scr("d_wiq", (D, 3 * D))
    d_woq = scr("d_woq", (D, D))
    d_qT = scr("d_qT", (T, D))
    d_heads = scr("d_heads", (T, D), F32)
    d_yqT = scr("d_yqT", (T, D))
    d_g1 = scr("d_g1", (T, D))
    d_srcB = scr("d_srcB", (D, T), F32)

    oes = ExitStack()
    with oes:
        b = Bld(nc, oes)
        SEM = b.sem

        identB = b.sb(oes, "identB", [128, 128], BF16)
        identF = b.sb(oes, "identF", [128, 128], F32)
        identB2 = b.sb(oes, "identB2", [128, 128], BF16)
        onesT = b.sb(oes, "onesT", [128, 128], F32)
        sel64b = b.sb(oes, "sel64b", [128, 2], BF16)
        ropePb = b.sb(oes, "ropePb", [128, 128], BF16)
        cbias = b.sb(oes, "cbias", [128, 4], F32)
        scol = b.sb(oes, "scol", [128, 16], F32)
        sc_row = b.sb(oes, "sc_row", [128, 16], F32)
        negB0 = b.sb(oes, "negB0", [128, 16], F32)
        negB1 = b.sb(oes, "negB1", [128, 16], F32)
        nrm = b.sb(oes, "nrm", [128, 128], F32)
        bqkv_s = b.sb(oes, "bqkv_s", [128, 3 * FC], F32)
        bout_s = b.sb(oes, "bout_s", [128, FC], F32)
        b1q_s = b.sb(oes, "b1q_s", [128, HC], F32)
        b2q_s = b.sb(oes, "b2q_s", [128, FC], F32)
        b1q16 = b.sb(oes, "b1q16", [128, HC], F32)
        inb_s = b.sb(oes, "inb_s", [128, 3 * FC], F32)
        outb_s = b.sb(oes, "outb_s", [128, FC], F32)
        outb2_s = b.sb(oes, "outb2_s", [128, FC], F32)
        b1L_s = b.sb(oes, "b1L_s", [128, HC], F32)
        b2L_s = b.sb(oes, "b2L_s", [128, FC], F32)
        A1 = b.sb(oes, "A1", [128, FC, T], BF16)

        with nc.Block() as blk:
            @blk.sync
            def _(s):
                for dst, src in [(bqkv_s, bqkvf), (bout_s, boutf), (b1q_s, b1f0),
                                 (b2q_s, b2f0), (inb_s, inbf), (outb_s, outbf),
                                 (b1L_s, b1Lf), (b2L_s, b2Lf)]:
                    s.dma_start(out=dst[:, :], in_=src[:, :]).then_inc(SEM["LS"], 16)
                    b.c["LS"] += 16

            @blk.gpsimd
            def _(gp):
                gp.dma_start(out=ropePb[:, :], in_=ropeP[:, :]).then_inc(SEM["GS"], 16)
                b.c["GS"] += 16
                for t, v in [(identB, 1.0), (identF, 1.0), (onesT, 1.0), (sel64b, 1.0)]:
                    gp.memset(t[:, :], v)
                    gp.drain()
                gp.memset(cbias[:, :], 0.0)
                gp.drain()
                gp.memset(cbias[:, 0:1], 1e-5)
                gp.drain()
                gp.memset(scol[:, :], 1.0)
                gp.drain()
                gp.memset(sc_row[:, :], 1.0)
                gp.drain()
                for ident in (identB, identF):
                    gp.affine_select(out=ident[:, :], in_=ident[:, :], pattern=[[-1, 128]],
                                     compare_op=ALU.is_equal, fill=0.0, base=0,
                                     channel_multiplier=1)
                    gp.drain()
                gp.affine_select(out=sel64b[:, :], in_=sel64b[:, :], pattern=[[64, 2]],
                                 compare_op=ALU.is_ge, fill=0.0, base=0,
                                 channel_multiplier=-1)
                gp.drain()
                gp.affine_select(out=sel64b[:, :], in_=sel64b[:, :], pattern=[[-64, 2]],
                                 compare_op=ALU.is_ge, fill=0.0, base=63,
                                 channel_multiplier=1)
                gp.drain()

        with ExitStack() as ses:
            scr1 = b.sb(ses, "s0_scr1", [128, 128], BF16)
            i32b = b.sb(ses, "s0_i32", [128, HC], I32)
            ops = []

            def mk_diag(dst, sh):
                def f(gp):
                    gp.memset(dst[:, :], 1.0)
                    gp.drain()
                    return gp.affine_select(out=dst[:, :], in_=dst[:, :],
                                            pattern=[[-1, 128]], compare_op=ALU.is_equal,
                                            fill=0.0, base=sh, channel_multiplier=1)
                return f

            ops.append(("gp", mk_diag(identB2, 0), False))
            ops.append(("gp", mk_diag(scr1, 64), False))
            ops.append(("ve", lambda ve: ve.tensor_tensor(identB2[:, :], identB2[:, :], scr1[:, :], ALU.add), False))
            ops.append(("gp", mk_diag(scr1, -64), False))
            ops.append(("ve", lambda ve: ve.tensor_tensor(identB2[:, :], identB2[:, :], scr1[:, :], ALU.add), False))
            for t, w in [(b1q_s, HC), (b2q_s, FC)]:
                ops.append(("ve", (lambda t=t, w=w: lambda ve: ve.tensor_scalar(
                    t[:, 0:w], t[:, 0:w], 64.0, 127.0, ALU.mult, ALU.min))(), False))
                ops.append(("ve", (lambda t=t, w=w: lambda ve: ve.tensor_scalar(
                    i32b[:, 0:w], t[:, 0:w], -128.0, None, ALU.max))(), False))
                ops.append(("ve", (lambda t=t, w=w: lambda ve: ve.tensor_scalar(
                    t[:, 0:w], i32b[:, 0:w], 0.015625, None, ALU.mult))(), False))
            ops.append(("ve", lambda ve: ve.tensor_scalar(b1q16[:, :], b1q_s[:, :], 16.0, None, ALU.mult), False))
            b.seq("s0b", ops)

        def ln_stage(name, src_dram, dst_F, quant_dst):
            with ExitStack() as ses:
                xt = b.sb(ses, f"{name}_x", [128, D], F32)
                xn = b.sb(ses, f"{name}_xn", [128, D], BF16)
                xq = b.sb(ses, f"{name}_xq", [128, D], BF16)
                xqf = b.sb(ses, f"{name}_xqf", [128, D], F32)
                i32t = b.sb(ses, f"{name}_i32", [128, D], I32)
                bn6 = b.sb(ses, f"{name}_bn", [128, 6 * (D // 512)], F32)
                mv = b.sb(ses, f"{name}_mv", [128, 4], F32)
                fst = b.sb(ses, f"{name}_fs", [128, FC, 128], BF16)
                fst2 = b.sb(ses, f"{name}_fs2", [128, FC, 128], BF16)
                tps = b.ps(ses, f"{name}_tp", [128, 128], BF16)
                nst = D // 512
                for kt in range(KT):
                    ops = []
                    ops.append(("sy", (lambda kt=kt: lambda s: s.dma_start(
                        out=xt[:, :], in_=src_dram[kt * 128:(kt + 1) * 128, :]))(), True))

                    def bnf(ve):
                        for j in range(nst):
                            ve.bn_stats(bn6[:, j * 6:(j + 1) * 6], xt[:, j * 512:(j + 1) * 512])
                            ve.drain()
                        return ve.bn_aggr(mv[:, 0:2], bn6[:, 0:6 * nst])
                    ops.append(("ve", bnf, False))
                    ops.append(("sc", lambda sc: sc.activation(mv[:, 2:3], mv[:, 1:2],
                                                               AF.Sqrt, bias=cbias[:, 0:1]), False))

                    def appf(ve):
                        ve.reciprocal(mv[:, 3:4], mv[:, 2:3])
                        ve.drain()
                        return ve.tensor_scalar(xn[:, :], xt[:, :], mv[:, 0:1], mv[:, 3:4],
                                                ALU.subtract, ALU.mult)
                    ops.append(("ve", appf, False))
                    if quant_dst is not None:
                        def qf1(ve):
                            ve.tensor_scalar(xqf[:, :], xt[:, :], mv[:, 0:1], mv[:, 3:4],
                                             ALU.subtract, ALU.mult)
                            ve.drain()
                            ve.tensor_scalar(xqf[:, :], xqf[:, :], 16.0, 127.0, ALU.mult, ALU.min)
                            ve.drain()
                            return ve.tensor_scalar(i32t[:, :], xqf[:, :], -128.0, None, ALU.max)
                        ops.append(("ve", qf1, False))
                        ops.append(("sc", lambda sc: sc.activation(xq[:, :], i32t[:, :],
                                                                   AF.Identity, scale=0.0625), False))
                    pairs = [(xn, fst, dst_F)] + ([(xq, fst2, quant_dst)] if quant_dst is not None else [])
                    for (tmj, fs_, dd) in pairs:
                        for fc in range(FC):
                            ops.append(("te", (lambda tmj=tmj, fc=fc: lambda te: te.transpose(
                                tps[:, :], tmj[:, fc * 128:(fc + 1) * 128], identB[:, :]))(), False))
                            ops.append(("ve", (lambda fs_=fs_, fc=fc: lambda ve: ve.tensor_copy(
                                fs_[:, fc, :], tps[:, :]))(), False))
                        ops.append(("sy", (lambda fs_=fs_, dd=dd, kt=kt: lambda s: s.dma_start(
                            out=dd.ap().rearrange("(c p) t -> p c t", p=128)[:, :, kt * 128:(kt + 1) * 128],
                            in_=fs_[:, :, :]))(), True))
                    b.seq(f"{name}_{kt}", ops)

        def norm_stage(name, QF, KF, negB_dst, scale):
            with ExitStack() as ses:
                q2t = b.sb(ses, f"{name}_q2", [128, T], BF16)
                nb2 = b.sb(ses, f"{name}_nb", [128, 16], F32)
                nps_ = b.ps(ses, f"{name}_ps", [128, 512])
                tpp = b.ps(ses, f"{name}_tp", [128, 16])
                ops = []
                for t_i, SRC in enumerate((QF, KF)):
                    for fc in range(FC):
                        ops.append(("sc", (lambda SRC=SRC, fc=fc: lambda sc: sc.activation(
                            q2t[:, :], SRC[:, fc, :], AF.Square))(), False))
                        for tc in range(TC):
                            ops.append(("te", (lambda tc=tc: lambda te: te.matmul(
                                nps_[0:2, :], sel64b[:, :], q2t[:, tc * 512:(tc + 1) * 512],
                                start=True, stop=True))(), False))
                            col = t_i * (FC * TC) + fc * TC + tc
                            ops.append(("ve", (lambda col=col: lambda ve: ve.reduce_max(
                                nrm[0:2, col:col + 1], nps_[0:2, :], AX.X))(), False))

                def comb(ve):
                    v = nrm[0:2, 0:2 * FC * TC].rearrange("p (g c) -> p g c", c=TC)
                    o = nrm[0:2, 64:64 + 2 * FC].rearrange("p g -> p g 1" if False else "p (g u) -> p g u", u=1)
                    ve.tensor_reduce(o, v, AX.X, ALU.max)
                    ve.drain()
                    return ve.tensor_tensor(nrm[0:2, 96:96 + FC], nrm[0:2, 64:64 + FC],
                                            nrm[0:2, 64 + FC:64 + 2 * FC], ALU.mult)
                ops.append(("ve", comb, False))
                ops.append(("sc", lambda sc: sc.activation(nrm[0:2, 96:96 + FC],
                                                           nrm[0:2, 96:96 + FC], AF.Sqrt), False))
                ops.append(("ve", lambda ve: ve.tensor_scalar(
                    nrm[0:2, 96:96 + FC], nrm[0:2, 96:96 + FC], -1.01 * scale, None, ALU.mult), False))
                ops.append(("gp", lambda gp: gp.dma_start(
                    out=nb2[0:1, 0:2 * FC],
                    in_=nrm[0:2, 96:96 + FC], allow_slow_non_contiguous=True), True))
                ops.append(("te", lambda te: te.matmul(tpp[:, 0:H], onesT[0:1, :],
                                                       nb2[0:1, 0:H], start=True, stop=True), False))
                ops.append(("ve", lambda ve: ve.tensor_copy(negB_dst[:, 0:H], tpp[:, 0:H]), False))
                b.seq(name, ops)

        def attn_stage(name, QF, KF, VTs, negB, masks, ctx_dst, causal):
            SEMl = b.sem
            with ExitStack() as ses:
                PT = b.sb(ses, f"{name}_PT", [128, KT, 512], BF16)
                rr = b.sb(ses, f"{name}_rr", [128, 512], F32)
                ctmp = b.sb(ses, f"{name}_ct", [128, 512], BF16)
                ps_s = [b.ps(ses, f"{name}_s{j}", [128, 512]) for j in range(2)]
                ps_c = b.ps(ses, f"{name}_c", [65, 512])
                ps_b = b.ps(ses, f"{name}_b", [64, 512])
                ps_p = b.ps(ses, f"{name}_p", [128, 512])
                for qc in range(TC):
                    nkt = min(KT, 4 * qc + 4) if causal else KT
                    base = b.snap()
                    def m_of(h):
                        return (4 if causal else 0)
                    def ve_base(h):
                        return sum(m_of(x) + 3 + (x % 2) for x in range(h))
                    def te_base(h):
                        return sum(nkt + 2 + (x % 2) for x in range(h))
                    def sc_cnt(h, kt):
                        return h * nkt + kt + 1
                    with nc.named_scope(f"attn_{name}_{qc}"), nc.Block() as blk:
                        @blk.tensor
                        def _(te, qc=qc, nkt=nkt, base=base):
                            te.wait_ge(SEMl["LS"], base["LS"])
                            te.wait_ge(SEMl["GS"], base["GS"])
                            bank_user = [None, None]
                            for h in range(H):
                                off = (h % 2) * 64
                                fc = h // 2
                                hcol = (h % 2) * FC + h // 2
                                for kt in range(nkt):
                                    bu = bank_user[kt % 2]
                                    if bu is not None:
                                        te.wait_ge(SEMl["SC"], base["SC"] + sc_cnt(*bu))
                                    te.matmul(ps_s[kt % 2][:, :],
                                              KF[off:off + 64, fc, kt * 128:(kt + 1) * 128],
                                              QF[off:off + 64, fc, qc * 512:(qc + 1) * 512],
                                              start=True, stop=True).then_inc(SEMl["MM"], 1)
                                    bank_user[kt % 2] = (h, kt)
                                te.wait_ge(SEMl["SC"], base["SC"] + (h + 1) * nkt)
                                te.wait_ge(SEMl["EV"], base["EV"] + ve_base(h) + m_of(h))
                                if h > 0:
                                    te.wait_ge(SEMl["EV"],
                                               base["EV"] + ve_base(h - 1) + m_of(h - 1) + 2)
                                inst = None
                                for kt in range(nkt):
                                    inst = te.matmul(ps_c[:, :], VTs[:, kt, h, 0:65], PT[:, kt, :],
                                                     start=(kt == 0), stop=(kt == nkt - 1))
                                inst.then_inc(SEMl["MM"], 1)
                                te.wait_ge(SEMl["EV"], base["EV"] + ve_base(h) + m_of(h) + 1)
                                te.matmul(ps_b[0:64, :], onesT[64:65, 0:64], rr[64:65, :],
                                          start=True, stop=True).then_inc(SEMl["MM"], 1)
                                if h % 2 == 1:
                                    te.wait_ge(SEMl["EV"], base["EV"] + ve_base(h) + m_of(h) + 3)
                                    te.matmul(ps_p[64:128, :], identB[0:64, 0:64], ctmp[0:64, :],
                                              start=True, stop=True,
                                              tile_position=(0, 64)).then_inc(SEMl["MM"], 1)

                        @blk.scalar
                        def _(sc, qc=qc, nkt=nkt, base=base):
                            sc.wait_ge(SEMl["LS"], base["LS"])
                            sc.wait_ge(SEMl["GS"], base["GS"])
                            for h in range(H):
                                hcol = (h % 2) * FC + h // 2
                                for kt in range(nkt):
                                    sc.wait_ge(SEMl["MM"], base["MM"] + te_base(h) + kt + 1)
                                    sc.activation(PT[:, kt, :], ps_s[kt % 2][:, :], AF.Exp,
                                                  bias=negB[:, hcol:hcol + 1],
                                                  scale=0.125).then_inc(SEMl["SC"], 1)

                        @blk.vector
                        def _(ve, qc=qc, nkt=nkt, base=base):
                            ve.wait_ge(SEMl["LS"], base["LS"])
                            ve.wait_ge(SEMl["GS"], base["GS"])
                            for h in range(H):
                                fc = h // 2
                                if causal:
                                    for kt in range(4 * qc, nkt):
                                        ve.wait_ge(SEMl["SC"], base["SC"] + sc_cnt(h, kt))
                                        ve.tensor_tensor(PT[:, kt, :], PT[:, kt, :],
                                                         masks[:, kt - 4 * qc, :],
                                                         ALU.mult).then_inc(SEMl["EV"], 1)
                                        ve.drain()
                                ve.wait_ge(SEMl["MM"], base["MM"] + te_base(h) + nkt + 1)
                                ve.reciprocal(rr[64:65, :], ps_c[64:65, :]).then_inc(SEMl["EV"], 1)
                                ve.drain()
                                ve.tensor_copy(ctmp[0:64, :], ps_c[0:64, :]).then_inc(SEMl["EV"], 1)
                                ve.drain()
                                ve.wait_ge(SEMl["MM"], base["MM"] + te_base(h) + nkt + 2)
                                if h % 2 == 0:
                                    ve.tensor_tensor(ctx_dst[0:64, fc, qc * 512:(qc + 1) * 512],
                                                     ctmp[0:64, :], ps_b[0:64, :],
                                                     ALU.mult).then_inc(SEMl["EV"], 1)
                                    ve.drain()
                                else:
                                    ve.tensor_tensor(ctmp[0:64, :], ctmp[0:64, :],
                                                     ps_b[0:64, :], ALU.mult).then_inc(SEMl["EV"], 1)
                                    ve.drain()
                                    ve.wait_ge(SEMl["MM"], base["MM"] + te_base(h) + nkt + 3)
                                    ve.tensor_copy(ctx_dst[64:128, fc, qc * 512:(qc + 1) * 512],
                                                   ps_p[64:128, :]).then_inc(SEMl["EV"], 1)
                                    ve.drain()
                    nte = sum(nkt + 2 + (x % 2) for x in range(H))
                    b.c["MM"] += nte
                    b.c["SC"] += H * nkt
                    b.c["EV"] += sum(m_of(x) + 3 + (x % 2) for x in range(H))

        def quantw_stage(name, src, dst, rows, cols, scale_num, recip_scale):
            with ExitStack() as ses:
                wtmp = b.sb(ses, f"{name}_w", [128, min(cols, 2048)], F32)
                wi = b.sb(ses, f"{name}_i", [128, min(cols, 2048)], I32)
                wb = b.sb(ses, f"{name}_b", [128, min(cols, 2048)], BF16)
                CW = min(cols, 2048)
                ops = []
                for r in range(rows // 128):
                    for cb in range(cols // CW):
                        sl = (slice(r * 128, (r + 1) * 128), slice(cb * CW, (cb + 1) * CW))
                        ops.append(("sy", (lambda sl=sl: lambda s: s.dma_start(
                            out=wtmp[:, :], in_=src[sl[0], sl[1]]))(), True))
                        def q1(ve):
                            ve.tensor_scalar(wtmp[:, :], wtmp[:, :], scale_num, 127.0,
                                             ALU.mult, ALU.min)
                            ve.drain()
                            return ve.tensor_scalar(wi[:, :], wtmp[:, :], -128.0, None, ALU.max)
                        ops.append(("ve", q1, False))
                        ops.append(("sc", lambda sc: sc.activation(wb[:, :], wi[:, :],
                                                                   AF.Identity, scale=recip_scale), False))
                        ops.append(("sy", (lambda sl=sl: lambda s: s.dma_start(
                            out=dst[sl[0], sl[1]], in_=wb[:, :]))(), True))
                b.seq(name, ops)

        CUT = P["cut"]

        if CUT >= 2:
            ln_stage("s1", x_in, d_xn0F, d_xq0F)

        if CUT >= 3:
          with nc.Block() as blk:
            @blk.sync
            def _(s):
                s.dma_start(out=A1[:, :, :],
                            in_=d_xn0F.ap().rearrange("(c p) t -> p c t", p=128)).then_inc(SEM["LS"], 16)
                b.c["LS"] += 16

        if CUT >= 3:
          with ExitStack() as rl0:
            ctx0 = b.sb(rl0, "ctx0", [128, FC, T], BF16)
            QFr = b.sb(rl0, "QFr", [128, FC, T], BF16)
            KFr = b.sb(rl0, "KFr", [128, FC, T], BF16)
            VT = b.sb(rl0, "VT", [128, KT, H, 66], BF16)

            with ExitStack() as ses:
                wbuf = b.sb(ses, "s2_w", [128, 2, FC, 128], BF16)
                st = b.sb(ses, "s2_st", [128, 2, 512], BF16)

                def wload(gp, oi):
                    src = wqkv.ap().rearrange("(kc p) o -> p kc o", p=128)[:, :, oi * 128:(oi + 1) * 128]
                    gp.dma_start(out=wbuf[:, oi % 2, :, :], in_=src).then_inc(SEM["GS"], 16)

                def ev(sc, ps, i, oi, tc):
                    return sc.activation(st[:, i % 2, :], ps[:, :], AF.Identity,
                                         bias=bqkv_s[:, oi:oi + 1])

                def stf(s, ps, i, oi, tc):
                    dd = (d_QF, d_KF, d_VF)[oi // FC]
                    o = oi % FC
                    return s.dma_start(out=dd[o * 128:(o + 1) * 128, tc * 512:(tc + 1) * 512],
                                       in_=st[:, i % 2, :])

                b.proj("s2", 3 * FC, TC, FC, [128, 512],
                       lambda kc, oi, tc: wbuf[:, oi % 2, kc, :],
                       lambda kc, oi, tc: A1[:, kc, tc * 512:(tc + 1) * 512],
                       steps=[("sc", ev), ("sy", stf)], wload=(wload, 1), nps=2)

            with nc.Block() as blk:
                @blk.sync
                def _(s):
                    s.dma_start(out=QFr[:, :, :],
                                in_=d_QF.ap().rearrange("(c p) t -> p c t", p=128)).then_inc(SEM["LS"], 16)
                    s.dma_start(out=KFr[:, :, :],
                                in_=d_KF.ap().rearrange("(c p) t -> p c t", p=128)).then_inc(SEM["LS"], 16)
                    b.c["LS"] += 32

            if CUT >= 4:
             with ExitStack() as ses:
                vfch = b.sb(ses, "s2b_v", [128, T], BF16)
                vtps = b.ps(ses, "s2b_ps", [128, 128], BF16)
                ops = [("gp", lambda gp: gp.memset(VT[:, :, :, 64:65], 1.0), False)]
                for fc in range(FC):
                    ops.append(("sy", (lambda fc=fc: lambda s: s.dma_start(
                        out=vfch[:, :], in_=d_VF[fc * 128:(fc + 1) * 128, :]))(), True))
                    for kt in range(KT):
                        ops.append(("te", (lambda kt=kt: lambda te: te.transpose(
                            vtps[:, :], vfch[:, kt * 128:(kt + 1) * 128], identB[:, :]))(), False))
                        ops.append(("ve", (lambda fc=fc, kt=kt: lambda ve: ve.tensor_copy(
                            VT[:, kt, 2 * fc, 0:64], vtps[:, 0:64]))(), False))
                        ops.append(("ve", (lambda fc=fc, kt=kt: lambda ve: ve.tensor_copy(
                            VT[:, kt, 2 * fc + 1, 0:64], vtps[:, 64:128]))(), False))
                b.seq("s2b", ops)

            if CUT >= 4:
                norm_stage("s2n", QFr, KFr, negB0, 0.125)

            if CUT >= 5:
             with ExitStack() as ses:
                masks0 = b.sb(ses, "masks0", [128, 4, 512], BF16)
                with nc.Block() as blk:
                    @blk.gpsimd
                    def _(gp):
                        gp.memset(masks0[:, :, :], 1.0)
                        gp.drain()
                        for m in range(4):
                            gp.affine_select(out=masks0[:, m, :], in_=masks0[:, m, :],
                                             pattern=[[1, 512]], compare_op=ALU.is_ge,
                                             fill=0.0, base=-128 * m, channel_multiplier=-1)
                            gp.drain()
                attn_stage("s3", QFr, KFr, VT, negB0, masks0, ctx0, causal=True)

            if CUT >= 5:
             with ExitStack() as ses:
                wout_s = b.sb(ses, "s3o_w", [128, FC, D], BF16)
                st3 = b.sb(ses, "s3o_st", [128, 2, 512], BF16)
                with nc.Block() as blk:
                    @blk.gpsimd
                    def _(gp):
                        gp.dma_start(out=wout_s[:, :, :],
                                     in_=wout0.ap().rearrange("(kc p) o -> p kc o", p=128)).then_inc(SEM["GS"], 16)
                        b.c["GS"] += 16

                def ev3(sc, ps, i, oi, tc):
                    return sc.activation(st3[:, i % 2, :], ps[:, :], AF.Identity,
                                         bias=bout_s[:, oi:oi + 1])

                def st3f(s, ps, i, oi, tc):
                    return s.dma_start(out=d_s0[oi * 128:(oi + 1) * 128, tc * 512:(tc + 1) * 512],
                                       in_=st3[:, i % 2, :])

                b.proj("s3o", FC, TC, FC, [128, 512],
                       lambda kc, oi, tc: wout_s[:, kc, oi * 128:(oi + 1) * 128],
                       lambda kc, oi, tc: ctx0[:, kc, tc * 512:(tc + 1) * 512],
                       steps=[("sc", ev3), ("sy", st3f)], nps=2)

        if CUT >= 6:
            quantw_stage("s4w1", w1t, d_w1q, D, F, 64.0, 0.015625)
            quantw_stage("s4w2", w2t, d_w2q, F, D, 64.0, 0.015625)
        if CUT >= 6:
         with ExitStack() as rs4:
            xqF = b.sb(rs4, "xqF", [128, FC, T], BF16)
            hstage = b.sb(rs4, "hstage", [128, HC, 512], BF16)
            with nc.Block() as blk:
                @blk.sync
                def _(s):
                    s.dma_start(out=xqF[:, :, :],
                                in_=d_xq0F.ap().rearrange("(c p) t -> p c t", p=128)).then_inc(SEM["LS"], 16)
                    b.c["LS"] += 16
            for tc in range(TC):
                with ExitStack() as ses:
                    wb1 = b.sb(ses, "s4_w1", [128, 2, FC, 128], BF16)
                    i32a = b.sb(ses, "s4_i32", [128, 2, 512], I32)

                    def wl1(gp, oi):
                        src = d_w1q.ap().rearrange("(kc p) o -> p kc o", p=128)[:, :, oi * 128:(oi + 1) * 128]
                        gp.dma_start(out=wb1[:, oi % 2, :, :], in_=src).then_inc(SEM["GS"], 16)

                    def evA1(sc, ps, i, oi, tc_):
                        return sc.activation(i32a[:, i % 2, :], ps[:, :], AF.Identity,
                                             scale=16.0, bias=b1q16[:, oi:oi + 1])

                    def evA2(ve, ps, i, oi, tc_):
                        return ve.tensor_scalar(i32a[:, i % 2, :], i32a[:, i % 2, :],
                                                0.0, 127.0, ALU.max, ALU.min)

                    def evA3(sc, ps, i, oi, tc_):
                        return sc.activation(hstage[:, oi, :], i32a[:, i % 2, :],
                                             AF.Identity, scale=0.0625)

                    b.proj(f"s4a_{tc}", HC, 1, FC, [128, 512],
                           lambda kc, oi, tc_: wb1[:, oi % 2, kc, :],
                           (lambda tc: lambda kc, oi, tc_: xqF[:, kc, tc * 512:(tc + 1) * 512])(tc),
                           steps=[("sc", evA1), ("ve", evA2), ("sc", evA3)],
                           wload=(wl1, 1), nps=2)
                with ExitStack() as ses:
                    wb2 = b.sb(ses, "s4_w2", [128, 2, HC, 128], BF16)
                    gst = b.sb(ses, "s4_gst", [128, 2, 512], BF16)

                    def wl2(gp, oi):
                        src = d_w2q.ap().rearrange("(kc p) o -> p kc o", p=128)[:, :, oi * 128:(oi + 1) * 128]
                        gp.dma_start(out=wb2[:, oi % 2, :, :], in_=src).then_inc(SEM["GS"], 16)

                    def evB1(sc, ps, i, oi, tc_):
                        return sc.activation(gst[:, i % 2, :], ps[:, :], AF.Silu,
                                             bias=b2q_s[:, oi:oi + 1])

                    def stB(s, ps, i, oi, tc_):
                        return (lambda tc: s.dma_start(
                            out=d_g0[oi * 128:(oi + 1) * 128, tc * 512:(tc + 1) * 512],
                            in_=gst[:, i % 2, :]))(tc)

                    b.proj(f"s4b_{tc}", FC, 1, HC, [128, 512],
                           lambda kc, oi, tc_: wb2[:, oi % 2, kc, :],
                           lambda kc, oi, tc_: hstage[:, kc, :],
                           steps=[("sc", evB1), ("sy", stB)], wload=(wl2, 1), nps=2)

        if CUT >= 7:
         with ExitStack() as ses:
            xt5 = b.sb(ses, "s5_x", [128, D], F32)
            sT5 = b.sb(ses, "s5_s", [128, D], BF16)
            gT5 = b.sb(ses, "s5_g", [128, D], BF16)
            sg5 = b.sb(ses, "s5_sg", [128, D], BF16)
            x15 = b.sb(ses, "s5_x1", [128, D], F32)
            for kt in range(KT):
                ops = []
                ops.append(("sy", (lambda kt=kt: lambda s: s.dma_start(
                    out=xt5[:, :], in_=x_in[kt * 128:(kt + 1) * 128, :]))(), True))
                ops.append(("sy", (lambda kt=kt: lambda s: s.dma_start_transpose(
                    sT5[:, :], d_s0[:, kt * 128:(kt + 1) * 128]))(), True))
                ops.append(("sy", (lambda kt=kt: lambda s: s.dma_start_transpose(
                    gT5[:, :], d_g0[:, kt * 128:(kt + 1) * 128]))(), True))
                ops.append(("ve", lambda ve: ve.tensor_tensor(sg5[:, :], sT5[:, :], gT5[:, :], ALU.mult), False))
                ops.append(("ve", lambda ve: ve.tensor_tensor(x15[:, :], xt5[:, :], sg5[:, :], ALU.add), False))
                ops.append(("sy", (lambda kt=kt: lambda s: s.dma_start(
                    out=d_x1[kt * 128:(kt + 1) * 128, :], in_=x15[:, :]))(), True))
                b.seq(f"s5_{kt}", ops)

        if CUT >= 7:
            ln_stage("s6", d_x1, d_xn1F, None)
        if CUT >= 7:
         with nc.Block() as blk:
            @blk.sync
            def _(s):
                s.dma_start(out=A1[:, :, :],
                            in_=d_xn1F.ap().rearrange("(c p) t -> p c t", p=128)).then_inc(SEM["LS"], 16)
                b.c["LS"] += 16

        if CUT >= 8 or CUT == 1:
            build_s7(b, nc, locals())

        if CUT >= 9:
            build_s8(b, nc, locals())

        if CUT >= 10:
         with ExitStack() as ses:
            x19 = b.sb(ses, "s9_x1", [128, D], F32)
            yq9 = b.sb(ses, "s9_yq", [128, D], BF16)
            g9 = b.sb(ses, "s9_g", [128, D], BF16)
            sg9 = b.sb(ses, "s9_sg", [128, D], BF16)
            o9 = b.sb(ses, "s9_o", [128, D], F32)
            for kt in range(KT):
                ops = []
                ops.append(("sy", (lambda kt=kt: lambda s: s.dma_start(
                    out=x19[:, :], in_=d_x1[kt * 128:(kt + 1) * 128, :]))(), True))
                ops.append(("sy", (lambda kt=kt: lambda s: s.dma_start(
                    out=yq9[:, :], in_=d_yqT[kt * 128:(kt + 1) * 128, :]))(), True))
                ops.append(("sy", (lambda kt=kt: lambda s: s.dma_start(
                    out=g9[:, :], in_=d_g1[kt * 128:(kt + 1) * 128, :]))(), True))
                ops.append(("ve", lambda ve: ve.scalar_tensor_tensor(
                    sg9[:, :], yq9[:, :], scol[:, 11:12], g9[:, :], ALU.mult, ALU.mult), False))
                ops.append(("ve", lambda ve: ve.tensor_tensor(o9[:, :], x19[:, :], sg9[:, :], ALU.add), False))
                ops.append(("sy", (lambda kt=kt: lambda s: s.dma_start(
                    out=out[kt * 128:(kt + 1) * 128, :], in_=o9[:, :]))(), True))
                b.seq(f"s9_{kt}", ops)

        with nc.Block() as blk:
            @blk.sync
            def _(s):
                s.wait_ge(SEM["LS"], b.c["LS"])

    return nc


def build_s7(b, nc, L):
    (T, D, F, H, E, HD, FC, HC, TC, KT) = (L["T"], L["D"], L["F"], L["H"], L["E"],
                                           L["HD"], L["FC"], L["HC"], L["TC"], L["KT"])
    C7 = L["P"].get("cut7", 99) if isinstance(L.get("P"), dict) else 99
    SEM = b.sem
    A1, scol, sc_row, nrm = L["A1"], L["scol"], L["sc_row"], L["nrm"]
    inb_s, outb_s, outb2_s = L["inb_s"], L["outb_s"], L["outb2_s"]
    identB, identF, onesT = L["identB"], L["identF"], L["onesT"]
    wit, wot = L["wit"], L["wot"]
    d_wiq, d_woq, d_qT, d_kv_in, d_kv_all = (L["d_wiq"], L["d_woq"], L["d_qT"],
                                             L["d_kv_in"], L["d_kv_all"])
    d_heads, d_yqT = L["d_heads"], L["d_yqT"]
    ag_in, ag_out = L["ag_in"], L["ag_out"]
    RG = [list(range(NCORES))]

    C_SWI, C_RWI, C_SWO, C_RWO = 0, 1, 2, 3
    C_SX, C_RSX, C_FX, C_SQ, C_FQ, C_ATT = 4, 5, 6, 7, 8, 9
    C_SH, C_SY, C_FH, C_FY, C_SHWO, C_RSY = 10, 11, 12, 13, 14, 15

    def bcast_scol(ops, tpp):
        ops.append(("te", lambda te: te.matmul(tpp[:, 0:16], onesT[0:1, :],
                                               sc_row[0:1, 0:16], start=True, stop=True), False))
        ops.append(("ve", lambda ve: ve.tensor_copy(scol[:, :], tpp[:, 0:16]), False))

    def fakequantw(name, src, dst, rows, cols, c_scale, c_recip):
        nch = rows // 128
        CW = min(cols, 1024)
        ncb = cols // CW
        with ExitStack() as ses:
            wtmp = b.sb(ses, f"{name}_w", [128, CW], F32)
            wi = b.sb(ses, f"{name}_i", [128, CW], I32)
            wb = b.sb(ses, f"{name}_b", [128, CW], BF16)
            tpp = b.ps(ses, f"{name}_tp", [128, 128])
            ops = []
            for r in range(nch):
                for cb in range(ncb):
                    sl = (slice(r * 128, (r + 1) * 128), slice(cb * CW, (cb + 1) * CW))
                    ops.append(("sy", (lambda sl=sl: lambda s: s.dma_start(
                        out=wtmp[:, :], in_=src[sl[0], sl[1]]))(), True))
                    col = r * ncb + cb
                    ops.append(("ve", (lambda col=col: lambda ve: ve.reduce_max(
                        nrm[:, col:col + 1], wtmp[:, :], AX.X,
                        apply_absolute_value=True))(), False))
            nc_ = nch * ncb
            ops.append(("ve", lambda ve: ve.reduce_max(
                nrm[:, 120:121], nrm[:, 0:nc_], AX.X), False))
            ops.append(("te", lambda te: te.transpose(tpp[0:1, 0:128], nrm[:, 120:121],
                                                      identF[:, :]), False))
            ops.append(("ve", lambda ve: ve.reduce_max(nrm[0:1, 121:122], tpp[0:1, 0:128], AX.X), False))
            ops.append(("ve", (lambda: lambda ve: ve.tensor_scalar(
                sc_row[0:1, c_scale:c_scale + 1], nrm[0:1, 121:122],
                1.0 / 127.0, 1e-8, ALU.mult, ALU.max))(), False))
            ops.append(("ve", lambda ve: ve.reciprocal(
                sc_row[0:1, c_recip:c_recip + 1], sc_row[0:1, c_scale:c_scale + 1]), False))
            bcast_scol(ops, tpp)
            b.seq(f"{name}_p1", ops)
            ops = []
            for r in range(nch):
                for cb in range(ncb):
                    sl = (slice(r * 128, (r + 1) * 128), slice(cb * CW, (cb + 1) * CW))
                    ops.append(("sy", (lambda sl=sl: lambda s: s.dma_start(
                        out=wtmp[:, :], in_=src[sl[0], sl[1]]))(), True))
                    ops.append(("ve", lambda ve: ve.tensor_scalar(
                        wi[:, :], wtmp[:, :], scol[:, c_recip:c_recip + 1], None, ALU.mult), False))
                    ops.append(("sc", lambda sc: sc.activation(wb[:, :], wi[:, :], AF.Identity), False))
                    ops.append(("sy", (lambda sl=sl: lambda s: s.dma_start(
                        out=dst[sl[0], sl[1]], in_=wb[:, :]))(), True))
            b.seq(f"{name}_p2", ops)

    fakequantw("s7wi", wit, d_wiq, D, 3 * D, C_SWI, C_RWI)
    fakequantw("s7wo", wot, d_woq, D, D, C_SWO, C_RWO)
    if C7 < 2:
        return

    with ExitStack() as r7:
        hqF = b.sb(r7, "hqF", [128, FC, T], BF16)
        g7 = b.sb(r7, "g7", [128, 16], F32)
        agt = b.sb(r7, "agt", [NCORES, 16], F32)
        i7one = b.sb(r7, "i7one", [128, 2], I32)

        with ExitStack() as ses:
            wb7 = b.sb(ses, "s7a_w", [128, 2, FC, 128], BF16)
            xst = b.sb(ses, "s7a_st", [128, 2, 512], BF16)

            def wl(gp, oi):
                src = d_wiq.ap().rearrange("(kc p) o -> p kc o", p=128)[:, :, oi * 128:(oi + 1) * 128]
                gp.dma_start(out=wb7[:, oi % 2, :, :], in_=src).then_inc(SEM["GS"], 16)

            def ev1(sc, ps, i, oi, tc):
                return sc.activation(xst[:, i % 2, :], ps[:, :], AF.Identity,
                                     scale=scol[:, C_SWI:C_SWI + 1], bias=inb_s[:, oi:oi + 1])

            def ev2(ve, ps, i, oi, tc):
                return ve.reduce_max(nrm[:, i:i + 1], xst[:, i % 2, :], AX.X,
                                     apply_absolute_value=True)

            b.proj("s7a", 3 * FC, TC, FC, [128, 512],
                   lambda kc, oi, tc: wb7[:, oi % 2, kc, :],
                   lambda kc, oi, tc: A1[:, kc, tc * 512:(tc + 1) * 512],
                   steps=[("sc", ev1), ("ve", ev2)], wload=(wl, 1), nps=2)

        if C7 < 3:
            return
        with ExitStack() as ses:
            tpp = b.ps(ses, "s7g1_tp", [128, 128])
            ops = []
            ops.append(("ve", lambda ve: ve.reduce_max(nrm[:, 120:121], nrm[:, 0:FC * TC], AX.X), False))
            ops.append(("ve", lambda ve: ve.reduce_max(nrm[:, 121:122], nrm[:, 0:3 * FC * TC], AX.X), False))
            ops.append(("te", lambda te: te.transpose(tpp[0:2, 0:128], nrm[:, 120:122], identF[:, :]), False))
            ops.append(("ve", lambda ve: ve.reduce_max(g7[0:2, 0:1], tpp[0:2, 0:128], AX.X), False))
            ops.append(("gp", lambda gp: gp.dma_start(
                out=ag_in[0].ap()[0:1, 0:2], in_=g7[0:2, 0:1],
                allow_slow_non_contiguous=True), True))
            import os as _os
            if _os.environ.get("NOCC"):
                ops.append(("gp", lambda gp: gp.dma_start(
                    out=ag_out[0].ap()[0:1, :], in_=ag_in[0].ap()), True))
            else:
                ops.append(("gp", lambda gp: gp.collective_compute(
                    "AllGather", ALU.bypass, replica_groups=RG,
                    ins=[ag_in[0].ap().opt()], outs=[ag_out[0].ap().opt()]), "cc"))
            ops.append(("gp", lambda gp: gp.dma_start(out=agt[:, :], in_=ag_out[0].ap()), True))
            ops.append(("te", lambda te: te.transpose(tpp[0:16, 0:NCORES], agt[:, :], identF[0:NCORES, 0:NCORES]), False))
            ops.append(("ve", lambda ve: ve.reduce_max(g7[0:16, 1:2], tpp[0:16, 0:NCORES], AX.X), False))
            ops.append(("te", lambda te: te.transpose(tpp[0:1, 0:128], g7[:, 1:2], identF[:, :]), False))
            ops.append(("ve", lambda ve: ve.tensor_copy(g7[0:1, 0:16], tpp[0:1, 0:16]), False))
            ops.append(("ve", lambda ve: ve.tensor_scalar(
                sc_row[0:1, C_SX:C_SX + 1], g7[0:1, 1:2], 1.0 / 127.0, 1e-8,
                ALU.mult, ALU.max), False))
            ops.append(("ve", lambda ve: ve.reciprocal(
                sc_row[0:1, C_RSX:C_RSX + 1], sc_row[0:1, C_SX:C_SX + 1]), False))
            ops.append(("ve", lambda ve: ve.tensor_tensor(
                sc_row[0:1, C_FX:C_FX + 1], sc_row[0:1, C_SWI:C_SWI + 1],
                sc_row[0:1, C_RSX:C_RSX + 1], ALU.mult), False))
            ops.append(("ve", lambda ve: ve.tensor_tensor(
                g7[0:1, 2:3], g7[0:1, 0:1], sc_row[0:1, C_RSX:C_RSX + 1], ALU.mult), False))
            ops.append(("ve", lambda ve: ve.tensor_scalar(
                i7one[0:1, 0:1], g7[0:1, 2:3], 127.0, None, ALU.min), False))
            ops.append(("ve", lambda ve: ve.tensor_tensor(
                g7[0:1, 3:4], i7one[0:1, 0:1], sc_row[0:1, C_SX:C_SX + 1], ALU.mult), False))
            ops.append(("ve", lambda ve: ve.tensor_scalar(
                g7[0:1, 3:4], g7[0:1, 3:4], 0.125, None, ALU.mult), False))
            ops.append(("ve", lambda ve: ve.tensor_scalar(
                sc_row[0:1, C_SQ:C_SQ + 1], g7[0:1, 3:4], 1.0 / 127.0, 1e-8,
                ALU.mult, ALU.max), False))
            ops.append(("ve", lambda ve: ve.reciprocal(g7[0:1, 4:5], sc_row[0:1, C_SQ:C_SQ + 1]), False))
            ops.append(("ve", lambda ve: ve.tensor_tensor(
                g7[0:1, 4:5], g7[0:1, 4:5], sc_row[0:1, C_SX:C_SX + 1], ALU.mult), False))
            ops.append(("ve", lambda ve: ve.tensor_scalar(
                sc_row[0:1, C_FQ:C_FQ + 1], g7[0:1, 4:5], 0.125, None, ALU.mult), False))
            ops.append(("ve", lambda ve: ve.tensor_tensor(
                sc_row[0:1, C_ATT:C_ATT + 1], sc_row[0:1, C_SQ:C_SQ + 1],
                sc_row[0:1, C_SX:C_SX + 1], ALU.mult), False))
            bcast_scol(ops, tpp)
            ops.append(("ve", lambda ve: ve.tensor_scalar(
                inb_s[:, :], inb_s[:, :], scol[:, C_RSX:C_RSX + 1], None, ALU.mult), False))
            b.seq("s7ag1", ops)

        if C7 < 4:
            return
        with ExitStack() as ses:
            wb7 = b.sb(ses, "s7b_w", [128, 2, FC, 128], BF16)
            i7a = b.sb(ses, "s7b_i1", [128, 2, 512], I32)
            i7b = b.sb(ses, "s7b_i2", [128, 2, 512], I32)
            st7 = b.sb(ses, "s7b_st", [128, 2, 512], BF16)
            tT7 = b.sb(ses, "s7b_tt", [128, 2, 512], BF16)
            pst7 = [b.ps(ses, f"s7b_tp{j}", [128, 512], BF16) for j in range(2)]

            def wl(gp, oi):
                src = d_wiq.ap().rearrange("(kc p) o -> p kc o", p=128)[:, :, oi * 128:(oi + 1) * 128]
                gp.dma_start(out=wb7[:, oi % 2, :, :], in_=src).then_inc(SEM["GS"], 16)

            def e1(sc, ps, i, oi, tc):
                return sc.activation(i7a[:, i % 2, :], ps[:, :], AF.Identity,
                                     scale=scol[:, C_FX:C_FX + 1], bias=inb_s[:, oi:oi + 1])

            def e2(sc, ps, i, oi, tc):
                if oi < FC:
                    return sc.activation(i7b[:, i % 2, :], i7a[:, i % 2, :],
                                         AF.Identity, scale=scol[:, C_FQ:C_FQ + 1])
                return sc.nop()

            def e3(ve, ps, i, oi, tc):
                src = i7b if oi < FC else i7a
                return ve.tensor_copy(st7[:, i % 2, :], src[:, i % 2, :])

            def e4(te, ps, i, oi, tc):
                inst = None
                for qs in range(4):
                    inst = te.transpose(pst7[i % 2][:, qs * 128:(qs + 1) * 128],
                                        st7[:, i % 2, qs * 128:(qs + 1) * 128], identB[:, :])
                return inst

            def e5(ve, ps, i, oi, tc):
                return ve.tensor_copy(tT7[:, i % 2, :], pst7[i % 2][:, :])

            def e6(s, ps, i, oi, tc):
                if oi < FC:
                    dst = d_qT.ap().rearrange("(tc qs p) c -> p tc qs c", p=128, qs=4)[
                        :, tc, :, oi * 128:(oi + 1) * 128]
                elif oi < 2 * FC:
                    dst = d_kv_in.ap().rearrange("(tc qs p) c -> p tc qs c", p=128, qs=4)[
                        :, tc, :, (oi - FC) * 128:(oi - FC + 1) * 128]
                else:
                    dst = d_kv_in.ap().rearrange("(tc qs p) c -> p tc qs c", p=128, qs=4)[
                        :, tc, :, D + (oi - 2 * FC) * 128:D + (oi - 2 * FC + 1) * 128]
                return s.dma_start(out=dst, in_=tT7[:, i % 2, :].rearrange("p (qs c) -> p qs c", qs=4))

            b.proj("s7b", 3 * FC, TC, FC, [128, 512],
                   lambda kc, oi, tc: wb7[:, oi % 2, kc, :],
                   lambda kc, oi, tc: A1[:, kc, tc * 512:(tc + 1) * 512],
                   steps=[("sc", e1), ("sc", e2), ("ve", e3), ("te", e4),
                          ("ve", e5), ("sy", e6)],
                   wload=(wl, 1), nps=2)

        if C7 < 5:
            return
        ops = [("gp", lambda gp: gp.collective_compute(
            "AllGather", ALU.bypass, replica_groups=RG,
            ins=[d_kv_in.ap().opt()], outs=[d_kv_all.ap().opt()]), "cc")]
        b.seq("s7ag2", ops)

        if C7 < 6:
            return
        with ExitStack() as ses:
            qTt = b.sb(ses, "s7d_q", [128, D], BF16)
            kvl = b.sb(ses, "s7d_kv", [128, NCORES, 2 * D], BF16)
            tmpb = b.sb(ses, "s7d_t", [128, D], BF16)
            tmpf = b.sb(ses, "s7d_tf", [128, D], F32)
            acc = b.sb(ses, "s7d_a", [128, D], F32)
            sct = b.sb(ses, "s7d_s", [128, 128], F32)
            pexp = b.sb(ses, "s7d_p", [128, 128], F32)
            m16 = b.sb(ses, "s7d_m", [128, 16], F32)
            r16 = b.sb(ses, "s7d_r", [128, 16], F32)
            hn = b.sb(ses, "s7d_h", [128, D], F32)
            for sck in range(KT):
                ops = []
                ops.append(("sy", (lambda sck=sck: lambda s: s.dma_start(
                    out=qTt[:, :], in_=d_qT[sck * 128:(sck + 1) * 128, :]))(), True))
                for l in range(NCORES):
                    ops.append(("sy", (lambda sck=sck, l=l: lambda s: s.dma_start(
                        out=kvl[:, l, :],
                        in_=d_kv_all[l * T + sck * 128: l * T + (sck + 1) * 128, :]))(), True))
                for l in range(NCORES):
                    ops.append(("ve", (lambda l=l: lambda ve: ve.tensor_tensor(
                        tmpb[:, :], qTt[:, :], kvl[:, l, 0:D], ALU.mult))(), False))
                    ops.append(("ve", (lambda l=l: lambda ve: ve.tensor_reduce(
                        sct[:, :].rearrange("p (h l) -> p h l", l=NCORES)[:, :, l:l + 1],
                        tmpb[:, :].rearrange("p (h d) -> p h d", d=HD),
                        AX.X, ALU.add))(), False))
                ops.append(("ve", lambda ve: ve.tensor_reduce(
                    m16[:, :].rearrange("p (h u) -> p h u", u=1),
                    sct[:, :].rearrange("p (h l) -> p h l", l=NCORES), AX.X, ALU.max), False))
                ops.append(("ve", lambda ve: ve.tensor_tensor(
                    sct[:, :].rearrange("p (h l) -> p h l", l=NCORES),
                    sct[:, :].rearrange("p (h l) -> p h l", l=NCORES),
                    m16[:, :].rearrange("p (h u) -> p h u", u=1).broadcast_to((128, 16, NCORES)),
                    ALU.subtract), False))
                ops.append(("sc", lambda sc: sc.activation(
                    pexp[:, :], sct[:, :], AF.Exp, scale=scol[:, C_ATT:C_ATT + 1]), False))
                ops.append(("ve", lambda ve: ve.tensor_reduce(
                    m16[:, :].rearrange("p (h u) -> p h u", u=1),
                    pexp[:, :].rearrange("p (h l) -> p h l", l=NCORES), AX.X, ALU.add), False))
                ops.append(("ve", lambda ve: ve.reciprocal(r16[:, :], m16[:, :]), False))
                ops.append(("ve", lambda ve: ve.tensor_scalar(
                    r16[:, :], r16[:, :], scol[:, C_SX:C_SX + 1], None, ALU.mult), False))
                ops.append(("ve", lambda ve: ve.memset(acc[:, :], 0.0), False))
                for l in range(NCORES):
                    ops.append(("ve", (lambda l=l: lambda ve: ve.tensor_tensor(
                        tmpf[:, :].rearrange("p (h d) -> p h d", d=HD),
                        kvl[:, l, D:2 * D].rearrange("p (h d) -> p h d", d=HD),
                        pexp[:, :].rearrange("p (h l) -> p h l", l=NCORES)[:, :, l:l + 1]
                        .broadcast_to((128, 16, HD)),
                        ALU.mult))(), False))
                    ops.append(("ve", lambda ve: ve.tensor_tensor(
                        acc[:, :], acc[:, :], tmpf[:, :], ALU.add), False))
                ops.append(("ve", lambda ve: ve.tensor_tensor(
                    hn[:, :].rearrange("p (h d) -> p h d", d=HD),
                    acc[:, :].rearrange("p (h d) -> p h d", d=HD),
                    r16[:, :].rearrange("p (h u) -> p h u", u=1).broadcast_to((128, 16, HD)),
                    ALU.mult), False))
                ops.append(("ve", (lambda sck=sck: lambda ve: ve.reduce_max(
                    nrm[:, sck:sck + 1], hn[:, :], AX.X, apply_absolute_value=True))(), False))
                ops.append(("sy", (lambda sck=sck: lambda s: s.dma_start(
                    out=d_heads[sck * 128:(sck + 1) * 128, :], in_=hn[:, :]))(), True))
                b.seq(f"s7d_{sck}", ops)

        if C7 < 7:
            return
        with ExitStack() as ses:
            tpp = b.ps(ses, "s7g3_tp", [128, 128])
            ops = []
            ops.append(("ve", lambda ve: ve.reduce_max(nrm[:, 120:121], nrm[:, 0:KT], AX.X), False))
            ops.append(("te", lambda te: te.transpose(tpp[0:1, 0:128], nrm[:, 120:121], identF[:, :]), False))
            ops.append(("ve", lambda ve: ve.reduce_max(g7[0:1, 0:1], tpp[0:1, 0:128], AX.X), False))
            ops.append(("gp", lambda gp: gp.dma_start(
                out=ag_in[1].ap()[0:1, 0:1], in_=g7[0:1, 0:1],
                allow_slow_non_contiguous=True), True))
            ops.append(("gp", lambda gp: gp.collective_compute(
                "AllGather", ALU.bypass, replica_groups=RG,
                ins=[ag_in[1].ap().opt()], outs=[ag_out[1].ap().opt()]), "cc"))
            ops.append(("gp", lambda gp: gp.dma_start(out=agt[:, :], in_=ag_out[1].ap()), True))
            ops.append(("te", lambda te: te.transpose(tpp[0:16, 0:NCORES], agt[:, :], identF[0:NCORES, 0:NCORES]), False))
            ops.append(("ve", lambda ve: ve.reduce_max(g7[0:1, 1:2], tpp[0:1, 0:NCORES], AX.X), False))
            ops.append(("ve", lambda ve: ve.tensor_scalar(
                sc_row[0:1, C_SH:C_SH + 1], g7[0:1, 1:2], 1.0 / 127.0, 1e-8,
                ALU.mult, ALU.max), False))
            ops.append(("ve", lambda ve: ve.reciprocal(
                sc_row[0:1, C_FH:C_FH + 1], sc_row[0:1, C_SH:C_SH + 1]), False))
            ops.append(("ve", lambda ve: ve.tensor_tensor(
                sc_row[0:1, C_SHWO:C_SHWO + 1], sc_row[0:1, C_SH:C_SH + 1],
                sc_row[0:1, C_SWO:C_SWO + 1], ALU.mult), False))
            bcast_scol(ops, tpp)
            b.seq("s7ag3", ops)

        if C7 < 8:
            return
        with ExitStack() as ses:
            ht = b.sb(ses, "s7e_h", [128, D], F32)
            hi = b.sb(ses, "s7e_i", [128, D], I32)
            hb = b.sb(ses, "s7e_b", [128, D], BF16)
            tpe = b.ps(ses, "s7e_tp", [128, 128], BF16)
            for kt in range(KT):
                ops = []
                ops.append(("sy", (lambda kt=kt: lambda s: s.dma_start(
                    out=ht[:, :], in_=d_heads[kt * 128:(kt + 1) * 128, :]))(), True))
                ops.append(("sc", lambda sc: sc.activation(
                    hi[:, :], ht[:, :], AF.Identity, scale=scol[:, C_FH:C_FH + 1]), False))
                ops.append(("ve", lambda ve: ve.tensor_copy(hb[:, :], hi[:, :]), False))
                for fc in range(FC):
                    ops.append(("te", (lambda fc=fc: lambda te: te.transpose(
                        tpe[:, :], hb[:, fc * 128:(fc + 1) * 128], identB[:, :]))(), False))
                    ops.append(("ve", (lambda fc=fc, kt=kt: lambda ve: ve.tensor_copy(
                        hqF[:, fc, kt * 128:(kt + 1) * 128], tpe[:, :]))(), False))
                b.seq(f"s7e_{kt}", ops)

        with ExitStack() as ses:
            wb7 = b.sb(ses, "s7f_w", [128, 2, FC, 128], BF16)
            yst = b.sb(ses, "s7f_st", [128, 2, 512], BF16)

            def wl(gp, oi):
                src = d_woq.ap().rearrange("(kc p) o -> p kc o", p=128)[:, :, oi * 128:(oi + 1) * 128]
                gp.dma_start(out=wb7[:, oi % 2, :, :], in_=src).then_inc(SEM["GS"], 16)

            def f1(sc, ps, i, oi, tc):
                return sc.activation(yst[:, i % 2, :], ps[:, :], AF.Identity,
                                     scale=scol[:, C_SHWO:C_SHWO + 1], bias=outb_s[:, oi:oi + 1])

            def f2(ve, ps, i, oi, tc):
                return ve.reduce_max(nrm[:, i:i + 1], yst[:, i % 2, :], AX.X,
                                     apply_absolute_value=True)

            b.proj("s7f", FC, TC, FC, [128, 512],
                   lambda kc, oi, tc: wb7[:, oi % 2, kc, :],
                   lambda kc, oi, tc: hqF[:, kc, tc * 512:(tc + 1) * 512],
                   steps=[("sc", f1), ("ve", f2)], wload=(wl, 1), nps=2)

        with ExitStack() as ses:
            tpp = b.ps(ses, "s7g4_tp", [128, 128])
            ops = []
            ops.append(("ve", lambda ve: ve.reduce_max(nrm[:, 120:121], nrm[:, 0:FC * TC], AX.X), False))
            ops.append(("te", lambda te: te.transpose(tpp[0:1, 0:128], nrm[:, 120:121], identF[:, :]), False))
            ops.append(("ve", lambda ve: ve.reduce_max(g7[0:1, 0:1], tpp[0:1, 0:128], AX.X), False))
            ops.append(("gp", lambda gp: gp.dma_start(
                out=ag_in[2].ap()[0:1, 0:1], in_=g7[0:1, 0:1],
                allow_slow_non_contiguous=True), True))
            ops.append(("gp", lambda gp: gp.collective_compute(
                "AllGather", ALU.bypass, replica_groups=RG,
                ins=[ag_in[2].ap().opt()], outs=[ag_out[2].ap().opt()]), "cc"))
            ops.append(("gp", lambda gp: gp.dma_start(out=agt[:, :], in_=ag_out[2].ap()), True))
            ops.append(("te", lambda te: te.transpose(tpp[0:16, 0:NCORES], agt[:, :], identF[0:NCORES, 0:NCORES]), False))
            ops.append(("ve", lambda ve: ve.reduce_max(g7[0:1, 1:2], tpp[0:1, 0:NCORES], AX.X), False))
            ops.append(("ve", lambda ve: ve.tensor_scalar(
                sc_row[0:1, C_SY:C_SY + 1], g7[0:1, 1:2], 1.0 / 127.0, 1e-8,
                ALU.mult, ALU.max), False))
            ops.append(("ve", lambda ve: ve.reciprocal(
                sc_row[0:1, C_RSY:C_RSY + 1], sc_row[0:1, C_SY:C_SY + 1]), False))
            ops.append(("ve", lambda ve: ve.tensor_tensor(
                sc_row[0:1, C_FY:C_FY + 1], sc_row[0:1, C_SHWO:C_SHWO + 1],
                sc_row[0:1, C_RSY:C_RSY + 1], ALU.mult), False))
            bcast_scol(ops, tpp)
            ops2 = [("ve", lambda ve: ve.tensor_scalar(
                outb2_s[:, :], outb_s[:, :], scol[:, C_RSY:C_RSY + 1], None, ALU.mult), False)]
            b.seq("s7ag4", ops + ops2)

        with ExitStack() as ses:
            wb7 = b.sb(ses, "s7g_w", [128, 2, FC, 128], BF16)
            i7a = b.sb(ses, "s7g_i", [128, 2, 512], I32)
            st7 = b.sb(ses, "s7g_st", [128, 2, 512], BF16)
            tT7 = b.sb(ses, "s7g_tt", [128, 2, 512], BF16)
            pst7 = [b.ps(ses, f"s7g_tp{j}", [128, 512], BF16) for j in range(2)]

            def wl(gp, oi):
                src = d_woq.ap().rearrange("(kc p) o -> p kc o", p=128)[:, :, oi * 128:(oi + 1) * 128]
                gp.dma_start(out=wb7[:, oi % 2, :, :], in_=src).then_inc(SEM["GS"], 16)

            def g1(sc, ps, i, oi, tc):
                return sc.activation(i7a[:, i % 2, :], ps[:, :], AF.Identity,
                                     scale=scol[:, C_FY:C_FY + 1], bias=outb2_s[:, oi:oi + 1])

            def g2(ve, ps, i, oi, tc):
                return ve.tensor_copy(st7[:, i % 2, :], i7a[:, i % 2, :])

            def g3(te, ps, i, oi, tc):
                inst = None
                for qs in range(4):
                    inst = te.transpose(pst7[i % 2][:, qs * 128:(qs + 1) * 128],
                                        st7[:, i % 2, qs * 128:(qs + 1) * 128], identB[:, :])
                return inst

            def g4(ve, ps, i, oi, tc):
                return ve.tensor_copy(tT7[:, i % 2, :], pst7[i % 2][:, :])

            def g5(s, ps, i, oi, tc):
                dst = d_yqT.ap().rearrange("(tc qs p) c -> p tc qs c", p=128, qs=4)[
                    :, tc, :, oi * 128:(oi + 1) * 128]
                return s.dma_start(out=dst, in_=tT7[:, i % 2, :].rearrange("p (qs c) -> p qs c", qs=4))

            b.proj("s7g", FC, TC, FC, [128, 512],
                   lambda kc, oi, tc: wb7[:, oi % 2, kc, :],
                   lambda kc, oi, tc: hqF[:, kc, tc * 512:(tc + 1) * 512],
                   steps=[("sc", g1), ("ve", g2), ("te", g3), ("ve", g4), ("sy", g5)],
                   wload=(wl, 1), nps=2)


def build_s8(b, nc, L):
    (T, D, F, H, E, HD, FC, HC, TC, KT, NROT) = (
        L["T"], L["D"], L["F"], L["H"], L["E"], L["HD"], L["FC"], L["HC"],
        L["TC"], L["KT"], L["NROT"])
    HE, OC = H * E, H * E * HD // 128
    SEM = b.sem
    A1, scol, nrm, cbias = L["A1"], L["scol"], L["nrm"], L["cbias"]
    identB, identB2, identF, onesT = L["identB"], L["identB2"], L["identF"], L["onesT"]
    ropePb, negB1 = L["ropePb"], L["negB1"]
    b1L_s, b2L_s = L["b1L_s"], L["b2L_s"]
    wq2, wk2, wv4, wo2, wselv, wselo = (L["wq2"], L["wk2"], L["wv4"], L["wo2"],
                                        L["wselv"], L["wselo"])
    ln1g, ln1b, ln2g, ln2b = L["ln1g"], L["ln1b"], L["ln2g"], L["ln2b"]
    w1L, w2L = L["w1L"], L["w2L"]
    d_g1, d_srcB = L["d_g1"], L["d_srcB"]
    norm_stage, attn_stage = L["norm_stage"], L["attn_stage"]
    half = NROT // 2

    with ExitStack() as r8:
        ctx1 = b.sb(r8, "ctx1", [128, FC, T], BF16)
        gvT = b.sb(r8, "gvT", [128, KT, 128], BF16)
        goF = b.sb(r8, "goF", [128, T], BF16)

        with ExitStack() as ra:
            CC_t = b.sb(ra, "CC_t", [128, T], BF16)
            SS_t = b.sb(ra, "SS_t", [128, T], BF16)

            with ExitStack() as ses:
                ang = b.sb(ses, "s8a_a", [128, T], F32)
                angr = b.sb(ses, "s8a_ar", [128, T], F32)
                sinv = b.sb(ses, "s8a_sv", [128, T], F32)
                kint = b.sb(ses, "s8a_k", [128, T], I32)
                jv = b.sb(ses, "s8a_j", [128, 6], F32)
                tid = ang
                ops = []
                ops.append(("gp", lambda gp: gp.iota(
                    tid[:, :], [[1, T]], base=0, channel_multiplier=0,
                    allow_small_or_imprecise_dtypes=True), False))
                ops.append(("gp", lambda gp: gp.iota(
                    jv[:, 0:1], [[0, 1]], base=0, channel_multiplier=1,
                    allow_small_or_imprecise_dtypes=True), False))
                def jb(ve):
                    ve.tensor_scalar(jv[64:96, 0:1], jv[64:96, 0:1], 64.0, None, ALU.subtract)
                    ve.drain()
                    ve.tensor_scalar(jv[:, 1:2], jv[:, 0:1], 16.0, -16.0, ALU.is_ge, ALU.mult)
                    ve.drain()
                    return ve.tensor_tensor(jv[:, 0:1], jv[:, 0:1], jv[:, 1:2], ALU.add)
                ops.append(("ve", jb, False))
                ops.append(("gp", lambda gp: gp.iota(
                    jv[:, 4:5], [[0, 1]], base=0, channel_multiplier=1,
                    allow_small_or_imprecise_dtypes=True), False))
                def sgn_build(ve):
                    ve.tensor_scalar(jv[:, 5:6], jv[:, 4:5], 64.0, -64.0, ALU.is_ge, ALU.mult)
                    ve.drain()
                    ve.tensor_tensor(jv[:, 4:5], jv[:, 4:5], jv[:, 5:6], ALU.add)
                    ve.drain()
                    ve.tensor_scalar(jv[:, 2:3], jv[:, 4:5], 16.0, None, ALU.is_lt)
                    ve.drain()
                    ve.tensor_scalar(jv[:, 3:4], jv[:, 4:5], 32.0, None, ALU.is_lt)
                    ve.drain()
                    return ve.scalar_tensor_tensor(jv[:, 4:5], jv[:, 2:3], -2.0,
                                                   jv[:, 3:4], ALU.mult, ALU.add)
                ops.append(("ve", sgn_build, False))
                ops.append(("sc", lambda sc: sc.activation(
                    jv[:, 1:2], jv[:, 0:1], AF.Exp, scale=-LN1E4 / half), False))
                ops.append(("ve", lambda ve: ve.tensor_tensor(
                    ang[:, :], tid[:, :], jv[:, 1:2].broadcast_to((128, T)), ALU.mult), False))
                ops.append(("ve", lambda ve: ve.tensor_scalar(
                    kint[:, :], ang[:, :], 1.0 / TWO_PI, None, ALU.mult), False))
                ops.append(("ve", lambda ve: ve.scalar_tensor_tensor(
                    angr[:, :], kint[:, :], -TWO_PI, ang[:, :], ALU.mult, ALU.add), False))
                ops.append(("sc", lambda sc: sc.activation(sinv[:, :], angr[:, :], AF.Sin), False))
                ops.append(("ve", lambda ve: ve.tensor_tensor(
                    SS_t[:, :], sinv[:, :], jv[:, 4:5].broadcast_to((128, T)), ALU.mult), False))
                ops.append(("ve", lambda ve: ve.tensor_scalar(
                    ang[:, :], ang[:, :], TWO_PI / 4.0, None, ALU.add), False))
                ops.append(("ve", lambda ve: ve.tensor_scalar(
                    kint[:, :], ang[:, :], 1.0 / TWO_PI, None, ALU.mult), False))
                ops.append(("ve", lambda ve: ve.scalar_tensor_tensor(
                    angr[:, :], kint[:, :], -TWO_PI, ang[:, :], ALU.mult, ALU.add), False))
                ops.append(("sc", lambda sc: sc.activation(sinv[:, :], angr[:, :], AF.Sin), False))
                ops.append(("ve", lambda ve: ve.tensor_tensor(
                    sinv[:, :], sinv[:, :], jv[:, 3:4].broadcast_to((128, T)), ALU.mult), False))
                ops.append(("ve", lambda ve: ve.tensor_scalar(
                    jv[:, 5:6], jv[:, 3:4], -1.0, 1.0, ALU.mult, ALU.add), False))
                ops.append(("ve", lambda ve: ve.tensor_tensor(
                    CC_t[:, :], sinv[:, :], jv[:, 5:6].broadcast_to((128, T)), ALU.add), False))
                b.seq("s8a", ops)

            qmF = b.sb(ra, "qmF", [128, FC, T], BF16)
            kmF = b.sb(ra, "kmF", [128, FC, T], BF16)
            VT1 = b.sb(ra, "VT1", [128, KT, H, 66], BF16)

            for (wsrc, dstF, nm) in [(wq2, qmF, "s8bq"), (wk2, kmF, "s8bk")]:
                with ExitStack() as ses:
                    wb8 = b.sb(ses, f"{nm}_w", [128, 2, FC, 128], BF16)

                    def wl(gp, oi, wsrc=wsrc, wb8=wb8):
                        src = wsrc.ap().rearrange("(kc p) o -> p kc o", p=128)[:, :, oi * 128:(oi + 1) * 128]
                        gp.dma_start(out=wb8[:, oi % 2, :, :], in_=src).then_inc(SEM["GS"], 16)

                    def e1(sc, ps, i, oi, tc, dstF=dstF):
                        return sc.activation(dstF[:, oi, tc * 512:(tc + 1) * 512],
                                             ps[:, :], AF.Identity)

                    b.proj(nm, FC, TC, FC, [128, 512],
                           (lambda wb8=wb8: lambda kc, oi, tc: wb8[:, oi % 2, kc, :])(),
                           lambda kc, oi, tc: A1[:, kc, tc * 512:(tc + 1) * 512],
                           steps=[("sc", e1)], wload=(wl, 1), nps=2)
                with ExitStack() as ses:
                    u8 = b.sb(ses, f"{nm}_u", [128, 512], BF16)
                    t8_ = b.sb(ses, f"{nm}_t2", [128, 512], BF16)
                    psr = b.ps(ses, f"{nm}_ps", [128, 512])
                    for fc in range(FC):
                        ops = []
                        for tc in range(TC):
                            ops.append(("te", (lambda fc=fc, tc=tc, dstF=dstF: lambda te: te.matmul(
                                psr[:, :], ropePb[:, :], dstF[:, fc, tc * 512:(tc + 1) * 512],
                                start=True, stop=True))(), False))
                            ops.append(("ve", (lambda tc=tc: lambda ve: ve.tensor_tensor(
                                u8[:, :], psr[:, :], SS_t[:, tc * 512:(tc + 1) * 512],
                                ALU.mult))(), False))
                            ops.append(("ve", (lambda fc=fc, tc=tc, dstF=dstF: lambda ve: ve.tensor_tensor(
                                t8_[:, :], dstF[:, fc, tc * 512:(tc + 1) * 512],
                                CC_t[:, tc * 512:(tc + 1) * 512], ALU.mult))(), False))
                            ops.append(("ve", (lambda fc=fc, tc=tc, dstF=dstF: lambda ve: ve.tensor_tensor(
                                dstF[:, fc, tc * 512:(tc + 1) * 512], t8_[:, :], u8[:, :],
                                ALU.add))(), False))
                        b.seq(f"{nm}_rope{fc}", ops)

            norm_stage("s8n", qmF, kmF, negB1, 0.125)

            with ExitStack() as ses:
                wsel_s = b.sb(ses, "s8c_w", [128, 2, FC, 128], BF16)
                gt = b.sb(ses, "s8c_g", [128, 128], F32)
                eqt = b.sb(ses, "s8c_e", [128, 128], F32)
                m1g = b.sb(ses, "s8c_m", [128, 16], F32)
                psg = b.ps(ses, "s8c_ps", [128, 128])
                pst = b.ps(ses, "s8c_pt", [128, 128])
                with nc.Block() as blk:
                    @blk.gpsimd
                    def _(gp):
                        gp.dma_start(out=wsel_s[:, 0, :, :],
                                     in_=wselv.ap().rearrange("(kc p) o -> p kc o", p=128)).then_inc(SEM["GS"], 16)
                        gp.dma_start(out=wsel_s[:, 1, :, :],
                                     in_=wselo.ap().rearrange("(kc p) o -> p kc o", p=128)).then_inc(SEM["GS"], 16)
                        b.c["GS"] += 32
                for gidx in range(2):
                    for kt in range(KT):
                        ops = []
                        def mm(te, gidx=gidx, kt=kt):
                            inst = None
                            for dc in range(FC):
                                inst = te.matmul(psg[:, :], A1[:, dc, kt * 128:(kt + 1) * 128],
                                                 wsel_s[:, gidx, dc, :],
                                                 start=(dc == 0), stop=(dc == FC - 1))
                            return inst
                        ops.append(("te", mm, False))
                        ops.append(("sc", lambda sc: sc.activation(gt[:, :], psg[:, :], AF.Sigmoid), False))
                        gv_ = gt[:, :].rearrange("p (h e) -> p h e", e=E)
                        m1v = m1g[:, :].rearrange("p (h u) -> p h u", u=1)
                        ops.append(("ve", lambda ve: ve.tensor_reduce(m1v, gv_, AX.X, ALU.max), False))
                        ops.append(("ve", lambda ve: ve.tensor_tensor(
                            eqt[:, :].rearrange("p (h e) -> p h e", e=E), gv_,
                            m1v.broadcast_to((128, H, E)), ALU.is_equal), False))
                        ops.append(("ve", lambda ve: ve.scalar_tensor_tensor(
                            eqt[:, :], eqt[:, :], -1e30, gt[:, :], ALU.mult, ALU.add), False))
                        ops.append(("ve", lambda ve: ve.tensor_reduce(
                            m1v, eqt[:, :].rearrange("p (h e) -> p h e", e=E), AX.X, ALU.max), False))
                        ops.append(("ve", lambda ve: ve.tensor_tensor(
                            eqt[:, :].rearrange("p (h e) -> p h e", e=E), gv_,
                            m1v.broadcast_to((128, H, E)), ALU.is_ge), False))
                        if gidx == 0:
                            ops.append(("ve", (lambda kt=kt: lambda ve: ve.tensor_tensor(
                                gvT[:, kt, :], gt[:, :], eqt[:, :], ALU.mult))(), False))
                        else:
                            ops.append(("ve", lambda ve: ve.tensor_tensor(
                                gt[:, :], gt[:, :], eqt[:, :], ALU.mult), False))
                            ops.append(("te", lambda te: te.transpose(
                                pst[:, :], gt[:, :], identF[:, :]), False))
                            ops.append(("ve", (lambda kt=kt: lambda ve: ve.tensor_copy(
                                goF[:, kt * 128:(kt + 1) * 128], pst[:, :]))(), False))
                        b.seq(f"s8c_{gidx}_{kt}", ops)

            with ExitStack() as ses:
                NG = 2
                wvs = b.sb(ses, "s8d_w", [128, NG, FC, E * HD], BF16)
                vtmp = b.sb(ses, "s8d_t", [128, E * HD], F32)
                psv = [b.ps(ses, f"s8d_p{j}", [128, E * HD]) for j in range(NG)]
                ops0 = [("gp", lambda gp: gp.memset(VT1[:, :, :, 64:65], 1.0), False)]
                b.seq("s8d_ones", ops0)
                for hg in range(H // NG):
                    with nc.Block() as blk:
                        @blk.gpsimd
                        def _(gp, hg=hg):
                            for j in range(NG):
                                src = wv4.ap()[hg * NG + j, :, :].rearrange(
                                    "(kc p) o -> p kc o", p=128)
                                gp.dma_start(out=wvs[:, j, :, :], in_=src).then_inc(SEM["GS"], 16)
                                b.c["GS"] += 16
                    for kt in range(KT):
                        ops = []
                        def mm(te, hg=hg, kt=kt):
                            inst = None
                            for dc in range(FC):
                                for j in range(NG):
                                    inst = te.matmul(psv[j][:, :],
                                                     A1[:, dc, kt * 128:(kt + 1) * 128],
                                                     wvs[:, j, dc, :],
                                                     start=(dc == 0), stop=(dc == FC - 1))
                            return inst
                        ops.append(("te", mm, False))
                        for j in range(NG):
                            h = hg * NG + j
                            def gvm(ve, j=j, h=h, kt=kt):
                                return ve.tensor_tensor(
                                    vtmp[:, :].rearrange("p (pp e) -> p e pp", e=E),
                                    psv[j][:, :].rearrange("p (e pp) -> p e pp", e=E),
                                    gvT[:, kt, h * E:(h + 1) * E].rearrange(
                                        "p (e u) -> p e u", u=1).broadcast_to((128, E, HD)),
                                    ALU.mult)
                            ops.append(("ve", gvm, False))
                            def rd(ve, j=j, h=h, kt=kt):
                                with nc.allow_low_precision(reason="bf16 store of f32-accumulated expert sum"):
                                    return ve.tensor_reduce(
                                        VT1[:, kt, h, 0:64].rearrange("p (d u) -> p d u", u=1),
                                        vtmp[:, :].rearrange("p (d e) -> p d e", e=E),
                                        AX.X, ALU.add)
                            ops.append(("ve", rd, False))
                        b.seq(f"s8d_{hg}_{kt}", ops)

            attn_stage("s8e", qmF, kmF, VT1, negB1, None, ctx1, causal=False)

        with ExitStack() as rf:
            SELh = b.sb(rf, "SELh", [128, OC * 128], BF16)
            with nc.Block() as blk:
                @blk.gpsimd
                def _(gp):
                    gp.memset(SELh[:, :], 1.0)
                    gp.drain()
                    gp.affine_select(
                        out=SELh[:, :].rearrange("p (c h j) -> p c h j", h=2, j=64),
                        in_=SELh[:, :].rearrange("p (c h j) -> p c h j", h=2, j=64),
                        pattern=[[-2, OC], [-1, 2], [0, 64]],
                        compare_op=ALU.is_equal, fill=0.0, base=0, channel_multiplier=1)
                    gp.drain()
            with ExitStack() as ses:
                wo_s = b.sb(ses, "s8f_w", [128, 8, D], BF16)
                cR = b.sb(ses, "s8f_cr", [128, 2, 512], BF16)
                ohb = b.sb(ses, "s8f_oh", [128, 8, 512], BF16)
                ost = b.sb(ses, "s8f_ost", [128, 2, 512], F32)
                pcr = b.ps(ses, "s8f_pc", [128, 512])
                pgr = [b.ps(ses, f"s8f_pg{j}", [128, 512]) for j in range(2)]
                pso = b.ps(ses, "s8f_po", [128, 512])
                for grp in range(OC // 8):
                    with nc.Block() as blk:
                        @blk.gpsimd
                        def _(gp, grp=grp):
                            src = wo2.ap()[grp * 1024:(grp + 1) * 1024, :].rearrange(
                                "(kc p) o -> p kc o", p=128)
                            gp.dma_start(out=wo_s[:, :, :], in_=src).then_inc(SEM["GS"], 16)
                            b.c["GS"] += 16
                    for tc in range(TC):
                        ops = []
                        for jj in range(2):
                            hh = 2 * grp + jj
                            def crm(te, hh=hh, tc=tc):
                                off = (hh % 2) * 64
                                return te.matmul(pcr[:, :], identB2[off:off + 64, :],
                                                 ctx1[off:off + 64, hh // 2, tc * 512:(tc + 1) * 512],
                                                 start=True, stop=True)
                            ops.append(("te", crm, False))
                            def cre(ve, jj=jj):
                                return ve.tensor_copy(cR[:, jj, :], pcr[:, :])
                            ops.append(("ve", cre, False))
                        for c8 in range(8):
                            c = grp * 8 + c8
                            jj = (c // 4) % 2
                            def grm(te, c=c, tc=tc, c8=c8):
                                return te.matmul(pgr[c8 % 2][:, :], SELh[:, c * 128:(c + 1) * 128],
                                                 goF[:, tc * 512:(tc + 1) * 512],
                                                 start=True, stop=True)
                            ops.append(("te", grm, False))
                            def ohm(ve, c8=c8, jj=jj):
                                return ve.tensor_tensor(ohb[:, c8, :], cR[:, jj, :],
                                                        pgr[c8 % 2][:, :], ALU.mult)
                            ops.append(("ve", ohm, False))
                        for ofc in range(FC):
                            def opm(te, ofc=ofc, tc=tc):
                                inst = None
                                for c8 in range(8):
                                    inst = te.matmul(pso[:, :], wo_s[:, c8, ofc * 128:(ofc + 1) * 128],
                                                     ohb[:, c8, :], start=(c8 == 0), stop=(c8 == 7))
                                return inst
                            ops.append(("te", opm, False))
                            if grp == 0:
                                def acc0(ve, ofc=ofc, tc=tc):
                                    return ve.tensor_tensor(
                                        ost[:, ofc % 2, :], pso[:, :],
                                        A1[:, ofc, tc * 512:(tc + 1) * 512], ALU.add)
                                ops.append(("ve", acc0, False))
                            else:
                                def accn(ve, ofc=ofc):
                                    return ve.tensor_copy(ost[:, ofc % 2, :], pso[:, :])
                                ops.append(("ve", accn, False))
                            def accd(gp, ofc=ofc, tc=tc, grp=grp):
                                return gp.dma_start(
                                    out=d_srcB[ofc * 128:(ofc + 1) * 128, tc * 512:(tc + 1) * 512],
                                    in_=ost[:, ofc % 2, :],
                                    accum_op=(ALU.bypass if grp == 0 else ALU.add))
                            ops.append(("gp", accd, True))
                        b.seq(f"s8f_{grp}_{tc}", ops)

        if True:
            with ExitStack() as rg:
                xlnF = b.sb(rg, "xlnF", [128, FC, T], BF16)
                gbx = b.sb(rg, "s8_gb1", [128, D], F32)
                bbx = b.sb(rg, "s8_bb1", [128, D], F32)
                ln_tmaj(b, nc, L, "s8g", d_srcB, xlnF, ln1g, ln1b, gbx, bbx,
                        silu_out=None, g1_dst=None)

                with ExitStack() as ses:
                    hst = b.sb(ses, "s8h_h", [128, HC, 512], BF16)
                    for tc in range(TC):
                        with ExitStack() as s2:
                            wb8 = b.sb(s2, "s8h_w1", [128, 2, FC, 128], BF16)

                            def wl1(gp, oi, wb8=wb8):
                                src = w1L.ap().rearrange("(kc p) o -> p kc o", p=128)[
                                    :, :, oi * 128:(oi + 1) * 128]
                                gp.dma_start(out=wb8[:, oi % 2, :, :], in_=src).then_inc(SEM["GS"], 16)

                            def h1(sc, ps, i, oi, tc_, hst=hst):
                                return sc.activation(hst[:, oi, :], ps[:, :], AF.Relu,
                                                     bias=b1L_s[:, oi:oi + 1])

                            b.proj(f"s8h1_{tc}", HC, 1, FC, [128, 512],
                                   (lambda wb8=wb8: lambda kc, oi, tc_: wb8[:, oi % 2, kc, :])(),
                                   (lambda tc=tc: lambda kc, oi, tc_: xlnF[:, kc, tc * 512:(tc + 1) * 512])(),
                                   steps=[("sc", h1)], wload=(wl1, 1), nps=2)
                        with ExitStack() as s2:
                            wb9 = b.sb(s2, "s8h_w2", [128, 2, HC, 128], BF16)

                            def wl2(gp, oi, wb9=wb9):
                                src = w2L.ap().rearrange("(kc p) o -> p kc o", p=128)[
                                    :, :, oi * 128:(oi + 1) * 128]
                                gp.dma_start(out=wb9[:, oi % 2, :, :], in_=src).then_inc(SEM["GS"], 16)

                            mst = b.sb(s2, "s8h_mst", [128, 2, 512], F32)

                            def h2(ve, ps, i, oi, tc_, mst=mst):
                                return ve.tensor_scalar(mst[:, i % 2, :], ps[:, :],
                                                        b2L_s[:, oi:oi + 1], None, ALU.add)

                            def h3(gp, ps, i, oi, tc_, mst=mst, tc=tc):
                                return gp.dma_start(
                                    out=d_srcB[oi * 128:(oi + 1) * 128, tc * 512:(tc + 1) * 512],
                                    in_=mst[:, i % 2, :], accum_op=ALU.add)

                            b.proj(f"s8h2_{tc}", FC, 1, HC, [128, 512],
                                   (lambda wb9=wb9: lambda kc, oi, tc_: wb9[:, oi % 2, kc, :])(),
                                   (lambda hst=hst: lambda kc, oi, tc_: hst[:, kc, :])(),
                                   steps=[("ve", h2), ("gp", h3)], wload=(wl2, 1), nps=2)

                ln_tmaj(b, nc, L, "s8i", d_srcB, None, ln2g, ln2b, gbx, bbx,
                        silu_out=True, g1_dst=d_g1)


LN1E4 = float(np.log(10000.0))


def ln_tmaj(b, nc, L, name, src_dram_F, dst_F, grow, brow, gbx, bbx, silu_out, g1_dst):
    (T, D, FC, KT) = (L["T"], L["D"], L["FC"], L["KT"])
    SEM = b.sem
    identF, identB, onesT, cbias = L["identF"], L["identB"], L["onesT"], L["cbias"]
    with ExitStack() as ses:
        xF = b.sb(ses, f"{name}_xF", [128, FC, 128], F32)
        xT = b.sb(ses, f"{name}_xT", [128, D], F32)
        xl = b.sb(ses, f"{name}_xl", [128, D], BF16)
        bn6 = b.sb(ses, f"{name}_bn", [128, 6 * (D // 512)], F32)
        mv = b.sb(ses, f"{name}_mv", [128, 4], F32)
        tps = b.ps(ses, f"{name}_tp", [128, 128])
        tpsb = b.ps(ses, f"{name}_tpb", [128, 128], BF16)
        tpb = b.ps(ses, f"{name}_tb", [128, 512])
        nst = D // 512
        base = b.snap()
        with nc.Block() as blk:
            @blk.sync
            def _(s):
                s.dma_start(out=gbx[0:1, :], in_=grow[:, :]).then_inc(SEM["LS"], 16)
                s.dma_start(out=bbx[0:1, :], in_=brow[:, :]).then_inc(SEM["LS"], 16)
                b.c["LS"] += 32
        ops = []
        for row in (gbx, bbx):
            for c in range(D // 512):
                ops.append(("te", (lambda row=row, c=c: lambda te: te.matmul(
                    tpb[:, :], onesT[0:1, :], row[0:1, c * 512:(c + 1) * 512],
                    start=True, stop=True))(), False))
                ops.append(("ve", (lambda row=row, c=c: lambda ve: ve.tensor_copy(
                    row[:, c * 512:(c + 1) * 512], tpb[:, :]))(), False))
        b.seq(f"{name}_gb", ops)
        for kt in range(KT):
            ops = []
            ops.append(("sy", (lambda kt=kt: lambda s: s.dma_start(
                out=xF[:, :, :],
                in_=src_dram_F.ap().rearrange("(c p) t -> p c t", p=128)[
                    :, :, kt * 128:(kt + 1) * 128]))(), True))
            for fc in range(FC):
                ops.append(("te", (lambda fc=fc: lambda te: te.transpose(
                    tps[:, :], xF[:, fc, :], identF[:, :]))(), False))
                ops.append(("ve", (lambda fc=fc: lambda ve: ve.tensor_copy(
                    xT[:, fc * 128:(fc + 1) * 128], tps[:, :]))(), False))

            def bnf(ve):
                for j in range(nst):
                    ve.bn_stats(bn6[:, j * 6:(j + 1) * 6], xT[:, j * 512:(j + 1) * 512])
                    ve.drain()
                return ve.bn_aggr(mv[:, 0:2], bn6[:, 0:6 * nst])
            ops.append(("ve", bnf, False))
            ops.append(("sc", lambda sc: sc.activation(mv[:, 2:3], mv[:, 1:2],
                                                       AF.Sqrt, bias=cbias[:, 0:1]), False))

            def appf(ve):
                ve.reciprocal(mv[:, 3:4], mv[:, 2:3])
                ve.drain()
                ve.tensor_scalar(xT[:, :], xT[:, :], mv[:, 0:1], mv[:, 3:4],
                                 ALU.subtract, ALU.mult)
                ve.drain()
                ve.tensor_tensor(xT[:, :], xT[:, :], gbx[:, :], ALU.mult)
                ve.drain()
                return ve.tensor_tensor(xT[:, :], xT[:, :], bbx[:, :], ALU.add)
            ops.append(("ve", appf, False))
            if silu_out:
                ops.append(("sc", lambda sc: sc.activation(xl[:, :], xT[:, :], AF.Silu), False))
                ops.append(("sy", (lambda kt=kt: lambda s: s.dma_start(
                    out=g1_dst[kt * 128:(kt + 1) * 128, :], in_=xl[:, :]))(), True))
            else:
                ops.append(("ve", lambda ve: ve.tensor_copy(xl[:, :], xT[:, :]), False))
                for fc in range(FC):
                    ops.append(("te", (lambda fc=fc: lambda te: te.transpose(
                        tpsb[:, :], xl[:, fc * 128:(fc + 1) * 128], identB[:, :]))(), False))
                    ops.append(("ve", (lambda fc=fc, kt=kt: lambda ve: ve.tensor_copy(
                        dst_F[:, fc, kt * 128:(kt + 1) * 128], tpsb[:, :]))(), False))
                    ops.append(("te", (lambda fc=fc: lambda te: te.transpose(
                        tps[:, :], xT[:, fc * 128:(fc + 1) * 128], identF[:, :]))(), False))
                    ops.append(("ve", (lambda fc=fc: lambda ve: ve.tensor_copy(
                        xF[:, fc, :], tps[:, :]))(), False))
                ops.append(("sy", (lambda kt=kt: lambda s: s.dma_start(
                    out=src_dram_F.ap().rearrange("(c p) t -> p c t", p=128)[
                        :, :, kt * 128:(kt + 1) * 128], in_=xF[:, :, :]))(), True))
            b.seq(f"{name}_{kt}", ops)


def host_layout(inputs, core):
    g = lambda k: np.ascontiguousarray(np.asarray(inputs[k], dtype=np.float32))
    T, D, F, H, E, HD = 2048, 1024, 4096, 16, 8, 64
    m = {}
    m["x"] = np.ascontiguousarray(np.asarray(inputs["L0_attn_x"][core], dtype=np.float32))
    m["wqkv"] = g("L0_attn_Wqkv")
    m["bqkvf"] = np.ascontiguousarray(g("L0_attn_bqkv").reshape(24, 128).T)
    m["wout0"] = g("L0_attn_Wout")
    m["boutf"] = np.ascontiguousarray(g("L0_attn_bout").reshape(8, 128).T)
    m["w1t"] = np.ascontiguousarray(g("L0_ffn_w1").T)
    m["b1f0"] = np.ascontiguousarray(g("L0_ffn_b1").reshape(32, 128).T)
    m["w2t"] = np.ascontiguousarray(g("L0_ffn_w2").T)
    m["b2f0"] = np.ascontiguousarray(g("L0_ffn_b2").reshape(8, 128).T)
    m["wit"] = np.ascontiguousarray(g("L1_attn_in_w").T)
    m["inbf"] = np.ascontiguousarray(g("L1_attn_in_b").reshape(24, 128).T)
    m["wot"] = np.ascontiguousarray(g("L1_attn_out_w").T)
    m["outbf"] = np.ascontiguousarray(g("L1_attn_out_b").reshape(8, 128).T)
    m["wq2"] = np.ascontiguousarray(g("L1_ffn_Wq").transpose(1, 0, 2).reshape(D, D))
    m["wk2"] = np.ascontiguousarray(g("L1_ffn_Wk").transpose(1, 0, 2).reshape(D, D))
    m["wv4"] = np.ascontiguousarray(g("L1_ffn_Wv").transpose(0, 2, 1, 3).reshape(H, D, E * HD))
    m["wo2"] = np.ascontiguousarray(g("L1_ffn_Wo").reshape(H * E * HD, D))
    m["wselv"] = np.ascontiguousarray(g("L1_ffn_Wsel_v").transpose(2, 0, 1).reshape(D, H * E))
    m["wselo"] = np.ascontiguousarray(g("L1_ffn_Wsel_o").transpose(2, 0, 1).reshape(D, H * E))
    m["ln1g"] = g("L1_ffn_ln1_g").reshape(1, D)
    m["ln1b"] = g("L1_ffn_ln1_b").reshape(1, D)
    m["ln2g"] = g("L1_ffn_ln2_g").reshape(1, D)
    m["ln2b"] = g("L1_ffn_ln2_b").reshape(1, D)
    m["w1L"] = g("L1_ffn_W1")
    m["b1Lf"] = np.ascontiguousarray(g("L1_ffn_b1").reshape(32, 128).T)
    m["w2L"] = g("L1_ffn_W2")
    m["b2Lf"] = np.ascontiguousarray(g("L1_ffn_b2").reshape(8, 128).T)
    perm = np.arange(128)
    for bb in (0, 64):
        perm[bb:bb + 16] = np.arange(bb + 16, bb + 32)
        perm[bb + 16:bb + 32] = np.arange(bb, bb + 16)
    Psel = np.zeros((128, 128), np.float32)
    Psel[perm, np.arange(128)] = 1.0
    m["ropeP"] = Psel
    return m


def kernel(**inputs):
    P = default_params()
    nc = build(P)
    in_maps = [host_layout(inputs, c) for c in range(NCORES)]
    res = run_bass_kernel_spmd(nc, in_maps, core_ids=list(range(NCORES)))
    return np.stack([np.asarray(res.results[c]["out"], dtype=np.float32)
                     for c in range(NCORES)], axis=0)


# version 03
# speedup vs baseline: 1.3346x; 1.1583x over previous
import numpy as np
import concourse.bass as bass
import concourse.mybir as mybir
from concourse.bass_utils import run_bass_kernel_spmd
from contextlib import ExitStack

F32 = mybir.dt.float32
BF16 = mybir.dt.bfloat16
I32 = mybir.dt.int32
AF = mybir.ActivationFunctionType
ALU = mybir.AluOpType
AX = mybir.AxisListType

NCORES = 8
TWO_PI = 6.283185307179586

ENGN = {"te": "tensor", "ve": "vector", "sc": "scalar", "gp": "gpsimd", "sy": "sync"}
ENG_SEM = {"te": "MM", "ve": "EV", "sc": "SC", "gp": "GS", "sy": "LS"}


def default_params():
    return dict(T=2048, D=1024, HD=64, F=4096, E=8, NROT=32, dbg=(), cut=99)


class Bld:
    def __init__(self, nc, es):
        self.nc = nc
        self.es = es
        self.sem = {}
        for s in ["LS", "GS", "AS", "MM", "EV", "SC", "CC", "X1", "X2", "X3", "X4", "X5"]:
            self.sem[s] = es.enter_context(nc.semaphore(s))
        self.c = {s: 0 for s in self.sem}

    _uid = [0]

    def _nm(self, name):
        self._uid[0] += 1
        return f"{name}_u{self._uid[0]}"

    def sb(self, es_, name, shape, dt=BF16):
        return es_.enter_context(self.nc.sbuf_tensor(self._nm(name), shape, dt))

    def ps(self, es_, name, shape, dt=F32):
        return es_.enter_context(self.nc.psum_tensor(self._nm(name), shape, dt))

    def snap(self):
        return dict(self.c)

    def seq(self, name, ops):
        nc = self.nc
        base = self.snap()
        prior, counts, incs = [], {k: 0 for k in ENG_SEM}, []
        for (ek, fn, is_dma) in ops:
            prior.append(dict(counts))
            inc = 16 if is_dma is True else 1
            counts[ek] += inc
            incs.append(inc)
        with self.nc.named_scope(f"seq_{name}"), nc.Block() as blk:
            for ek in ["te", "ve", "sc", "gp", "sy"]:
                if not any(e == ek for (e, _, _) in ops):
                    continue

                def mk(ek):
                    def closure(eng):
                        eng.wait_ge(self.sem["LS"], base["LS"])
                        eng.wait_ge(self.sem["GS"], base["GS"])
                        for i, (e, fn, is_dma) in enumerate(ops):
                            if e != ek:
                                continue
                            for oe, osem in ENG_SEM.items():
                                if oe != ek and prior[i][oe] > 0:
                                    eng.wait_ge(self.sem[osem], base[osem] + prior[i][oe])
                            if prior[i][ek] > 0:
                                eng.wait_ge(self.sem[ENG_SEM[ek]], base[ENG_SEM[ek]] + prior[i][ek])
                            inst = fn(eng)
                            assert inst is not None, f"seq op {i} in {name} returned None"
                            inst.then_inc(self.sem[ENG_SEM[ek]], incs[i])
                            if ek in ("ve", "sc", "gp") and not is_dma:
                                eng.drain()
                    return closure

                getattr(blk, ENGN[ek])(mk(ek))
        for ek, n in counts.items():
            self.c[ENG_SEM[ek]] += n

    def seqw(self, name, waves):
        nc = self.nc
        base = self.snap()
        ops = []
        for w, wl in enumerate(waves):
            for (ek, fn, is_dma) in wl:
                ops.append((w, ek, fn, is_dma))
        wave_tot = []
        cur = {k: 0 for k in ENG_SEM}
        prior = []
        incs = []
        wstart = []
        lastw = -1
        for (w, ek, fn, is_dma) in ops:
            if w != lastw:
                for _ in range(w - lastw):
                    wave_tot.append(dict(cur))
                lastw = w
            prior.append(dict(cur))
            inc = 16 if is_dma is True else 1
            cur[ek] += inc
            incs.append(inc)
        while len(wave_tot) <= len(waves):
            wave_tot.append(dict(cur))
        with self.nc.named_scope(f"sqw_{name}"), nc.Block() as blk:
            for ek in ["te", "ve", "sc", "gp", "sy"]:
                if not any(e == ek for (_, e, _, _) in ops):
                    continue

                def mk(ek):
                    def closure(eng):
                        eng.wait_ge(self.sem["LS"], base["LS"])
                        eng.wait_ge(self.sem["GS"], base["GS"])
                        for i, (w, e, fn, is_dma) in enumerate(ops):
                            if e != ek:
                                continue
                            floor = wave_tot[max(w - 1, 0)]
                            for oe, osem in ENG_SEM.items():
                                sw = prior[i][oe] if prior[i][oe] > wave_tot[w][oe] else 0
                                tgt = max(floor[oe], sw)
                                if tgt > 0:
                                    eng.wait_ge(self.sem[osem], base[osem] + tgt)
                            inst = fn(eng)
                            assert inst is not None, f"seqw op {i} in {name} None"
                            inst.then_inc(self.sem[ENG_SEM[ek]], incs[i])
                            if ek in ("ve", "sc", "gp") and not is_dma:
                                eng.drain()
                    return closure

                getattr(blk, ENGN[ek])(mk(ek))
        for ek, n in cur.items():
            self.c[ENG_SEM[ek]] += n

    def proj(self, name, n_oi, n_tc, n_kc, psum_shape, stat_fn, mov_fn, steps,
             wload=None, nps=2):
        nc = self.nc
        base = self.snap()
        ntiles = n_oi * n_tc
        nsteps = len(steps)
        step_inc = [16 if ek in ("sy", "gp") else 1 for (ek, _) in steps]
        bsem = ["MM"] + [f"X{j}" for j in range(1, nsteps)]
        binc = [1] + step_inc[:-1]
        wfn, wn = wload if wload else (None, 0)
        with ExitStack() as pes:
            psl = [pes.enter_context(nc.psum_tensor(f"{name}_ps{j}", list(psum_shape), F32))
                   for j in range(nps)]
            def emit_step(eng, j, i, oi, tc):
                sek, sfn = steps[j]
                eng.wait_ge(self.sem[bsem[j]], base[bsem[j]] + (i + 1) * binc[j])
                inst = sfn(eng, psl[i % nps], i, oi, tc)
                osem = "EV" if j == nsteps - 1 else bsem[j + 1]
                inst.then_inc(self.sem[osem], step_inc[j])
                if sek in ("ve", "sc"):
                    eng.drain()

            with self.nc.named_scope(f"proj_{name}"), nc.Block() as blk:
                @blk.tensor
                def _(te):
                    te.wait_ge(self.sem["LS"], base["LS"])
                    te.wait_ge(self.sem["GS"], base["GS"])
                    for i in range(ntiles):
                        oi, tc = divmod(i, n_tc)
                        if wfn and tc == 0:
                            te.wait_ge(self.sem["GS"], base["GS"] + 16 * wn * (oi + 1))
                        if i >= nps:
                            te.wait_ge(self.sem["EV"],
                                       base["EV"] + (i - nps + 1) * step_inc[-1])
                        inst = None
                        for kc in range(n_kc):
                            inst = te.matmul(psl[i % nps][:, :], stat_fn(kc, oi, tc),
                                             mov_fn(kc, oi, tc),
                                             start=(kc == 0), stop=(kc == n_kc - 1))
                        inst.then_inc(self.sem["MM"], 1)
                        for j, (sek, _) in enumerate(steps):
                            if sek == "te":
                                emit_step(te, j, i, oi, tc)

                engs = []
                for (ek, _) in steps:
                    if ek not in engs and ek not in ("te", "gp"):
                        engs.append(ek)
                for ek in engs:
                    def mk(ek):
                        def closure(eng):
                            eng.wait_ge(self.sem["LS"], base["LS"])
                            eng.wait_ge(self.sem["GS"], base["GS"])
                            for i in range(ntiles):
                                oi, tc = divmod(i, n_tc)
                                for j, (sek, _) in enumerate(steps):
                                    if sek != ek:
                                        continue
                                    emit_step(eng, j, i, oi, tc)
                        return closure
                    getattr(blk, ENGN[ek])(mk(ek))

                has_gp_steps = any(ek == "gp" for (ek, _) in steps)
                if wfn or has_gp_steps:
                    @blk.gpsimd
                    def _(gp):
                        gp.wait_ge(self.sem["LS"], base["LS"])
                        if wfn and not has_gp_steps:
                            for oi in range(n_oi):
                                if oi >= 2:
                                    gp.wait_ge(self.sem["MM"], base["MM"] + (oi - 1) * n_tc)
                                wfn(gp, oi)
                        else:
                            for i in range(ntiles):
                                oi, tc = divmod(i, n_tc)
                                if wfn and tc == 0:
                                    if oi >= 2:
                                        gp.wait_ge(self.sem["MM"], base["MM"] + (oi - 1) * n_tc)
                                    wfn(gp, oi)
                                for j, (sek, _) in enumerate(steps):
                                    if sek == "gp":
                                        emit_step(gp, j, i, oi, tc)

        self.c["MM"] += ntiles
        for j in range(1, nsteps):
            self.c[f"X{j}"] += ntiles * step_inc[j - 1]
        self.c["EV"] += ntiles * step_inc[-1]
        self.c["GS"] += 16 * wn * n_oi


def build(P):
    T, D, HD, F, E, NROT = P["T"], P["D"], P["HD"], P["F"], P["E"], P["NROT"]
    H = D // HD
    FC = D // 128
    HC = F // 128
    TC = T // 512
    KT = T // 128
    HE = H * E
    OC = HE * HD // 128
    GRP = OC // 8
    dbg = set(P["dbg"])

    nc = bass.Bass("TRN2", target_bir_lowering=False)

    def din(name, shape):
        return nc.dram_tensor(name, list(shape), F32, kind="ExternalInput")

    x_in = din("x", (T, D))
    wqkv = din("wqkv", (D, 3 * D))
    bqkvf = din("bqkvf", (128, 3 * FC))
    wout0 = din("wout0", (D, D))
    boutf = din("boutf", (128, FC))
    w1t = din("w1t", (D, F))
    b1f0 = din("b1f0", (128, HC))
    w2t = din("w2t", (F, D))
    b2f0 = din("b2f0", (128, FC))
    wit = din("wit", (D, 3 * D))
    inbf = din("inbf", (128, 3 * FC))
    wot = din("wot", (D, D))
    outbf = din("outbf", (128, FC))
    wq2 = din("wq2", (D, D))
    wk2 = din("wk2", (D, D))
    wv4 = din("wv4", (H, D, E * HD))
    wo2 = din("wo2", (HE * HD, D))
    wselv = din("wselv", (D, HE))
    wselo = din("wselo", (D, HE))
    ln1g = din("ln1g", (1, D))
    ln1b = din("ln1b", (1, D))
    ln2g = din("ln2g", (1, D))
    ln2b = din("ln2b", (1, D))
    w1L = din("w1L", (D, F))
    b1Lf = din("b1Lf", (128, HC))
    w2L = din("w2L", (F, D))
    b2Lf = din("b2Lf", (128, FC))
    ropeP = din("ropeP", (128, 128))

    out = nc.dram_tensor("out", [T, D], F32, kind="ExternalOutput")

    def scr(name, shape, dt=BF16):
        kind = "ExternalOutput" if name in dbg else "Internal"
        return nc.dram_tensor(name, list(shape), dt, kind=kind)

    ag_in = [nc.dram_tensor(f"ag_in{i}", [1, 16], F32) for i in range(3)]
    ag_out = [nc.dram_tensor(f"ag_out{i}", [NCORES, 16], F32) for i in range(3)]
    d_kv_in = scr("d_kv_in", (T, 2 * D))
    d_kv_all = nc.dram_tensor("d_kv_all", [NCORES * T, 2 * D], BF16, addr_space="Shared")

    d_xn0F = scr("d_xn0F", (D, T))
    d_xq0F = scr("d_xq0F", (D, T))
    d_QF = scr("d_QF", (D, T))
    d_KF = scr("d_KF", (D, T))
    d_VF = scr("d_VF", (D, T))
    d_s0 = scr("d_s0", (D, T))
    d_g0 = scr("d_g0", (D, T))
    d_w1q = scr("d_w1q", (D, F))
    d_w2q = scr("d_w2q", (F, D))
    d_x1 = scr("d_x1", (T, D), F32)
    d_xn1F = scr("d_xn1F", (D, T))
    d_wiq = scr("d_wiq", (D, 3 * D))
    d_woq = scr("d_woq", (D, D))
    d_qT = scr("d_qT", (T, D))
    d_heads = scr("d_heads", (T, D), F32)
    d_yqT = scr("d_yqT", (T, D))
    d_g1 = scr("d_g1", (T, D))
    d_srcB = scr("d_srcB", (D, T), F32)

    oes = ExitStack()
    with oes:
        b = Bld(nc, oes)
        SEM = b.sem

        identB = b.sb(oes, "identB", [128, 128], BF16)
        identF = b.sb(oes, "identF", [128, 128], F32)
        identB2 = b.sb(oes, "identB2", [128, 128], BF16)
        onesT = b.sb(oes, "onesT", [128, 128], F32)
        sel64b = b.sb(oes, "sel64b", [128, 2], BF16)
        ropePb = b.sb(oes, "ropePb", [128, 128], BF16)
        cbias = b.sb(oes, "cbias", [128, 4], F32)
        scol = b.sb(oes, "scol", [128, 16], F32)
        sc_row = b.sb(oes, "sc_row", [128, 16], F32)
        negB0 = b.sb(oes, "negB0", [128, 16], F32)
        negB1 = b.sb(oes, "negB1", [128, 16], F32)
        nrm = b.sb(oes, "nrm", [128, 128], F32)
        bqkv_s = b.sb(oes, "bqkv_s", [128, 3 * FC], F32)
        bout_s = b.sb(oes, "bout_s", [128, FC], F32)
        b1q_s = b.sb(oes, "b1q_s", [128, HC], F32)
        b2q_s = b.sb(oes, "b2q_s", [128, FC], F32)
        b1q16 = b.sb(oes, "b1q16", [128, HC], F32)
        inb_s = b.sb(oes, "inb_s", [128, 3 * FC], F32)
        outb_s = b.sb(oes, "outb_s", [128, FC], F32)
        outb2_s = b.sb(oes, "outb2_s", [128, FC], F32)
        b1L_s = b.sb(oes, "b1L_s", [128, HC], F32)
        b2L_s = b.sb(oes, "b2L_s", [128, FC], F32)
        A1 = b.sb(oes, "A1", [128, FC, T], BF16)

        with nc.Block() as blk:
            @blk.sync
            def _(s):
                for dst, src in [(bqkv_s, bqkvf), (bout_s, boutf), (b1q_s, b1f0),
                                 (b2q_s, b2f0), (inb_s, inbf), (outb_s, outbf),
                                 (b1L_s, b1Lf), (b2L_s, b2Lf)]:
                    s.dma_start(out=dst[:, :], in_=src[:, :]).then_inc(SEM["LS"], 16)
                    b.c["LS"] += 16

            @blk.gpsimd
            def _(gp):
                gp.dma_start(out=ropePb[:, :], in_=ropeP[:, :]).then_inc(SEM["GS"], 16)
                b.c["GS"] += 16
                for t, v in [(identB, 1.0), (identF, 1.0), (onesT, 1.0), (sel64b, 1.0)]:
                    gp.memset(t[:, :], v)
                    gp.drain()
                gp.memset(cbias[:, :], 0.0)
                gp.drain()
                gp.memset(cbias[:, 0:1], 1e-5)
                gp.drain()
                gp.memset(scol[:, :], 1.0)
                gp.drain()
                gp.memset(sc_row[:, :], 1.0)
                gp.drain()
                for ident in (identB, identF):
                    gp.affine_select(out=ident[:, :], in_=ident[:, :], pattern=[[-1, 128]],
                                     compare_op=ALU.is_equal, fill=0.0, base=0,
                                     channel_multiplier=1)
                    gp.drain()
                gp.affine_select(out=sel64b[:, :], in_=sel64b[:, :], pattern=[[64, 2]],
                                 compare_op=ALU.is_ge, fill=0.0, base=0,
                                 channel_multiplier=-1)
                gp.drain()
                gp.affine_select(out=sel64b[:, :], in_=sel64b[:, :], pattern=[[-64, 2]],
                                 compare_op=ALU.is_ge, fill=0.0, base=63,
                                 channel_multiplier=1)
                gp.drain()

        with ExitStack() as ses:
            scr1 = b.sb(ses, "s0_scr1", [128, 128], BF16)
            i32b = b.sb(ses, "s0_i32", [128, HC], I32)
            ops = []

            def mk_diag(dst, sh):
                def f(gp):
                    gp.memset(dst[:, :], 1.0)
                    gp.drain()
                    return gp.affine_select(out=dst[:, :], in_=dst[:, :],
                                            pattern=[[-1, 128]], compare_op=ALU.is_equal,
                                            fill=0.0, base=sh, channel_multiplier=1)
                return f

            ops.append(("gp", mk_diag(identB2, 0), False))
            ops.append(("gp", mk_diag(scr1, 64), False))
            ops.append(("ve", lambda ve: ve.tensor_tensor(identB2[:, :], identB2[:, :], scr1[:, :], ALU.add), False))
            ops.append(("gp", mk_diag(scr1, -64), False))
            ops.append(("ve", lambda ve: ve.tensor_tensor(identB2[:, :], identB2[:, :], scr1[:, :], ALU.add), False))
            for t, w in [(b1q_s, HC), (b2q_s, FC)]:
                ops.append(("ve", (lambda t=t, w=w: lambda ve: ve.tensor_scalar(
                    t[:, 0:w], t[:, 0:w], 64.0, 127.0, ALU.mult, ALU.min))(), False))
                ops.append(("ve", (lambda t=t, w=w: lambda ve: ve.tensor_scalar(
                    i32b[:, 0:w], t[:, 0:w], -128.0, None, ALU.max))(), False))
                ops.append(("ve", (lambda t=t, w=w: lambda ve: ve.tensor_scalar(
                    t[:, 0:w], i32b[:, 0:w], 0.015625, None, ALU.mult))(), False))
            ops.append(("ve", lambda ve: ve.tensor_scalar(b1q16[:, :], b1q_s[:, :], 16.0, None, ALU.mult), False))
            b.seq("s0b", ops)

        def ln_stage(name, src_dram, dst_F, quant_dst):
            with ExitStack() as ses:
                xt = b.sb(ses, f"{name}_x", [128, 2, D], F32)
                xn = b.sb(ses, f"{name}_xn", [128, 2, D], BF16)
                xq = b.sb(ses, f"{name}_xq", [128, 2, D], BF16)
                xqf = b.sb(ses, f"{name}_xqf", [128, 2, D], F32)
                i32t = b.sb(ses, f"{name}_i32", [128, 2, D], I32)
                bn6 = b.sb(ses, f"{name}_bn", [128, 2, 6 * (D // 512)], F32)
                mv = b.sb(ses, f"{name}_mv", [128, 2, 4], F32)
                fst = b.sb(ses, f"{name}_fs", [128, 2, FC, 128], BF16)
                fst2 = b.sb(ses, f"{name}_fs2", [128, 2, FC, 128], BF16)
                tps = [b.ps(ses, f"{name}_tp{j}", [128, FC, 128], BF16) for j in range(2)]
                nst = D // 512
                waves = []
                for kt in range(KT):
                    p = kt % 2
                    ops = []
                    ops.append(("sy", (lambda kt=kt, p=p: lambda s: s.dma_start(
                        out=xt[:, p, :], in_=src_dram[kt * 128:(kt + 1) * 128, :]))(), True))

                    def bnf(ve, p=p):
                        for j in range(nst):
                            ve.bn_stats(bn6[:, p, j * 6:(j + 1) * 6], xt[:, p, j * 512:(j + 1) * 512])
                            ve.drain()
                        return ve.bn_aggr(mv[:, p, 0:2], bn6[:, p, 0:6 * nst])
                    ops.append(("ve", bnf, False))
                    ops.append(("sc", (lambda p=p: lambda sc: sc.activation(
                        mv[:, p, 2:3], mv[:, p, 1:2], AF.Sqrt, bias=cbias[:, 0:1]))(), False))

                    def appf(ve, p=p):
                        ve.reciprocal(mv[:, p, 3:4], mv[:, p, 2:3])
                        ve.drain()
                        return ve.tensor_scalar(xn[:, p, :], xt[:, p, :], mv[:, p, 0:1],
                                                mv[:, p, 3:4], ALU.subtract, ALU.mult)
                    ops.append(("ve", appf, False))
                    if quant_dst is not None:
                        def qf1(ve, p=p):
                            ve.tensor_scalar(xqf[:, p, :], xt[:, p, :], mv[:, p, 0:1],
                                             mv[:, p, 3:4], ALU.subtract, ALU.mult)
                            ve.drain()
                            ve.tensor_scalar(xqf[:, p, :], xqf[:, p, :], 16.0, 127.0,
                                             ALU.mult, ALU.min)
                            ve.drain()
                            return ve.tensor_scalar(i32t[:, p, :], xqf[:, p, :], -128.0,
                                                    None, ALU.max)
                        ops.append(("ve", qf1, False))
                        ops.append(("sc", (lambda p=p: lambda sc: sc.activation(
                            xq[:, p, :], i32t[:, p, :], AF.Identity, scale=0.0625))(), False))
                    pairs = [(xn, fst, dst_F)] + ([(xq, fst2, quant_dst)] if quant_dst is not None else [])
                    for (tmj, fs_, dd) in pairs:
                        def trb(te, tmj=tmj, p=p):
                            inst = None
                            for fc in range(FC):
                                inst = te.transpose(tps[p][:, fc, :],
                                                    tmj[:, p, fc * 128:(fc + 1) * 128],
                                                    identB[:, :])
                            return inst
                        ops.append(("te", trb, False))
                        ops.append(("ve", (lambda fs_=fs_, p=p: lambda ve: ve.tensor_copy(
                            fs_[:, p, :, :], tps[p][:, :, :]))(), False))
                        ops.append(("sy", (lambda fs_=fs_, dd=dd, kt=kt, p=p: lambda s: s.dma_start(
                            out=dd.ap().rearrange("(c p) t -> p c t", p=128)[:, :, kt * 128:(kt + 1) * 128],
                            in_=fs_[:, p, :, :]))(), True))
                    waves.append(ops)
                b.seqw(name, waves)

        def norm_stage(name, QF, KF, negB_dst, scale):
            with ExitStack() as ses:
                q2t = b.sb(ses, f"{name}_q2", [128, T], BF16)
                nb2 = b.sb(ses, f"{name}_nb", [128, 16], F32)
                nps_ = b.ps(ses, f"{name}_ps", [128, 512])
                tpp = b.ps(ses, f"{name}_tp", [128, 16])
                ops = []
                for t_i, SRC in enumerate((QF, KF)):
                    for fc in range(FC):
                        ops.append(("sc", (lambda SRC=SRC, fc=fc: lambda sc: sc.activation(
                            q2t[:, :], SRC[:, fc, :], AF.Square))(), False))
                        for tc in range(TC):
                            ops.append(("te", (lambda tc=tc: lambda te: te.matmul(
                                nps_[0:2, :], sel64b[:, :], q2t[:, tc * 512:(tc + 1) * 512],
                                start=True, stop=True))(), False))
                            col = t_i * (FC * TC) + fc * TC + tc
                            ops.append(("ve", (lambda col=col: lambda ve: ve.reduce_max(
                                nrm[0:2, col:col + 1], nps_[0:2, :], AX.X))(), False))

                def comb(ve):
                    v = nrm[0:2, 0:2 * FC * TC].rearrange("p (g c) -> p g c", c=TC)
                    o = nrm[0:2, 64:64 + 2 * FC].rearrange("p g -> p g 1" if False else "p (g u) -> p g u", u=1)
                    ve.tensor_reduce(o, v, AX.X, ALU.max)
                    ve.drain()
                    return ve.tensor_tensor(nrm[0:2, 96:96 + FC], nrm[0:2, 64:64 + FC],
                                            nrm[0:2, 64 + FC:64 + 2 * FC], ALU.mult)
                ops.append(("ve", comb, False))
                ops.append(("sc", lambda sc: sc.activation(nrm[0:2, 96:96 + FC],
                                                           nrm[0:2, 96:96 + FC], AF.Sqrt), False))
                ops.append(("ve", lambda ve: ve.tensor_scalar(
                    nrm[0:2, 96:96 + FC], nrm[0:2, 96:96 + FC], -1.01 * scale, None, ALU.mult), False))
                ops.append(("gp", lambda gp: gp.dma_start(
                    out=nb2[0:1, 0:2 * FC],
                    in_=nrm[0:2, 96:96 + FC], allow_slow_non_contiguous=True), True))
                ops.append(("te", lambda te: te.matmul(tpp[:, 0:H], onesT[0:1, :],
                                                       nb2[0:1, 0:H], start=True, stop=True), False))
                ops.append(("ve", lambda ve: ve.tensor_copy(negB_dst[:, 0:H], tpp[:, 0:H]), False))
                b.seq(name, ops)

        def attn_stage(name, QF, KF, VTs, negB, masks, ctx_dst, causal):
            SEMl = b.sem
            with ExitStack() as ses:
                PT = b.sb(ses, f"{name}_PT", [128, KT, 512], BF16)
                rr = b.sb(ses, f"{name}_rr", [128, 512], F32)
                ctmp = b.sb(ses, f"{name}_ct", [128, 512], BF16)
                ps_s = [b.ps(ses, f"{name}_s{j}", [128, 512]) for j in range(2)]
                ps_c = b.ps(ses, f"{name}_c", [65, 512])
                ps_b = b.ps(ses, f"{name}_b", [64, 512])
                ps_p = b.ps(ses, f"{name}_p", [128, 512])
                for qc in range(TC):
                    nkt = min(KT, 4 * qc + 4) if causal else KT
                    base = b.snap()
                    def m_of(h):
                        return (4 if causal else 0)
                    def ve_base(h):
                        return sum(m_of(x) + 3 + (x % 2) for x in range(h))
                    def te_base(h):
                        return sum(nkt + 2 + (x % 2) for x in range(h))
                    def sc_cnt(h, kt):
                        return h * nkt + kt + 1
                    with nc.named_scope(f"attn_{name}_{qc}"), nc.Block() as blk:
                        @blk.tensor
                        def _(te, qc=qc, nkt=nkt, base=base):
                            te.wait_ge(SEMl["LS"], base["LS"])
                            te.wait_ge(SEMl["GS"], base["GS"])
                            bank_user = [None, None]
                            for h in range(H):
                                off = (h % 2) * 64
                                fc = h // 2
                                hcol = (h % 2) * FC + h // 2
                                for kt in range(nkt):
                                    bu = bank_user[kt % 2]
                                    if bu is not None:
                                        te.wait_ge(SEMl["SC"], base["SC"] + sc_cnt(*bu))
                                    te.matmul(ps_s[kt % 2][:, :],
                                              KF[off:off + 64, fc, kt * 128:(kt + 1) * 128],
                                              QF[off:off + 64, fc, qc * 512:(qc + 1) * 512],
                                              start=True, stop=True).then_inc(SEMl["MM"], 1)
                                    bank_user[kt % 2] = (h, kt)
                                te.wait_ge(SEMl["SC"], base["SC"] + (h + 1) * nkt)
                                te.wait_ge(SEMl["EV"], base["EV"] + ve_base(h) + m_of(h))
                                if h > 0:
                                    te.wait_ge(SEMl["EV"],
                                               base["EV"] + ve_base(h - 1) + m_of(h - 1) + 2)
                                inst = None
                                for kt in range(nkt):
                                    inst = te.matmul(ps_c[:, :], VTs[:, kt, h, 0:65], PT[:, kt, :],
                                                     start=(kt == 0), stop=(kt == nkt - 1))
                                inst.then_inc(SEMl["MM"], 1)
                                te.wait_ge(SEMl["EV"], base["EV"] + ve_base(h) + m_of(h) + 1)
                                te.matmul(ps_b[0:64, :], onesT[64:65, 0:64], rr[64:65, :],
                                          start=True, stop=True).then_inc(SEMl["MM"], 1)
                                if h % 2 == 1:
                                    te.wait_ge(SEMl["EV"], base["EV"] + ve_base(h) + m_of(h) + 3)
                                    te.matmul(ps_p[64:128, :], identB[0:64, 0:64], ctmp[0:64, :],
                                              start=True, stop=True,
                                              tile_position=(0, 64)).then_inc(SEMl["MM"], 1)

                        @blk.scalar
                        def _(sc, qc=qc, nkt=nkt, base=base):
                            sc.wait_ge(SEMl["LS"], base["LS"])
                            sc.wait_ge(SEMl["GS"], base["GS"])
                            for h in range(H):
                                hcol = (h % 2) * FC + h // 2
                                for kt in range(nkt):
                                    sc.wait_ge(SEMl["MM"], base["MM"] + te_base(h) + kt + 1)
                                    sc.activation(PT[:, kt, :], ps_s[kt % 2][:, :], AF.Exp,
                                                  bias=negB[:, hcol:hcol + 1],
                                                  scale=0.125).then_inc(SEMl["SC"], 1)

                        @blk.vector
                        def _(ve, qc=qc, nkt=nkt, base=base):
                            ve.wait_ge(SEMl["LS"], base["LS"])
                            ve.wait_ge(SEMl["GS"], base["GS"])
                            for h in range(H):
                                fc = h // 2
                                if causal:
                                    for kt in range(4 * qc, nkt):
                                        ve.wait_ge(SEMl["SC"], base["SC"] + sc_cnt(h, kt))
                                        ve.tensor_tensor(PT[:, kt, :], PT[:, kt, :],
                                                         masks[:, kt - 4 * qc, :],
                                                         ALU.mult).then_inc(SEMl["EV"], 1)
                                        ve.drain()
                                ve.wait_ge(SEMl["MM"], base["MM"] + te_base(h) + nkt + 1)
                                ve.reciprocal(rr[64:65, :], ps_c[64:65, :]).then_inc(SEMl["EV"], 1)
                                ve.drain()
                                ve.tensor_copy(ctmp[0:64, :], ps_c[0:64, :]).then_inc(SEMl["EV"], 1)
                                ve.drain()
                                ve.wait_ge(SEMl["MM"], base["MM"] + te_base(h) + nkt + 2)
                                if h % 2 == 0:
                                    ve.tensor_tensor(ctx_dst[0:64, fc, qc * 512:(qc + 1) * 512],
                                                     ctmp[0:64, :], ps_b[0:64, :],
                                                     ALU.mult).then_inc(SEMl["EV"], 1)
                                    ve.drain()
                                else:
                                    ve.tensor_tensor(ctmp[0:64, :], ctmp[0:64, :],
                                                     ps_b[0:64, :], ALU.mult).then_inc(SEMl["EV"], 1)
                                    ve.drain()
                                    ve.wait_ge(SEMl["MM"], base["MM"] + te_base(h) + nkt + 3)
                                    ve.tensor_copy(ctx_dst[64:128, fc, qc * 512:(qc + 1) * 512],
                                                   ps_p[64:128, :]).then_inc(SEMl["EV"], 1)
                                    ve.drain()
                    nte = sum(nkt + 2 + (x % 2) for x in range(H))
                    b.c["MM"] += nte
                    b.c["SC"] += H * nkt
                    b.c["EV"] += sum(m_of(x) + 3 + (x % 2) for x in range(H))

        def quantw_stage(name, src, dst, rows, cols, scale_num, recip_scale):
            with ExitStack() as ses:
                wtmp = b.sb(ses, f"{name}_w", [128, min(cols, 2048)], F32)
                wi = b.sb(ses, f"{name}_i", [128, min(cols, 2048)], I32)
                wb = b.sb(ses, f"{name}_b", [128, min(cols, 2048)], BF16)
                CW = min(cols, 2048)
                ops = []
                for r in range(rows // 128):
                    for cb in range(cols // CW):
                        sl = (slice(r * 128, (r + 1) * 128), slice(cb * CW, (cb + 1) * CW))
                        ops.append(("sy", (lambda sl=sl: lambda s: s.dma_start(
                            out=wtmp[:, :], in_=src[sl[0], sl[1]]))(), True))
                        def q1(ve):
                            ve.tensor_scalar(wtmp[:, :], wtmp[:, :], scale_num, 127.0,
                                             ALU.mult, ALU.min)
                            ve.drain()
                            return ve.tensor_scalar(wi[:, :], wtmp[:, :], -128.0, None, ALU.max)
                        ops.append(("ve", q1, False))
                        ops.append(("sc", lambda sc: sc.activation(wb[:, :], wi[:, :],
                                                                   AF.Identity, scale=recip_scale), False))
                        ops.append(("sy", (lambda sl=sl: lambda s: s.dma_start(
                            out=dst[sl[0], sl[1]], in_=wb[:, :]))(), True))
                b.seq(name, ops)

        CUT = P["cut"]

        if CUT >= 2:
            ln_stage("s1", x_in, d_xn0F, d_xq0F)

        if CUT >= 3:
          with nc.Block() as blk:
            @blk.sync
            def _(s):
                s.dma_start(out=A1[:, :, :],
                            in_=d_xn0F.ap().rearrange("(c p) t -> p c t", p=128)).then_inc(SEM["LS"], 16)
                b.c["LS"] += 16

        if CUT >= 3:
          with ExitStack() as rl0:
            ctx0 = b.sb(rl0, "ctx0", [128, FC, T], BF16)
            QFr = b.sb(rl0, "QFr", [128, FC, T], BF16)
            KFr = b.sb(rl0, "KFr", [128, FC, T], BF16)
            VT = b.sb(rl0, "VT", [128, KT, H, 66], BF16)

            with ExitStack() as ses:
                wbuf = b.sb(ses, "s2_w", [128, 2, FC, 128], BF16)
                st = b.sb(ses, "s2_st", [128, 2, 512], BF16)

                def wload(gp, oi):
                    src = wqkv.ap().rearrange("(kc p) o -> p kc o", p=128)[:, :, oi * 128:(oi + 1) * 128]
                    gp.dma_start(out=wbuf[:, oi % 2, :, :], in_=src).then_inc(SEM["GS"], 16)

                def ev(sc, ps, i, oi, tc):
                    return sc.activation(st[:, i % 2, :], ps[:, :], AF.Identity,
                                         bias=bqkv_s[:, oi:oi + 1])

                def stf(s, ps, i, oi, tc):
                    dd = (d_QF, d_KF, d_VF)[oi // FC]
                    o = oi % FC
                    return s.dma_start(out=dd[o * 128:(o + 1) * 128, tc * 512:(tc + 1) * 512],
                                       in_=st[:, i % 2, :])

                b.proj("s2", 3 * FC, TC, FC, [128, 512],
                       lambda kc, oi, tc: wbuf[:, oi % 2, kc, :],
                       lambda kc, oi, tc: A1[:, kc, tc * 512:(tc + 1) * 512],
                       steps=[("sc", ev), ("sy", stf)], wload=(wload, 1), nps=2)

            with nc.Block() as blk:
                @blk.sync
                def _(s):
                    s.dma_start(out=QFr[:, :, :],
                                in_=d_QF.ap().rearrange("(c p) t -> p c t", p=128)).then_inc(SEM["LS"], 16)
                    s.dma_start(out=KFr[:, :, :],
                                in_=d_KF.ap().rearrange("(c p) t -> p c t", p=128)).then_inc(SEM["LS"], 16)
                    b.c["LS"] += 32

            if CUT >= 4:
             with ExitStack() as ses:
                vfch = b.sb(ses, "s2b_v", [128, 2, T], BF16)
                vtps = [b.ps(ses, f"s2b_ps{j}", [128, KT, 128], BF16) for j in range(2)]
                b.seq("s2b_ones", [("gp", lambda gp: gp.memset(VT[:, :, :, 64:65], 1.0), False)])
                waves = []
                for fc in range(FC):
                    p = fc % 2
                    ops = []
                    ops.append(("sy", (lambda fc=fc, p=p: lambda s: s.dma_start(
                        out=vfch[:, p, :], in_=d_VF[fc * 128:(fc + 1) * 128, :]))(), True))
                    def trb(te, p=p):
                        inst = None
                        for kt in range(KT):
                            inst = te.transpose(vtps[p][:, kt, :],
                                                vfch[:, p, kt * 128:(kt + 1) * 128],
                                                identB[:, :])
                        return inst
                    ops.append(("te", trb, False))
                    ops.append(("ve", (lambda fc=fc, p=p: lambda ve: ve.tensor_copy(
                        VT[:, :, 2 * fc, 0:64], vtps[p][:, :, 0:64]))(), False))
                    ops.append(("ve", (lambda fc=fc, p=p: lambda ve: ve.tensor_copy(
                        VT[:, :, 2 * fc + 1, 0:64], vtps[p][:, :, 64:128]))(), False))
                    waves.append(ops)
                b.seqw("s2b", waves)

            if CUT >= 4:
                norm_stage("s2n", QFr, KFr, negB0, 0.125)

            if CUT >= 5:
             with ExitStack() as ses:
                masks0 = b.sb(ses, "masks0", [128, 4, 512], BF16)
                with nc.Block() as blk:
                    @blk.gpsimd
                    def _(gp):
                        gp.memset(masks0[:, :, :], 1.0)
                        gp.drain()
                        for m in range(4):
                            gp.affine_select(out=masks0[:, m, :], in_=masks0[:, m, :],
                                             pattern=[[1, 512]], compare_op=ALU.is_ge,
                                             fill=0.0, base=-128 * m, channel_multiplier=-1)
                            gp.drain()
                attn_stage("s3", QFr, KFr, VT, negB0, masks0, ctx0, causal=True)

            if CUT >= 5:
             with ExitStack() as ses:
                wout_s = b.sb(ses, "s3o_w", [128, FC, D], BF16)
                st3 = b.sb(ses, "s3o_st", [128, 2, 512], BF16)
                with nc.Block() as blk:
                    @blk.gpsimd
                    def _(gp):
                        gp.dma_start(out=wout_s[:, :, :],
                                     in_=wout0.ap().rearrange("(kc p) o -> p kc o", p=128)).then_inc(SEM["GS"], 16)
                        b.c["GS"] += 16

                def ev3(sc, ps, i, oi, tc):
                    return sc.activation(st3[:, i % 2, :], ps[:, :], AF.Identity,
                                         bias=bout_s[:, oi:oi + 1])

                def st3f(s, ps, i, oi, tc):
                    return s.dma_start(out=d_s0[oi * 128:(oi + 1) * 128, tc * 512:(tc + 1) * 512],
                                       in_=st3[:, i % 2, :])

                b.proj("s3o", FC, TC, FC, [128, 512],
                       lambda kc, oi, tc: wout_s[:, kc, oi * 128:(oi + 1) * 128],
                       lambda kc, oi, tc: ctx0[:, kc, tc * 512:(tc + 1) * 512],
                       steps=[("sc", ev3), ("sy", st3f)], nps=2)

        if CUT >= 6:
            quantw_stage("s4w1", w1t, d_w1q, D, F, 64.0, 0.015625)
            quantw_stage("s4w2", w2t, d_w2q, F, D, 64.0, 0.015625)
        if CUT >= 6:
         with ExitStack() as rs4:
            xqF = b.sb(rs4, "xqF", [128, FC, T], BF16)
            hstage = b.sb(rs4, "hstage", [128, HC, 512], BF16)
            with nc.Block() as blk:
                @blk.sync
                def _(s):
                    s.dma_start(out=xqF[:, :, :],
                                in_=d_xq0F.ap().rearrange("(c p) t -> p c t", p=128)).then_inc(SEM["LS"], 16)
                    b.c["LS"] += 16
            for tc in range(TC):
                with ExitStack() as ses:
                    wb1 = b.sb(ses, "s4_w1", [128, 2, FC, 128], BF16)
                    i32a = b.sb(ses, "s4_i32", [128, 2, 512], I32)

                    def wl1(gp, oi):
                        src = d_w1q.ap().rearrange("(kc p) o -> p kc o", p=128)[:, :, oi * 128:(oi + 1) * 128]
                        gp.dma_start(out=wb1[:, oi % 2, :, :], in_=src).then_inc(SEM["GS"], 16)

                    def evA1(sc, ps, i, oi, tc_):
                        return sc.activation(i32a[:, i % 2, :], ps[:, :], AF.Identity,
                                             scale=16.0, bias=b1q16[:, oi:oi + 1])

                    def evA2(ve, ps, i, oi, tc_):
                        return ve.tensor_scalar(i32a[:, i % 2, :], i32a[:, i % 2, :],
                                                0.0, 127.0, ALU.max, ALU.min)

                    def evA3(sc, ps, i, oi, tc_):
                        return sc.activation(hstage[:, oi, :], i32a[:, i % 2, :],
                                             AF.Identity, scale=0.0625)

                    b.proj(f"s4a_{tc}", HC, 1, FC, [128, 512],
                           lambda kc, oi, tc_: wb1[:, oi % 2, kc, :],
                           (lambda tc: lambda kc, oi, tc_: xqF[:, kc, tc * 512:(tc + 1) * 512])(tc),
                           steps=[("sc", evA1), ("ve", evA2), ("sc", evA3)],
                           wload=(wl1, 1), nps=2)
                with ExitStack() as ses:
                    wb2 = b.sb(ses, "s4_w2", [128, 2, HC, 128], BF16)
                    gst = b.sb(ses, "s4_gst", [128, 2, 512], BF16)

                    def wl2(gp, oi):
                        src = d_w2q.ap().rearrange("(kc p) o -> p kc o", p=128)[:, :, oi * 128:(oi + 1) * 128]
                        gp.dma_start(out=wb2[:, oi % 2, :, :], in_=src).then_inc(SEM["GS"], 16)

                    def evB1(sc, ps, i, oi, tc_):
                        return sc.activation(gst[:, i % 2, :], ps[:, :], AF.Silu,
                                             bias=b2q_s[:, oi:oi + 1])

                    def stB(s, ps, i, oi, tc_):
                        return (lambda tc: s.dma_start(
                            out=d_g0[oi * 128:(oi + 1) * 128, tc * 512:(tc + 1) * 512],
                            in_=gst[:, i % 2, :]))(tc)

                    b.proj(f"s4b_{tc}", FC, 1, HC, [128, 512],
                           lambda kc, oi, tc_: wb2[:, oi % 2, kc, :],
                           lambda kc, oi, tc_: hstage[:, kc, :],
                           steps=[("sc", evB1), ("sy", stB)], wload=(wl2, 1), nps=2)

        if CUT >= 7:
         with ExitStack() as ses:
            xt5 = b.sb(ses, "s5_x", [128, D], F32)
            sT5 = b.sb(ses, "s5_s", [128, D], BF16)
            gT5 = b.sb(ses, "s5_g", [128, D], BF16)
            sg5 = b.sb(ses, "s5_sg", [128, D], BF16)
            x15 = b.sb(ses, "s5_x1", [128, D], F32)
            for kt in range(KT):
                ops = []
                ops.append(("sy", (lambda kt=kt: lambda s: s.dma_start(
                    out=xt5[:, :], in_=x_in[kt * 128:(kt + 1) * 128, :]))(), True))
                ops.append(("sy", (lambda kt=kt: lambda s: s.dma_start_transpose(
                    sT5[:, :], d_s0[:, kt * 128:(kt + 1) * 128]))(), True))
                ops.append(("sy", (lambda kt=kt: lambda s: s.dma_start_transpose(
                    gT5[:, :], d_g0[:, kt * 128:(kt + 1) * 128]))(), True))
                ops.append(("ve", lambda ve: ve.tensor_tensor(sg5[:, :], sT5[:, :], gT5[:, :], ALU.mult), False))
                ops.append(("ve", lambda ve: ve.tensor_tensor(x15[:, :], xt5[:, :], sg5[:, :], ALU.add), False))
                ops.append(("sy", (lambda kt=kt: lambda s: s.dma_start(
                    out=d_x1[kt * 128:(kt + 1) * 128, :], in_=x15[:, :]))(), True))
                b.seq(f"s5_{kt}", ops)

        if CUT >= 7:
            ln_stage("s6", d_x1, d_xn1F, None)
        if CUT >= 7:
         with nc.Block() as blk:
            @blk.sync
            def _(s):
                s.dma_start(out=A1[:, :, :],
                            in_=d_xn1F.ap().rearrange("(c p) t -> p c t", p=128)).then_inc(SEM["LS"], 16)
                b.c["LS"] += 16

        if CUT >= 8 or CUT == 1:
            build_s7(b, nc, locals())

        if CUT >= 9:
            build_s8(b, nc, locals())

        if CUT >= 10:
         with ExitStack() as ses:
            x19 = b.sb(ses, "s9_x1", [128, D], F32)
            yq9 = b.sb(ses, "s9_yq", [128, D], BF16)
            g9 = b.sb(ses, "s9_g", [128, D], BF16)
            sg9 = b.sb(ses, "s9_sg", [128, D], BF16)
            o9 = b.sb(ses, "s9_o", [128, D], F32)
            for kt in range(KT):
                ops = []
                ops.append(("sy", (lambda kt=kt: lambda s: s.dma_start(
                    out=x19[:, :], in_=d_x1[kt * 128:(kt + 1) * 128, :]))(), True))
                ops.append(("sy", (lambda kt=kt: lambda s: s.dma_start(
                    out=yq9[:, :], in_=d_yqT[kt * 128:(kt + 1) * 128, :]))(), True))
                ops.append(("sy", (lambda kt=kt: lambda s: s.dma_start(
                    out=g9[:, :], in_=d_g1[kt * 128:(kt + 1) * 128, :]))(), True))
                ops.append(("ve", lambda ve: ve.scalar_tensor_tensor(
                    sg9[:, :], yq9[:, :], scol[:, 11:12], g9[:, :], ALU.mult, ALU.mult), False))
                ops.append(("ve", lambda ve: ve.tensor_tensor(o9[:, :], x19[:, :], sg9[:, :], ALU.add), False))
                ops.append(("sy", (lambda kt=kt: lambda s: s.dma_start(
                    out=out[kt * 128:(kt + 1) * 128, :], in_=o9[:, :]))(), True))
                b.seq(f"s9_{kt}", ops)

        with nc.Block() as blk:
            @blk.sync
            def _(s):
                s.wait_ge(SEM["LS"], b.c["LS"])

    return nc


def build_s7(b, nc, L):
    (T, D, F, H, E, HD, FC, HC, TC, KT) = (L["T"], L["D"], L["F"], L["H"], L["E"],
                                           L["HD"], L["FC"], L["HC"], L["TC"], L["KT"])
    C7 = L["P"].get("cut7", 99) if isinstance(L.get("P"), dict) else 99
    SEM = b.sem
    A1, scol, sc_row, nrm = L["A1"], L["scol"], L["sc_row"], L["nrm"]
    inb_s, outb_s, outb2_s = L["inb_s"], L["outb_s"], L["outb2_s"]
    identB, identF, onesT = L["identB"], L["identF"], L["onesT"]
    wit, wot = L["wit"], L["wot"]
    d_wiq, d_woq, d_qT, d_kv_in, d_kv_all = (L["d_wiq"], L["d_woq"], L["d_qT"],
                                             L["d_kv_in"], L["d_kv_all"])
    d_heads, d_yqT = L["d_heads"], L["d_yqT"]
    ag_in, ag_out = L["ag_in"], L["ag_out"]
    RG = [list(range(NCORES))]

    C_SWI, C_RWI, C_SWO, C_RWO = 0, 1, 2, 3
    C_SX, C_RSX, C_FX, C_SQ, C_FQ, C_ATT = 4, 5, 6, 7, 8, 9
    C_SH, C_SY, C_FH, C_FY, C_SHWO, C_RSY = 10, 11, 12, 13, 14, 15

    def bcast_scol(ops, tpp):
        ops.append(("te", lambda te: te.matmul(tpp[:, 0:16], onesT[0:1, :],
                                               sc_row[0:1, 0:16], start=True, stop=True), False))
        ops.append(("ve", lambda ve: ve.tensor_copy(scol[:, :], tpp[:, 0:16]), False))

    def fakequantw(name, src, dst, rows, cols, c_scale, c_recip):
        nch = rows // 128
        CW = min(cols, 1024)
        ncb = cols // CW
        with ExitStack() as ses:
            wtmp = b.sb(ses, f"{name}_w", [128, CW], F32)
            wi = b.sb(ses, f"{name}_i", [128, CW], I32)
            wb = b.sb(ses, f"{name}_b", [128, CW], BF16)
            tpp = b.ps(ses, f"{name}_tp", [128, 128])
            ops = []
            for r in range(nch):
                for cb in range(ncb):
                    sl = (slice(r * 128, (r + 1) * 128), slice(cb * CW, (cb + 1) * CW))
                    ops.append(("sy", (lambda sl=sl: lambda s: s.dma_start(
                        out=wtmp[:, :], in_=src[sl[0], sl[1]]))(), True))
                    col = r * ncb + cb
                    ops.append(("ve", (lambda col=col: lambda ve: ve.reduce_max(
                        nrm[:, col:col + 1], wtmp[:, :], AX.X,
                        apply_absolute_value=True))(), False))
            nc_ = nch * ncb
            ops.append(("ve", lambda ve: ve.reduce_max(
                nrm[:, 120:121], nrm[:, 0:nc_], AX.X), False))
            ops.append(("te", lambda te: te.transpose(tpp[0:1, 0:128], nrm[:, 120:121],
                                                      identF[:, :]), False))
            ops.append(("ve", lambda ve: ve.reduce_max(nrm[0:1, 121:122], tpp[0:1, 0:128], AX.X), False))
            ops.append(("ve", (lambda: lambda ve: ve.tensor_scalar(
                sc_row[0:1, c_scale:c_scale + 1], nrm[0:1, 121:122],
                1.0 / 127.0, 1e-8, ALU.mult, ALU.max))(), False))
            ops.append(("ve", lambda ve: ve.reciprocal(
                sc_row[0:1, c_recip:c_recip + 1], sc_row[0:1, c_scale:c_scale + 1]), False))
            bcast_scol(ops, tpp)
            b.seq(f"{name}_p1", ops)
            ops = []
            for r in range(nch):
                for cb in range(ncb):
                    sl = (slice(r * 128, (r + 1) * 128), slice(cb * CW, (cb + 1) * CW))
                    ops.append(("sy", (lambda sl=sl: lambda s: s.dma_start(
                        out=wtmp[:, :], in_=src[sl[0], sl[1]]))(), True))
                    ops.append(("ve", lambda ve: ve.tensor_scalar(
                        wi[:, :], wtmp[:, :], scol[:, c_recip:c_recip + 1], None, ALU.mult), False))
                    ops.append(("sc", lambda sc: sc.activation(wb[:, :], wi[:, :], AF.Identity), False))
                    ops.append(("sy", (lambda sl=sl: lambda s: s.dma_start(
                        out=dst[sl[0], sl[1]], in_=wb[:, :]))(), True))
            b.seq(f"{name}_p2", ops)

    fakequantw("s7wi", wit, d_wiq, D, 3 * D, C_SWI, C_RWI)
    fakequantw("s7wo", wot, d_woq, D, D, C_SWO, C_RWO)
    if C7 < 2:
        return

    with ExitStack() as r7:
        hqF = b.sb(r7, "hqF", [128, FC, T], BF16)
        g7 = b.sb(r7, "g7", [128, 16], F32)
        agt = b.sb(r7, "agt", [NCORES, 16], F32)
        i7one = b.sb(r7, "i7one", [128, 2], I32)

        with ExitStack() as ses:
            wb7 = b.sb(ses, "s7a_w", [128, 2, FC, 128], BF16)
            xst = b.sb(ses, "s7a_st", [128, 2, 512], BF16)

            def wl(gp, oi):
                src = d_wiq.ap().rearrange("(kc p) o -> p kc o", p=128)[:, :, oi * 128:(oi + 1) * 128]
                gp.dma_start(out=wb7[:, oi % 2, :, :], in_=src).then_inc(SEM["GS"], 16)

            def ev1(sc, ps, i, oi, tc):
                return sc.activation(xst[:, i % 2, :], ps[:, :], AF.Identity,
                                     scale=scol[:, C_SWI:C_SWI + 1], bias=inb_s[:, oi:oi + 1])

            def ev2(ve, ps, i, oi, tc):
                return ve.reduce_max(nrm[:, i:i + 1], xst[:, i % 2, :], AX.X,
                                     apply_absolute_value=True)

            b.proj("s7a", 3 * FC, TC, FC, [128, 512],
                   lambda kc, oi, tc: wb7[:, oi % 2, kc, :],
                   lambda kc, oi, tc: A1[:, kc, tc * 512:(tc + 1) * 512],
                   steps=[("sc", ev1), ("ve", ev2)], wload=(wl, 1), nps=2)

        if C7 < 3:
            return
        with ExitStack() as ses:
            tpp = b.ps(ses, "s7g1_tp", [128, 128])
            ops = []
            ops.append(("ve", lambda ve: ve.reduce_max(nrm[:, 120:121], nrm[:, 0:FC * TC], AX.X), False))
            ops.append(("ve", lambda ve: ve.reduce_max(nrm[:, 121:122], nrm[:, 0:3 * FC * TC], AX.X), False))
            ops.append(("te", lambda te: te.transpose(tpp[0:2, 0:128], nrm[:, 120:122], identF[:, :]), False))
            ops.append(("ve", lambda ve: ve.reduce_max(g7[0:2, 0:1], tpp[0:2, 0:128], AX.X), False))
            ops.append(("gp", lambda gp: gp.dma_start(
                out=ag_in[0].ap()[0:1, 0:2], in_=g7[0:2, 0:1],
                allow_slow_non_contiguous=True), True))
            import os as _os
            if _os.environ.get("NOCC"):
                ops.append(("gp", lambda gp: gp.dma_start(
                    out=ag_out[0].ap()[0:1, :], in_=ag_in[0].ap()), True))
            else:
                ops.append(("gp", lambda gp: gp.collective_compute(
                    "AllGather", ALU.bypass, replica_groups=RG,
                    ins=[ag_in[0].ap().opt()], outs=[ag_out[0].ap().opt()]), "cc"))
            ops.append(("gp", lambda gp: gp.dma_start(out=agt[:, :], in_=ag_out[0].ap()), True))
            ops.append(("te", lambda te: te.transpose(tpp[0:16, 0:NCORES], agt[:, :], identF[0:NCORES, 0:NCORES]), False))
            ops.append(("ve", lambda ve: ve.reduce_max(g7[0:16, 1:2], tpp[0:16, 0:NCORES], AX.X), False))
            ops.append(("te", lambda te: te.transpose(tpp[0:1, 0:128], g7[:, 1:2], identF[:, :]), False))
            ops.append(("ve", lambda ve: ve.tensor_copy(g7[0:1, 0:16], tpp[0:1, 0:16]), False))
            ops.append(("ve", lambda ve: ve.tensor_scalar(
                sc_row[0:1, C_SX:C_SX + 1], g7[0:1, 1:2], 1.0 / 127.0, 1e-8,
                ALU.mult, ALU.max), False))
            ops.append(("ve", lambda ve: ve.reciprocal(
                sc_row[0:1, C_RSX:C_RSX + 1], sc_row[0:1, C_SX:C_SX + 1]), False))
            ops.append(("ve", lambda ve: ve.tensor_tensor(
                sc_row[0:1, C_FX:C_FX + 1], sc_row[0:1, C_SWI:C_SWI + 1],
                sc_row[0:1, C_RSX:C_RSX + 1], ALU.mult), False))
            ops.append(("ve", lambda ve: ve.tensor_tensor(
                g7[0:1, 2:3], g7[0:1, 0:1], sc_row[0:1, C_RSX:C_RSX + 1], ALU.mult), False))
            ops.append(("ve", lambda ve: ve.tensor_scalar(
                i7one[0:1, 0:1], g7[0:1, 2:3], 127.0, None, ALU.min), False))
            ops.append(("ve", lambda ve: ve.tensor_tensor(
                g7[0:1, 3:4], i7one[0:1, 0:1], sc_row[0:1, C_SX:C_SX + 1], ALU.mult), False))
            ops.append(("ve", lambda ve: ve.tensor_scalar(
                g7[0:1, 3:4], g7[0:1, 3:4], 0.125, None, ALU.mult), False))
            ops.append(("ve", lambda ve: ve.tensor_scalar(
                sc_row[0:1, C_SQ:C_SQ + 1], g7[0:1, 3:4], 1.0 / 127.0, 1e-8,
                ALU.mult, ALU.max), False))
            ops.append(("ve", lambda ve: ve.reciprocal(g7[0:1, 4:5], sc_row[0:1, C_SQ:C_SQ + 1]), False))
            ops.append(("ve", lambda ve: ve.tensor_tensor(
                g7[0:1, 4:5], g7[0:1, 4:5], sc_row[0:1, C_SX:C_SX + 1], ALU.mult), False))
            ops.append(("ve", lambda ve: ve.tensor_scalar(
                sc_row[0:1, C_FQ:C_FQ + 1], g7[0:1, 4:5], 0.125, None, ALU.mult), False))
            ops.append(("ve", lambda ve: ve.tensor_tensor(
                sc_row[0:1, C_ATT:C_ATT + 1], sc_row[0:1, C_SQ:C_SQ + 1],
                sc_row[0:1, C_SX:C_SX + 1], ALU.mult), False))
            bcast_scol(ops, tpp)
            ops.append(("ve", lambda ve: ve.tensor_scalar(
                inb_s[:, :], inb_s[:, :], scol[:, C_RSX:C_RSX + 1], None, ALU.mult), False))
            b.seq("s7ag1", ops)

        if C7 < 4:
            return
        with ExitStack() as ses:
            wb7 = b.sb(ses, "s7b_w", [128, 2, FC, 128], BF16)
            i7a = b.sb(ses, "s7b_i1", [128, 2, 512], I32)
            i7b = b.sb(ses, "s7b_i2", [128, 2, 512], I32)
            st7 = b.sb(ses, "s7b_st", [128, 2, 512], BF16)
            tT7 = b.sb(ses, "s7b_tt", [128, 2, 512], BF16)
            pst7 = [b.ps(ses, f"s7b_tp{j}", [128, 512], BF16) for j in range(2)]

            def wl(gp, oi):
                src = d_wiq.ap().rearrange("(kc p) o -> p kc o", p=128)[:, :, oi * 128:(oi + 1) * 128]
                gp.dma_start(out=wb7[:, oi % 2, :, :], in_=src).then_inc(SEM["GS"], 16)

            def e1(sc, ps, i, oi, tc):
                return sc.activation(i7a[:, i % 2, :], ps[:, :], AF.Identity,
                                     scale=scol[:, C_FX:C_FX + 1], bias=inb_s[:, oi:oi + 1])

            def e2(sc, ps, i, oi, tc):
                if oi < FC:
                    return sc.activation(i7b[:, i % 2, :], i7a[:, i % 2, :],
                                         AF.Identity, scale=scol[:, C_FQ:C_FQ + 1])
                return sc.nop()

            def e3(ve, ps, i, oi, tc):
                src = i7b if oi < FC else i7a
                return ve.tensor_copy(st7[:, i % 2, :], src[:, i % 2, :])

            def e4(te, ps, i, oi, tc):
                inst = None
                for qs in range(4):
                    inst = te.transpose(pst7[i % 2][:, qs * 128:(qs + 1) * 128],
                                        st7[:, i % 2, qs * 128:(qs + 1) * 128], identB[:, :])
                return inst

            def e5(ve, ps, i, oi, tc):
                return ve.tensor_copy(tT7[:, i % 2, :], pst7[i % 2][:, :])

            def e6(s, ps, i, oi, tc):
                if oi < FC:
                    dst = d_qT.ap().rearrange("(tc qs p) c -> p tc qs c", p=128, qs=4)[
                        :, tc, :, oi * 128:(oi + 1) * 128]
                elif oi < 2 * FC:
                    dst = d_kv_in.ap().rearrange("(tc qs p) c -> p tc qs c", p=128, qs=4)[
                        :, tc, :, (oi - FC) * 128:(oi - FC + 1) * 128]
                else:
                    dst = d_kv_in.ap().rearrange("(tc qs p) c -> p tc qs c", p=128, qs=4)[
                        :, tc, :, D + (oi - 2 * FC) * 128:D + (oi - 2 * FC + 1) * 128]
                return s.dma_start(out=dst, in_=tT7[:, i % 2, :].rearrange("p (qs c) -> p qs c", qs=4))

            b.proj("s7b", 3 * FC, TC, FC, [128, 512],
                   lambda kc, oi, tc: wb7[:, oi % 2, kc, :],
                   lambda kc, oi, tc: A1[:, kc, tc * 512:(tc + 1) * 512],
                   steps=[("sc", e1), ("sc", e2), ("ve", e3), ("te", e4),
                          ("ve", e5), ("sy", e6)],
                   wload=(wl, 1), nps=2)

        if C7 < 5:
            return
        ops = [("gp", lambda gp: gp.collective_compute(
            "AllGather", ALU.bypass, replica_groups=RG,
            ins=[d_kv_in.ap().opt()], outs=[d_kv_all.ap().opt()]), "cc")]
        b.seq("s7ag2", ops)

        if C7 < 6:
            return
        with ExitStack() as ses:
            qTt = b.sb(ses, "s7d_q", [128, 2, D], BF16)
            kvl = b.sb(ses, "s7d_kv", [128, 2, NCORES, 2 * D], BF16)
            tmpb = b.sb(ses, "s7d_t", [128, 2, D], BF16)
            tmpf = b.sb(ses, "s7d_tf", [128, 2, D], F32)
            acc = b.sb(ses, "s7d_a", [128, 2, D], F32)
            sct = b.sb(ses, "s7d_s", [128, 2, 128], F32)
            pexp = b.sb(ses, "s7d_p", [128, 2, 128], F32)
            m16 = b.sb(ses, "s7d_m", [128, 2, 16], F32)
            r16 = b.sb(ses, "s7d_r", [128, 2, 16], F32)
            hn = b.sb(ses, "s7d_h", [128, 2, D], F32)
            waves = []
            for sck in range(KT):
                p = sck % 2
                ops = []
                ops.append(("sy", (lambda sck=sck, p=p: lambda s: s.dma_start(
                    out=qTt[:, p, :], in_=d_qT[sck * 128:(sck + 1) * 128, :]))(), True))
                for l in range(NCORES):
                    ops.append(("sy", (lambda sck=sck, l=l, p=p: lambda s: s.dma_start(
                        out=kvl[:, p, l, :],
                        in_=d_kv_all[l * T + sck * 128: l * T + (sck + 1) * 128, :]))(), True))
                for l in range(NCORES):
                    ops.append(("ve", (lambda l=l, p=p: lambda ve: ve.tensor_tensor(
                        tmpb[:, p, :], qTt[:, p, :], kvl[:, p, l, 0:D], ALU.mult))(), False))
                    ops.append(("ve", (lambda l=l, p=p: lambda ve: ve.tensor_reduce(
                        sct[:, p, :].rearrange("p (h l) -> p h l", l=NCORES)[:, :, l:l + 1],
                        tmpb[:, p, :].rearrange("p (h d) -> p h d", d=HD),
                        AX.X, ALU.add))(), False))
                ops.append(("ve", (lambda p=p: lambda ve: ve.tensor_reduce(
                    m16[:, p, :].rearrange("p (h u) -> p h u", u=1),
                    sct[:, p, :].rearrange("p (h l) -> p h l", l=NCORES), AX.X, ALU.max))(), False))
                ops.append(("ve", (lambda p=p: lambda ve: ve.tensor_tensor(
                    sct[:, p, :].rearrange("p (h l) -> p h l", l=NCORES),
                    sct[:, p, :].rearrange("p (h l) -> p h l", l=NCORES),
                    m16[:, p, :].rearrange("p (h u) -> p h u", u=1).broadcast_to((128, 16, NCORES)),
                    ALU.subtract))(), False))
                ops.append(("sc", (lambda p=p: lambda sc: sc.activation(
                    pexp[:, p, :], sct[:, p, :], AF.Exp, scale=scol[:, C_ATT:C_ATT + 1]))(), False))
                ops.append(("ve", (lambda p=p: lambda ve: ve.tensor_reduce(
                    m16[:, p, :].rearrange("p (h u) -> p h u", u=1),
                    pexp[:, p, :].rearrange("p (h l) -> p h l", l=NCORES), AX.X, ALU.add))(), False))
                ops.append(("ve", (lambda p=p: lambda ve: ve.reciprocal(r16[:, p, :], m16[:, p, :]))(), False))
                ops.append(("ve", (lambda p=p: lambda ve: ve.tensor_scalar(
                    r16[:, p, :], r16[:, p, :], scol[:, C_SX:C_SX + 1], None, ALU.mult))(), False))
                ops.append(("ve", (lambda p=p: lambda ve: ve.memset(acc[:, p, :], 0.0))(), False))
                for l in range(NCORES):
                    ops.append(("ve", (lambda l=l, p=p: lambda ve: ve.tensor_tensor(
                        tmpf[:, p, :].rearrange("p (h d) -> p h d", d=HD),
                        kvl[:, p, l, D:2 * D].rearrange("p (h d) -> p h d", d=HD),
                        pexp[:, p, :].rearrange("p (h l) -> p h l", l=NCORES)[:, :, l:l + 1]
                        .broadcast_to((128, 16, HD)),
                        ALU.mult))(), False))
                    ops.append(("ve", (lambda p=p: lambda ve: ve.tensor_tensor(
                        acc[:, p, :], acc[:, p, :], tmpf[:, p, :], ALU.add))(), False))
                ops.append(("ve", (lambda p=p: lambda ve: ve.tensor_tensor(
                    hn[:, p, :].rearrange("p (h d) -> p h d", d=HD),
                    acc[:, p, :].rearrange("p (h d) -> p h d", d=HD),
                    r16[:, p, :].rearrange("p (h u) -> p h u", u=1).broadcast_to((128, 16, HD)),
                    ALU.mult))(), False))
                ops.append(("ve", (lambda sck=sck, p=p: lambda ve: ve.reduce_max(
                    nrm[:, sck:sck + 1], hn[:, p, :], AX.X, apply_absolute_value=True))(), False))
                ops.append(("sy", (lambda sck=sck, p=p: lambda s: s.dma_start(
                    out=d_heads[sck * 128:(sck + 1) * 128, :], in_=hn[:, p, :]))(), True))
                waves.append(ops)
            b.seqw("s7d", waves)

        if C7 < 7:
            return
        with ExitStack() as ses:
            tpp = b.ps(ses, "s7g3_tp", [128, 128])
            ops = []
            ops.append(("ve", lambda ve: ve.reduce_max(nrm[:, 120:121], nrm[:, 0:KT], AX.X), False))
            ops.append(("te", lambda te: te.transpose(tpp[0:1, 0:128], nrm[:, 120:121], identF[:, :]), False))
            ops.append(("ve", lambda ve: ve.reduce_max(g7[0:1, 0:1], tpp[0:1, 0:128], AX.X), False))
            ops.append(("gp", lambda gp: gp.dma_start(
                out=ag_in[1].ap()[0:1, 0:1], in_=g7[0:1, 0:1],
                allow_slow_non_contiguous=True), True))
            ops.append(("gp", lambda gp: gp.collective_compute(
                "AllGather", ALU.bypass, replica_groups=RG,
                ins=[ag_in[1].ap().opt()], outs=[ag_out[1].ap().opt()]), "cc"))
            ops.append(("gp", lambda gp: gp.dma_start(out=agt[:, :], in_=ag_out[1].ap()), True))
            ops.append(("te", lambda te: te.transpose(tpp[0:16, 0:NCORES], agt[:, :], identF[0:NCORES, 0:NCORES]), False))
            ops.append(("ve", lambda ve: ve.reduce_max(g7[0:1, 1:2], tpp[0:1, 0:NCORES], AX.X), False))
            ops.append(("ve", lambda ve: ve.tensor_scalar(
                sc_row[0:1, C_SH:C_SH + 1], g7[0:1, 1:2], 1.0 / 127.0, 1e-8,
                ALU.mult, ALU.max), False))
            ops.append(("ve", lambda ve: ve.reciprocal(
                sc_row[0:1, C_FH:C_FH + 1], sc_row[0:1, C_SH:C_SH + 1]), False))
            ops.append(("ve", lambda ve: ve.tensor_tensor(
                sc_row[0:1, C_SHWO:C_SHWO + 1], sc_row[0:1, C_SH:C_SH + 1],
                sc_row[0:1, C_SWO:C_SWO + 1], ALU.mult), False))
            bcast_scol(ops, tpp)
            b.seq("s7ag3", ops)

        if C7 < 8:
            return
        with ExitStack() as ses:
            ht = b.sb(ses, "s7e_h", [128, 2, D], F32)
            hi = b.sb(ses, "s7e_i", [128, 2, D], I32)
            hb = b.sb(ses, "s7e_b", [128, 2, D], BF16)
            tpe = [b.ps(ses, f"s7e_tp{j}", [128, FC, 128], BF16) for j in range(2)]
            waves = []
            for kt in range(KT):
                p = kt % 2
                ops = []
                ops.append(("sy", (lambda kt=kt, p=p: lambda s: s.dma_start(
                    out=ht[:, p, :], in_=d_heads[kt * 128:(kt + 1) * 128, :]))(), True))
                ops.append(("sc", (lambda p=p: lambda sc: sc.activation(
                    hi[:, p, :], ht[:, p, :], AF.Identity, scale=scol[:, C_FH:C_FH + 1]))(), False))
                ops.append(("ve", (lambda p=p: lambda ve: ve.tensor_copy(hb[:, p, :], hi[:, p, :]))(), False))
                def trb(te, p=p):
                    inst = None
                    for fc in range(FC):
                        inst = te.transpose(tpe[p][:, fc, :],
                                            hb[:, p, fc * 128:(fc + 1) * 128], identB[:, :])
                    return inst
                ops.append(("te", trb, False))
                ops.append(("ve", (lambda kt=kt, p=p: lambda ve: ve.tensor_copy(
                    hqF[:, :, kt * 128:(kt + 1) * 128], tpe[p][:, :, :]))(), False))
                waves.append(ops)
            b.seqw("s7e", waves)

        with ExitStack() as ses:
            wb7 = b.sb(ses, "s7f_w", [128, 2, FC, 128], BF16)
            yst = b.sb(ses, "s7f_st", [128, 2, 512], BF16)

            def wl(gp, oi):
                src = d_woq.ap().rearrange("(kc p) o -> p kc o", p=128)[:, :, oi * 128:(oi + 1) * 128]
                gp.dma_start(out=wb7[:, oi % 2, :, :], in_=src).then_inc(SEM["GS"], 16)

            def f1(sc, ps, i, oi, tc):
                return sc.activation(yst[:, i % 2, :], ps[:, :], AF.Identity,
                                     scale=scol[:, C_SHWO:C_SHWO + 1], bias=outb_s[:, oi:oi + 1])

            def f2(ve, ps, i, oi, tc):
                return ve.reduce_max(nrm[:, i:i + 1], yst[:, i % 2, :], AX.X,
                                     apply_absolute_value=True)

            b.proj("s7f", FC, TC, FC, [128, 512],
                   lambda kc, oi, tc: wb7[:, oi % 2, kc, :],
                   lambda kc, oi, tc: hqF[:, kc, tc * 512:(tc + 1) * 512],
                   steps=[("sc", f1), ("ve", f2)], wload=(wl, 1), nps=2)

        with ExitStack() as ses:
            tpp = b.ps(ses, "s7g4_tp", [128, 128])
            ops = []
            ops.append(("ve", lambda ve: ve.reduce_max(nrm[:, 120:121], nrm[:, 0:FC * TC], AX.X), False))
            ops.append(("te", lambda te: te.transpose(tpp[0:1, 0:128], nrm[:, 120:121], identF[:, :]), False))
            ops.append(("ve", lambda ve: ve.reduce_max(g7[0:1, 0:1], tpp[0:1, 0:128], AX.X), False))
            ops.append(("gp", lambda gp: gp.dma_start(
                out=ag_in[2].ap()[0:1, 0:1], in_=g7[0:1, 0:1],
                allow_slow_non_contiguous=True), True))
            ops.append(("gp", lambda gp: gp.collective_compute(
                "AllGather", ALU.bypass, replica_groups=RG,
                ins=[ag_in[2].ap().opt()], outs=[ag_out[2].ap().opt()]), "cc"))
            ops.append(("gp", lambda gp: gp.dma_start(out=agt[:, :], in_=ag_out[2].ap()), True))
            ops.append(("te", lambda te: te.transpose(tpp[0:16, 0:NCORES], agt[:, :], identF[0:NCORES, 0:NCORES]), False))
            ops.append(("ve", lambda ve: ve.reduce_max(g7[0:1, 1:2], tpp[0:1, 0:NCORES], AX.X), False))
            ops.append(("ve", lambda ve: ve.tensor_scalar(
                sc_row[0:1, C_SY:C_SY + 1], g7[0:1, 1:2], 1.0 / 127.0, 1e-8,
                ALU.mult, ALU.max), False))
            ops.append(("ve", lambda ve: ve.reciprocal(
                sc_row[0:1, C_RSY:C_RSY + 1], sc_row[0:1, C_SY:C_SY + 1]), False))
            ops.append(("ve", lambda ve: ve.tensor_tensor(
                sc_row[0:1, C_FY:C_FY + 1], sc_row[0:1, C_SHWO:C_SHWO + 1],
                sc_row[0:1, C_RSY:C_RSY + 1], ALU.mult), False))
            bcast_scol(ops, tpp)
            ops2 = [("ve", lambda ve: ve.tensor_scalar(
                outb2_s[:, :], outb_s[:, :], scol[:, C_RSY:C_RSY + 1], None, ALU.mult), False)]
            b.seq("s7ag4", ops + ops2)

        with ExitStack() as ses:
            wb7 = b.sb(ses, "s7g_w", [128, 2, FC, 128], BF16)
            i7a = b.sb(ses, "s7g_i", [128, 2, 512], I32)
            st7 = b.sb(ses, "s7g_st", [128, 2, 512], BF16)
            tT7 = b.sb(ses, "s7g_tt", [128, 2, 512], BF16)
            pst7 = [b.ps(ses, f"s7g_tp{j}", [128, 512], BF16) for j in range(2)]

            def wl(gp, oi):
                src = d_woq.ap().rearrange("(kc p) o -> p kc o", p=128)[:, :, oi * 128:(oi + 1) * 128]
                gp.dma_start(out=wb7[:, oi % 2, :, :], in_=src).then_inc(SEM["GS"], 16)

            def g1(sc, ps, i, oi, tc):
                return sc.activation(i7a[:, i % 2, :], ps[:, :], AF.Identity,
                                     scale=scol[:, C_FY:C_FY + 1], bias=outb2_s[:, oi:oi + 1])

            def g2(ve, ps, i, oi, tc):
                return ve.tensor_copy(st7[:, i % 2, :], i7a[:, i % 2, :])

            def g3(te, ps, i, oi, tc):
                inst = None
                for qs in range(4):
                    inst = te.transpose(pst7[i % 2][:, qs * 128:(qs + 1) * 128],
                                        st7[:, i % 2, qs * 128:(qs + 1) * 128], identB[:, :])
                return inst

            def g4(ve, ps, i, oi, tc):
                return ve.tensor_copy(tT7[:, i % 2, :], pst7[i % 2][:, :])

            def g5(s, ps, i, oi, tc):
                dst = d_yqT.ap().rearrange("(tc qs p) c -> p tc qs c", p=128, qs=4)[
                    :, tc, :, oi * 128:(oi + 1) * 128]
                return s.dma_start(out=dst, in_=tT7[:, i % 2, :].rearrange("p (qs c) -> p qs c", qs=4))

            b.proj("s7g", FC, TC, FC, [128, 512],
                   lambda kc, oi, tc: wb7[:, oi % 2, kc, :],
                   lambda kc, oi, tc: hqF[:, kc, tc * 512:(tc + 1) * 512],
                   steps=[("sc", g1), ("ve", g2), ("te", g3), ("ve", g4), ("sy", g5)],
                   wload=(wl, 1), nps=2)


def build_s8(b, nc, L):
    (T, D, F, H, E, HD, FC, HC, TC, KT, NROT) = (
        L["T"], L["D"], L["F"], L["H"], L["E"], L["HD"], L["FC"], L["HC"],
        L["TC"], L["KT"], L["NROT"])
    HE, OC = H * E, H * E * HD // 128
    SEM = b.sem
    A1, scol, nrm, cbias = L["A1"], L["scol"], L["nrm"], L["cbias"]
    identB, identB2, identF, onesT = L["identB"], L["identB2"], L["identF"], L["onesT"]
    ropePb, negB1 = L["ropePb"], L["negB1"]
    b1L_s, b2L_s = L["b1L_s"], L["b2L_s"]
    wq2, wk2, wv4, wo2, wselv, wselo = (L["wq2"], L["wk2"], L["wv4"], L["wo2"],
                                        L["wselv"], L["wselo"])
    ln1g, ln1b, ln2g, ln2b = L["ln1g"], L["ln1b"], L["ln2g"], L["ln2b"]
    w1L, w2L = L["w1L"], L["w2L"]
    d_g1, d_srcB = L["d_g1"], L["d_srcB"]
    norm_stage, attn_stage = L["norm_stage"], L["attn_stage"]
    half = NROT // 2

    with ExitStack() as r8:
        ctx1 = b.sb(r8, "ctx1", [128, FC, T], BF16)
        gvT = b.sb(r8, "gvT", [128, KT, 128], BF16)
        goF = b.sb(r8, "goF", [128, T], BF16)

        with ExitStack() as ra:
            CC_t = b.sb(ra, "CC_t", [128, T], BF16)
            SS_t = b.sb(ra, "SS_t", [128, T], BF16)

            with ExitStack() as ses:
                ang = b.sb(ses, "s8a_a", [128, T], F32)
                angr = b.sb(ses, "s8a_ar", [128, T], F32)
                sinv = b.sb(ses, "s8a_sv", [128, T], F32)
                kint = b.sb(ses, "s8a_k", [128, T], I32)
                jv = b.sb(ses, "s8a_j", [128, 6], F32)
                tid = ang
                ops = []
                ops.append(("gp", lambda gp: gp.iota(
                    tid[:, :], [[1, T]], base=0, channel_multiplier=0,
                    allow_small_or_imprecise_dtypes=True), False))
                ops.append(("gp", lambda gp: gp.iota(
                    jv[:, 0:1], [[0, 1]], base=0, channel_multiplier=1,
                    allow_small_or_imprecise_dtypes=True), False))
                def jb(ve):
                    ve.tensor_scalar(jv[64:96, 0:1], jv[64:96, 0:1], 64.0, None, ALU.subtract)
                    ve.drain()
                    ve.tensor_scalar(jv[:, 1:2], jv[:, 0:1], 16.0, -16.0, ALU.is_ge, ALU.mult)
                    ve.drain()
                    return ve.tensor_tensor(jv[:, 0:1], jv[:, 0:1], jv[:, 1:2], ALU.add)
                ops.append(("ve", jb, False))
                ops.append(("gp", lambda gp: gp.iota(
                    jv[:, 4:5], [[0, 1]], base=0, channel_multiplier=1,
                    allow_small_or_imprecise_dtypes=True), False))
                def sgn_build(ve):
                    ve.tensor_scalar(jv[:, 5:6], jv[:, 4:5], 64.0, -64.0, ALU.is_ge, ALU.mult)
                    ve.drain()
                    ve.tensor_tensor(jv[:, 4:5], jv[:, 4:5], jv[:, 5:6], ALU.add)
                    ve.drain()
                    ve.tensor_scalar(jv[:, 2:3], jv[:, 4:5], 16.0, None, ALU.is_lt)
                    ve.drain()
                    ve.tensor_scalar(jv[:, 3:4], jv[:, 4:5], 32.0, None, ALU.is_lt)
                    ve.drain()
                    return ve.scalar_tensor_tensor(jv[:, 4:5], jv[:, 2:3], -2.0,
                                                   jv[:, 3:4], ALU.mult, ALU.add)
                ops.append(("ve", sgn_build, False))
                ops.append(("sc", lambda sc: sc.activation(
                    jv[:, 1:2], jv[:, 0:1], AF.Exp, scale=-LN1E4 / half), False))
                ops.append(("ve", lambda ve: ve.tensor_tensor(
                    ang[:, :], tid[:, :], jv[:, 1:2].broadcast_to((128, T)), ALU.mult), False))
                ops.append(("ve", lambda ve: ve.tensor_scalar(
                    kint[:, :], ang[:, :], 1.0 / TWO_PI, None, ALU.mult), False))
                ops.append(("ve", lambda ve: ve.scalar_tensor_tensor(
                    angr[:, :], kint[:, :], -TWO_PI, ang[:, :], ALU.mult, ALU.add), False))
                ops.append(("sc", lambda sc: sc.activation(sinv[:, :], angr[:, :], AF.Sin), False))
                ops.append(("ve", lambda ve: ve.tensor_tensor(
                    SS_t[:, :], sinv[:, :], jv[:, 4:5].broadcast_to((128, T)), ALU.mult), False))
                ops.append(("ve", lambda ve: ve.tensor_scalar(
                    ang[:, :], ang[:, :], TWO_PI / 4.0, None, ALU.add), False))
                ops.append(("ve", lambda ve: ve.tensor_scalar(
                    kint[:, :], ang[:, :], 1.0 / TWO_PI, None, ALU.mult), False))
                ops.append(("ve", lambda ve: ve.scalar_tensor_tensor(
                    angr[:, :], kint[:, :], -TWO_PI, ang[:, :], ALU.mult, ALU.add), False))
                ops.append(("sc", lambda sc: sc.activation(sinv[:, :], angr[:, :], AF.Sin), False))
                ops.append(("ve", lambda ve: ve.tensor_tensor(
                    sinv[:, :], sinv[:, :], jv[:, 3:4].broadcast_to((128, T)), ALU.mult), False))
                ops.append(("ve", lambda ve: ve.tensor_scalar(
                    jv[:, 5:6], jv[:, 3:4], -1.0, 1.0, ALU.mult, ALU.add), False))
                ops.append(("ve", lambda ve: ve.tensor_tensor(
                    CC_t[:, :], sinv[:, :], jv[:, 5:6].broadcast_to((128, T)), ALU.add), False))
                b.seq("s8a", ops)

            qmF = b.sb(ra, "qmF", [128, FC, T], BF16)
            kmF = b.sb(ra, "kmF", [128, FC, T], BF16)
            VT1 = b.sb(ra, "VT1", [128, KT, H, 66], BF16)

            for (wsrc, dstF, nm) in [(wq2, qmF, "s8bq"), (wk2, kmF, "s8bk")]:
                with ExitStack() as ses:
                    wb8 = b.sb(ses, f"{nm}_w", [128, 2, FC, 128], BF16)

                    def wl(gp, oi, wsrc=wsrc, wb8=wb8):
                        src = wsrc.ap().rearrange("(kc p) o -> p kc o", p=128)[:, :, oi * 128:(oi + 1) * 128]
                        gp.dma_start(out=wb8[:, oi % 2, :, :], in_=src).then_inc(SEM["GS"], 16)

                    def e1(sc, ps, i, oi, tc, dstF=dstF):
                        return sc.activation(dstF[:, oi, tc * 512:(tc + 1) * 512],
                                             ps[:, :], AF.Identity)

                    b.proj(nm, FC, TC, FC, [128, 512],
                           (lambda wb8=wb8: lambda kc, oi, tc: wb8[:, oi % 2, kc, :])(),
                           lambda kc, oi, tc: A1[:, kc, tc * 512:(tc + 1) * 512],
                           steps=[("sc", e1)], wload=(wl, 1), nps=2)
                with ExitStack() as ses:
                    u8 = b.sb(ses, f"{nm}_u", [128, 512], BF16)
                    t8_ = b.sb(ses, f"{nm}_t2", [128, 512], BF16)
                    psr = b.ps(ses, f"{nm}_ps", [128, 512])
                    for fc in range(FC):
                        ops = []
                        for tc in range(TC):
                            ops.append(("te", (lambda fc=fc, tc=tc, dstF=dstF: lambda te: te.matmul(
                                psr[:, :], ropePb[:, :], dstF[:, fc, tc * 512:(tc + 1) * 512],
                                start=True, stop=True))(), False))
                            ops.append(("ve", (lambda tc=tc: lambda ve: ve.tensor_tensor(
                                u8[:, :], psr[:, :], SS_t[:, tc * 512:(tc + 1) * 512],
                                ALU.mult))(), False))
                            ops.append(("ve", (lambda fc=fc, tc=tc, dstF=dstF: lambda ve: ve.tensor_tensor(
                                t8_[:, :], dstF[:, fc, tc * 512:(tc + 1) * 512],
                                CC_t[:, tc * 512:(tc + 1) * 512], ALU.mult))(), False))
                            ops.append(("ve", (lambda fc=fc, tc=tc, dstF=dstF: lambda ve: ve.tensor_tensor(
                                dstF[:, fc, tc * 512:(tc + 1) * 512], t8_[:, :], u8[:, :],
                                ALU.add))(), False))
                        b.seq(f"{nm}_rope{fc}", ops)

            norm_stage("s8n", qmF, kmF, negB1, 0.125)

            with ExitStack() as ses:
                wsel_s = b.sb(ses, "s8c_w", [128, 2, FC, 128], BF16)
                gt = b.sb(ses, "s8c_g", [128, 128], F32)
                eqt = b.sb(ses, "s8c_e", [128, 128], F32)
                m1g = b.sb(ses, "s8c_m", [128, 16], F32)
                psg = b.ps(ses, "s8c_ps", [128, 128])
                pst = b.ps(ses, "s8c_pt", [128, 128])
                with nc.Block() as blk:
                    @blk.gpsimd
                    def _(gp):
                        gp.dma_start(out=wsel_s[:, 0, :, :],
                                     in_=wselv.ap().rearrange("(kc p) o -> p kc o", p=128)).then_inc(SEM["GS"], 16)
                        gp.dma_start(out=wsel_s[:, 1, :, :],
                                     in_=wselo.ap().rearrange("(kc p) o -> p kc o", p=128)).then_inc(SEM["GS"], 16)
                        b.c["GS"] += 32
                for gidx in range(2):
                    for kt in range(KT):
                        ops = []
                        def mm(te, gidx=gidx, kt=kt):
                            inst = None
                            for dc in range(FC):
                                inst = te.matmul(psg[:, :], A1[:, dc, kt * 128:(kt + 1) * 128],
                                                 wsel_s[:, gidx, dc, :],
                                                 start=(dc == 0), stop=(dc == FC - 1))
                            return inst
                        ops.append(("te", mm, False))
                        ops.append(("sc", lambda sc: sc.activation(gt[:, :], psg[:, :], AF.Sigmoid), False))
                        gv_ = gt[:, :].rearrange("p (h e) -> p h e", e=E)
                        m1v = m1g[:, :].rearrange("p (h u) -> p h u", u=1)
                        ops.append(("ve", lambda ve: ve.tensor_reduce(m1v, gv_, AX.X, ALU.max), False))
                        ops.append(("ve", lambda ve: ve.tensor_tensor(
                            eqt[:, :].rearrange("p (h e) -> p h e", e=E), gv_,
                            m1v.broadcast_to((128, H, E)), ALU.is_equal), False))
                        ops.append(("ve", lambda ve: ve.scalar_tensor_tensor(
                            eqt[:, :], eqt[:, :], -1e30, gt[:, :], ALU.mult, ALU.add), False))
                        ops.append(("ve", lambda ve: ve.tensor_reduce(
                            m1v, eqt[:, :].rearrange("p (h e) -> p h e", e=E), AX.X, ALU.max), False))
                        ops.append(("ve", lambda ve: ve.tensor_tensor(
                            eqt[:, :].rearrange("p (h e) -> p h e", e=E), gv_,
                            m1v.broadcast_to((128, H, E)), ALU.is_ge), False))
                        if gidx == 0:
                            ops.append(("ve", (lambda kt=kt: lambda ve: ve.tensor_tensor(
                                gvT[:, kt, :], gt[:, :], eqt[:, :], ALU.mult))(), False))
                        else:
                            ops.append(("ve", lambda ve: ve.tensor_tensor(
                                gt[:, :], gt[:, :], eqt[:, :], ALU.mult), False))
                            ops.append(("te", lambda te: te.transpose(
                                pst[:, :], gt[:, :], identF[:, :]), False))
                            ops.append(("ve", (lambda kt=kt: lambda ve: ve.tensor_copy(
                                goF[:, kt * 128:(kt + 1) * 128], pst[:, :]))(), False))
                        b.seq(f"s8c_{gidx}_{kt}", ops)

            with ExitStack() as ses:
                NG = 2
                wvs = b.sb(ses, "s8d_w", [128, NG, FC, E * HD], BF16)
                vtmp = b.sb(ses, "s8d_t", [128, 2, E * HD], F32)
                psv = [b.ps(ses, f"s8d_p{j}", [128, E * HD]) for j in range(2 * NG)]
                ops0 = [("gp", lambda gp: gp.memset(VT1[:, :, :, 64:65], 1.0), False)]
                b.seq("s8d_ones", ops0)
                for hg in range(H // NG):
                    with nc.Block() as blk:
                        @blk.gpsimd
                        def _(gp, hg=hg):
                            for j in range(NG):
                                src = wv4.ap()[hg * NG + j, :, :].rearrange(
                                    "(kc p) o -> p kc o", p=128)
                                gp.dma_start(out=wvs[:, j, :, :], in_=src).then_inc(SEM["GS"], 16)
                                b.c["GS"] += 16
                    waves = []
                    for kt in range(KT):
                        p = kt % 2
                        ops = []
                        def mm(te, hg=hg, kt=kt, p=p):
                            inst = None
                            for dc in range(FC):
                                for j in range(NG):
                                    inst = te.matmul(psv[p * NG + j][:, :],
                                                     A1[:, dc, kt * 128:(kt + 1) * 128],
                                                     wvs[:, j, dc, :],
                                                     start=(dc == 0), stop=(dc == FC - 1))
                            return inst
                        ops.append(("te", mm, False))
                        for j in range(NG):
                            h = hg * NG + j
                            def gvm(ve, j=j, h=h, kt=kt, p=p):
                                return ve.tensor_tensor(
                                    vtmp[:, p, :].rearrange("p (pp e) -> p e pp", e=E),
                                    psv[p * NG + j][:, :].rearrange("p (e pp) -> p e pp", e=E),
                                    gvT[:, kt, h * E:(h + 1) * E].rearrange(
                                        "p (e u) -> p e u", u=1).broadcast_to((128, E, HD)),
                                    ALU.mult)
                            ops.append(("ve", gvm, False))
                            def rd(ve, j=j, h=h, kt=kt, p=p):
                                with nc.allow_low_precision(reason="bf16 store of f32-accumulated expert sum"):
                                    return ve.tensor_reduce(
                                        VT1[:, kt, h, 0:64].rearrange("p (d u) -> p d u", u=1),
                                        vtmp[:, p, :].rearrange("p (d e) -> p d e", e=E),
                                        AX.X, ALU.add)
                            ops.append(("ve", rd, False))
                        waves.append(ops)
                    b.seqw(f"s8d_{hg}", waves)

            attn_stage("s8e", qmF, kmF, VT1, negB1, None, ctx1, causal=False)

        with ExitStack() as rf:
            SELh = b.sb(rf, "SELh", [128, OC * 128], BF16)
            with nc.Block() as blk:
                @blk.gpsimd
                def _(gp):
                    gp.memset(SELh[:, :], 1.0)
                    gp.drain()
                    gp.affine_select(
                        out=SELh[:, :].rearrange("p (c h j) -> p c h j", h=2, j=64),
                        in_=SELh[:, :].rearrange("p (c h j) -> p c h j", h=2, j=64),
                        pattern=[[-2, OC], [-1, 2], [0, 64]],
                        compare_op=ALU.is_equal, fill=0.0, base=0, channel_multiplier=1)
                    gp.drain()
            with ExitStack() as ses:
                wo_s = b.sb(ses, "s8f_w", [128, 8, D], BF16)
                cR = b.sb(ses, "s8f_cr", [128, 2, 2, 512], BF16)
                ohb = b.sb(ses, "s8f_oh", [128, 2, 8, 512], BF16)
                ost = b.sb(ses, "s8f_ost", [128, 2, 4, 512], F32)
                pcr = [b.ps(ses, f"s8f_pc{j}", [128, 512]) for j in range(2)]
                pgr = [b.ps(ses, f"s8f_pg{j}", [128, 512]) for j in range(4)]
                pso = [b.ps(ses, f"s8f_po{j}", [128, 512]) for j in range(2)]
                for grp in range(OC // 8):
                    with nc.Block() as blk:
                        @blk.gpsimd
                        def _(gp, grp=grp):
                            src = wo2.ap()[grp * 1024:(grp + 1) * 1024, :].rearrange(
                                "(kc p) o -> p kc o", p=128)
                            gp.dma_start(out=wo_s[:, :, :], in_=src).then_inc(SEM["GS"], 16)
                            b.c["GS"] += 16
                    waves = []
                    for tc in range(TC):
                        pw = tc % 2
                        ops = []
                        for jj in range(2):
                            hh = 2 * grp + jj
                            def crm(te, hh=hh, tc=tc, pw=pw):
                                off = (hh % 2) * 64
                                return te.matmul(pcr[pw][:, :], identB2[off:off + 64, :],
                                                 ctx1[off:off + 64, hh // 2, tc * 512:(tc + 1) * 512],
                                                 start=True, stop=True)
                            ops.append(("te", crm, False))
                            def cre(ve, jj=jj, pw=pw):
                                return ve.tensor_copy(cR[:, pw, jj, :], pcr[pw][:, :])
                            ops.append(("ve", cre, False))
                        for c8 in range(8):
                            c = grp * 8 + c8
                            jj = (c // 4) % 2
                            def grm(te, c=c, tc=tc, c8=c8, pw=pw):
                                return te.matmul(pgr[pw * 2 + c8 % 2][:, :],
                                                 SELh[:, c * 128:(c + 1) * 128],
                                                 goF[:, tc * 512:(tc + 1) * 512],
                                                 start=True, stop=True)
                            ops.append(("te", grm, False))
                            def ohm(ve, c8=c8, jj=jj, pw=pw):
                                return ve.tensor_tensor(ohb[:, pw, c8, :], cR[:, pw, jj, :],
                                                        pgr[pw * 2 + c8 % 2][:, :], ALU.mult)
                            ops.append(("ve", ohm, False))
                        for ofc in range(FC):
                            def opm(te, ofc=ofc, tc=tc, pw=pw):
                                inst = None
                                for c8 in range(8):
                                    inst = te.matmul(pso[pw][:, :],
                                                     wo_s[:, c8, ofc * 128:(ofc + 1) * 128],
                                                     ohb[:, pw, c8, :],
                                                     start=(c8 == 0), stop=(c8 == 7))
                                return inst
                            ops.append(("te", opm, False))
                            if grp == 0:
                                def acc0(ve, ofc=ofc, tc=tc, pw=pw):
                                    return ve.tensor_tensor(
                                        ost[:, pw, ofc % 4, :], pso[pw][:, :],
                                        A1[:, ofc, tc * 512:(tc + 1) * 512], ALU.add)
                                ops.append(("ve", acc0, False))
                            else:
                                def accn(ve, ofc=ofc, pw=pw):
                                    return ve.tensor_copy(ost[:, pw, ofc % 4, :], pso[pw][:, :])
                                ops.append(("ve", accn, False))
                            def accd(gp, ofc=ofc, tc=tc, grp=grp, pw=pw):
                                return gp.dma_start(
                                    out=d_srcB[ofc * 128:(ofc + 1) * 128, tc * 512:(tc + 1) * 512],
                                    in_=ost[:, pw, ofc % 4, :],
                                    accum_op=(ALU.bypass if grp == 0 else ALU.add))
                            ops.append(("gp", accd, True))
                        waves.append(ops)
                    b.seqw(f"s8f_{grp}", waves)

        if True:
            with ExitStack() as rg:
                xlnF = b.sb(rg, "xlnF", [128, FC, T], BF16)
                gbx = b.sb(rg, "s8_gb1", [128, D], F32)
                bbx = b.sb(rg, "s8_bb1", [128, D], F32)
                ln_tmaj(b, nc, L, "s8g", d_srcB, xlnF, ln1g, ln1b, gbx, bbx,
                        silu_out=None, g1_dst=None)

                with ExitStack() as ses:
                    hst = b.sb(ses, "s8h_h", [128, HC, 512], BF16)
                    for tc in range(TC):
                        with ExitStack() as s2:
                            wb8 = b.sb(s2, "s8h_w1", [128, 2, FC, 128], BF16)

                            def wl1(gp, oi, wb8=wb8):
                                src = w1L.ap().rearrange("(kc p) o -> p kc o", p=128)[
                                    :, :, oi * 128:(oi + 1) * 128]
                                gp.dma_start(out=wb8[:, oi % 2, :, :], in_=src).then_inc(SEM["GS"], 16)

                            def h1(sc, ps, i, oi, tc_, hst=hst):
                                return sc.activation(hst[:, oi, :], ps[:, :], AF.Relu,
                                                     bias=b1L_s[:, oi:oi + 1])

                            b.proj(f"s8h1_{tc}", HC, 1, FC, [128, 512],
                                   (lambda wb8=wb8: lambda kc, oi, tc_: wb8[:, oi % 2, kc, :])(),
                                   (lambda tc=tc: lambda kc, oi, tc_: xlnF[:, kc, tc * 512:(tc + 1) * 512])(),
                                   steps=[("sc", h1)], wload=(wl1, 1), nps=2)
                        with ExitStack() as s2:
                            wb9 = b.sb(s2, "s8h_w2", [128, 2, HC, 128], BF16)

                            def wl2(gp, oi, wb9=wb9):
                                src = w2L.ap().rearrange("(kc p) o -> p kc o", p=128)[
                                    :, :, oi * 128:(oi + 1) * 128]
                                gp.dma_start(out=wb9[:, oi % 2, :, :], in_=src).then_inc(SEM["GS"], 16)

                            mst = b.sb(s2, "s8h_mst", [128, 2, 512], F32)

                            def h2(ve, ps, i, oi, tc_, mst=mst):
                                return ve.tensor_scalar(mst[:, i % 2, :], ps[:, :],
                                                        b2L_s[:, oi:oi + 1], None, ALU.add)

                            def h3(gp, ps, i, oi, tc_, mst=mst, tc=tc):
                                return gp.dma_start(
                                    out=d_srcB[oi * 128:(oi + 1) * 128, tc * 512:(tc + 1) * 512],
                                    in_=mst[:, i % 2, :], accum_op=ALU.add)

                            b.proj(f"s8h2_{tc}", FC, 1, HC, [128, 512],
                                   (lambda wb9=wb9: lambda kc, oi, tc_: wb9[:, oi % 2, kc, :])(),
                                   (lambda hst=hst: lambda kc, oi, tc_: hst[:, kc, :])(),
                                   steps=[("ve", h2), ("gp", h3)], wload=(wl2, 1), nps=2)

                ln_tmaj(b, nc, L, "s8i", d_srcB, None, ln2g, ln2b, gbx, bbx,
                        silu_out=True, g1_dst=d_g1)


LN1E4 = float(np.log(10000.0))


def ln_tmaj(b, nc, L, name, src_dram_F, dst_F, grow, brow, gbx, bbx, silu_out, g1_dst):
    (T, D, FC, KT) = (L["T"], L["D"], L["FC"], L["KT"])
    SEM = b.sem
    identF, identB, onesT, cbias = L["identF"], L["identB"], L["onesT"], L["cbias"]
    with ExitStack() as ses:
        xF = b.sb(ses, f"{name}_xF", [128, 2, FC, 128], F32)
        xT = b.sb(ses, f"{name}_xT", [128, 2, D], F32)
        xl = b.sb(ses, f"{name}_xl", [128, 2, D], BF16)
        bn6 = b.sb(ses, f"{name}_bn", [128, 2, 6 * (D // 512)], F32)
        mv = b.sb(ses, f"{name}_mv", [128, 2, 4], F32)
        tps = [b.ps(ses, f"{name}_tp{j}", [128, FC, 128], F32) for j in range(2)]
        tpsb = [b.ps(ses, f"{name}_tpb{j}", [128, FC, 128], BF16) for j in range(2)]
        tpb = b.ps(ses, f"{name}_tb", [128, 512])
        nst = D // 512
        base = b.snap()
        with nc.Block() as blk:
            @blk.sync
            def _(s):
                s.dma_start(out=gbx[0:1, :], in_=grow[:, :]).then_inc(SEM["LS"], 16)
                s.dma_start(out=bbx[0:1, :], in_=brow[:, :]).then_inc(SEM["LS"], 16)
                b.c["LS"] += 32
        ops = []
        for row in (gbx, bbx):
            for c in range(D // 512):
                ops.append(("te", (lambda row=row, c=c: lambda te: te.matmul(
                    tpb[:, :], onesT[0:1, :], row[0:1, c * 512:(c + 1) * 512],
                    start=True, stop=True))(), False))
                ops.append(("ve", (lambda row=row, c=c: lambda ve: ve.tensor_copy(
                    row[:, c * 512:(c + 1) * 512], tpb[:, :]))(), False))
        b.seq(f"{name}_gb", ops)
        waves = []
        for kt in range(KT):
            p = kt % 2
            ops = []
            ops.append(("sy", (lambda kt=kt, p=p: lambda s: s.dma_start(
                out=xF[:, p, :, :],
                in_=src_dram_F.ap().rearrange("(c p) t -> p c t", p=128)[
                    :, :, kt * 128:(kt + 1) * 128]))(), True))
            def trf(te, p=p):
                inst = None
                for fc in range(FC):
                    inst = te.transpose(tps[p][:, fc, :], xF[:, p, fc, :], identF[:, :])
                return inst
            ops.append(("te", trf, False))
            ops.append(("ve", (lambda p=p: lambda ve: ve.tensor_copy(
                xT[:, p, :].rearrange("q (c j) -> q c j", j=128), tps[p][:, :, :]))(), False))

            def bnf(ve, p=p):
                for j in range(nst):
                    ve.bn_stats(bn6[:, p, j * 6:(j + 1) * 6], xT[:, p, j * 512:(j + 1) * 512])
                    ve.drain()
                return ve.bn_aggr(mv[:, p, 0:2], bn6[:, p, 0:6 * nst])
            ops.append(("ve", bnf, False))
            ops.append(("sc", (lambda p=p: lambda sc: sc.activation(
                mv[:, p, 2:3], mv[:, p, 1:2], AF.Sqrt, bias=cbias[:, 0:1]))(), False))

            def appf(ve, p=p):
                ve.reciprocal(mv[:, p, 3:4], mv[:, p, 2:3])
                ve.drain()
                ve.tensor_scalar(xT[:, p, :], xT[:, p, :], mv[:, p, 0:1], mv[:, p, 3:4],
                                 ALU.subtract, ALU.mult)
                ve.drain()
                ve.tensor_tensor(xT[:, p, :], xT[:, p, :], gbx[:, :], ALU.mult)
                ve.drain()
                return ve.tensor_tensor(xT[:, p, :], xT[:, p, :], bbx[:, :], ALU.add)
            ops.append(("ve", appf, False))
            if silu_out:
                ops.append(("sc", (lambda p=p: lambda sc: sc.activation(
                    xl[:, p, :], xT[:, p, :], AF.Silu))(), False))
                ops.append(("sy", (lambda kt=kt, p=p: lambda s: s.dma_start(
                    out=g1_dst[kt * 128:(kt + 1) * 128, :], in_=xl[:, p, :]))(), True))
            else:
                ops.append(("ve", (lambda p=p: lambda ve: ve.tensor_copy(
                    xl[:, p, :], xT[:, p, :]))(), False))
                def trb(te, p=p):
                    inst = None
                    for fc in range(FC):
                        inst = te.transpose(tpsb[p][:, fc, :],
                                            xl[:, p, fc * 128:(fc + 1) * 128], identB[:, :])
                    return inst
                ops.append(("te", trb, False))
                ops.append(("ve", (lambda kt=kt, p=p: lambda ve: ve.tensor_copy(
                    dst_F[:, :, kt * 128:(kt + 1) * 128], tpsb[p][:, :, :]))(), False))
                def trf2(te, p=p):
                    inst = None
                    for fc in range(FC):
                        inst = te.transpose(tps[p][:, fc, :],
                                            xT[:, p, fc * 128:(fc + 1) * 128], identF[:, :])
                    return inst
                ops.append(("te", trf2, False))
                ops.append(("ve", (lambda p=p: lambda ve: ve.tensor_copy(
                    xF[:, p, :, :], tps[p][:, :, :]))(), False))
                ops.append(("sy", (lambda kt=kt, p=p: lambda s: s.dma_start(
                    out=src_dram_F.ap().rearrange("(c p) t -> p c t", p=128)[
                        :, :, kt * 128:(kt + 1) * 128], in_=xF[:, p, :, :]))(), True))
            waves.append(ops)
        b.seqw(name, waves)


def host_layout(inputs, core):
    g = lambda k: np.ascontiguousarray(np.asarray(inputs[k], dtype=np.float32))
    T, D, F, H, E, HD = 2048, 1024, 4096, 16, 8, 64
    m = {}
    m["x"] = np.ascontiguousarray(np.asarray(inputs["L0_attn_x"][core], dtype=np.float32))
    m["wqkv"] = g("L0_attn_Wqkv")
    m["bqkvf"] = np.ascontiguousarray(g("L0_attn_bqkv").reshape(24, 128).T)
    m["wout0"] = g("L0_attn_Wout")
    m["boutf"] = np.ascontiguousarray(g("L0_attn_bout").reshape(8, 128).T)
    m["w1t"] = np.ascontiguousarray(g("L0_ffn_w1").T)
    m["b1f0"] = np.ascontiguousarray(g("L0_ffn_b1").reshape(32, 128).T)
    m["w2t"] = np.ascontiguousarray(g("L0_ffn_w2").T)
    m["b2f0"] = np.ascontiguousarray(g("L0_ffn_b2").reshape(8, 128).T)
    m["wit"] = np.ascontiguousarray(g("L1_attn_in_w").T)
    m["inbf"] = np.ascontiguousarray(g("L1_attn_in_b").reshape(24, 128).T)
    m["wot"] = np.ascontiguousarray(g("L1_attn_out_w").T)
    m["outbf"] = np.ascontiguousarray(g("L1_attn_out_b").reshape(8, 128).T)
    m["wq2"] = np.ascontiguousarray(g("L1_ffn_Wq").transpose(1, 0, 2).reshape(D, D))
    m["wk2"] = np.ascontiguousarray(g("L1_ffn_Wk").transpose(1, 0, 2).reshape(D, D))
    m["wv4"] = np.ascontiguousarray(g("L1_ffn_Wv").transpose(0, 2, 1, 3).reshape(H, D, E * HD))
    m["wo2"] = np.ascontiguousarray(g("L1_ffn_Wo").reshape(H * E * HD, D))
    m["wselv"] = np.ascontiguousarray(g("L1_ffn_Wsel_v").transpose(2, 0, 1).reshape(D, H * E))
    m["wselo"] = np.ascontiguousarray(g("L1_ffn_Wsel_o").transpose(2, 0, 1).reshape(D, H * E))
    m["ln1g"] = g("L1_ffn_ln1_g").reshape(1, D)
    m["ln1b"] = g("L1_ffn_ln1_b").reshape(1, D)
    m["ln2g"] = g("L1_ffn_ln2_g").reshape(1, D)
    m["ln2b"] = g("L1_ffn_ln2_b").reshape(1, D)
    m["w1L"] = g("L1_ffn_W1")
    m["b1Lf"] = np.ascontiguousarray(g("L1_ffn_b1").reshape(32, 128).T)
    m["w2L"] = g("L1_ffn_W2")
    m["b2Lf"] = np.ascontiguousarray(g("L1_ffn_b2").reshape(8, 128).T)
    perm = np.arange(128)
    for bb in (0, 64):
        perm[bb:bb + 16] = np.arange(bb + 16, bb + 32)
        perm[bb + 16:bb + 32] = np.arange(bb, bb + 16)
    Psel = np.zeros((128, 128), np.float32)
    Psel[perm, np.arange(128)] = 1.0
    m["ropeP"] = Psel
    return m


def kernel(**inputs):
    P = default_params()
    nc = build(P)
    in_maps = [host_layout(inputs, c) for c in range(NCORES)]
    res = run_bass_kernel_spmd(nc, in_maps, core_ids=list(range(NCORES)))
    return np.stack([np.asarray(res.results[c]["out"], dtype=np.float32)
                     for c in range(NCORES)], axis=0)


# version 05
# speedup vs baseline: 1.5723x; 1.1781x over previous
import numpy as np
import concourse.bass as bass
import concourse.mybir as mybir
from concourse.bass_utils import run_bass_kernel_spmd
from contextlib import ExitStack

F32 = mybir.dt.float32
BF16 = mybir.dt.bfloat16
I32 = mybir.dt.int32
AF = mybir.ActivationFunctionType
ALU = mybir.AluOpType
AX = mybir.AxisListType

NCORES = 8
TWO_PI = 6.283185307179586

ENGN = {"te": "tensor", "ve": "vector", "sc": "scalar", "gp": "gpsimd", "sy": "sync"}
ENG_SEM = {"te": "MM", "ve": "EV", "sc": "SC", "gp": "GS", "sy": "LS"}


def default_params():
    return dict(T=2048, D=1024, HD=64, F=4096, E=8, NROT=32, dbg=(), cut=99)


class Bld:
    def __init__(self, nc, es):
        self.nc = nc
        self.es = es
        self.sem = {}
        for s in ["LS", "GS", "AS", "MM", "EV", "SC", "CC", "X1", "X2", "X3", "X4", "X5"]:
            self.sem[s] = es.enter_context(nc.semaphore(s))
        self.c = {s: 0 for s in self.sem}

    _uid = [0]

    def _nm(self, name):
        self._uid[0] += 1
        return f"{name}_u{self._uid[0]}"

    def sb(self, es_, name, shape, dt=BF16):
        return es_.enter_context(self.nc.sbuf_tensor(self._nm(name), shape, dt))

    def ps(self, es_, name, shape, dt=F32):
        return es_.enter_context(self.nc.psum_tensor(self._nm(name), shape, dt))

    def snap(self):
        return dict(self.c)

    def seq(self, name, ops):
        nc = self.nc
        base = self.snap()
        prior, counts, incs = [], {k: 0 for k in ENG_SEM}, []
        for (ek, fn, is_dma) in ops:
            prior.append(dict(counts))
            inc = 16 if is_dma is True else 1
            counts[ek] += inc
            incs.append(inc)
        with self.nc.named_scope(f"seq_{name}"), nc.Block() as blk:
            for ek in ["te", "ve", "sc", "gp", "sy"]:
                if not any(e == ek for (e, _, _) in ops):
                    continue

                def mk(ek):
                    def closure(eng):
                        eng.wait_ge(self.sem["LS"], base["LS"])
                        eng.wait_ge(self.sem["GS"], base["GS"])
                        for i, (e, fn, is_dma) in enumerate(ops):
                            if e != ek:
                                continue
                            for oe, osem in ENG_SEM.items():
                                if oe != ek and prior[i][oe] > 0:
                                    eng.wait_ge(self.sem[osem], base[osem] + prior[i][oe])
                            if prior[i][ek] > 0:
                                eng.wait_ge(self.sem[ENG_SEM[ek]], base[ENG_SEM[ek]] + prior[i][ek])
                            inst = fn(eng)
                            assert inst is not None, f"seq op {i} in {name} returned None"
                            inst.then_inc(self.sem[ENG_SEM[ek]], incs[i])
                            if ek in ("ve", "sc", "gp") and not is_dma:
                                eng.drain()
                    return closure

                getattr(blk, ENGN[ek])(mk(ek))
        for ek, n in counts.items():
            self.c[ENG_SEM[ek]] += n

    def seqw(self, name, waves):
        nc = self.nc
        base = self.snap()
        ops = []
        for w, wl in enumerate(waves):
            for (ek, fn, is_dma) in wl:
                ops.append((w, ek, fn, is_dma))
        wave_tot = []
        cur = {k: 0 for k in ENG_SEM}
        prior = []
        incs = []
        wstart = []
        lastw = -1
        for (w, ek, fn, is_dma) in ops:
            if w != lastw:
                for _ in range(w - lastw):
                    wave_tot.append(dict(cur))
                lastw = w
            prior.append(dict(cur))
            inc = 16 if is_dma is True else 1
            cur[ek] += inc
            incs.append(inc)
        while len(wave_tot) <= len(waves):
            wave_tot.append(dict(cur))
        with self.nc.named_scope(f"sqw_{name}"), nc.Block() as blk:
            for ek in ["te", "ve", "sc", "gp", "sy"]:
                if not any(e == ek for (_, e, _, _) in ops):
                    continue

                def mk(ek):
                    def closure(eng):
                        eng.wait_ge(self.sem["LS"], base["LS"])
                        eng.wait_ge(self.sem["GS"], base["GS"])
                        for i, (w, e, fn, is_dma) in enumerate(ops):
                            if e != ek:
                                continue
                            floor = wave_tot[max(w - 1, 0)]
                            for oe, osem in ENG_SEM.items():
                                sw = prior[i][oe] if prior[i][oe] > wave_tot[w][oe] else 0
                                tgt = max(floor[oe], sw)
                                if tgt > 0:
                                    eng.wait_ge(self.sem[osem], base[osem] + tgt)
                            inst = fn(eng)
                            assert inst is not None, f"seqw op {i} in {name} None"
                            inst.then_inc(self.sem[ENG_SEM[ek]], incs[i])
                            if ek in ("ve", "sc", "gp") and not is_dma:
                                eng.drain()
                    return closure

                getattr(blk, ENGN[ek])(mk(ek))
        for ek, n in cur.items():
            self.c[ENG_SEM[ek]] += n

    def proj(self, name, n_oi, n_tc, n_kc, psum_shape, stat_fn, mov_fn, steps,
             wload=None, nps=2):
        nc = self.nc
        base = self.snap()
        ntiles = n_oi * n_tc
        nsteps = len(steps)
        step_inc = [16 if ek in ("sy", "gp") else 1 for (ek, _) in steps]
        bsem = ["MM"] + [f"X{j}" for j in range(1, nsteps)]
        binc = [1] + step_inc[:-1]
        wfn, wn = wload if wload else (None, 0)
        with ExitStack() as pes:
            psl = [pes.enter_context(nc.psum_tensor(f"{name}_ps{j}", list(psum_shape), F32))
                   for j in range(nps)]
            def emit_step(eng, j, i, oi, tc):
                sek, sfn = steps[j]
                eng.wait_ge(self.sem[bsem[j]], base[bsem[j]] + (i + 1) * binc[j])
                inst = sfn(eng, psl[i % nps], i, oi, tc)
                osem = "EV" if j == nsteps - 1 else bsem[j + 1]
                inst.then_inc(self.sem[osem], step_inc[j])
                if sek in ("ve", "sc"):
                    eng.drain()

            with self.nc.named_scope(f"proj_{name}"), nc.Block() as blk:
                @blk.tensor
                def _(te):
                    te.wait_ge(self.sem["LS"], base["LS"])
                    te.wait_ge(self.sem["GS"], base["GS"])
                    for i in range(ntiles):
                        oi, tc = divmod(i, n_tc)
                        if wfn and tc == 0:
                            te.wait_ge(self.sem["GS"], base["GS"] + 16 * wn * (oi + 1))
                        if i >= nps:
                            te.wait_ge(self.sem["EV"],
                                       base["EV"] + (i - nps + 1) * step_inc[-1])
                        inst = None
                        for kc in range(n_kc):
                            inst = te.matmul(psl[i % nps][:, :], stat_fn(kc, oi, tc),
                                             mov_fn(kc, oi, tc),
                                             start=(kc == 0), stop=(kc == n_kc - 1))
                        inst.then_inc(self.sem["MM"], 1)
                        for j, (sek, _) in enumerate(steps):
                            if sek == "te":
                                emit_step(te, j, i, oi, tc)

                engs = []
                for (ek, _) in steps:
                    if ek not in engs and ek not in ("te", "gp"):
                        engs.append(ek)
                for ek in engs:
                    def mk(ek):
                        def closure(eng):
                            eng.wait_ge(self.sem["LS"], base["LS"])
                            eng.wait_ge(self.sem["GS"], base["GS"])
                            for i in range(ntiles):
                                oi, tc = divmod(i, n_tc)
                                for j, (sek, _) in enumerate(steps):
                                    if sek != ek:
                                        continue
                                    emit_step(eng, j, i, oi, tc)
                        return closure
                    getattr(blk, ENGN[ek])(mk(ek))

                has_gp_steps = any(ek == "gp" for (ek, _) in steps)
                if wfn or has_gp_steps:
                    @blk.gpsimd
                    def _(gp):
                        gp.wait_ge(self.sem["LS"], base["LS"])
                        if wfn and not has_gp_steps:
                            for oi in range(n_oi):
                                if oi >= 2:
                                    gp.wait_ge(self.sem["MM"], base["MM"] + (oi - 1) * n_tc)
                                wfn(gp, oi)
                        else:
                            for i in range(ntiles):
                                oi, tc = divmod(i, n_tc)
                                if wfn and tc == 0:
                                    if oi >= 2:
                                        gp.wait_ge(self.sem["MM"], base["MM"] + (oi - 1) * n_tc)
                                    wfn(gp, oi)
                                for j, (sek, _) in enumerate(steps):
                                    if sek == "gp":
                                        emit_step(gp, j, i, oi, tc)

        self.c["MM"] += ntiles
        for j in range(1, nsteps):
            self.c[f"X{j}"] += ntiles * step_inc[j - 1]
        self.c["EV"] += ntiles * step_inc[-1]
        self.c["GS"] += 16 * wn * n_oi


def build(P):
    T, D, HD, F, E, NROT = P["T"], P["D"], P["HD"], P["F"], P["E"], P["NROT"]
    H = D // HD
    FC = D // 128
    HC = F // 128
    TC = T // 512
    KT = T // 128
    HE = H * E
    OC = HE * HD // 128
    GRP = OC // 8
    dbg = set(P["dbg"])

    nc = bass.Bass("TRN2", target_bir_lowering=False)

    def din(name, shape):
        return nc.dram_tensor(name, list(shape), F32, kind="ExternalInput")

    x_in = din("x", (T, D))
    wqkv = din("wqkv", (D, 3 * D))
    bqkvf = din("bqkvf", (128, 3 * FC))
    wout0 = din("wout0", (D, D))
    boutf = din("boutf", (128, FC))
    w1t = din("w1t", (D, F))
    b1f0 = din("b1f0", (128, HC))
    w2t = din("w2t", (F, D))
    b2f0 = din("b2f0", (128, FC))
    wit = din("wit", (D, 3 * D))
    inbf = din("inbf", (128, 3 * FC))
    wot = din("wot", (D, D))
    outbf = din("outbf", (128, FC))
    wq2 = din("wq2", (D, D))
    wk2 = din("wk2", (D, D))
    wv4 = din("wv4", (H, D, E * HD))
    wo2 = din("wo2", (HE * HD, D))
    wselv = din("wselv", (D, HE))
    wselo = din("wselo", (D, HE))
    ln1g = din("ln1g", (1, D))
    ln1b = din("ln1b", (1, D))
    ln2g = din("ln2g", (1, D))
    ln2b = din("ln2b", (1, D))
    w1L = din("w1L", (D, F))
    b1Lf = din("b1Lf", (128, HC))
    w2L = din("w2L", (F, D))
    b2Lf = din("b2Lf", (128, FC))
    ropeP = din("ropeP", (128, 128))

    out = nc.dram_tensor("out", [T, D], F32, kind="ExternalOutput")

    def scr(name, shape, dt=BF16):
        kind = "ExternalOutput" if name in dbg else "Internal"
        return nc.dram_tensor(name, list(shape), dt, kind=kind)

    ag_in = [nc.dram_tensor(f"ag_in{i}", [1, 16], F32) for i in range(3)]
    ag_out = [nc.dram_tensor(f"ag_out{i}", [NCORES, 16], F32) for i in range(3)]
    d_kv_in = scr("d_kv_in", (T, 2 * D))
    d_kv_all = nc.dram_tensor("d_kv_all", [NCORES * T, 2 * D], BF16, addr_space="Shared")

    d_xn0F = scr("d_xn0F", (D, T))
    d_xq0F = scr("d_xq0F", (D, T))
    d_QF = scr("d_QF", (D, T))
    d_KF = scr("d_KF", (D, T))
    d_VF = scr("d_VF", (D, T))
    d_s0 = scr("d_s0", (D, T))
    d_g0 = scr("d_g0", (D, T))
    d_w1q = scr("d_w1q", (D, F))
    d_w2q = scr("d_w2q", (F, D))
    d_x1 = scr("d_x1", (T, D), F32)
    d_xn1F = scr("d_xn1F", (D, T))
    d_wiq = scr("d_wiq", (D, 3 * D))
    d_woq = scr("d_woq", (D, D))
    d_qT = scr("d_qT", (T, D))
    d_heads = scr("d_heads", (T, D), F32)
    d_yqT = scr("d_yqT", (T, D))
    d_g1 = scr("d_g1", (T, D))
    d_srcB = scr("d_srcB", (D, T), F32)

    oes = ExitStack()
    with oes:
        b = Bld(nc, oes)
        SEM = b.sem

        identB = b.sb(oes, "identB", [128, 128], BF16)
        identF = b.sb(oes, "identF", [128, 128], F32)
        identB2 = b.sb(oes, "identB2", [128, 128], BF16)
        onesT = b.sb(oes, "onesT", [128, 128], F32)
        sel64b = b.sb(oes, "sel64b", [128, 2], BF16)
        ropePb = b.sb(oes, "ropePb", [128, 128], BF16)
        cbias = b.sb(oes, "cbias", [128, 4], F32)
        scol = b.sb(oes, "scol", [128, 16], F32)
        sc_row = b.sb(oes, "sc_row", [128, 16], F32)
        negB0 = b.sb(oes, "negB0", [128, 16], F32)
        negB1 = b.sb(oes, "negB1", [128, 16], F32)
        nrm = b.sb(oes, "nrm", [128, 128], F32)
        bqkv_s = b.sb(oes, "bqkv_s", [128, 3 * FC], F32)
        bout_s = b.sb(oes, "bout_s", [128, FC], F32)
        b1q_s = b.sb(oes, "b1q_s", [128, HC], F32)
        b2q_s = b.sb(oes, "b2q_s", [128, FC], F32)
        b1q16 = b.sb(oes, "b1q16", [128, HC], F32)
        inb_s = b.sb(oes, "inb_s", [128, 3 * FC], F32)
        outb_s = b.sb(oes, "outb_s", [128, FC], F32)
        outb2_s = b.sb(oes, "outb2_s", [128, FC], F32)
        b1L_s = b.sb(oes, "b1L_s", [128, HC], F32)
        b2L_s = b.sb(oes, "b2L_s", [128, FC], F32)
        A1 = b.sb(oes, "A1", [128, FC, T], BF16)

        with nc.Block() as blk:
            @blk.sync
            def _(s):
                for dst, src in [(bqkv_s, bqkvf), (bout_s, boutf), (b1q_s, b1f0),
                                 (b2q_s, b2f0), (inb_s, inbf), (outb_s, outbf),
                                 (b1L_s, b1Lf), (b2L_s, b2Lf)]:
                    s.dma_start(out=dst[:, :], in_=src[:, :]).then_inc(SEM["LS"], 16)
                    b.c["LS"] += 16

            @blk.gpsimd
            def _(gp):
                gp.dma_start(out=ropePb[:, :], in_=ropeP[:, :]).then_inc(SEM["GS"], 16)
                b.c["GS"] += 16
                for t, v in [(identB, 1.0), (identF, 1.0), (onesT, 1.0), (sel64b, 1.0)]:
                    gp.memset(t[:, :], v)
                    gp.drain()
                gp.memset(cbias[:, :], 0.0)
                gp.drain()
                gp.memset(cbias[:, 0:1], 1e-5)
                gp.drain()
                gp.memset(scol[:, :], 1.0)
                gp.drain()
                gp.memset(sc_row[:, :], 1.0)
                gp.drain()
                for ident in (identB, identF):
                    gp.affine_select(out=ident[:, :], in_=ident[:, :], pattern=[[-1, 128]],
                                     compare_op=ALU.is_equal, fill=0.0, base=0,
                                     channel_multiplier=1)
                    gp.drain()
                gp.affine_select(out=sel64b[:, :], in_=sel64b[:, :], pattern=[[64, 2]],
                                 compare_op=ALU.is_ge, fill=0.0, base=0,
                                 channel_multiplier=-1)
                gp.drain()
                gp.affine_select(out=sel64b[:, :], in_=sel64b[:, :], pattern=[[-64, 2]],
                                 compare_op=ALU.is_ge, fill=0.0, base=63,
                                 channel_multiplier=1)
                gp.drain()

        with ExitStack() as ses:
            scr1 = b.sb(ses, "s0_scr1", [128, 128], BF16)
            i32b = b.sb(ses, "s0_i32", [128, HC], I32)
            ops = []

            def mk_diag(dst, sh):
                def f(gp):
                    gp.memset(dst[:, :], 1.0)
                    gp.drain()
                    return gp.affine_select(out=dst[:, :], in_=dst[:, :],
                                            pattern=[[-1, 128]], compare_op=ALU.is_equal,
                                            fill=0.0, base=sh, channel_multiplier=1)
                return f

            ops.append(("gp", mk_diag(identB2, 0), False))
            ops.append(("gp", mk_diag(scr1, 64), False))
            ops.append(("ve", lambda ve: ve.tensor_tensor(identB2[:, :], identB2[:, :], scr1[:, :], ALU.add), False))
            ops.append(("gp", mk_diag(scr1, -64), False))
            ops.append(("ve", lambda ve: ve.tensor_tensor(identB2[:, :], identB2[:, :], scr1[:, :], ALU.add), False))
            for t, w in [(b1q_s, HC), (b2q_s, FC)]:
                ops.append(("ve", (lambda t=t, w=w: lambda ve: ve.tensor_scalar(
                    t[:, 0:w], t[:, 0:w], 64.0, 127.0, ALU.mult, ALU.min))(), False))
                ops.append(("ve", (lambda t=t, w=w: lambda ve: ve.tensor_scalar(
                    i32b[:, 0:w], t[:, 0:w], -128.0, None, ALU.max))(), False))
                ops.append(("ve", (lambda t=t, w=w: lambda ve: ve.tensor_scalar(
                    t[:, 0:w], i32b[:, 0:w], 0.015625, None, ALU.mult))(), False))
            ops.append(("ve", lambda ve: ve.tensor_scalar(b1q16[:, :], b1q_s[:, :], 16.0, None, ALU.mult), False))
            b.seq("s0b", ops)

        def ln_stage(name, src_dram, dst_F, quant_dst):
            with ExitStack() as ses:
                xt = b.sb(ses, f"{name}_x", [128, 2, D], F32)
                xn = b.sb(ses, f"{name}_xn", [128, 2, D], BF16)
                xq = b.sb(ses, f"{name}_xq", [128, 2, D], BF16)
                xqf = b.sb(ses, f"{name}_xqf", [128, 2, D], F32)
                i32t = b.sb(ses, f"{name}_i32", [128, 2, D], I32)
                bn6 = b.sb(ses, f"{name}_bn", [128, 2, 6 * (D // 512)], F32)
                mv = b.sb(ses, f"{name}_mv", [128, 2, 4], F32)
                fst = b.sb(ses, f"{name}_fs", [128, 2, FC, 128], BF16)
                fst2 = b.sb(ses, f"{name}_fs2", [128, 2, FC, 128], BF16)
                tps = [b.ps(ses, f"{name}_tp{j}", [128, FC, 128], BF16) for j in range(2)]
                nst = D // 512
                waves = []
                for kt in range(KT):
                    p = kt % 2
                    ops = []
                    ops.append(("sy", (lambda kt=kt, p=p: lambda s: s.dma_start(
                        out=xt[:, p, :], in_=src_dram[kt * 128:(kt + 1) * 128, :]))(), True))

                    def bnf(ve, p=p):
                        for j in range(nst):
                            ve.bn_stats(bn6[:, p, j * 6:(j + 1) * 6], xt[:, p, j * 512:(j + 1) * 512])
                            ve.drain()
                        return ve.bn_aggr(mv[:, p, 0:2], bn6[:, p, 0:6 * nst])
                    ops.append(("ve", bnf, False))
                    ops.append(("sc", (lambda p=p: lambda sc: sc.activation(
                        mv[:, p, 2:3], mv[:, p, 1:2], AF.Sqrt, bias=cbias[:, 0:1]))(), False))

                    def appf(ve, p=p):
                        ve.reciprocal(mv[:, p, 3:4], mv[:, p, 2:3])
                        ve.drain()
                        return ve.tensor_scalar(xn[:, p, :], xt[:, p, :], mv[:, p, 0:1],
                                                mv[:, p, 3:4], ALU.subtract, ALU.mult)
                    ops.append(("ve", appf, False))
                    if quant_dst is not None:
                        def qf1(ve, p=p):
                            ve.tensor_scalar(xqf[:, p, :], xt[:, p, :], mv[:, p, 0:1],
                                             mv[:, p, 3:4], ALU.subtract, ALU.mult)
                            ve.drain()
                            ve.tensor_scalar(xqf[:, p, :], xqf[:, p, :], 16.0, 127.0,
                                             ALU.mult, ALU.min)
                            ve.drain()
                            return ve.tensor_scalar(i32t[:, p, :], xqf[:, p, :], -128.0,
                                                    None, ALU.max)
                        ops.append(("ve", qf1, False))
                        ops.append(("sc", (lambda p=p: lambda sc: sc.activation(
                            xq[:, p, :], i32t[:, p, :], AF.Identity, scale=0.0625))(), False))
                    pairs = [(xn, fst, dst_F)] + ([(xq, fst2, quant_dst)] if quant_dst is not None else [])
                    for (tmj, fs_, dd) in pairs:
                        def trb(te, tmj=tmj, p=p):
                            inst = None
                            for fc in range(FC):
                                inst = te.transpose(tps[p][:, fc, :],
                                                    tmj[:, p, fc * 128:(fc + 1) * 128],
                                                    identB[:, :])
                            return inst
                        ops.append(("te", trb, False))
                        ops.append(("ve", (lambda fs_=fs_, p=p: lambda ve: ve.tensor_copy(
                            fs_[:, p, :, :], tps[p][:, :, :]))(), False))
                        ops.append(("sy", (lambda fs_=fs_, dd=dd, kt=kt, p=p: lambda s: s.dma_start(
                            out=dd.ap().rearrange("(c p) t -> p c t", p=128)[:, :, kt * 128:(kt + 1) * 128],
                            in_=fs_[:, p, :, :]))(), True))
                    waves.append(ops)
                b.seqw(name, waves)

        def norm_stage(name, QF, KF, negB_dst, scale):
            with ExitStack() as ses:
                q2t = b.sb(ses, f"{name}_q2", [128, T], BF16)
                nb2 = b.sb(ses, f"{name}_nb", [128, 16], F32)
                nps_ = b.ps(ses, f"{name}_ps", [128, 512])
                tpp = b.ps(ses, f"{name}_tp", [128, 16])
                ops = []
                for t_i, SRC in enumerate((QF, KF)):
                    for fc in range(FC):
                        ops.append(("sc", (lambda SRC=SRC, fc=fc: lambda sc: sc.activation(
                            q2t[:, :], SRC[:, fc, :], AF.Square))(), False))
                        for tc in range(TC):
                            ops.append(("te", (lambda tc=tc: lambda te: te.matmul(
                                nps_[0:2, :], sel64b[:, :], q2t[:, tc * 512:(tc + 1) * 512],
                                start=True, stop=True))(), False))
                            col = t_i * (FC * TC) + fc * TC + tc
                            ops.append(("ve", (lambda col=col: lambda ve: ve.reduce_max(
                                nrm[0:2, col:col + 1], nps_[0:2, :], AX.X))(), False))

                def comb(ve):
                    v = nrm[0:2, 0:2 * FC * TC].rearrange("p (g c) -> p g c", c=TC)
                    o = nrm[0:2, 64:64 + 2 * FC].rearrange("p g -> p g 1" if False else "p (g u) -> p g u", u=1)
                    ve.tensor_reduce(o, v, AX.X, ALU.max)
                    ve.drain()
                    return ve.tensor_tensor(nrm[0:2, 96:96 + FC], nrm[0:2, 64:64 + FC],
                                            nrm[0:2, 64 + FC:64 + 2 * FC], ALU.mult)
                ops.append(("ve", comb, False))
                ops.append(("sc", lambda sc: sc.activation(nrm[0:2, 96:96 + FC],
                                                           nrm[0:2, 96:96 + FC], AF.Sqrt), False))
                ops.append(("ve", lambda ve: ve.tensor_scalar(
                    nrm[0:2, 96:96 + FC], nrm[0:2, 96:96 + FC], -1.01 * scale, None, ALU.mult), False))
                ops.append(("gp", lambda gp: gp.dma_start(
                    out=nb2[0:1, 0:2 * FC],
                    in_=nrm[0:2, 96:96 + FC], allow_slow_non_contiguous=True), True))
                ops.append(("te", lambda te: te.matmul(tpp[:, 0:H], onesT[0:1, :],
                                                       nb2[0:1, 0:H], start=True, stop=True), False))
                ops.append(("ve", lambda ve: ve.tensor_copy(negB_dst[:, 0:H], tpp[:, 0:H]), False))
                b.seq(name, ops)

        def attn_stage(name, QF, KF, VTs, negB, masks, ctx_dst, causal):
            SEMl = b.sem
            with ExitStack() as ses:
                PT = b.sb(ses, f"{name}_PT", [128, KT, 512], BF16)
                rr = b.sb(ses, f"{name}_rr", [128, 512], F32)
                ctmp = b.sb(ses, f"{name}_ct", [128, 512], BF16)
                ps_s = [b.ps(ses, f"{name}_s{j}", [128, 512]) for j in range(2)]
                ps_c = b.ps(ses, f"{name}_c", [65, 512])
                ps_b = b.ps(ses, f"{name}_b", [64, 512])
                ps_p = b.ps(ses, f"{name}_p", [128, 512])
                for qc in range(TC):
                    nkt = min(KT, 4 * qc + 4) if causal else KT
                    base = b.snap()
                    def m_of(h):
                        return (4 if causal else 0)
                    def ve_base(h):
                        return sum(m_of(x) + 3 + (x % 2) for x in range(h))
                    def te_base(h):
                        return sum(nkt + 2 + (x % 2) for x in range(h))
                    def sc_cnt(h, kt):
                        return h * nkt + kt + 1
                    with nc.named_scope(f"attn_{name}_{qc}"), nc.Block() as blk:
                        @blk.tensor
                        def _(te, qc=qc, nkt=nkt, base=base):
                            te.wait_ge(SEMl["LS"], base["LS"])
                            te.wait_ge(SEMl["GS"], base["GS"])
                            bank_user = [None, None]
                            for h in range(H):
                                off = (h % 2) * 64
                                fc = h // 2
                                hcol = (h % 2) * FC + h // 2
                                for kt in range(nkt):
                                    bu = bank_user[kt % 2]
                                    if bu is not None:
                                        te.wait_ge(SEMl["SC"], base["SC"] + sc_cnt(*bu))
                                    te.matmul(ps_s[kt % 2][:, :],
                                              KF[off:off + 64, fc, kt * 128:(kt + 1) * 128],
                                              QF[off:off + 64, fc, qc * 512:(qc + 1) * 512],
                                              start=True, stop=True).then_inc(SEMl["MM"], 1)
                                    bank_user[kt % 2] = (h, kt)
                                te.wait_ge(SEMl["SC"], base["SC"] + (h + 1) * nkt)
                                te.wait_ge(SEMl["EV"], base["EV"] + ve_base(h) + m_of(h))
                                if h > 0:
                                    te.wait_ge(SEMl["EV"],
                                               base["EV"] + ve_base(h - 1) + m_of(h - 1) + 2)
                                inst = None
                                for kt in range(nkt):
                                    inst = te.matmul(ps_c[:, :], VTs[:, kt, h, 0:65], PT[:, kt, :],
                                                     start=(kt == 0), stop=(kt == nkt - 1))
                                inst.then_inc(SEMl["MM"], 1)
                                te.wait_ge(SEMl["EV"], base["EV"] + ve_base(h) + m_of(h) + 1)
                                te.matmul(ps_b[0:64, :], onesT[64:65, 0:64], rr[64:65, :],
                                          start=True, stop=True).then_inc(SEMl["MM"], 1)
                                if h % 2 == 1:
                                    te.wait_ge(SEMl["EV"], base["EV"] + ve_base(h) + m_of(h) + 3)
                                    te.matmul(ps_p[64:128, :], identB[0:64, 0:64], ctmp[0:64, :],
                                              start=True, stop=True,
                                              tile_position=(0, 64)).then_inc(SEMl["MM"], 1)

                        @blk.scalar
                        def _(sc, qc=qc, nkt=nkt, base=base):
                            sc.wait_ge(SEMl["LS"], base["LS"])
                            sc.wait_ge(SEMl["GS"], base["GS"])
                            for h in range(H):
                                hcol = (h % 2) * FC + h // 2
                                for kt in range(nkt):
                                    sc.wait_ge(SEMl["MM"], base["MM"] + te_base(h) + kt + 1)
                                    sc.activation(PT[:, kt, :], ps_s[kt % 2][:, :], AF.Exp,
                                                  bias=negB[:, hcol:hcol + 1],
                                                  scale=0.125).then_inc(SEMl["SC"], 1)

                        @blk.vector
                        def _(ve, qc=qc, nkt=nkt, base=base):
                            ve.wait_ge(SEMl["LS"], base["LS"])
                            ve.wait_ge(SEMl["GS"], base["GS"])
                            for h in range(H):
                                fc = h // 2
                                if causal:
                                    for kt in range(4 * qc, nkt):
                                        ve.wait_ge(SEMl["SC"], base["SC"] + sc_cnt(h, kt))
                                        ve.tensor_tensor(PT[:, kt, :], PT[:, kt, :],
                                                         masks[:, kt - 4 * qc, :],
                                                         ALU.mult).then_inc(SEMl["EV"], 1)
                                        ve.drain()
                                ve.wait_ge(SEMl["MM"], base["MM"] + te_base(h) + nkt + 1)
                                ve.reciprocal(rr[64:65, :], ps_c[64:65, :]).then_inc(SEMl["EV"], 1)
                                ve.drain()
                                ve.tensor_copy(ctmp[0:64, :], ps_c[0:64, :]).then_inc(SEMl["EV"], 1)
                                ve.drain()
                                ve.wait_ge(SEMl["MM"], base["MM"] + te_base(h) + nkt + 2)
                                if h % 2 == 0:
                                    ve.tensor_tensor(ctx_dst[0:64, fc, qc * 512:(qc + 1) * 512],
                                                     ctmp[0:64, :], ps_b[0:64, :],
                                                     ALU.mult).then_inc(SEMl["EV"], 1)
                                    ve.drain()
                                else:
                                    ve.tensor_tensor(ctmp[0:64, :], ctmp[0:64, :],
                                                     ps_b[0:64, :], ALU.mult).then_inc(SEMl["EV"], 1)
                                    ve.drain()
                                    ve.wait_ge(SEMl["MM"], base["MM"] + te_base(h) + nkt + 3)
                                    ve.tensor_copy(ctx_dst[64:128, fc, qc * 512:(qc + 1) * 512],
                                                   ps_p[64:128, :]).then_inc(SEMl["EV"], 1)
                                    ve.drain()
                    nte = sum(nkt + 2 + (x % 2) for x in range(H))
                    b.c["MM"] += nte
                    b.c["SC"] += H * nkt
                    b.c["EV"] += sum(m_of(x) + 3 + (x % 2) for x in range(H))

        def quantw_stage(name, src, dst, rows, cols, scale_num, recip_scale):
            with ExitStack() as ses:
                wtmp = b.sb(ses, f"{name}_w", [128, min(cols, 2048)], F32)
                wi = b.sb(ses, f"{name}_i", [128, min(cols, 2048)], I32)
                wb = b.sb(ses, f"{name}_b", [128, min(cols, 2048)], BF16)
                CW = min(cols, 2048)
                ops = []
                for r in range(rows // 128):
                    for cb in range(cols // CW):
                        sl = (slice(r * 128, (r + 1) * 128), slice(cb * CW, (cb + 1) * CW))
                        ops.append(("sy", (lambda sl=sl: lambda s: s.dma_start(
                            out=wtmp[:, :], in_=src[sl[0], sl[1]]))(), True))
                        def q1(ve):
                            ve.tensor_scalar(wtmp[:, :], wtmp[:, :], scale_num, 127.0,
                                             ALU.mult, ALU.min)
                            ve.drain()
                            return ve.tensor_scalar(wi[:, :], wtmp[:, :], -128.0, None, ALU.max)
                        ops.append(("ve", q1, False))
                        ops.append(("sc", lambda sc: sc.activation(wb[:, :], wi[:, :],
                                                                   AF.Identity, scale=recip_scale), False))
                        ops.append(("sy", (lambda sl=sl: lambda s: s.dma_start(
                            out=dst[sl[0], sl[1]], in_=wb[:, :]))(), True))
                b.seq(name, ops)

        CUT = P["cut"]

        if CUT >= 2:
            ln_stage("s1", x_in, d_xn0F, d_xq0F)

        if CUT >= 3:
          with nc.Block() as blk:
            @blk.sync
            def _(s):
                s.dma_start(out=A1[:, :, :],
                            in_=d_xn0F.ap().rearrange("(c p) t -> p c t", p=128)).then_inc(SEM["LS"], 16)
                b.c["LS"] += 16

        if CUT >= 3:
          with ExitStack() as rl0:
            ctx0 = b.sb(rl0, "ctx0", [128, FC, T], BF16)
            QFr = b.sb(rl0, "QFr", [128, FC, T], BF16)
            KFr = b.sb(rl0, "KFr", [128, FC, T], BF16)
            VT = b.sb(rl0, "VT", [128, KT, H, 66], BF16)

            with ExitStack() as ses:
                wbuf = b.sb(ses, "s2_w", [128, 2, FC, 128], BF16)
                st = b.sb(ses, "s2_st", [128, 2, 512], BF16)

                def wload(gp, oi):
                    src = wqkv.ap().rearrange("(kc p) o -> p kc o", p=128)[:, :, oi * 128:(oi + 1) * 128]
                    gp.dma_start(out=wbuf[:, oi % 2, :, :], in_=src).then_inc(SEM["GS"], 16)

                def ev(sc, ps, i, oi, tc):
                    return sc.activation(st[:, i % 2, :], ps[:, :], AF.Identity,
                                         bias=bqkv_s[:, oi:oi + 1])

                def stf(s, ps, i, oi, tc):
                    dd = (d_QF, d_KF, d_VF)[oi // FC]
                    o = oi % FC
                    return s.dma_start(out=dd[o * 128:(o + 1) * 128, tc * 512:(tc + 1) * 512],
                                       in_=st[:, i % 2, :])

                b.proj("s2", 3 * FC, TC, FC, [128, 512],
                       lambda kc, oi, tc: wbuf[:, oi % 2, kc, :],
                       lambda kc, oi, tc: A1[:, kc, tc * 512:(tc + 1) * 512],
                       steps=[("sc", ev), ("sy", stf)], wload=(wload, 1), nps=2)

            with nc.Block() as blk:
                @blk.sync
                def _(s):
                    s.dma_start(out=QFr[:, :, :],
                                in_=d_QF.ap().rearrange("(c p) t -> p c t", p=128)).then_inc(SEM["LS"], 16)
                    s.dma_start(out=KFr[:, :, :],
                                in_=d_KF.ap().rearrange("(c p) t -> p c t", p=128)).then_inc(SEM["LS"], 16)
                    b.c["LS"] += 32

            if CUT >= 4:
             with ExitStack() as ses:
                vfch = b.sb(ses, "s2b_v", [128, 2, T], BF16)
                vtps = [b.ps(ses, f"s2b_ps{j}", [128, KT, 128], BF16) for j in range(2)]
                b.seq("s2b_ones", [("gp", lambda gp: gp.memset(VT[:, :, :, 64:65], 1.0), False)])
                waves = []
                for fc in range(FC):
                    p = fc % 2
                    ops = []
                    ops.append(("sy", (lambda fc=fc, p=p: lambda s: s.dma_start(
                        out=vfch[:, p, :], in_=d_VF[fc * 128:(fc + 1) * 128, :]))(), True))
                    def trb(te, p=p):
                        inst = None
                        for kt in range(KT):
                            inst = te.transpose(vtps[p][:, kt, :],
                                                vfch[:, p, kt * 128:(kt + 1) * 128],
                                                identB[:, :])
                        return inst
                    ops.append(("te", trb, False))
                    ops.append(("ve", (lambda fc=fc, p=p: lambda ve: ve.tensor_copy(
                        VT[:, :, 2 * fc, 0:64], vtps[p][:, :, 0:64]))(), False))
                    ops.append(("ve", (lambda fc=fc, p=p: lambda ve: ve.tensor_copy(
                        VT[:, :, 2 * fc + 1, 0:64], vtps[p][:, :, 64:128]))(), False))
                    waves.append(ops)
                b.seqw("s2b", waves)

            if CUT >= 4:
                norm_stage("s2n", QFr, KFr, negB0, 0.125)

            if CUT >= 5:
             with ExitStack() as ses:
                masks0 = b.sb(ses, "masks0", [128, 4, 512], BF16)
                with nc.Block() as blk:
                    @blk.gpsimd
                    def _(gp):
                        gp.memset(masks0[:, :, :], 1.0)
                        gp.drain()
                        for m in range(4):
                            gp.affine_select(out=masks0[:, m, :], in_=masks0[:, m, :],
                                             pattern=[[1, 512]], compare_op=ALU.is_ge,
                                             fill=0.0, base=-128 * m, channel_multiplier=-1)
                            gp.drain()
                attn_stage("s3", QFr, KFr, VT, negB0, masks0, ctx0, causal=True)

            if CUT >= 5:
             with ExitStack() as ses:
                wout_s = b.sb(ses, "s3o_w", [128, FC, D], BF16)
                st3 = b.sb(ses, "s3o_st", [128, 2, 512], BF16)
                with nc.Block() as blk:
                    @blk.gpsimd
                    def _(gp):
                        gp.dma_start(out=wout_s[:, :, :],
                                     in_=wout0.ap().rearrange("(kc p) o -> p kc o", p=128)).then_inc(SEM["GS"], 16)
                        b.c["GS"] += 16

                def ev3(sc, ps, i, oi, tc):
                    return sc.activation(st3[:, i % 2, :], ps[:, :], AF.Identity,
                                         bias=bout_s[:, oi:oi + 1])

                def st3f(s, ps, i, oi, tc):
                    return s.dma_start(out=d_s0[oi * 128:(oi + 1) * 128, tc * 512:(tc + 1) * 512],
                                       in_=st3[:, i % 2, :])

                b.proj("s3o", FC, TC, FC, [128, 512],
                       lambda kc, oi, tc: wout_s[:, kc, oi * 128:(oi + 1) * 128],
                       lambda kc, oi, tc: ctx0[:, kc, tc * 512:(tc + 1) * 512],
                       steps=[("sc", ev3), ("sy", st3f)], nps=2)

        if CUT >= 6:
            quantw_stage("s4w1", w1t, d_w1q, D, F, 64.0, 0.015625)
            quantw_stage("s4w2", w2t, d_w2q, F, D, 64.0, 0.015625)
        if CUT >= 6:
         with ExitStack() as rs4:
            xqF = b.sb(rs4, "xqF", [128, FC, T], BF16)
            hstage = b.sb(rs4, "hstage", [128, HC, 512], BF16)
            with nc.Block() as blk:
                @blk.sync
                def _(s):
                    s.dma_start(out=xqF[:, :, :],
                                in_=d_xq0F.ap().rearrange("(c p) t -> p c t", p=128)).then_inc(SEM["LS"], 16)
                    b.c["LS"] += 16
            for tc in range(TC):
                with ExitStack() as ses:
                    wb1 = b.sb(ses, "s4_w1", [128, 2, FC, 128], BF16)
                    i32a = b.sb(ses, "s4_i32", [128, 2, 512], I32)

                    def wl1(gp, oi):
                        src = d_w1q.ap().rearrange("(kc p) o -> p kc o", p=128)[:, :, oi * 128:(oi + 1) * 128]
                        gp.dma_start(out=wb1[:, oi % 2, :, :], in_=src).then_inc(SEM["GS"], 16)

                    def evA1(sc, ps, i, oi, tc_):
                        return sc.activation(i32a[:, i % 2, :], ps[:, :], AF.Identity,
                                             scale=16.0, bias=b1q16[:, oi:oi + 1])

                    def evA2(ve, ps, i, oi, tc_):
                        return ve.tensor_scalar(i32a[:, i % 2, :], i32a[:, i % 2, :],
                                                0.0, 127.0, ALU.max, ALU.min)

                    def evA3(sc, ps, i, oi, tc_):
                        return sc.activation(hstage[:, oi, :], i32a[:, i % 2, :],
                                             AF.Identity, scale=0.0625)

                    b.proj(f"s4a_{tc}", HC, 1, FC, [128, 512],
                           lambda kc, oi, tc_: wb1[:, oi % 2, kc, :],
                           (lambda tc: lambda kc, oi, tc_: xqF[:, kc, tc * 512:(tc + 1) * 512])(tc),
                           steps=[("sc", evA1), ("ve", evA2), ("sc", evA3)],
                           wload=(wl1, 1), nps=2)
                with ExitStack() as ses:
                    wb2 = b.sb(ses, "s4_w2", [128, 2, HC, 128], BF16)
                    gst = b.sb(ses, "s4_gst", [128, 2, 512], BF16)

                    def wl2(gp, oi):
                        src = d_w2q.ap().rearrange("(kc p) o -> p kc o", p=128)[:, :, oi * 128:(oi + 1) * 128]
                        gp.dma_start(out=wb2[:, oi % 2, :, :], in_=src).then_inc(SEM["GS"], 16)

                    def evB1(sc, ps, i, oi, tc_):
                        return sc.activation(gst[:, i % 2, :], ps[:, :], AF.Silu,
                                             bias=b2q_s[:, oi:oi + 1])

                    def stB(s, ps, i, oi, tc_):
                        return (lambda tc: s.dma_start(
                            out=d_g0[oi * 128:(oi + 1) * 128, tc * 512:(tc + 1) * 512],
                            in_=gst[:, i % 2, :]))(tc)

                    b.proj(f"s4b_{tc}", FC, 1, HC, [128, 512],
                           lambda kc, oi, tc_: wb2[:, oi % 2, kc, :],
                           lambda kc, oi, tc_: hstage[:, kc, :],
                           steps=[("sc", evB1), ("sy", stB)], wload=(wl2, 1), nps=2)

        if CUT >= 7:
         with ExitStack() as ses:
            xt5 = b.sb(ses, "s5_x", [128, D], F32)
            sT5 = b.sb(ses, "s5_s", [128, D], BF16)
            gT5 = b.sb(ses, "s5_g", [128, D], BF16)
            sg5 = b.sb(ses, "s5_sg", [128, D], BF16)
            x15 = b.sb(ses, "s5_x1", [128, D], F32)
            for kt in range(KT):
                ops = []
                ops.append(("sy", (lambda kt=kt: lambda s: s.dma_start(
                    out=xt5[:, :], in_=x_in[kt * 128:(kt + 1) * 128, :]))(), True))
                ops.append(("sy", (lambda kt=kt: lambda s: s.dma_start_transpose(
                    sT5[:, :], d_s0[:, kt * 128:(kt + 1) * 128]))(), True))
                ops.append(("sy", (lambda kt=kt: lambda s: s.dma_start_transpose(
                    gT5[:, :], d_g0[:, kt * 128:(kt + 1) * 128]))(), True))
                ops.append(("ve", lambda ve: ve.tensor_tensor(sg5[:, :], sT5[:, :], gT5[:, :], ALU.mult), False))
                ops.append(("ve", lambda ve: ve.tensor_tensor(x15[:, :], xt5[:, :], sg5[:, :], ALU.add), False))
                ops.append(("sy", (lambda kt=kt: lambda s: s.dma_start(
                    out=d_x1[kt * 128:(kt + 1) * 128, :], in_=x15[:, :]))(), True))
                b.seq(f"s5_{kt}", ops)

        if CUT >= 7:
            ln_stage("s6", d_x1, d_xn1F, None)
        if CUT >= 7:
         with nc.Block() as blk:
            @blk.sync
            def _(s):
                s.dma_start(out=A1[:, :, :],
                            in_=d_xn1F.ap().rearrange("(c p) t -> p c t", p=128)).then_inc(SEM["LS"], 16)
                b.c["LS"] += 16

        if CUT >= 8 or CUT == 1:
            build_s7(b, nc, locals())

        if CUT >= 9:
            build_s8(b, nc, locals())

        if CUT >= 10:
         with ExitStack() as ses:
            x19 = b.sb(ses, "s9_x1", [128, D], F32)
            yq9 = b.sb(ses, "s9_yq", [128, D], BF16)
            g9 = b.sb(ses, "s9_g", [128, D], BF16)
            sg9 = b.sb(ses, "s9_sg", [128, D], BF16)
            o9 = b.sb(ses, "s9_o", [128, D], F32)
            for kt in range(KT):
                ops = []
                ops.append(("sy", (lambda kt=kt: lambda s: s.dma_start(
                    out=x19[:, :], in_=d_x1[kt * 128:(kt + 1) * 128, :]))(), True))
                ops.append(("sy", (lambda kt=kt: lambda s: s.dma_start(
                    out=yq9[:, :], in_=d_yqT[kt * 128:(kt + 1) * 128, :]))(), True))
                ops.append(("sy", (lambda kt=kt: lambda s: s.dma_start(
                    out=g9[:, :], in_=d_g1[kt * 128:(kt + 1) * 128, :]))(), True))
                ops.append(("ve", lambda ve: ve.scalar_tensor_tensor(
                    sg9[:, :], yq9[:, :], scol[:, 11:12], g9[:, :], ALU.mult, ALU.mult), False))
                ops.append(("ve", lambda ve: ve.tensor_tensor(o9[:, :], x19[:, :], sg9[:, :], ALU.add), False))
                ops.append(("sy", (lambda kt=kt: lambda s: s.dma_start(
                    out=out[kt * 128:(kt + 1) * 128, :], in_=o9[:, :]))(), True))
                b.seq(f"s9_{kt}", ops)

        with nc.Block() as blk:
            @blk.sync
            def _(s):
                s.wait_ge(SEM["LS"], b.c["LS"])

    return nc


def build_s7(b, nc, L):
    (T, D, F, H, E, HD, FC, HC, TC, KT) = (L["T"], L["D"], L["F"], L["H"], L["E"],
                                           L["HD"], L["FC"], L["HC"], L["TC"], L["KT"])
    C7 = L["P"].get("cut7", 99) if isinstance(L.get("P"), dict) else 99
    SEM = b.sem
    A1, scol, sc_row, nrm = L["A1"], L["scol"], L["sc_row"], L["nrm"]
    inb_s, outb_s, outb2_s = L["inb_s"], L["outb_s"], L["outb2_s"]
    identB, identF, onesT = L["identB"], L["identF"], L["onesT"]
    wit, wot = L["wit"], L["wot"]
    d_wiq, d_woq, d_qT, d_kv_in, d_kv_all = (L["d_wiq"], L["d_woq"], L["d_qT"],
                                             L["d_kv_in"], L["d_kv_all"])
    d_heads, d_yqT = L["d_heads"], L["d_yqT"]
    ag_in, ag_out = L["ag_in"], L["ag_out"]
    RG = [list(range(NCORES))]

    C_SWI, C_RWI, C_SWO, C_RWO = 0, 1, 2, 3
    C_SX, C_RSX, C_FX, C_SQ, C_FQ, C_ATT = 4, 5, 6, 7, 8, 9
    C_SH, C_SY, C_FH, C_FY, C_SHWO, C_RSY = 10, 11, 12, 13, 14, 15

    def bcast_scol(ops, tpp):
        ops.append(("te", lambda te: te.matmul(tpp[:, 0:16], onesT[0:1, :],
                                               sc_row[0:1, 0:16], start=True, stop=True), False))
        ops.append(("ve", lambda ve: ve.tensor_copy(scol[:, :], tpp[:, 0:16]), False))

    def fakequantw(name, src, dst, rows, cols, c_scale, c_recip):
        nch = rows // 128
        CW = min(cols, 1024)
        ncb = cols // CW
        with ExitStack() as ses:
            wtmp = b.sb(ses, f"{name}_w", [128, CW], F32)
            wi = b.sb(ses, f"{name}_i", [128, CW], I32)
            wb = b.sb(ses, f"{name}_b", [128, CW], BF16)
            tpp = b.ps(ses, f"{name}_tp", [128, 128])
            ops = []
            for r in range(nch):
                for cb in range(ncb):
                    sl = (slice(r * 128, (r + 1) * 128), slice(cb * CW, (cb + 1) * CW))
                    ops.append(("sy", (lambda sl=sl: lambda s: s.dma_start(
                        out=wtmp[:, :], in_=src[sl[0], sl[1]]))(), True))
                    col = r * ncb + cb
                    ops.append(("ve", (lambda col=col: lambda ve: ve.reduce_max(
                        nrm[:, col:col + 1], wtmp[:, :], AX.X,
                        apply_absolute_value=True))(), False))
            nc_ = nch * ncb
            ops.append(("ve", lambda ve: ve.reduce_max(
                nrm[:, 120:121], nrm[:, 0:nc_], AX.X), False))
            ops.append(("te", lambda te: te.transpose(tpp[0:1, 0:128], nrm[:, 120:121],
                                                      identF[:, :]), False))
            ops.append(("ve", lambda ve: ve.reduce_max(nrm[0:1, 121:122], tpp[0:1, 0:128], AX.X), False))
            ops.append(("ve", (lambda: lambda ve: ve.tensor_scalar(
                sc_row[0:1, c_scale:c_scale + 1], nrm[0:1, 121:122],
                1.0 / 127.0, 1e-8, ALU.mult, ALU.max))(), False))
            ops.append(("ve", lambda ve: ve.reciprocal(
                sc_row[0:1, c_recip:c_recip + 1], sc_row[0:1, c_scale:c_scale + 1]), False))
            bcast_scol(ops, tpp)
            b.seq(f"{name}_p1", ops)
            ops = []
            for r in range(nch):
                for cb in range(ncb):
                    sl = (slice(r * 128, (r + 1) * 128), slice(cb * CW, (cb + 1) * CW))
                    ops.append(("sy", (lambda sl=sl: lambda s: s.dma_start(
                        out=wtmp[:, :], in_=src[sl[0], sl[1]]))(), True))
                    ops.append(("ve", lambda ve: ve.tensor_scalar(
                        wi[:, :], wtmp[:, :], scol[:, c_recip:c_recip + 1], None, ALU.mult), False))
                    ops.append(("sc", lambda sc: sc.activation(wb[:, :], wi[:, :], AF.Identity), False))
                    ops.append(("sy", (lambda sl=sl: lambda s: s.dma_start(
                        out=dst[sl[0], sl[1]], in_=wb[:, :]))(), True))
            b.seq(f"{name}_p2", ops)

    fakequantw("s7wi", wit, d_wiq, D, 3 * D, C_SWI, C_RWI)
    fakequantw("s7wo", wot, d_woq, D, D, C_SWO, C_RWO)
    if C7 < 2:
        return

    with ExitStack() as r7:
        hqF = b.sb(r7, "hqF", [128, FC, T], BF16)
        g7 = b.sb(r7, "g7", [128, 16], F32)
        agt = b.sb(r7, "agt", [NCORES, 16], F32)
        i7one = b.sb(r7, "i7one", [128, 2], I32)

        with ExitStack() as ses:
            wb7 = b.sb(ses, "s7a_w", [128, 2, FC, 128], BF16)
            xst = b.sb(ses, "s7a_st", [128, 2, 512], BF16)

            def wl(gp, oi):
                src = d_wiq.ap().rearrange("(kc p) o -> p kc o", p=128)[:, :, oi * 128:(oi + 1) * 128]
                gp.dma_start(out=wb7[:, oi % 2, :, :], in_=src).then_inc(SEM["GS"], 16)

            def ev1(sc, ps, i, oi, tc):
                return sc.activation(xst[:, i % 2, :], ps[:, :], AF.Identity,
                                     scale=scol[:, C_SWI:C_SWI + 1], bias=inb_s[:, oi:oi + 1])

            def ev2(ve, ps, i, oi, tc):
                return ve.reduce_max(nrm[:, i:i + 1], xst[:, i % 2, :], AX.X,
                                     apply_absolute_value=True)

            b.proj("s7a", 3 * FC, TC, FC, [128, 512],
                   lambda kc, oi, tc: wb7[:, oi % 2, kc, :],
                   lambda kc, oi, tc: A1[:, kc, tc * 512:(tc + 1) * 512],
                   steps=[("sc", ev1), ("ve", ev2)], wload=(wl, 1), nps=2)

        if C7 < 3:
            return
        with ExitStack() as ses:
            tpp = b.ps(ses, "s7g1_tp", [128, 128])
            ops = []
            ops.append(("ve", lambda ve: ve.reduce_max(nrm[:, 120:121], nrm[:, 0:FC * TC], AX.X), False))
            ops.append(("ve", lambda ve: ve.reduce_max(nrm[:, 121:122], nrm[:, 0:3 * FC * TC], AX.X), False))
            ops.append(("te", lambda te: te.transpose(tpp[0:2, 0:128], nrm[:, 120:122], identF[:, :]), False))
            ops.append(("ve", lambda ve: ve.reduce_max(g7[0:2, 0:1], tpp[0:2, 0:128], AX.X), False))
            ops.append(("gp", lambda gp: gp.dma_start(
                out=ag_in[0].ap()[0:1, 0:2], in_=g7[0:2, 0:1],
                allow_slow_non_contiguous=True), True))
            import os as _os
            if _os.environ.get("NOCC"):
                ops.append(("gp", lambda gp: gp.dma_start(
                    out=ag_out[0].ap()[0:1, :], in_=ag_in[0].ap()), True))
            else:
                ops.append(("gp", lambda gp: gp.collective_compute(
                    "AllGather", ALU.bypass, replica_groups=RG,
                    ins=[ag_in[0].ap().opt()], outs=[ag_out[0].ap().opt()]), "cc"))
            ops.append(("gp", lambda gp: gp.dma_start(out=agt[:, :], in_=ag_out[0].ap()), True))
            ops.append(("te", lambda te: te.transpose(tpp[0:16, 0:NCORES], agt[:, :], identF[0:NCORES, 0:NCORES]), False))
            ops.append(("ve", lambda ve: ve.reduce_max(g7[0:16, 1:2], tpp[0:16, 0:NCORES], AX.X), False))
            ops.append(("te", lambda te: te.transpose(tpp[0:1, 0:128], g7[:, 1:2], identF[:, :]), False))
            ops.append(("ve", lambda ve: ve.tensor_copy(g7[0:1, 0:16], tpp[0:1, 0:16]), False))
            ops.append(("ve", lambda ve: ve.tensor_scalar(
                sc_row[0:1, C_SX:C_SX + 1], g7[0:1, 1:2], 1.0 / 127.0, 1e-8,
                ALU.mult, ALU.max), False))
            ops.append(("ve", lambda ve: ve.reciprocal(
                sc_row[0:1, C_RSX:C_RSX + 1], sc_row[0:1, C_SX:C_SX + 1]), False))
            ops.append(("ve", lambda ve: ve.tensor_tensor(
                sc_row[0:1, C_FX:C_FX + 1], sc_row[0:1, C_SWI:C_SWI + 1],
                sc_row[0:1, C_RSX:C_RSX + 1], ALU.mult), False))
            ops.append(("ve", lambda ve: ve.tensor_tensor(
                g7[0:1, 2:3], g7[0:1, 0:1], sc_row[0:1, C_RSX:C_RSX + 1], ALU.mult), False))
            ops.append(("ve", lambda ve: ve.tensor_scalar(
                i7one[0:1, 0:1], g7[0:1, 2:3], 127.0, None, ALU.min), False))
            ops.append(("ve", lambda ve: ve.tensor_tensor(
                g7[0:1, 3:4], i7one[0:1, 0:1], sc_row[0:1, C_SX:C_SX + 1], ALU.mult), False))
            ops.append(("ve", lambda ve: ve.tensor_scalar(
                g7[0:1, 3:4], g7[0:1, 3:4], 0.125, None, ALU.mult), False))
            ops.append(("ve", lambda ve: ve.tensor_scalar(
                sc_row[0:1, C_SQ:C_SQ + 1], g7[0:1, 3:4], 1.0 / 127.0, 1e-8,
                ALU.mult, ALU.max), False))
            ops.append(("ve", lambda ve: ve.reciprocal(g7[0:1, 4:5], sc_row[0:1, C_SQ:C_SQ + 1]), False))
            ops.append(("ve", lambda ve: ve.tensor_tensor(
                g7[0:1, 4:5], g7[0:1, 4:5], sc_row[0:1, C_SX:C_SX + 1], ALU.mult), False))
            ops.append(("ve", lambda ve: ve.tensor_scalar(
                sc_row[0:1, C_FQ:C_FQ + 1], g7[0:1, 4:5], 0.125, None, ALU.mult), False))
            ops.append(("ve", lambda ve: ve.tensor_tensor(
                sc_row[0:1, C_ATT:C_ATT + 1], sc_row[0:1, C_SQ:C_SQ + 1],
                sc_row[0:1, C_SX:C_SX + 1], ALU.mult), False))
            bcast_scol(ops, tpp)
            ops.append(("ve", lambda ve: ve.tensor_scalar(
                inb_s[:, :], inb_s[:, :], scol[:, C_RSX:C_RSX + 1], None, ALU.mult), False))
            b.seq("s7ag1", ops)

        if C7 < 4:
            return
        with ExitStack() as ses:
            wb7 = b.sb(ses, "s7b_w", [128, 2, FC, 128], BF16)
            i7a = b.sb(ses, "s7b_i1", [128, 2, 512], I32)
            i7b = b.sb(ses, "s7b_i2", [128, 2, 512], I32)
            st7 = b.sb(ses, "s7b_st", [128, 2, 512], BF16)
            tT7 = b.sb(ses, "s7b_tt", [128, 2, 512], BF16)
            pst7 = [b.ps(ses, f"s7b_tp{j}", [128, 512], BF16) for j in range(2)]

            def wl(gp, oi):
                src = d_wiq.ap().rearrange("(kc p) o -> p kc o", p=128)[:, :, oi * 128:(oi + 1) * 128]
                gp.dma_start(out=wb7[:, oi % 2, :, :], in_=src).then_inc(SEM["GS"], 16)

            def e1(sc, ps, i, oi, tc):
                return sc.activation(i7a[:, i % 2, :], ps[:, :], AF.Identity,
                                     scale=scol[:, C_FX:C_FX + 1], bias=inb_s[:, oi:oi + 1])

            def e2(sc, ps, i, oi, tc):
                if oi < FC:
                    return sc.activation(i7b[:, i % 2, :], i7a[:, i % 2, :],
                                         AF.Identity, scale=scol[:, C_FQ:C_FQ + 1])
                return sc.nop()

            def e3(ve, ps, i, oi, tc):
                src = i7b if oi < FC else i7a
                return ve.tensor_copy(st7[:, i % 2, :], src[:, i % 2, :])

            def e4(te, ps, i, oi, tc):
                inst = None
                for qs in range(4):
                    inst = te.transpose(pst7[i % 2][:, qs * 128:(qs + 1) * 128],
                                        st7[:, i % 2, qs * 128:(qs + 1) * 128], identB[:, :])
                return inst

            def e5(ve, ps, i, oi, tc):
                return ve.tensor_copy(tT7[:, i % 2, :], pst7[i % 2][:, :])

            def e6(s, ps, i, oi, tc):
                if oi < FC:
                    dst = d_qT.ap().rearrange("(tc qs p) c -> p tc qs c", p=128, qs=4)[
                        :, tc, :, oi * 128:(oi + 1) * 128]
                elif oi < 2 * FC:
                    dst = d_kv_in.ap().rearrange("(tc qs p) c -> p tc qs c", p=128, qs=4)[
                        :, tc, :, (oi - FC) * 128:(oi - FC + 1) * 128]
                else:
                    dst = d_kv_in.ap().rearrange("(tc qs p) c -> p tc qs c", p=128, qs=4)[
                        :, tc, :, D + (oi - 2 * FC) * 128:D + (oi - 2 * FC + 1) * 128]
                return s.dma_start(out=dst, in_=tT7[:, i % 2, :].rearrange("p (qs c) -> p qs c", qs=4))

            b.proj("s7b", 3 * FC, TC, FC, [128, 512],
                   lambda kc, oi, tc: wb7[:, oi % 2, kc, :],
                   lambda kc, oi, tc: A1[:, kc, tc * 512:(tc + 1) * 512],
                   steps=[("sc", e1), ("sc", e2), ("ve", e3), ("te", e4),
                          ("ve", e5), ("sy", e6)],
                   wload=(wl, 1), nps=2)

        if C7 < 5:
            return
        ops = [("gp", lambda gp: gp.collective_compute(
            "AllGather", ALU.bypass, replica_groups=RG,
            ins=[d_kv_in.ap().opt()], outs=[d_kv_all.ap().opt()]), "cc")]
        b.seq("s7ag2", ops)

        if C7 < 6:
            return
        with ExitStack() as ses:
            qTt = b.sb(ses, "s7d_q", [128, 2, D], BF16)
            kvl = b.sb(ses, "s7d_kv", [128, 2, NCORES, 2 * D], BF16)
            tmpw = b.sb(ses, "s7d_t", [128, NCORES * D], BF16)
            acc = b.sb(ses, "s7d_a", [128, 2, D], F32)
            sct = b.sb(ses, "s7d_s", [128, 2, 128], F32)
            pexp = b.sb(ses, "s7d_p", [128, 2, 128], BF16)
            m16 = b.sb(ses, "s7d_m", [128, 2, 16], F32)
            r16 = b.sb(ses, "s7d_r", [128, 2, 16], F32)
            hn = b.sb(ses, "s7d_h", [128, 2, D], F32)
            waves = []
            for sck in range(KT):
                p = sck % 2
                ops = []
                ops.append(("sy", (lambda sck=sck, p=p: lambda s: s.dma_start(
                    out=qTt[:, p, :], in_=d_qT[sck * 128:(sck + 1) * 128, :]))(), True))
                for l in range(NCORES):
                    eng = "sy" if l < NCORES // 2 else "gp"
                    ops.append((eng, (lambda sck=sck, l=l, p=p: lambda s: s.dma_start(
                        out=kvl[:, p, l, :],
                        in_=d_kv_all[l * T + sck * 128: l * T + (sck + 1) * 128, :]))(), True))
                ops.append(("ve", (lambda p=p: lambda ve: ve.tensor_tensor(
                    tmpw[:, :].rearrange("p (l h d) -> p l h d", h=H, d=HD),
                    qTt[:, p, :].rearrange("p (u h d) -> p u h d", u=1, d=HD)
                       .broadcast_to((128, NCORES, H, HD)),
                    kvl[:, p, :, 0:D].rearrange("p l (h d) -> p l h d", d=HD),
                    ALU.mult))(), False))
                ops.append(("ve", (lambda p=p: lambda ve: ve.tensor_reduce(
                    sct[:, p, :].rearrange("p (h l u) -> p l h u", l=NCORES, u=1),
                    tmpw[:, :].rearrange("p (l h d) -> p l h d", h=H, d=HD),
                    AX.X, ALU.add))(), False))
                ops.append(("ve", (lambda p=p: lambda ve: ve.tensor_reduce(
                    m16[:, p, :].rearrange("p (h u) -> p h u", u=1),
                    sct[:, p, :].rearrange("p (h l) -> p h l", l=NCORES), AX.X, ALU.max))(), False))
                ops.append(("ve", (lambda p=p: lambda ve: ve.tensor_tensor(
                    sct[:, p, :].rearrange("p (h l) -> p h l", l=NCORES),
                    sct[:, p, :].rearrange("p (h l) -> p h l", l=NCORES),
                    m16[:, p, :].rearrange("p (h u) -> p h u", u=1).broadcast_to((128, 16, NCORES)),
                    ALU.subtract))(), False))
                ops.append(("sc", (lambda p=p: lambda sc: sc.activation(
                    pexp[:, p, :], sct[:, p, :], AF.Exp, scale=scol[:, C_ATT:C_ATT + 1]))(), False))
                ops.append(("ve", (lambda p=p: lambda ve: ve.tensor_reduce(
                    m16[:, p, :].rearrange("p (h u) -> p h u", u=1),
                    pexp[:, p, :].rearrange("p (h l) -> p h l", l=NCORES), AX.X, ALU.add))(), False))
                ops.append(("ve", (lambda p=p: lambda ve: ve.reciprocal(r16[:, p, :], m16[:, p, :]))(), False))
                ops.append(("ve", (lambda p=p: lambda ve: ve.tensor_scalar(
                    r16[:, p, :], r16[:, p, :], scol[:, C_SX:C_SX + 1], None, ALU.mult))(), False))
                ops.append(("ve", (lambda p=p: lambda ve: ve.tensor_tensor(
                    tmpw[:, :].rearrange("p (l h d) -> p l h d", h=H, d=HD),
                    kvl[:, p, :, D:2 * D].rearrange("p l (h d) -> p l h d", d=HD),
                    pexp[:, p, :].rearrange("p (h l u) -> p l h u", l=NCORES, u=1)
                        .broadcast_to((128, NCORES, H, HD)),
                    ALU.mult))(), False))
                ops.append(("ve", (lambda p=p: lambda ve: ve.tensor_reduce(
                    acc[:, p, :].rearrange("p (q u) -> p q u", u=1),
                    tmpw[:, :].rearrange("p (l q) -> p q l", q=D),
                    AX.X, ALU.add))(), False))
                ops.append(("ve", (lambda p=p: lambda ve: ve.tensor_tensor(
                    hn[:, p, :].rearrange("p (h d) -> p h d", d=HD),
                    acc[:, p, :].rearrange("p (h d) -> p h d", d=HD),
                    r16[:, p, :].rearrange("p (h u) -> p h u", u=1).broadcast_to((128, 16, HD)),
                    ALU.mult))(), False))
                ops.append(("ve", (lambda sck=sck, p=p: lambda ve: ve.reduce_max(
                    nrm[:, sck:sck + 1], hn[:, p, :], AX.X, apply_absolute_value=True))(), False))
                ops.append(("sy", (lambda sck=sck, p=p: lambda s: s.dma_start(
                    out=d_heads[sck * 128:(sck + 1) * 128, :], in_=hn[:, p, :]))(), True))
                waves.append(ops)
            b.seqw("s7d", waves)

        if C7 < 7:
            return
        with ExitStack() as ses:
            tpp = b.ps(ses, "s7g3_tp", [128, 128])
            ops = []
            ops.append(("ve", lambda ve: ve.reduce_max(nrm[:, 120:121], nrm[:, 0:KT], AX.X), False))
            ops.append(("te", lambda te: te.transpose(tpp[0:1, 0:128], nrm[:, 120:121], identF[:, :]), False))
            ops.append(("ve", lambda ve: ve.reduce_max(g7[0:1, 0:1], tpp[0:1, 0:128], AX.X), False))
            ops.append(("gp", lambda gp: gp.dma_start(
                out=ag_in[1].ap()[0:1, 0:1], in_=g7[0:1, 0:1],
                allow_slow_non_contiguous=True), True))
            ops.append(("gp", lambda gp: gp.collective_compute(
                "AllGather", ALU.bypass, replica_groups=RG,
                ins=[ag_in[1].ap().opt()], outs=[ag_out[1].ap().opt()]), "cc"))
            ops.append(("gp", lambda gp: gp.dma_start(out=agt[:, :], in_=ag_out[1].ap()), True))
            ops.append(("te", lambda te: te.transpose(tpp[0:16, 0:NCORES], agt[:, :], identF[0:NCORES, 0:NCORES]), False))
            ops.append(("ve", lambda ve: ve.reduce_max(g7[0:1, 1:2], tpp[0:1, 0:NCORES], AX.X), False))
            ops.append(("ve", lambda ve: ve.tensor_scalar(
                sc_row[0:1, C_SH:C_SH + 1], g7[0:1, 1:2], 1.0 / 127.0, 1e-8,
                ALU.mult, ALU.max), False))
            ops.append(("ve", lambda ve: ve.reciprocal(
                sc_row[0:1, C_FH:C_FH + 1], sc_row[0:1, C_SH:C_SH + 1]), False))
            ops.append(("ve", lambda ve: ve.tensor_tensor(
                sc_row[0:1, C_SHWO:C_SHWO + 1], sc_row[0:1, C_SH:C_SH + 1],
                sc_row[0:1, C_SWO:C_SWO + 1], ALU.mult), False))
            bcast_scol(ops, tpp)
            b.seq("s7ag3", ops)

        if C7 < 8:
            return
        with ExitStack() as ses:
            ht = b.sb(ses, "s7e_h", [128, 2, D], F32)
            hi = b.sb(ses, "s7e_i", [128, 2, D], I32)
            hb = b.sb(ses, "s7e_b", [128, 2, D], BF16)
            tpe = [b.ps(ses, f"s7e_tp{j}", [128, FC, 128], BF16) for j in range(2)]
            waves = []
            for kt in range(KT):
                p = kt % 2
                ops = []
                ops.append(("sy", (lambda kt=kt, p=p: lambda s: s.dma_start(
                    out=ht[:, p, :], in_=d_heads[kt * 128:(kt + 1) * 128, :]))(), True))
                ops.append(("sc", (lambda p=p: lambda sc: sc.activation(
                    hi[:, p, :], ht[:, p, :], AF.Identity, scale=scol[:, C_FH:C_FH + 1]))(), False))
                ops.append(("ve", (lambda p=p: lambda ve: ve.tensor_copy(hb[:, p, :], hi[:, p, :]))(), False))
                def trb(te, p=p):
                    inst = None
                    for fc in range(FC):
                        inst = te.transpose(tpe[p][:, fc, :],
                                            hb[:, p, fc * 128:(fc + 1) * 128], identB[:, :])
                    return inst
                ops.append(("te", trb, False))
                ops.append(("ve", (lambda kt=kt, p=p: lambda ve: ve.tensor_copy(
                    hqF[:, :, kt * 128:(kt + 1) * 128], tpe[p][:, :, :]))(), False))
                waves.append(ops)
            b.seqw("s7e", waves)

        with ExitStack() as ses:
            wb7 = b.sb(ses, "s7f_w", [128, 2, FC, 128], BF16)
            yst = b.sb(ses, "s7f_st", [128, 2, 512], BF16)

            def wl(gp, oi):
                src = d_woq.ap().rearrange("(kc p) o -> p kc o", p=128)[:, :, oi * 128:(oi + 1) * 128]
                gp.dma_start(out=wb7[:, oi % 2, :, :], in_=src).then_inc(SEM["GS"], 16)

            def f1(sc, ps, i, oi, tc):
                return sc.activation(yst[:, i % 2, :], ps[:, :], AF.Identity,
                                     scale=scol[:, C_SHWO:C_SHWO + 1], bias=outb_s[:, oi:oi + 1])

            def f2(ve, ps, i, oi, tc):
                return ve.reduce_max(nrm[:, i:i + 1], yst[:, i % 2, :], AX.X,
                                     apply_absolute_value=True)

            b.proj("s7f", FC, TC, FC, [128, 512],
                   lambda kc, oi, tc: wb7[:, oi % 2, kc, :],
                   lambda kc, oi, tc: hqF[:, kc, tc * 512:(tc + 1) * 512],
                   steps=[("sc", f1), ("ve", f2)], wload=(wl, 1), nps=2)

        with ExitStack() as ses:
            tpp = b.ps(ses, "s7g4_tp", [128, 128])
            ops = []
            ops.append(("ve", lambda ve: ve.reduce_max(nrm[:, 120:121], nrm[:, 0:FC * TC], AX.X), False))
            ops.append(("te", lambda te: te.transpose(tpp[0:1, 0:128], nrm[:, 120:121], identF[:, :]), False))
            ops.append(("ve", lambda ve: ve.reduce_max(g7[0:1, 0:1], tpp[0:1, 0:128], AX.X), False))
            ops.append(("gp", lambda gp: gp.dma_start(
                out=ag_in[2].ap()[0:1, 0:1], in_=g7[0:1, 0:1],
                allow_slow_non_contiguous=True), True))
            ops.append(("gp", lambda gp: gp.collective_compute(
                "AllGather", ALU.bypass, replica_groups=RG,
                ins=[ag_in[2].ap().opt()], outs=[ag_out[2].ap().opt()]), "cc"))
            ops.append(("gp", lambda gp: gp.dma_start(out=agt[:, :], in_=ag_out[2].ap()), True))
            ops.append(("te", lambda te: te.transpose(tpp[0:16, 0:NCORES], agt[:, :], identF[0:NCORES, 0:NCORES]), False))
            ops.append(("ve", lambda ve: ve.reduce_max(g7[0:1, 1:2], tpp[0:1, 0:NCORES], AX.X), False))
            ops.append(("ve", lambda ve: ve.tensor_scalar(
                sc_row[0:1, C_SY:C_SY + 1], g7[0:1, 1:2], 1.0 / 127.0, 1e-8,
                ALU.mult, ALU.max), False))
            ops.append(("ve", lambda ve: ve.reciprocal(
                sc_row[0:1, C_RSY:C_RSY + 1], sc_row[0:1, C_SY:C_SY + 1]), False))
            ops.append(("ve", lambda ve: ve.tensor_tensor(
                sc_row[0:1, C_FY:C_FY + 1], sc_row[0:1, C_SHWO:C_SHWO + 1],
                sc_row[0:1, C_RSY:C_RSY + 1], ALU.mult), False))
            bcast_scol(ops, tpp)
            ops2 = [("ve", lambda ve: ve.tensor_scalar(
                outb2_s[:, :], outb_s[:, :], scol[:, C_RSY:C_RSY + 1], None, ALU.mult), False)]
            b.seq("s7ag4", ops + ops2)

        with ExitStack() as ses:
            wb7 = b.sb(ses, "s7g_w", [128, 2, FC, 128], BF16)
            i7a = b.sb(ses, "s7g_i", [128, 2, 512], I32)
            st7 = b.sb(ses, "s7g_st", [128, 2, 512], BF16)
            tT7 = b.sb(ses, "s7g_tt", [128, 2, 512], BF16)
            pst7 = [b.ps(ses, f"s7g_tp{j}", [128, 512], BF16) for j in range(2)]

            def wl(gp, oi):
                src = d_woq.ap().rearrange("(kc p) o -> p kc o", p=128)[:, :, oi * 128:(oi + 1) * 128]
                gp.dma_start(out=wb7[:, oi % 2, :, :], in_=src).then_inc(SEM["GS"], 16)

            def g1(sc, ps, i, oi, tc):
                return sc.activation(i7a[:, i % 2, :], ps[:, :], AF.Identity,
                                     scale=scol[:, C_FY:C_FY + 1], bias=outb2_s[:, oi:oi + 1])

            def g2(ve, ps, i, oi, tc):
                return ve.tensor_copy(st7[:, i % 2, :], i7a[:, i % 2, :])

            def g3(te, ps, i, oi, tc):
                inst = None
                for qs in range(4):
                    inst = te.transpose(pst7[i % 2][:, qs * 128:(qs + 1) * 128],
                                        st7[:, i % 2, qs * 128:(qs + 1) * 128], identB[:, :])
                return inst

            def g4(ve, ps, i, oi, tc):
                return ve.tensor_copy(tT7[:, i % 2, :], pst7[i % 2][:, :])

            def g5(s, ps, i, oi, tc):
                dst = d_yqT.ap().rearrange("(tc qs p) c -> p tc qs c", p=128, qs=4)[
                    :, tc, :, oi * 128:(oi + 1) * 128]
                return s.dma_start(out=dst, in_=tT7[:, i % 2, :].rearrange("p (qs c) -> p qs c", qs=4))

            b.proj("s7g", FC, TC, FC, [128, 512],
                   lambda kc, oi, tc: wb7[:, oi % 2, kc, :],
                   lambda kc, oi, tc: hqF[:, kc, tc * 512:(tc + 1) * 512],
                   steps=[("sc", g1), ("ve", g2), ("te", g3), ("ve", g4), ("sy", g5)],
                   wload=(wl, 1), nps=2)


def build_s8(b, nc, L):
    (T, D, F, H, E, HD, FC, HC, TC, KT, NROT) = (
        L["T"], L["D"], L["F"], L["H"], L["E"], L["HD"], L["FC"], L["HC"],
        L["TC"], L["KT"], L["NROT"])
    HE, OC = H * E, H * E * HD // 128
    SEM = b.sem
    A1, scol, nrm, cbias = L["A1"], L["scol"], L["nrm"], L["cbias"]
    identB, identB2, identF, onesT = L["identB"], L["identB2"], L["identF"], L["onesT"]
    ropePb, negB1 = L["ropePb"], L["negB1"]
    b1L_s, b2L_s = L["b1L_s"], L["b2L_s"]
    wq2, wk2, wv4, wo2, wselv, wselo = (L["wq2"], L["wk2"], L["wv4"], L["wo2"],
                                        L["wselv"], L["wselo"])
    ln1g, ln1b, ln2g, ln2b = L["ln1g"], L["ln1b"], L["ln2g"], L["ln2b"]
    w1L, w2L = L["w1L"], L["w2L"]
    d_g1, d_srcB = L["d_g1"], L["d_srcB"]
    norm_stage, attn_stage = L["norm_stage"], L["attn_stage"]
    half = NROT // 2

    with ExitStack() as r8:
        ctx1 = b.sb(r8, "ctx1", [128, FC, T], BF16)
        gvT = b.sb(r8, "gvT", [128, KT, 128], BF16)
        goF = b.sb(r8, "goF", [128, T], BF16)

        with ExitStack() as ra:
            CC_t = b.sb(ra, "CC_t", [128, T], BF16)
            SS_t = b.sb(ra, "SS_t", [128, T], BF16)

            with ExitStack() as ses:
                ang = b.sb(ses, "s8a_a", [128, T], F32)
                angr = b.sb(ses, "s8a_ar", [128, T], F32)
                sinv = b.sb(ses, "s8a_sv", [128, T], F32)
                kint = b.sb(ses, "s8a_k", [128, T], I32)
                jv = b.sb(ses, "s8a_j", [128, 6], F32)
                tid = ang
                ops = []
                ops.append(("gp", lambda gp: gp.iota(
                    tid[:, :], [[1, T]], base=0, channel_multiplier=0,
                    allow_small_or_imprecise_dtypes=True), False))
                ops.append(("gp", lambda gp: gp.iota(
                    jv[:, 0:1], [[0, 1]], base=0, channel_multiplier=1,
                    allow_small_or_imprecise_dtypes=True), False))
                def jb(ve):
                    ve.tensor_scalar(jv[64:96, 0:1], jv[64:96, 0:1], 64.0, None, ALU.subtract)
                    ve.drain()
                    ve.tensor_scalar(jv[:, 1:2], jv[:, 0:1], 16.0, -16.0, ALU.is_ge, ALU.mult)
                    ve.drain()
                    return ve.tensor_tensor(jv[:, 0:1], jv[:, 0:1], jv[:, 1:2], ALU.add)
                ops.append(("ve", jb, False))
                ops.append(("gp", lambda gp: gp.iota(
                    jv[:, 4:5], [[0, 1]], base=0, channel_multiplier=1,
                    allow_small_or_imprecise_dtypes=True), False))
                def sgn_build(ve):
                    ve.tensor_scalar(jv[:, 5:6], jv[:, 4:5], 64.0, -64.0, ALU.is_ge, ALU.mult)
                    ve.drain()
                    ve.tensor_tensor(jv[:, 4:5], jv[:, 4:5], jv[:, 5:6], ALU.add)
                    ve.drain()
                    ve.tensor_scalar(jv[:, 2:3], jv[:, 4:5], 16.0, None, ALU.is_lt)
                    ve.drain()
                    ve.tensor_scalar(jv[:, 3:4], jv[:, 4:5], 32.0, None, ALU.is_lt)
                    ve.drain()
                    return ve.scalar_tensor_tensor(jv[:, 4:5], jv[:, 2:3], -2.0,
                                                   jv[:, 3:4], ALU.mult, ALU.add)
                ops.append(("ve", sgn_build, False))
                ops.append(("sc", lambda sc: sc.activation(
                    jv[:, 1:2], jv[:, 0:1], AF.Exp, scale=-LN1E4 / half), False))
                ops.append(("ve", lambda ve: ve.tensor_tensor(
                    ang[:, :], tid[:, :], jv[:, 1:2].broadcast_to((128, T)), ALU.mult), False))
                ops.append(("ve", lambda ve: ve.tensor_scalar(
                    kint[:, :], ang[:, :], 1.0 / TWO_PI, None, ALU.mult), False))
                ops.append(("ve", lambda ve: ve.scalar_tensor_tensor(
                    angr[:, :], kint[:, :], -TWO_PI, ang[:, :], ALU.mult, ALU.add), False))
                ops.append(("sc", lambda sc: sc.activation(sinv[:, :], angr[:, :], AF.Sin), False))
                ops.append(("ve", lambda ve: ve.tensor_tensor(
                    SS_t[:, :], sinv[:, :], jv[:, 4:5].broadcast_to((128, T)), ALU.mult), False))
                ops.append(("ve", lambda ve: ve.tensor_scalar(
                    ang[:, :], ang[:, :], TWO_PI / 4.0, None, ALU.add), False))
                ops.append(("ve", lambda ve: ve.tensor_scalar(
                    kint[:, :], ang[:, :], 1.0 / TWO_PI, None, ALU.mult), False))
                ops.append(("ve", lambda ve: ve.scalar_tensor_tensor(
                    angr[:, :], kint[:, :], -TWO_PI, ang[:, :], ALU.mult, ALU.add), False))
                ops.append(("sc", lambda sc: sc.activation(sinv[:, :], angr[:, :], AF.Sin), False))
                ops.append(("ve", lambda ve: ve.tensor_tensor(
                    sinv[:, :], sinv[:, :], jv[:, 3:4].broadcast_to((128, T)), ALU.mult), False))
                ops.append(("ve", lambda ve: ve.tensor_scalar(
                    jv[:, 5:6], jv[:, 3:4], -1.0, 1.0, ALU.mult, ALU.add), False))
                ops.append(("ve", lambda ve: ve.tensor_tensor(
                    CC_t[:, :], sinv[:, :], jv[:, 5:6].broadcast_to((128, T)), ALU.add), False))
                b.seq("s8a", ops)

            qmF = b.sb(ra, "qmF", [128, FC, T], BF16)
            kmF = b.sb(ra, "kmF", [128, FC, T], BF16)
            VT1 = b.sb(ra, "VT1", [128, KT, H, 66], BF16)

            for (wsrc, dstF, nm) in [(wq2, qmF, "s8bq"), (wk2, kmF, "s8bk")]:
                with ExitStack() as ses:
                    wb8 = b.sb(ses, f"{nm}_w", [128, 2, FC, 128], BF16)

                    def wl(gp, oi, wsrc=wsrc, wb8=wb8):
                        src = wsrc.ap().rearrange("(kc p) o -> p kc o", p=128)[:, :, oi * 128:(oi + 1) * 128]
                        gp.dma_start(out=wb8[:, oi % 2, :, :], in_=src).then_inc(SEM["GS"], 16)

                    def e1(sc, ps, i, oi, tc, dstF=dstF):
                        return sc.activation(dstF[:, oi, tc * 512:(tc + 1) * 512],
                                             ps[:, :], AF.Identity)

                    b.proj(nm, FC, TC, FC, [128, 512],
                           (lambda wb8=wb8: lambda kc, oi, tc: wb8[:, oi % 2, kc, :])(),
                           lambda kc, oi, tc: A1[:, kc, tc * 512:(tc + 1) * 512],
                           steps=[("sc", e1)], wload=(wl, 1), nps=2)
                with ExitStack() as ses:
                    u8 = b.sb(ses, f"{nm}_u", [128, 512], BF16)
                    t8_ = b.sb(ses, f"{nm}_t2", [128, 512], BF16)
                    psr = b.ps(ses, f"{nm}_ps", [128, 512])
                    for fc in range(FC):
                        ops = []
                        for tc in range(TC):
                            ops.append(("te", (lambda fc=fc, tc=tc, dstF=dstF: lambda te: te.matmul(
                                psr[:, :], ropePb[:, :], dstF[:, fc, tc * 512:(tc + 1) * 512],
                                start=True, stop=True))(), False))
                            ops.append(("ve", (lambda tc=tc: lambda ve: ve.tensor_tensor(
                                u8[:, :], psr[:, :], SS_t[:, tc * 512:(tc + 1) * 512],
                                ALU.mult))(), False))
                            ops.append(("ve", (lambda fc=fc, tc=tc, dstF=dstF: lambda ve: ve.tensor_tensor(
                                t8_[:, :], dstF[:, fc, tc * 512:(tc + 1) * 512],
                                CC_t[:, tc * 512:(tc + 1) * 512], ALU.mult))(), False))
                            ops.append(("ve", (lambda fc=fc, tc=tc, dstF=dstF: lambda ve: ve.tensor_tensor(
                                dstF[:, fc, tc * 512:(tc + 1) * 512], t8_[:, :], u8[:, :],
                                ALU.add))(), False))
                        b.seq(f"{nm}_rope{fc}", ops)

            norm_stage("s8n", qmF, kmF, negB1, 0.125)

            with ExitStack() as ses:
                wsel_s = b.sb(ses, "s8c_w", [128, 2, FC, 128], BF16)
                gt = b.sb(ses, "s8c_g", [128, 128], F32)
                eqt = b.sb(ses, "s8c_e", [128, 128], F32)
                m1g = b.sb(ses, "s8c_m", [128, 16], F32)
                psg = b.ps(ses, "s8c_ps", [128, 128])
                pst = b.ps(ses, "s8c_pt", [128, 128])
                with nc.Block() as blk:
                    @blk.gpsimd
                    def _(gp):
                        gp.dma_start(out=wsel_s[:, 0, :, :],
                                     in_=wselv.ap().rearrange("(kc p) o -> p kc o", p=128)).then_inc(SEM["GS"], 16)
                        gp.dma_start(out=wsel_s[:, 1, :, :],
                                     in_=wselo.ap().rearrange("(kc p) o -> p kc o", p=128)).then_inc(SEM["GS"], 16)
                        b.c["GS"] += 32
                for gidx in range(2):
                    for kt in range(KT):
                        ops = []
                        def mm(te, gidx=gidx, kt=kt):
                            inst = None
                            for dc in range(FC):
                                inst = te.matmul(psg[:, :], A1[:, dc, kt * 128:(kt + 1) * 128],
                                                 wsel_s[:, gidx, dc, :],
                                                 start=(dc == 0), stop=(dc == FC - 1))
                            return inst
                        ops.append(("te", mm, False))
                        ops.append(("sc", lambda sc: sc.activation(gt[:, :], psg[:, :], AF.Sigmoid), False))
                        gv_ = gt[:, :].rearrange("p (h e) -> p h e", e=E)
                        m1v = m1g[:, :].rearrange("p (h u) -> p h u", u=1)
                        ops.append(("ve", lambda ve: ve.tensor_reduce(m1v, gv_, AX.X, ALU.max), False))
                        ops.append(("ve", lambda ve: ve.tensor_tensor(
                            eqt[:, :].rearrange("p (h e) -> p h e", e=E), gv_,
                            m1v.broadcast_to((128, H, E)), ALU.is_equal), False))
                        ops.append(("ve", lambda ve: ve.scalar_tensor_tensor(
                            eqt[:, :], eqt[:, :], -1e30, gt[:, :], ALU.mult, ALU.add), False))
                        ops.append(("ve", lambda ve: ve.tensor_reduce(
                            m1v, eqt[:, :].rearrange("p (h e) -> p h e", e=E), AX.X, ALU.max), False))
                        ops.append(("ve", lambda ve: ve.tensor_tensor(
                            eqt[:, :].rearrange("p (h e) -> p h e", e=E), gv_,
                            m1v.broadcast_to((128, H, E)), ALU.is_ge), False))
                        if gidx == 0:
                            ops.append(("ve", (lambda kt=kt: lambda ve: ve.tensor_tensor(
                                gvT[:, kt, :], gt[:, :], eqt[:, :], ALU.mult))(), False))
                        else:
                            ops.append(("ve", lambda ve: ve.tensor_tensor(
                                gt[:, :], gt[:, :], eqt[:, :], ALU.mult), False))
                            ops.append(("te", lambda te: te.transpose(
                                pst[:, :], gt[:, :], identF[:, :]), False))
                            ops.append(("ve", (lambda kt=kt: lambda ve: ve.tensor_copy(
                                goF[:, kt * 128:(kt + 1) * 128], pst[:, :]))(), False))
                        b.seq(f"s8c_{gidx}_{kt}", ops)

            with ExitStack() as ses:
                NG = 2
                wvs = b.sb(ses, "s8d_w", [128, NG, FC, E * HD], BF16)
                vtmp = b.sb(ses, "s8d_t", [128, 2, E * HD], F32)
                psv = [b.ps(ses, f"s8d_p{j}", [128, E * HD]) for j in range(2 * NG)]
                ops0 = [("gp", lambda gp: gp.memset(VT1[:, :, :, 64:65], 1.0), False)]
                b.seq("s8d_ones", ops0)
                for hg in range(H // NG):
                    with nc.Block() as blk:
                        @blk.gpsimd
                        def _(gp, hg=hg):
                            for j in range(NG):
                                src = wv4.ap()[hg * NG + j, :, :].rearrange(
                                    "(kc p) o -> p kc o", p=128)
                                gp.dma_start(out=wvs[:, j, :, :], in_=src).then_inc(SEM["GS"], 16)
                                b.c["GS"] += 16
                    waves = []
                    for kt in range(KT):
                        p = kt % 2
                        ops = []
                        def mm(te, hg=hg, kt=kt, p=p):
                            inst = None
                            for dc in range(FC):
                                for j in range(NG):
                                    inst = te.matmul(psv[p * NG + j][:, :],
                                                     A1[:, dc, kt * 128:(kt + 1) * 128],
                                                     wvs[:, j, dc, :],
                                                     start=(dc == 0), stop=(dc == FC - 1))
                            return inst
                        ops.append(("te", mm, False))
                        for j in range(NG):
                            h = hg * NG + j
                            def gvm(ve, j=j, h=h, kt=kt, p=p):
                                return ve.tensor_tensor(
                                    vtmp[:, p, :].rearrange("p (pp e) -> p e pp", e=E),
                                    psv[p * NG + j][:, :].rearrange("p (e pp) -> p e pp", e=E),
                                    gvT[:, kt, h * E:(h + 1) * E].rearrange(
                                        "p (e u) -> p e u", u=1).broadcast_to((128, E, HD)),
                                    ALU.mult)
                            ops.append(("ve", gvm, False))
                            def rd(ve, j=j, h=h, kt=kt, p=p):
                                with nc.allow_low_precision(reason="bf16 store of f32-accumulated expert sum"):
                                    return ve.tensor_reduce(
                                        VT1[:, kt, h, 0:64].rearrange("p (d u) -> p d u", u=1),
                                        vtmp[:, p, :].rearrange("p (d e) -> p d e", e=E),
                                        AX.X, ALU.add)
                            ops.append(("ve", rd, False))
                        waves.append(ops)
                    b.seqw(f"s8d_{hg}", waves)

            attn_stage("s8e", qmF, kmF, VT1, negB1, None, ctx1, causal=False)

        with ExitStack() as rf:
            SELh = b.sb(rf, "SELh", [128, OC * 128], BF16)
            with nc.Block() as blk:
                @blk.gpsimd
                def _(gp):
                    gp.memset(SELh[:, :], 1.0)
                    gp.drain()
                    gp.affine_select(
                        out=SELh[:, :].rearrange("p (c h j) -> p c h j", h=2, j=64),
                        in_=SELh[:, :].rearrange("p (c h j) -> p c h j", h=2, j=64),
                        pattern=[[-2, OC], [-1, 2], [0, 64]],
                        compare_op=ALU.is_equal, fill=0.0, base=0, channel_multiplier=1)
                    gp.drain()
            with ExitStack() as ses:
                wo_s = b.sb(ses, "s8f_w", [128, 2, 8, D], BF16)
                cR = b.sb(ses, "s8f_cr", [128, 2, 2, 512], BF16)
                ohb = b.sb(ses, "s8f_oh", [128, 2, 8, 512], BF16)
                ost = b.sb(ses, "s8f_ost", [128, 2, 4, 512], F32)
                pcr = [b.ps(ses, f"s8f_pc{j}", [128, 512]) for j in range(2)]
                pgr = [b.ps(ses, f"s8f_pg{j}", [128, 512]) for j in range(4)]
                pso = [b.ps(ses, f"s8f_po{j}", [128, 512]) for j in range(2)]
                NGRP = OC // 8
                TEW, VEW = 18, 18
                MM0, EV0, GS0, AS0 = b.c["MM"], b.c["EV"], b.c["GS"], b.c["AS"]

                def _wsrc(g):
                    return wo2.ap()[g * 1024:(g + 1) * 1024, :].rearrange(
                        "(kc p) o -> p kc o", p=128)
                with nc.named_scope("s8f"), nc.Block() as blk:
                    @blk.tensor
                    def _(te):
                        for grp in range(NGRP):
                            bMM = MM0 + grp * TC * TEW
                            bEV = EV0 + grp * TC * VEW
                            te.wait_ge(SEM["AS"], AS0 + 16 * (grp + 1))
                            for tc in range(TC):
                                pw = tc % 2
                                for jj in range(2):
                                    hh = 2 * grp + jj
                                    off = (hh % 2) * 64
                                    te.wait_ge(SEM["EV"], bEV + (tc - 1) * VEW + jj + 1)
                                    te.matmul(pcr[jj][:, :], identB2[off:off + 64, :],
                                              ctx1[off:off + 64, hh // 2, tc * 512:(tc + 1) * 512],
                                              start=True, stop=True).then_inc(SEM["MM"], 1)
                                for c8 in range(8):
                                    c = grp * 8 + c8
                                    bk = pw * 2 + c8 % 2
                                    w_tc, w_c8 = (tc, c8 - 2) if c8 >= 2 else (tc - 2, c8 + 6)
                                    te.wait_ge(SEM["EV"], bEV + w_tc * VEW + 2 + w_c8 + 1)
                                    te.matmul(pgr[bk][:, :], SELh[:, c * 128:(c + 1) * 128],
                                              goF[:, tc * 512:(tc + 1) * 512],
                                              start=True, stop=True).then_inc(SEM["MM"], 1)
                                for ofc in range(FC):
                                    if ofc == 0:
                                        te.wait_ge(SEM["EV"], bEV + tc * VEW + 10)
                                    if ofc >= 2:
                                        te.wait_ge(SEM["EV"], bEV + tc * VEW + 10 + ofc - 1)
                                    else:
                                        te.wait_ge(SEM["EV"], bEV + (tc - 1) * VEW + 10 + 6 + ofc + 1)
                                    inst = None
                                    for c8 in range(8):
                                        inst = te.matmul(pso[ofc % 2][:, :],
                                                         wo_s[:, grp % 2, c8, ofc * 128:(ofc + 1) * 128],
                                                         ohb[:, pw, c8, :],
                                                         start=(c8 == 0), stop=(c8 == 7))
                                    inst.then_inc(SEM["MM"], 1)

                    @blk.vector
                    def _(ve):
                        for grp in range(NGRP):
                            bMM = MM0 + grp * TC * TEW
                            bGS = GS0 + grp * 16 * TC * FC
                            for tc in range(TC):
                                pw = tc % 2
                                for jj in range(2):
                                    ve.wait_ge(SEM["MM"], bMM + tc * TEW + jj + 1)
                                    ve.tensor_copy(cR[:, pw, jj, :], pcr[jj][:, :]).then_inc(SEM["EV"], 1)
                                    ve.drain()
                                for c8 in range(8):
                                    bk = pw * 2 + c8 % 2
                                    jj = ((grp * 8 + c8) // 4) % 2
                                    ve.wait_ge(SEM["MM"], bMM + tc * TEW + 2 + c8 + 1)
                                    ve.tensor_tensor(ohb[:, pw, c8, :], cR[:, pw, jj, :],
                                                     pgr[bk][:, :], ALU.mult).then_inc(SEM["EV"], 1)
                                    ve.drain()
                                for ofc in range(FC):
                                    ve.wait_ge(SEM["MM"], bMM + tc * TEW + 10 + ofc + 1)
                                    if ofc >= 4:
                                        ve.wait_ge(SEM["GS"], bGS + 16 * (tc * FC + ofc - 4 + 1))
                                    else:
                                        gw = bGS + 16 * ((tc - 2) * FC + ofc + 4 + 1)
                                        if gw > GS0 - 16 * TC * FC:
                                            ve.wait_ge(SEM["GS"], max(gw, 0))
                                    if grp == 0:
                                        ve.tensor_tensor(ost[:, pw, ofc % 4, :], pso[ofc % 2][:, :],
                                                         A1[:, ofc, tc * 512:(tc + 1) * 512],
                                                         ALU.add).then_inc(SEM["EV"], 1)
                                    else:
                                        ve.tensor_copy(ost[:, pw, ofc % 4, :],
                                                       pso[ofc % 2][:, :]).then_inc(SEM["EV"], 1)
                                    ve.drain()

                    @blk.gpsimd
                    def _(gp):
                        gp.dma_start(out=wo_s[:, 0, :, :], in_=_wsrc(0)).then_inc(SEM["AS"], 16)
                        for grp in range(NGRP):
                            bEV = EV0 + grp * TC * VEW
                            if grp + 1 < NGRP:
                                gp.wait_ge(SEM["MM"], MM0 + grp * TC * TEW)
                                gp.dma_start(out=wo_s[:, (grp + 1) % 2, :, :],
                                             in_=_wsrc(grp + 1)).then_inc(SEM["AS"], 16)
                            for tc in range(TC):
                                pw = tc % 2
                                for ofc in range(FC):
                                    gp.wait_ge(SEM["EV"], bEV + tc * VEW + 10 + ofc + 1)
                                    gp.dma_start(
                                        out=d_srcB[ofc * 128:(ofc + 1) * 128, tc * 512:(tc + 1) * 512],
                                        in_=ost[:, pw, ofc % 4, :],
                                        accum_op=(ALU.bypass if grp == 0 else ALU.add)).then_inc(SEM["GS"], 16)
                b.c["AS"] += 16 * NGRP
                b.c["MM"] += NGRP * TC * TEW
                b.c["EV"] += NGRP * TC * VEW
                b.c["GS"] += NGRP * 16 * TC * FC

        if True:
            with ExitStack() as rg:
                xlnF = b.sb(rg, "xlnF", [128, FC, T], BF16)
                gbx = b.sb(rg, "s8_gb1", [128, D], F32)
                bbx = b.sb(rg, "s8_bb1", [128, D], F32)
                ln_tmaj(b, nc, L, "s8g", d_srcB, xlnF, ln1g, ln1b, gbx, bbx,
                        silu_out=None, g1_dst=None)

                with ExitStack() as ses:
                    hst = b.sb(ses, "s8h_h", [128, HC, 512], BF16)
                    for tc in range(TC):
                        with ExitStack() as s2:
                            wb8 = b.sb(s2, "s8h_w1", [128, 2, FC, 128], BF16)

                            def wl1(gp, oi, wb8=wb8):
                                src = w1L.ap().rearrange("(kc p) o -> p kc o", p=128)[
                                    :, :, oi * 128:(oi + 1) * 128]
                                gp.dma_start(out=wb8[:, oi % 2, :, :], in_=src).then_inc(SEM["GS"], 16)

                            def h1(sc, ps, i, oi, tc_, hst=hst):
                                return sc.activation(hst[:, oi, :], ps[:, :], AF.Relu,
                                                     bias=b1L_s[:, oi:oi + 1])

                            b.proj(f"s8h1_{tc}", HC, 1, FC, [128, 512],
                                   (lambda wb8=wb8: lambda kc, oi, tc_: wb8[:, oi % 2, kc, :])(),
                                   (lambda tc=tc: lambda kc, oi, tc_: xlnF[:, kc, tc * 512:(tc + 1) * 512])(),
                                   steps=[("sc", h1)], wload=(wl1, 1), nps=2)
                        with ExitStack() as s2:
                            wb9 = b.sb(s2, "s8h_w2", [128, 2, HC, 128], BF16)

                            def wl2(gp, oi, wb9=wb9):
                                src = w2L.ap().rearrange("(kc p) o -> p kc o", p=128)[
                                    :, :, oi * 128:(oi + 1) * 128]
                                gp.dma_start(out=wb9[:, oi % 2, :, :], in_=src).then_inc(SEM["GS"], 16)

                            mst = b.sb(s2, "s8h_mst", [128, 2, 512], F32)

                            def h2(ve, ps, i, oi, tc_, mst=mst):
                                return ve.tensor_scalar(mst[:, i % 2, :], ps[:, :],
                                                        b2L_s[:, oi:oi + 1], None, ALU.add)

                            def h3(gp, ps, i, oi, tc_, mst=mst, tc=tc):
                                return gp.dma_start(
                                    out=d_srcB[oi * 128:(oi + 1) * 128, tc * 512:(tc + 1) * 512],
                                    in_=mst[:, i % 2, :], accum_op=ALU.add)

                            b.proj(f"s8h2_{tc}", FC, 1, HC, [128, 512],
                                   (lambda wb9=wb9: lambda kc, oi, tc_: wb9[:, oi % 2, kc, :])(),
                                   (lambda hst=hst: lambda kc, oi, tc_: hst[:, kc, :])(),
                                   steps=[("ve", h2), ("gp", h3)], wload=(wl2, 1), nps=2)

                ln_tmaj(b, nc, L, "s8i", d_srcB, None, ln2g, ln2b, gbx, bbx,
                        silu_out=True, g1_dst=d_g1)


LN1E4 = float(np.log(10000.0))


def ln_tmaj(b, nc, L, name, src_dram_F, dst_F, grow, brow, gbx, bbx, silu_out, g1_dst):
    (T, D, FC, KT) = (L["T"], L["D"], L["FC"], L["KT"])
    SEM = b.sem
    identF, identB, onesT, cbias = L["identF"], L["identB"], L["onesT"], L["cbias"]
    with ExitStack() as ses:
        xF = b.sb(ses, f"{name}_xF", [128, 2, FC, 128], F32)
        xT = b.sb(ses, f"{name}_xT", [128, 2, D], F32)
        xl = b.sb(ses, f"{name}_xl", [128, 2, D], BF16)
        bn6 = b.sb(ses, f"{name}_bn", [128, 2, 6 * (D // 512)], F32)
        mv = b.sb(ses, f"{name}_mv", [128, 2, 4], F32)
        tps = [b.ps(ses, f"{name}_tp{j}", [128, FC, 128], F32) for j in range(2)]
        tpsb = [b.ps(ses, f"{name}_tpb{j}", [128, FC, 128], BF16) for j in range(2)]
        tpb = b.ps(ses, f"{name}_tb", [128, 512])
        nst = D // 512
        base = b.snap()
        with nc.Block() as blk:
            @blk.sync
            def _(s):
                s.dma_start(out=gbx[0:1, :], in_=grow[:, :]).then_inc(SEM["LS"], 16)
                s.dma_start(out=bbx[0:1, :], in_=brow[:, :]).then_inc(SEM["LS"], 16)
                b.c["LS"] += 32
        ops = []
        for row in (gbx, bbx):
            for c in range(D // 512):
                ops.append(("te", (lambda row=row, c=c: lambda te: te.matmul(
                    tpb[:, :], onesT[0:1, :], row[0:1, c * 512:(c + 1) * 512],
                    start=True, stop=True))(), False))
                ops.append(("ve", (lambda row=row, c=c: lambda ve: ve.tensor_copy(
                    row[:, c * 512:(c + 1) * 512], tpb[:, :]))(), False))
        b.seq(f"{name}_gb", ops)
        waves = []
        for kt in range(KT):
            p = kt % 2
            ops = []
            ops.append(("sy", (lambda kt=kt, p=p: lambda s: s.dma_start(
                out=xF[:, p, :, :],
                in_=src_dram_F.ap().rearrange("(c p) t -> p c t", p=128)[
                    :, :, kt * 128:(kt + 1) * 128]))(), True))
            def trf(te, p=p):
                inst = None
                for fc in range(FC):
                    inst = te.transpose(tps[p][:, fc, :], xF[:, p, fc, :], identF[:, :])
                return inst
            ops.append(("te", trf, False))
            ops.append(("ve", (lambda p=p: lambda ve: ve.tensor_copy(
                xT[:, p, :].rearrange("q (c j) -> q c j", j=128), tps[p][:, :, :]))(), False))

            def bnf(ve, p=p):
                for j in range(nst):
                    ve.bn_stats(bn6[:, p, j * 6:(j + 1) * 6], xT[:, p, j * 512:(j + 1) * 512])
                    ve.drain()
                return ve.bn_aggr(mv[:, p, 0:2], bn6[:, p, 0:6 * nst])
            ops.append(("ve", bnf, False))
            ops.append(("sc", (lambda p=p: lambda sc: sc.activation(
                mv[:, p, 2:3], mv[:, p, 1:2], AF.Sqrt, bias=cbias[:, 0:1]))(), False))

            def appf(ve, p=p):
                ve.reciprocal(mv[:, p, 3:4], mv[:, p, 2:3])
                ve.drain()
                ve.tensor_scalar(xT[:, p, :], xT[:, p, :], mv[:, p, 0:1], mv[:, p, 3:4],
                                 ALU.subtract, ALU.mult)
                ve.drain()
                ve.tensor_tensor(xT[:, p, :], xT[:, p, :], gbx[:, :], ALU.mult)
                ve.drain()
                return ve.tensor_tensor(xT[:, p, :], xT[:, p, :], bbx[:, :], ALU.add)
            ops.append(("ve", appf, False))
            if silu_out:
                ops.append(("sc", (lambda p=p: lambda sc: sc.activation(
                    xl[:, p, :], xT[:, p, :], AF.Silu))(), False))
                ops.append(("sy", (lambda kt=kt, p=p: lambda s: s.dma_start(
                    out=g1_dst[kt * 128:(kt + 1) * 128, :], in_=xl[:, p, :]))(), True))
            else:
                ops.append(("ve", (lambda p=p: lambda ve: ve.tensor_copy(
                    xl[:, p, :], xT[:, p, :]))(), False))
                def trb(te, p=p):
                    inst = None
                    for fc in range(FC):
                        inst = te.transpose(tpsb[p][:, fc, :],
                                            xl[:, p, fc * 128:(fc + 1) * 128], identB[:, :])
                    return inst
                ops.append(("te", trb, False))
                ops.append(("ve", (lambda kt=kt, p=p: lambda ve: ve.tensor_copy(
                    dst_F[:, :, kt * 128:(kt + 1) * 128], tpsb[p][:, :, :]))(), False))
                def trf2(te, p=p):
                    inst = None
                    for fc in range(FC):
                        inst = te.transpose(tps[p][:, fc, :],
                                            xT[:, p, fc * 128:(fc + 1) * 128], identF[:, :])
                    return inst
                ops.append(("te", trf2, False))
                ops.append(("ve", (lambda p=p: lambda ve: ve.tensor_copy(
                    xF[:, p, :, :], tps[p][:, :, :]))(), False))
                ops.append(("sy", (lambda kt=kt, p=p: lambda s: s.dma_start(
                    out=src_dram_F.ap().rearrange("(c p) t -> p c t", p=128)[
                        :, :, kt * 128:(kt + 1) * 128], in_=xF[:, p, :, :]))(), True))
            waves.append(ops)
        b.seqw(name, waves)


def host_layout(inputs, core):
    g = lambda k: np.ascontiguousarray(np.asarray(inputs[k], dtype=np.float32))
    T, D, F, H, E, HD = 2048, 1024, 4096, 16, 8, 64
    m = {}
    m["x"] = np.ascontiguousarray(np.asarray(inputs["L0_attn_x"][core], dtype=np.float32))
    m["wqkv"] = g("L0_attn_Wqkv")
    m["bqkvf"] = np.ascontiguousarray(g("L0_attn_bqkv").reshape(24, 128).T)
    m["wout0"] = g("L0_attn_Wout")
    m["boutf"] = np.ascontiguousarray(g("L0_attn_bout").reshape(8, 128).T)
    m["w1t"] = np.ascontiguousarray(g("L0_ffn_w1").T)
    m["b1f0"] = np.ascontiguousarray(g("L0_ffn_b1").reshape(32, 128).T)
    m["w2t"] = np.ascontiguousarray(g("L0_ffn_w2").T)
    m["b2f0"] = np.ascontiguousarray(g("L0_ffn_b2").reshape(8, 128).T)
    m["wit"] = np.ascontiguousarray(g("L1_attn_in_w").T)
    m["inbf"] = np.ascontiguousarray(g("L1_attn_in_b").reshape(24, 128).T)
    m["wot"] = np.ascontiguousarray(g("L1_attn_out_w").T)
    m["outbf"] = np.ascontiguousarray(g("L1_attn_out_b").reshape(8, 128).T)
    m["wq2"] = np.ascontiguousarray(g("L1_ffn_Wq").transpose(1, 0, 2).reshape(D, D))
    m["wk2"] = np.ascontiguousarray(g("L1_ffn_Wk").transpose(1, 0, 2).reshape(D, D))
    m["wv4"] = np.ascontiguousarray(g("L1_ffn_Wv").transpose(0, 2, 1, 3).reshape(H, D, E * HD))
    m["wo2"] = np.ascontiguousarray(g("L1_ffn_Wo").reshape(H * E * HD, D))
    m["wselv"] = np.ascontiguousarray(g("L1_ffn_Wsel_v").transpose(2, 0, 1).reshape(D, H * E))
    m["wselo"] = np.ascontiguousarray(g("L1_ffn_Wsel_o").transpose(2, 0, 1).reshape(D, H * E))
    m["ln1g"] = g("L1_ffn_ln1_g").reshape(1, D)
    m["ln1b"] = g("L1_ffn_ln1_b").reshape(1, D)
    m["ln2g"] = g("L1_ffn_ln2_g").reshape(1, D)
    m["ln2b"] = g("L1_ffn_ln2_b").reshape(1, D)
    m["w1L"] = g("L1_ffn_W1")
    m["b1Lf"] = np.ascontiguousarray(g("L1_ffn_b1").reshape(32, 128).T)
    m["w2L"] = g("L1_ffn_W2")
    m["b2Lf"] = np.ascontiguousarray(g("L1_ffn_b2").reshape(8, 128).T)
    perm = np.arange(128)
    for bb in (0, 64):
        perm[bb:bb + 16] = np.arange(bb + 16, bb + 32)
        perm[bb + 16:bb + 32] = np.arange(bb, bb + 16)
    Psel = np.zeros((128, 128), np.float32)
    Psel[perm, np.arange(128)] = 1.0
    m["ropeP"] = Psel
    return m


def kernel(**inputs):
    P = default_params()
    nc = build(P)
    in_maps = [host_layout(inputs, c) for c in range(NCORES)]
    res = run_bass_kernel_spmd(nc, in_maps, core_ids=list(range(NCORES)))
    return np.stack([np.asarray(res.results[c]["out"], dtype=np.float32)
                     for c in range(NCORES)], axis=0)
